# version 09
# baseline (speedup 1.0000x reference)
import numpy as np
import ml_dtypes
from contextlib import ExitStack

import concourse.bass as bass
import concourse.mybir as mybir
from concourse.bass_utils import run_bass_kernel_spmd
from concourse.expressions_rust import RuntimeValue as _RV

F32 = mybir.dt.float32
BF16 = mybir.dt.bfloat16
I32 = mybir.dt.int32
AF = mybir.ActivationFunctionType
ALU = mybir.AluOpType
AX = mybir.AxisListType

NEG = -1.0e30
LN_EPS = 1e-5
RMS_EPS = 1.1920929e-07

S = 4096
DM = 1024
NH = 8
DH = 64
HF = NH * DH


ENGS = ("pe", "act", "dve", "pool", "sp")
DQS = ("dpool", "dsp", "dact")
_Q2E = {"dpool": "pool", "dsp": "sp", "dact": "act"}


class Buf:
    __slots__ = ("name", "writer", "readers")

    def __init__(self, name):
        self.name = name
        self.writer = None
        self.readers = []


class Ctx:
    def __init__(self, nc, stack: ExitStack):
        self.nc = nc
        self.stack = stack
        self.eng = {"pe": nc.tensor, "act": nc.scalar, "dve": nc.vector,
                    "pool": nc.gpsimd, "sp": nc.sync}
        self.sems = {}
        for s in ENGS:
            self.sems[s] = stack.enter_context(nc.semaphore(f"S_{s}"))
        self.cnt = {s: 0 for s in ENGS}
        self.ops = {e: [] for e in ENGS}
        self.wm = {e: {} for e in ENGS}
        self._nbuf = 0
        self._consts = {}
        self._ctile = None
        self._cbuf = None

    def constcol(self, value):
        if value in self._consts:
            return self._consts[value]
        if self._ctile is None:
            self._ctile = self.nc.alloc_sbuf_tensor("cconsts", [128, 16], F32).ap()
            self._cbuf = self.buf("consts")
        i = len(self._consts)
        assert i < 16
        ap = self._ctile[:, i:i + 1]
        self.op("dve", lambda E, ap=ap, value=value: E.memset(ap, value),
                writes=(self._cbuf,))
        self._consts[value] = ap
        return ap

    def buf(self, name=None):
        self._nbuf += 1
        return Buf(name or f"b{self._nbuf}")

    def claim(self, new, olds):
        rs = []
        for o in olds:
            if o.writer is not None:
                rs.append(o.writer)
            rs += o.readers
            o.writer = None
            o.readers = []
        new.writer = None
        new.readers = rs

    def _wait(self, eng, src, cnt):
        if self.wm[eng].get(src, 0) >= cnt:
            return
        self.wm[eng][src] = cnt
        sem = self.sems[src]
        self.ops[eng].append(lambda E, sem=sem, cnt=cnt: E.wait_ge(sem, cnt))

    def _pre(self, eng, reads, writes):
        for b in reads:
            if b.writer is not None:
                self._wait(eng, *b.writer)
        for b in writes:
            if b.writer is not None:
                self._wait(eng, *b.writer)
            for r in b.readers:
                self._wait(eng, *r)

    def _post(self, me, reads, writes):
        for b in reads:
            b.readers.append(me)
        for b in writes:
            b.writer = me
            b.readers = []

    def op(self, eng, build, reads=(), writes=()):
        self._pre(eng, reads, writes)
        self.cnt[eng] += 1
        me = (eng, self.cnt[eng])
        sem = self.sems[eng]
        self.ops[eng].append(lambda E, b=build, s=sem: b(E).then_inc(s, 1))
        self._post(me, reads, writes)
        return me

    def group(self, eng, builds, reads=(), writes=()):
        self._pre(eng, reads, writes)
        self.cnt[eng] += 1
        me = (eng, self.cnt[eng])
        sem = self.sems[eng]
        for b in builds[:-1]:
            self.ops[eng].append(b)
        self.ops[eng].append(lambda E, b=builds[-1], s=sem: b(E).then_inc(s, 1))
        self._post(me, reads, writes)
        return me

    DMA_SLOTS = {"dpool": 40, "dsp": 30, "dact": 12}

    def _dma_key(self, q, bkey):
        if not hasattr(self, "_slotmap"):
            self._slotmap = {}
        sk = (q, bkey)
        if sk not in self._slotmap:
            nq = sum(1 for k in self._slotmap if k[0] == q)
            self._slotmap[sk] = nq % self.DMA_SLOTS[q]
        key = f"q:{q}:{self._slotmap[sk]}"
        if key not in self.sems:
            self.sems[key] = self.stack.enter_context(
                self.nc.semaphore(key.replace(":", "_")))
            self.cnt[key] = 0
        return key

    def dma(self, q, build, reads=(), writes=()):
        eng = _Q2E[q]
        assert len(writes) >= 1
        key = self._dma_key(q, writes[0].name)
        self._pre(eng, reads, writes)
        self.cnt[key] += 16
        me = (key, self.cnt[key])
        sem = self.sems[key]
        self.ops[eng].append(lambda E, b=build, s=sem: b(E).then_inc(s, 16))
        self._post(me, reads, writes)
        return me

    def dma_cond(self, q, build2, reads=(), writes=()):
        eng = _Q2E[q]
        assert eng == "pool"
        assert len(writes) >= 1
        key = self._dma_key(q, writes[0].name)
        self._pre(eng, reads, writes)
        self.cnt[key] += 16
        me = (key, self.cnt[key])
        sem = self.sems[key]

        def f(E, b2=build2, s=sem):
            with E.If_eq(self.hh_handle, 0):
                b2(E, 0).then_inc(s, 16)
            with E.Else():
                b2(E, 1).then_inc(s, 16)
        self.ops[eng].append(f)
        self._post(me, reads, writes)
        return me

    def raw(self, eng, fn):
        self.ops[eng].append(fn)

    def barrier_all(self, eng="sp"):
        for s in list(self.cnt):
            if s != eng and self.cnt[s] > 0:
                self._wait(eng, s, self.cnt[s])

    def run(self):
        nc = self.nc
        with nc.Block() as blk:
            for ename, dec in (("pe", blk.tensor), ("act", blk.scalar),
                               ("dve", blk.vector), ("pool", blk.gpsimd),
                               ("sp", blk.sync)):
                if self.ops[ename]:
                    def body(E, fns=self.ops[ename]):
                        for f in fns:
                            try:
                                f(E)
                            except Exception:
                                import traceback
                                traceback.print_exc()
                                raise
                    dec(body)


def e_load(ctx, dst_ap, src_ap, dstbuf, srcbufs=(), q="dpool"):
    return ctx.dma(q, lambda E: E.dma_start(out=dst_ap, in_=src_ap),
                   reads=srcbufs, writes=(dstbuf,))


def e_store(ctx, dst_ap, src_ap, srcbuf, dstbufs=(), q="dsp"):
    return ctx.dma(q, lambda E: E.dma_start(out=dst_ap, in_=src_ap),
                   reads=(srcbuf,), writes=dstbufs)


class LNScr:

    def __init__(self, ctx, nc, tag=""):
        al = nc.alloc_sbuf_tensor
        self.sq = [al(f"lnsq{tag}{i}", [128, 512], F32).ap() for i in range(2)]
        self.rows = al(f"lnrows{tag}", [1, 512 * 4], F32).ap()
        self.browA = al(f"lnbrA{tag}", [1, 512], BF16).ap()
        self.browB = al(f"lnbrB{tag}", [1, 512], BF16).ap()
        self.bA = al(f"lnbA{tag}", [128, 512], F32).ap()
        self.bB = al(f"lnbB{tag}", [128, 512], F32).ap()
        self.b_sq = [ctx.buf(), ctx.buf()]
        self.b_rows = ctx.buf()
        self.b_brow = ctx.buf()
        self.b_bc = ctx.buf()


def ln_block(ctx, scr, in_tiles, in_bufs, out_writes, T, eps, ones_col, ones_row,
             b_ones, bankA, bankB, b_bankA, b_bankB, nfeat=DM):
    nchunk = nfeat // 128
    for dc in range(nchunk):
        ctx.op("act",
               lambda E, dc=dc: E.activation(scr.sq[dc % 2][:, 0:T], in_tiles[dc],
                                             AF.Square),
               reads=(in_bufs[dc],), writes=(scr.b_sq[dc % 2],))
        ctx.group("pe",
                  [lambda E, dc=dc: E.matmul(bankA[0:1, 0:T], ones_col,
                                             in_tiles[dc],
                                             start=(dc == 0), stop=(dc == nchunk - 1))],
                  reads=(in_bufs[dc], b_ones), writes=(b_bankA,))
        ctx.group("pe",
                  [lambda E, dc=dc: E.matmul(bankB[0:1, 0:T], ones_col,
                                             scr.sq[dc % 2][:, 0:T],
                                             start=(dc == 0), stop=(dc == nchunk - 1))],
                  reads=(scr.b_sq[dc % 2], b_ones), writes=(b_bankB,))
    inv = 1.0 / nfeat
    ctx.op("dve", lambda E: E.tensor_scalar(scr.rows[:, 0:T], bankA[0:1, 0:T],
                                            inv, None, ALU.mult),
           reads=(b_bankA,), writes=(scr.b_rows,))
    ctx.op("act", lambda E: E.activation(scr.rows[:, 512:512 + T],
                                         scr.rows[:, 0:T], AF.Square),
           reads=(scr.b_rows,), writes=(scr.b_rows,))
    ctx.op("dve", lambda E: E.tensor_scalar(scr.rows[:, 1024:1024 + T],
                                            bankB[0:1, 0:T], inv, None, ALU.mult),
           reads=(b_bankB,), writes=(scr.b_rows,))
    ctx.op("dve", lambda E: E.tensor_tensor(scr.rows[:, 1024:1024 + T],
                                            scr.rows[:, 1024:1024 + T],
                                            scr.rows[:, 512:512 + T], ALU.subtract),
           reads=(scr.b_rows,), writes=(scr.b_rows,))
    c_eps = ctx.constcol(eps)
    ctx.op("act", lambda E: E.activation(scr.rows[:, 1024:1024 + T],
                                         scr.rows[:, 1024:1024 + T], AF.Ln,
                                         bias=c_eps[0:1, :]),
           reads=(scr.b_rows, ctx._cbuf), writes=(scr.b_rows,))
    ctx.op("act", lambda E: E.activation(scr.browA[:, 0:T],
                                         scr.rows[:, 1024:1024 + T], AF.Exp,
                                         scale=-0.5),
           reads=(scr.b_rows,), writes=(scr.b_brow,))
    ctx.op("dve", lambda E: E.tensor_copy(scr.browB[:, 0:T], scr.rows[:, 0:T]),
           reads=(scr.b_rows,), writes=(scr.b_brow,))
    ctx.op("dve", lambda E: E.tensor_tensor(scr.browB[:, 0:T], scr.browB[:, 0:T],
                                            scr.browA[:, 0:T], ALU.mult),
           reads=(scr.b_brow,), writes=(scr.b_brow,))
    ctx.group("pe", [lambda E: E.matmul(bankA[:, 0:T], ones_row, scr.browA[:, 0:T],
                                        start=True, stop=True)],
              reads=(scr.b_brow, b_ones), writes=(b_bankA,))
    ctx.group("pe", [lambda E: E.matmul(bankB[:, 0:T], ones_row, scr.browB[:, 0:T],
                                        start=True, stop=True)],
              reads=(scr.b_brow, b_ones), writes=(b_bankB,))
    ctx.op("dve", lambda E: E.tensor_copy(scr.bA[:, 0:T], bankA[:, 0:T]),
           reads=(b_bankA,), writes=(scr.b_bc,))
    ctx.op("dve", lambda E: E.tensor_copy(scr.bB[:, 0:T], bankB[:, 0:T]),
           reads=(b_bankA, b_bankB), writes=(scr.b_bc,))
    for fn in out_writes:
        fn(scr.bA[:, 0:T], scr.bB[:, 0:T], scr.b_bc)



def build_launchA():
    nc = bass.Bass("TRN2", target_bir_lowering=False)
    xT = nc.dram_tensor("xT", [DM, S], F32, kind="ExternalInput").ap()
    wqT = nc.dram_tensor("wqT", [DM, HF], F32, kind="ExternalInput").ap()
    wkT = nc.dram_tensor("wkT", [DM, HF], F32, kind="ExternalInput").ap()
    wvT = nc.dram_tensor("wvT", [DM, HF], F32, kind="ExternalInput").ap()
    bq = nc.dram_tensor("bq", [128, 4], F32, kind="ExternalInput").ap()
    bk = nc.dram_tensor("bk", [128, 4], F32, kind="ExternalInput").ap()
    bv = nc.dram_tensor("bv", [1, HF], F32, kind="ExternalInput").ap()
    masks = nc.dram_tensor("masks", [128, 4 * 512], F32, kind="ExternalInput").ap()
    zT_out = nc.dram_tensor("zT", [HF, S], F32, kind="ExternalOutput").ap()
    xn_d = nc.dram_tensor("xn_d", [DM, S], BF16, kind="Internal").ap()

    with ExitStack() as st:
        ctx = Ctx(nc, st)
        al = nc.alloc_sbuf_tensor

        arena0 = al("arena0", [128, 8192], F32)
        xtile = [arena0.ap()[:, 0:4096], arena0.ap()[:, 4096:8192]]
        V = arena0.ap().bitcast(BF16)
        sq = [al(f"sq{i}", [128, 512], BF16).ap() for i in range(2)]
        rstdrow = al("rstdrow", [1, S], BF16).ap()
        rstdB = al("rstdB", [128, S], F32).ap()
        xnc = [al(f"xnc{i}", [128, S], BF16).ap() for i in range(2)]
        wq_s = al("wq_s", [128, 8 * HF], BF16).ap()
        wk_s = al("wk_s", [128, 8 * HF], BF16).ap()
        wv_s = al("wv_s", [128, 8 * HF], BF16).ap()
        bq_s = al("bq_s", [128, 4], F32).ap()
        bk_s = al("bk_s", [128, 4], F32).ap()
        bv_row = al("bv_row", [1, HF], BF16).ap()
        bvB = al("bvB", [128, HF], F32).ap()
        ones_col = al("ones_col", [128, 1], BF16).ap()
        ones_row = al("ones_row", [1, 128], BF16).ap()
        mask_s = al("mask_s", [128, 4 * 512], F32).ap()
        QT = al("QT", [128, 4 * S], BF16).ap()
        KT = al("KT", [128, 4 * S], BF16).ap()
        m_s = al("m_s", [128, NH * 32], F32).ap()
        s_s = al("s_s", [128, NH * 32], F32).ap()
        nb_s = al("nb_s", [128, NH * 32], F32).ap()
        expsc = al("expsc", [128, 1024], BF16).ap()
        alpha = al("alpha", [128, 1024], BF16).ap()
        scr1 = al("scr1", [128, 1], F32).ap()
        zseg = [al(f"zseg{i}", [64, 512], F32).ap() for i in range(2)]

        PS = nc.alloc_psum_tensor("PS", [128, 4096], F32).ap()

        def bank(i, rows=128):
            return PS[0:rows, i * 512:(i + 1) * 512]

        def bank2(i):
            return PS[:, i * 512:(i + 2) * 512]

        b_xtile = [ctx.buf("xt0"), ctx.buf("xt1")]
        b_sq = [ctx.buf(), ctx.buf()]
        b_rstdrow = ctx.buf(); b_rstdB = ctx.buf()
        b_xnc = [ctx.buf("xnc0"), ctx.buf("xnc1")]
        b_xnd = ctx.buf("xn_dram")
        b_w = {k: ctx.buf() for k in ("q", "k", "v")}
        b_bq = ctx.buf(); b_bk = ctx.buf(); b_bvrow = ctx.buf(); b_bvB = ctx.buf()
        b_ones = ctx.buf()
        b_mask = ctx.buf()
        b_QT = ctx.buf(); b_KT = ctx.buf(); b_V = ctx.buf("V")
        b_m = ctx.buf(); b_ssum = ctx.buf(); b_nb = ctx.buf()
        b_expsc = ctx.buf(); b_alpha = ctx.buf()
        b_scr1 = ctx.buf()
        b_zseg = [ctx.buf(), ctx.buf()]
        b_zout = ctx.buf("zout")
        b_bank = [ctx.buf(f"bank{i}") for i in range(8)]

        ctx.op("dve", lambda E: E.memset(ones_col[:], 1.0), writes=(b_ones,))
        ctx.op("dve", lambda E: E.memset(ones_row[:], 1.0), writes=(b_ones,))
        e_load(ctx, mask_s[:], masks[:], b_mask)
        e_load(ctx, bq_s[:], bq[:], b_bq)
        e_load(ctx, bk_s[:], bk[:], b_bk)
        e_load(ctx, bv_row[:], bv[:], b_bvrow)
        e_load(ctx, wq_s.rearrange("p (c f) -> p c f", c=8),
               wqT.rearrange("(c p) f -> p c f", p=128), b_w["q"])
        e_load(ctx, wk_s.rearrange("p (c f) -> p c f", c=8),
               wkT.rearrange("(c p) f -> p c f", p=128), b_w["k"])
        e_load(ctx, wv_s.rearrange("p (c f) -> p c f", c=8),
               wvT.rearrange("(c p) f -> p c f", p=128), b_w["v"])

        ctx.group("pe", [lambda E: E.matmul(bank(1), ones_row[:], bv_row[:],
                                            start=True, stop=True)],
                  reads=(b_bvrow, b_ones), writes=(b_bank[1],))
        ctx.op("dve", lambda E: E.tensor_copy(bvB[:], bank(1)),
               reads=(b_bank[1],), writes=(b_bvB,))

        xTr = xT.rearrange("(c p) t -> c p t", p=128)
        for dc in range(8):
            e_load(ctx, xtile[dc % 2], xTr[dc], b_xtile[dc % 2])
            for tt in range(8):
                ctx.op("act",
                       lambda E, dc=dc, tt=tt: E.activation(
                           sq[tt % 2], xtile[dc % 2][:, tt * 512:(tt + 1) * 512],
                           AF.Square),
                       reads=(b_xtile[dc % 2],), writes=(b_sq[tt % 2],))
                ctx.group("pe",
                          [lambda E, dc=dc, tt=tt: E.matmul(
                              bank(tt, 1), ones_col[:], sq[tt % 2],
                              start=(dc == 0), stop=(dc == 7))],
                          reads=(b_sq[tt % 2], b_ones), writes=(b_bank[tt],))
        c_eps = ctx.constcol(RMS_EPS)
        for tt in range(8):
            ctx.op("act",
                   lambda E, tt=tt: E.activation(
                       rstdrow[:, tt * 512:(tt + 1) * 512], bank(tt, 1),
                       AF.Ln, bias=c_eps[0:1, :], scale=1.0 / DM),
                   reads=(b_bank[tt], ctx._cbuf), writes=(b_rstdrow,))
            ctx.op("act",
                   lambda E, tt=tt: E.activation(
                       rstdrow[:, tt * 512:(tt + 1) * 512],
                       rstdrow[:, tt * 512:(tt + 1) * 512],
                       AF.Exp, scale=-0.5),
                   reads=(b_rstdrow,), writes=(b_rstdrow,))
            ctx.group("pe",
                      [lambda E, tt=tt: E.matmul(
                          bank(tt), ones_row[:],
                          rstdrow[:, tt * 512:(tt + 1) * 512],
                          start=True, stop=True)],
                      reads=(b_rstdrow, b_ones), writes=(b_bank[tt],))
            ctx.op("dve",
                   lambda E, tt=tt: E.tensor_copy(
                       rstdB[:, tt * 512:(tt + 1) * 512], bank(tt)),
                   reads=(b_bank[tt],), writes=(b_rstdB,))

        xn_dr = xn_d.rearrange("(c p) t -> c p t", p=128)
        for dc in range(8):
            e_load(ctx, xtile[dc % 2], xTr[dc], b_xtile[dc % 2])
            ctx.op("dve",
                   lambda E, dc=dc: E.tensor_tensor(
                       xnc[dc % 2], xtile[dc % 2], rstdB[:], ALU.mult),
                   reads=(b_xtile[dc % 2], b_rstdB), writes=(b_xnc[dc % 2],))
            e_store(ctx, xn_dr[dc], xnc[dc % 2], b_xnc[dc % 2], dstbufs=(b_xnd,))

        for tt in range(8):
            for dc in range(8):
                xb = b_xnc[dc % 2]
                e_load(ctx, xnc[dc % 2][:, 0:512],
                       xn_dr[dc][:, tt * 512:(tt + 1) * 512], xb,
                       srcbufs=(b_xnd,))
                for fc in range(4):
                    ctx.group("pe",
                              [(lambda E, fc=fc, tt=tt, dc=dc: E.matmul(
                                  bank(fc),
                                  wq_s[:, dc * HF + fc * 128: dc * HF + fc * 128 + 128],
                                  xnc[dc % 2][:, 0:512],
                                  start=(dc == 0), stop=(dc == 7)))],
                              reads=(b_w["q"], xb), writes=(b_bank[fc],))
                    ctx.group("pe",
                              [(lambda E, fc=fc, tt=tt, dc=dc: E.matmul(
                                  bank(4 + fc),
                                  wk_s[:, dc * HF + fc * 128: dc * HF + fc * 128 + 128],
                                  xnc[dc % 2][:, 0:512],
                                  start=(dc == 0), stop=(dc == 7)))],
                              reads=(b_w["k"], xb), writes=(b_bank[4 + fc],))
            for fc in range(4):
                ctx.op("act",
                       lambda E, fc=fc, tt=tt: E.activation(
                           QT[:, fc * S + tt * 512: fc * S + (tt + 1) * 512],
                           bank(fc), AF.Identity, bias=bq_s[:, fc:fc + 1]),
                       reads=(b_bank[fc], b_bq), writes=(b_QT,))
                ctx.op("act",
                       lambda E, fc=fc, tt=tt: E.activation(
                           KT[:, fc * S + tt * 512: fc * S + (tt + 1) * 512],
                           bank(4 + fc), AF.Identity, bias=bk_s[:, fc:fc + 1]),
                       reads=(b_bank[4 + fc], b_bk), writes=(b_KT,))
        ctx.claim(b_V, b_xtile)
        for tt in range(8):
            for dc in range(8):
                xb = b_xnc[dc % 2]
                e_load(ctx, xnc[dc % 2][:, 0:512],
                       xn_dr[dc][:, tt * 512:(tt + 1) * 512], xb,
                       srcbufs=(b_xnd,))
                for sub in range(4):
                    ctx.group("pe",
                              [(lambda E, dc=dc, sub=sub: E.matmul(
                                  bank(sub),
                                  xnc[dc % 2][:, sub * 128:(sub + 1) * 128],
                                  wv_s[:, dc * HF:(dc + 1) * HF],
                                  start=(dc == 0), stop=(dc == 7)))],
                              reads=(b_w["v"], xb), writes=(b_bank[sub],))
            for sub in range(4):
                ti = tt * 4 + sub
                ctx.op("dve",
                       lambda E, ti=ti, sub=sub: E.tensor_tensor(
                           V[:, ti * HF:(ti + 1) * HF], bank(sub), bvB[:], ALU.add),
                       reads=(b_bank[sub], b_bvB), writes=(b_V,))

        def head_slices(h):
            return h // 2, (h % 2) * 64

        def s_matmul(E, h, kt, qt, pb):
            fc, p0 = head_slices(h)
            return E.matmul(
                bank(pb),
                KT[p0:p0 + 64, fc * S + kt * 128: fc * S + kt * 128 + 128],
                QT[p0:p0 + 64, fc * S + qt * 512: fc * S + (qt + 1) * 512],
                start=True, stop=True)

        def mask_add(E, kt, qt, pb):
            mi = (kt * 128 - qt * 512) // 128
            return E.tensor_tensor(bank(pb), bank(pb),
                                   mask_s[:, mi * 512:(mi + 1) * 512], ALU.add)

        def emit_score(h, kt, qt, pb):
            ctx.group("pe", [lambda E: s_matmul(E, h, kt, qt, pb)],
                      reads=(b_KT, b_QT), writes=(b_bank[pb],))
            if 0 <= kt * 128 - qt * 512 <= 384:
                ctx.op("dve", lambda E: mask_add(E, kt, qt, pb),
                       reads=(b_bank[pb], b_mask), writes=(b_bank[pb],))

        for h in range(NH):
            for kt in range(32):
                col = h * 32 + kt
                qts = [qt for qt in range(8) if kt * 128 - qt * 512 <= 384]
                first = True
                for i in range(0, len(qts), 2):
                    pair = qts[i:i + 2]
                    for pb, qt in enumerate(pair):
                        emit_score(h, kt, qt, pb)
                    wid = len(pair) * 512
                    rb = (b_bank[0], b_bank[1]) if len(pair) == 2 else (b_bank[0],)
                    ctx.op("dve",
                           lambda E, wid=wid: E.reduce_max(
                               scr1, PS[:, 0:wid], axis=AX.X),
                           reads=rb, writes=(b_scr1,))
                    if first:
                        ctx.op("dve",
                               lambda E, col=col: E.tensor_copy(
                                   m_s[:, col:col + 1], scr1),
                               reads=(b_scr1,), writes=(b_m,))
                        first = False
                    else:
                        ctx.op("dve",
                               lambda E, col=col: E.tensor_tensor(
                                   m_s[:, col:col + 1], m_s[:, col:col + 1],
                                   scr1, ALU.max),
                               reads=(b_scr1, b_m), writes=(b_m,))
            for kt in range(32):
                col = h * 32 + kt
                qts = [qt for qt in range(8) if kt * 128 - qt * 512 <= 384]
                ctx.op("dve",
                       lambda E, col=col: E.tensor_scalar(
                           nb_s[:, col:col + 1], m_s[:, col:col + 1], -1.0,
                           None, ALU.mult),
                       reads=(b_m,), writes=(b_nb,))
                first = True
                for i in range(0, len(qts), 2):
                    pair = qts[i:i + 2]
                    for pb, qt in enumerate(pair):
                        emit_score(h, kt, qt, pb)
                    wid = len(pair) * 512
                    rb = (b_bank[0], b_bank[1]) if len(pair) == 2 else (b_bank[0],)
                    ctx.op("act",
                           lambda E, wid=wid, col=col: E.activation(
                               expsc[:, 0:wid], PS[:, 0:wid], AF.Exp,
                               bias=nb_s[:, col:col + 1], accum_out=scr1),
                           reads=rb + (b_nb,), writes=(b_expsc, b_scr1))
                    if first:
                        ctx.op("dve",
                               lambda E, col=col: E.tensor_copy(
                                   s_s[:, col:col + 1], scr1),
                               reads=(b_scr1,), writes=(b_ssum,))
                        first = False
                    else:
                        ctx.op("dve",
                               lambda E, col=col: E.tensor_tensor(
                                   s_s[:, col:col + 1], s_s[:, col:col + 1],
                                   scr1, ALU.add),
                               reads=(b_scr1, b_ssum), writes=(b_ssum,))
                ctx.op("act",
                       lambda E, col=col: E.activation(
                           scr1, s_s[:, col:col + 1], AF.Ln),
                       reads=(b_ssum,), writes=(b_scr1,))
                ctx.op("dve",
                       lambda E, col=col: E.tensor_tensor(
                           scr1, scr1, m_s[:, col:col + 1], ALU.add),
                       reads=(b_scr1, b_m), writes=(b_scr1,))
                ctx.op("dve",
                       lambda E, col=col: E.tensor_scalar(
                           nb_s[:, col:col + 1], scr1, -1.0, None, ALU.mult),
                       reads=(b_scr1,), writes=(b_nb,))
            for qt in range(8):
                kts = [kt for kt in range(32) if kt * 128 - qt * 512 <= 384]
                zb = 2 + qt % 2
                nlive = len(kts)
                for j, kt in enumerate(kts):
                    col = h * 32 + kt
                    spb = j % 2
                    emit_score(h, kt, qt, spb)
                    ctx.op("act",
                           lambda E, col=col, spb=spb: E.activation(
                               alpha[:, spb * 512:(spb + 1) * 512],
                               bank(spb), AF.Exp, bias=nb_s[:, col:col + 1]),
                           reads=(b_bank[spb], b_nb), writes=(b_alpha,))
                    ctx.group("pe",
                              [lambda E, h=h, kt=kt, j=j, nlive=nlive, zb=zb, spb=spb:
                               E.matmul(bank(zb, 64),
                                        V[:, kt * HF + h * 64: kt * HF + (h + 1) * 64],
                                        alpha[:, spb * 512:(spb + 1) * 512],
                                        start=(j == 0), stop=(j == nlive - 1))],
                              reads=(b_V, b_alpha), writes=(b_bank[zb],))
                sb = qt % 2
                ctx.op("dve",
                       lambda E, zb=zb, sb=sb: E.tensor_copy(zseg[sb], bank(zb, 64)),
                       reads=(b_bank[zb],), writes=(b_zseg[sb],))
                e_store(ctx, zT_out[h * 64:(h + 1) * 64, qt * 512:(qt + 1) * 512],
                        zseg[sb], b_zseg[sb], dstbufs=(b_zout,))

        ctx.barrier_all("sp")
        ctx.run()
    return nc



TB = 2048


def build_launchB():
    nc = bass.Bass("TRN2", target_bir_lowering=False)
    xT_s = nc.dram_tensor("xT_s", [DM, TB], F32, kind="ExternalInput").ap()
    zT_s = nc.dram_tensor("zT_s", [DM, TB], F32, kind="ExternalInput").ap()
    woT = nc.dram_tensor("woT", [DM, DM], F32, kind="ExternalInput").ap()
    bo = nc.dram_tensor("bo", [128, 8], F32, kind="ExternalInput").ap()
    w1T = nc.dram_tensor("w1T", [DM, 4096], F32, kind="ExternalInput").ap()
    w2T = nc.dram_tensor("w2T", [DM, 4096], F32, kind="ExternalInput").ap()
    h1T = nc.dram_tensor("h1T", [4096, 4096], F32, kind="ExternalInput").ap()
    h2T = nc.dram_tensor("h2T", [4096, 4096], F32, kind="ExternalInput").ap()
    w3T = nc.dram_tensor("w3T", [4096, DM], F32, kind="ExternalInput").ap()
    s1 = nc.dram_tensor("s1", [128, 32], F32, kind="ExternalInput").ap()
    s2 = nc.dram_tensor("s2", [128, 32], F32, kind="ExternalInput").ap()
    sh1 = nc.dram_tensor("sh1", [128, 32], F32, kind="ExternalInput").ap()
    sh2 = nc.dram_tensor("sh2", [128, 32], F32, kind="ExternalInput").ap()
    s3 = nc.dram_tensor("s3", [128, 8], F32, kind="ExternalInput").ap()
    outT = nc.dram_tensor("outT", [DM, TB], F32, kind="ExternalOutput").ap()
    x1_d = nc.dram_tensor("x1_d", [DM, TB], F32, kind="Internal").ap()
    x1b_d = nc.dram_tensor("x1b_d", [DM, TB], BF16, kind="Internal").ap()

    with ExitStack() as st:
        ctx = Ctx(nc, st)
        al = nc.alloc_sbuf_tensor

        arena1 = al("arena1", [128, 8192], F32)
        zbuf = arena1.ap()[:, 0:2048].bitcast(BF16)
        xbuf = arena1.ap()[:, 4096:8192]
        x1buf = xbuf
        xin = arena1.ap()[:, 0:4096].bitcast(BF16)
        x2buf = arena1.ap()[:, 4096:8192]

        hA = al("hA", [128, 32 * 1024], BF16).ap()
        hB = al("hB", [128, 32 * 1024], BF16).ap()
        wbuf = [al(f"wbuf{i}", [128, 4096], BF16).ap() for i in range(2)]
        wobuf = al("wobuf", [128, 8 * 128], BF16).ap()
        rbuf = [al(f"rbuf{i}", [128, 512], F32).ap() for i in range(2)]
        t1 = [al(f"t1_{i}", [128, 512], BF16).ap() for i in range(2)]
        bo_s = al("bo_s", [128, 8], F32).ap()
        s1_s = al("s1_s", [128, 32], F32).ap()
        s2_s = al("s2_s", [128, 32], F32).ap()
        sh1_s = al("sh1_s", [128, 32], F32).ap()
        sh2_s = al("sh2_s", [128, 32], F32).ap()
        s3_s = al("s3_s", [128, 8], F32).ap()
        ones_col = al("ones_col", [128, 1], F32).ap()
        ones_row = al("ones_row", [1, 128], BF16).ap()
        scr = LNScr(ctx, nc)

        PS = nc.alloc_psum_tensor("PS", [128, 4096], F32).ap()

        def bank(i, rows=128):
            return PS[0:rows, i * 512:(i + 1) * 512]

        b_z = ctx.buf("zb"); b_x = ctx.buf("xb"); b_x1 = ctx.buf("x1b")
        b_xin = ctx.buf("xin"); b_x2 = ctx.buf("x2b")
        b_hA = ctx.buf("hA"); b_hB = ctx.buf("hB")
        b_w = [ctx.buf("w0"), ctx.buf("w1")]
        b_wo = ctx.buf("wo")
        b_r = [ctx.buf(), ctx.buf()]
        b_t1 = [ctx.buf(), ctx.buf()]
        b_sc = {k: ctx.buf(k) for k in ("bo", "s1", "s2", "sh1", "sh2", "s3")}
        b_ones = ctx.buf()
        b_bank = [ctx.buf(f"bank{i}") for i in range(8)]
        b_x1d = ctx.buf("x1_dram"); b_x1bd = ctx.buf("x1b_dram")
        b_out = ctx.buf("outd")

        ctx.op("dve", lambda E: E.memset(ones_col[:], 1.0), writes=(b_ones,))
        ctx.op("dve", lambda E: E.memset(ones_row[:], 1.0), writes=(b_ones,))
        e_load(ctx, bo_s[:], bo, b_sc["bo"])
        e_load(ctx, s1_s[:], s1, b_sc["s1"])
        e_load(ctx, s2_s[:], s2, b_sc["s2"])
        e_load(ctx, sh1_s[:], sh1, b_sc["sh1"])
        e_load(ctx, sh2_s[:], sh2, b_sc["sh2"])
        e_load(ctx, s3_s[:], s3, b_sc["s3"])

        zTr = zT_s.rearrange("(c p) t -> c p t", p=128)
        xTr = xT_s.rearrange("(c p) t -> c p t", p=128)
        x1dr = x1_d.rearrange("(c p) t -> c p t", p=128)
        x1bdr = x1b_d.rearrange("(c p) t -> c p t", p=128)
        outTr = outT.rearrange("(c p) t -> c p t", p=128)

        for tt in range(4):
            tsl = slice(tt * 512, (tt + 1) * 512)
            e_load(ctx, zbuf.rearrange("p (c t) -> p c t", c=8),
                   zTr[:, :, tsl].rearrange("c p t -> p c t"), b_z)
            e_load(ctx, xbuf.rearrange("p (c t) -> p c t", c=8),
                   xTr[:, :, tsl].rearrange("c p t -> p c t"), b_x)
            for fc in range(8):
                e_load(ctx, wobuf.rearrange("p (c f) -> p c f", c=8),
                       woT[:, fc * 128:(fc + 1) * 128].rearrange(
                           "(c p) f -> p c f", p=128), b_wo)
                pb = fc % 2
                ctx.group("pe",
                          [(lambda E, dc=dc, pb=pb: E.matmul(
                              bank(pb), wobuf[:, dc * 128:(dc + 1) * 128],
                              zbuf[:, dc * 512:(dc + 1) * 512],
                              start=(dc == 0), stop=(dc == 7)))
                           for dc in range(8)],
                          reads=(b_wo, b_z), writes=(b_bank[pb],))
                ctx.op("dve",
                       lambda E, fc=fc, pb=pb: E.scalar_tensor_tensor(
                           x1buf[:, fc * 512:(fc + 1) * 512], bank(pb),
                           bo_s[:, fc:fc + 1], xbuf[:, fc * 512:(fc + 1) * 512],
                           ALU.add, ALU.add),
                       reads=(b_bank[pb], b_sc["bo"], b_x), writes=(b_x,))
            x1tiles = [x1buf[:, fc * 512:(fc + 1) * 512] for fc in range(8)]

            def apply1(bcA, bcB, b_bc, tt=tt, x1tiles=x1tiles):
                for fc in range(8):
                    ctx.op("dve",
                           lambda E, fc=fc, bcA=bcA: E.tensor_tensor(
                               x1tiles[fc], x1tiles[fc], bcA, ALU.mult),
                           reads=(b_x, b_bc), writes=(b_x,))
                    ctx.op("dve",
                           lambda E, fc=fc, bcB=bcB: E.tensor_tensor(
                               x1tiles[fc], x1tiles[fc], bcB, ALU.subtract),
                           reads=(b_x, b_bc), writes=(b_x,))
                    ctx.op("dve",
                           lambda E, fc=fc: E.tensor_copy(
                               t1[fc % 2][:, 0:512], x1tiles[fc]),
                           reads=(b_x,), writes=(b_t1[fc % 2],))
                    e_store(ctx, x1dr[fc, :, tsl], x1tiles[fc], b_x,
                            dstbufs=(b_x1d,))
                    e_store(ctx, x1bdr[fc, :, tsl], t1[fc % 2][:, 0:512],
                            b_t1[fc % 2], dstbufs=(b_x1bd,), q="dact")

            ln_block(ctx, scr, x1tiles, [b_x] * 8, [apply1], 512, LN_EPS,
                     ones_col[:], ones_row[:], b_ones, bank(2), bank(3),
                     b_bank[2], b_bank[3])

        ctx.claim(b_xin, [b_z])
        ctx.claim(b_x2, [b_x])
        for th in range(2):
            hsl = slice(th * 1024, (th + 1) * 1024)
            e_load(ctx, xin.rearrange("p (c t) -> p c t", c=8),
                   x1bdr[:, :, hsl].rearrange("c p t -> p c t"), b_xin,
                   srcbufs=(b_x1bd,))
            for ft in range(32):
                wb = ft % 2
                e_load(ctx, wbuf[wb][:, 0:1024].rearrange("p (c f) -> p c f", c=8),
                       w1T[:, ft * 128:(ft + 1) * 128].rearrange(
                           "(c p) f -> p c f", p=128), b_w[wb])
                e_load(ctx, wbuf[wb][:, 1024:2048].rearrange("p (c f) -> p c f", c=8),
                       w2T[:, ft * 128:(ft + 1) * 128].rearrange(
                           "(c p) f -> p c f", p=128), b_w[wb])
                for sub in range(2):
                    ssl = slice(sub * 512, (sub + 1) * 512)
                    ctx.group("pe",
                              [(lambda E, dc=dc, wb=wb, sub=sub: E.matmul(
                                  bank(sub), wbuf[wb][:, dc * 128:(dc + 1) * 128],
                                  xin[:, dc * 1024 + sub * 512: dc * 1024 + (sub + 1) * 512],
                                  start=(dc == 0), stop=(dc == 7)))
                               for dc in range(8)],
                              reads=(b_w[wb], b_xin), writes=(b_bank[sub],))
                    ctx.group("pe",
                              [(lambda E, dc=dc, wb=wb, sub=sub: E.matmul(
                                  bank(2 + sub),
                                  wbuf[wb][:, 1024 + dc * 128: 1024 + (dc + 1) * 128],
                                  xin[:, dc * 1024 + sub * 512: dc * 1024 + (sub + 1) * 512],
                                  start=(dc == 0), stop=(dc == 7)))
                               for dc in range(8)],
                              reads=(b_w[wb], b_xin), writes=(b_bank[2 + sub],))
                    ctx.op("act",
                           lambda E, ft=ft, sub=sub: E.activation(
                               t1[sub][:, 0:512], bank(sub), AF.Silu,
                               scale=s1_s[:, ft:ft + 1]),
                           reads=(b_bank[sub], b_sc["s1"]), writes=(b_t1[sub],))
                    ctx.op("dve",
                           lambda E, ft=ft, sub=sub: E.scalar_tensor_tensor(
                               hA[:, ft * 1024 + sub * 512: ft * 1024 + (sub + 1) * 512],
                               bank(2 + sub), s2_s[:, ft:ft + 1], t1[sub][:, 0:512],
                               ALU.mult, ALU.mult),
                           reads=(b_bank[2 + sub], b_sc["s2"], b_t1[sub]),
                           writes=(b_hA,))
            for (hTd, ssc, bsc, src, bsrc, dst, bdst) in (
                    (h1T, sh1_s, "sh1", hA, b_hA, hB, b_hB),
                    (h2T, sh2_s, "sh2", hB, b_hB, hA, b_hA)):
                for ft in range(32):
                    wb = ft % 2
                    e_load(ctx, wbuf[wb].rearrange("p (c f) -> p c f", c=32),
                           hTd[:, ft * 128:(ft + 1) * 128].rearrange(
                               "(c p) f -> p c f", p=128), b_w[wb])
                    for sub in range(2):
                        pb = 2 * (ft % 2) + sub
                        ctx.group("pe",
                                  [(lambda E, dc=dc, wb=wb, sub=sub, src=src, pb=pb:
                                    E.matmul(
                                        bank(pb), wbuf[wb][:, dc * 128:(dc + 1) * 128],
                                        src[:, dc * 1024 + sub * 512: dc * 1024 + (sub + 1) * 512],
                                        start=(dc == 0), stop=(dc == 31)))
                                   for dc in range(32)],
                                  reads=(b_w[wb], bsrc), writes=(b_bank[pb],))
                        ctx.op("act",
                               lambda E, ft=ft, sub=sub, ssc=ssc, dst=dst, pb=pb:
                               E.activation(
                                   dst[:, ft * 1024 + sub * 512: ft * 1024 + (sub + 1) * 512],
                                   bank(pb), AF.Gelu, scale=ssc[:, ft:ft + 1]),
                               reads=(b_bank[pb], b_sc[bsc]), writes=(bdst,))
            for sub in range(2):
                gsl = slice(th * 1024 + sub * 512, th * 1024 + (sub + 1) * 512)
                for fc in range(8):
                    wb = fc % 2
                    e_load(ctx, wbuf[wb].rearrange("p (c f) -> p c f", c=32),
                           w3T[:, fc * 128:(fc + 1) * 128].rearrange(
                               "(c p) f -> p c f", p=128), b_w[wb])
                    rb = fc % 2
                    e_load(ctx, rbuf[rb][:], x1dr[fc, :, gsl], b_r[rb],
                           srcbufs=(b_x1d,), q="dact")
                    pb = 4 + fc % 2
                    ctx.group("pe",
                              [(lambda E, dc=dc, wb=wb, sub=sub, pb=pb: E.matmul(
                                  bank(pb), wbuf[wb][:, dc * 128:(dc + 1) * 128],
                                  hA[:, dc * 1024 + sub * 512: dc * 1024 + (sub + 1) * 512],
                                  start=(dc == 0), stop=(dc == 31)))
                               for dc in range(32)],
                              reads=(b_w[wb], b_hA), writes=(b_bank[pb],))
                    ctx.op("dve",
                           lambda E, fc=fc, rb=rb, pb=pb: E.scalar_tensor_tensor(
                               x2buf[:, fc * 512:(fc + 1) * 512], bank(pb),
                               s3_s[:, fc:fc + 1], rbuf[rb][:],
                               ALU.mult, ALU.add),
                           reads=(b_bank[pb], b_sc["s3"], b_r[rb]), writes=(b_x2,))
                x2tiles = [x2buf[:, fc * 512:(fc + 1) * 512] for fc in range(8)]

                def apply2(bcA, bcB, b_bc, gsl=gsl, x2tiles=x2tiles):
                    for fc in range(8):
                        ctx.op("dve",
                               lambda E, fc=fc, bcA=bcA: E.tensor_tensor(
                                   x2tiles[fc], x2tiles[fc], bcA, ALU.mult),
                               reads=(b_x2, b_bc), writes=(b_x2,))
                        ctx.op("dve",
                               lambda E, fc=fc, bcB=bcB: E.tensor_tensor(
                                   x2tiles[fc], x2tiles[fc], bcB, ALU.subtract),
                               reads=(b_x2, b_bc), writes=(b_x2,))
                        e_store(ctx, outTr[fc, :, gsl], x2tiles[fc], b_x2,
                                dstbufs=(b_out,))

                ln_block(ctx, scr, x2tiles, [b_x2] * 8, [apply2], 512, LN_EPS,
                         ones_col[:], ones_row[:], b_ones, bank(6), bank(7),
                         b_bank[6], b_bank[7])

        ctx.barrier_all("sp")
        ctx.run()
    return nc


def _prep_launchB(inputs, zT_full):
    x = np.asarray(inputs["L0_attn_x"])
    xTf = _t(x.reshape(16384, 1024))
    zTf = np.concatenate([zT_full[b] for b in range(4)], axis=1)
    woT = _t(np.asarray(inputs["L0_attn_W_o"]))
    bo = np.ascontiguousarray(np.asarray(inputs["L0_attn_b_o"]).reshape(8, 128).T)
    com = {
        "woT": woT, "bo": bo,
        "w1T": _t(inputs["L0_ffn_w1_w"]), "w2T": _t(inputs["L0_ffn_w2_w"]),
        "h1T": _t(inputs["L0_ffn_h1_w"]), "h2T": _t(inputs["L0_ffn_h2_w"]),
        "w3T": _t(inputs["L0_ffn_w3_w"]),
        "s1": np.ascontiguousarray(np.asarray(inputs["L0_ffn_w1_s"]).reshape(32, 128).T),
        "s2": np.ascontiguousarray(np.asarray(inputs["L0_ffn_w2_s"]).reshape(32, 128).T),
        "sh1": np.ascontiguousarray(np.asarray(inputs["L0_ffn_h1_s"]).reshape(32, 128).T),
        "sh2": np.ascontiguousarray(np.asarray(inputs["L0_ffn_h2_s"]).reshape(32, 128).T),
        "s3": np.ascontiguousarray(np.asarray(inputs["L0_ffn_w3_s"]).reshape(8, 128).T),
    }
    maps = []
    for c in range(8):
        tsl = slice(c * TB, (c + 1) * TB)
        m = dict(com)
        m["xT_s"] = np.ascontiguousarray(xTf[:, tsl])
        m["zT_s"] = np.ascontiguousarray(zTf[:, tsl])
        maps.append(m)
    return maps



def build_launchC():
    nc = bass.Bass("TRN2", target_bir_lowering=False)
    xT = nc.dram_tensor("xT", [DM, S], F32, kind="ExternalInput").ap()
    wq = nc.dram_tensor("wq", [DM, HF], F32, kind="ExternalInput").ap()
    wkv = nc.dram_tensor("wkv", [DM, 128], F32, kind="ExternalInput").ap()
    bqm = nc.dram_tensor("bqm", [128, 4], F32, kind="ExternalInput").ap()
    bkv = nc.dram_tensor("bkv", [128, 1], F32, kind="ExternalInput").ap()
    masksQK = nc.dram_tensor("masksQK", [128, 4 * 512], F32, kind="ExternalInput").ap()
    masksKQ = nc.dram_tensor("masksKQ", [128, 4 * 512], F32, kind="ExternalInput").ap()
    ident = nc.dram_tensor("ident", [128, 128], F32, kind="ExternalInput").ap()
    zT_out = nc.dram_tensor("zT", [HF, S], F32, kind="ExternalOutput").ap()
    nb_d = nc.dram_tensor("nb_d", [NH, 32, 128], F32, kind="Internal").ap()

    with ExitStack() as st:
        ctx = Ctx(nc, st)
        al = nc.alloc_sbuf_tensor

        xbc = [al(f"xbc{i}", [128, 512], BF16).ap() for i in range(2)]
        wq_s = al("wq_s", [128, 8 * HF], BF16).ap()
        wkv_s = al("wkv_s", [128, 8 * 128], BF16).ap()
        bq_s = al("bq_s", [128, 4], F32).ap()
        bqsc = al("bqsc", [128, 4], F32).ap()
        bkv_s = al("bkv_s", [128, 1], F32).ap()
        QT = al("QT", [64, NH * S], BF16).ap()
        KTa = al("KTa", [128, S], BF16).ap()
        V = al("V", [128, 32 * DH], BF16).ap()
        bvB = al("bvB", [128, DH], F32).ap()
        QTa = al("QTa", [128, S], BF16).ap()
        mq_s = al("mq_s", [128, NH * 32], F32).ap()
        sq_s = al("sq_s", [128, NH * 32], F32).ap()
        nbc_s = al("nbc_s", [128, 32], F32).ap()
        tp_s = al("tp_s", [32, 128], F32).ap()
        nbrow = al("nbrow", [1, S], F32).ap()
        nbrow2 = al("nbrow2", [1, S], F32).ap()
        nbbf = al("nbbf", [1, S], BF16).ap()
        corr = al("corr", [1, S], BF16).ap()
        corrB = al("corrB", [128, 512], F32).ap()
        mask_qk = al("mask_qk", [128, 4 * 512], F32).ap()
        mask_kq = al("mask_kq", [128, 4 * 512], F32).ap()
        id_s = al("id_s", [128, 128], F32).ap()
        alpha = al("alpha", [128, 512], BF16).ap()
        expsc = al("expsc", [128, 1024], BF16).ap()
        scr1 = al("scr1", [128, 1], F32).ap()
        zseg = [al(f"zseg{i}", [64, 512], F32).ap() for i in range(2)]
        ones_row = al("ones_row", [1, 128], BF16).ap()

        PS = nc.alloc_psum_tensor("PS", [128, 4096], F32).ap()

        def bank(i, rows=128):
            return PS[0:rows, i * 512:(i + 1) * 512]

        b_xbc = [ctx.buf("xbc0"), ctx.buf("xbc1")]
        b_wq = ctx.buf(); b_wkv = ctx.buf()
        b_bq = ctx.buf(); b_bqsc = ctx.buf(); b_bkv = ctx.buf()
        b_QT = ctx.buf(); b_KTa = ctx.buf(); b_V = ctx.buf(); b_bvB = ctx.buf()
        b_QTa = ctx.buf()
        b_mq = ctx.buf(); b_sq = ctx.buf(); b_nbc = ctx.buf(); b_tp = ctx.buf()
        b_rows = ctx.buf(); b_nbbf = ctx.buf(); b_corr = ctx.buf(); b_corrB = ctx.buf()
        b_mm = {"qk": ctx.buf(), "kq": ctx.buf()}
        b_id = ctx.buf()
        b_alpha = ctx.buf(); b_expsc = ctx.buf(); b_scr1 = ctx.buf()
        b_zseg = [ctx.buf(), ctx.buf()]
        b_zout = ctx.buf("zout")
        b_nbd = ctx.buf("nb_dram")
        b_ones = ctx.buf()
        b_bank = [ctx.buf(f"bank{i}") for i in range(8)]

        ctx.op("dve", lambda E: E.memset(ones_row[:], 1.0), writes=(b_ones,))
        e_load(ctx, mask_qk[:], masksQK, b_mm["qk"])
        e_load(ctx, mask_kq[:], masksKQ, b_mm["kq"])
        e_load(ctx, id_s[:], ident, b_id)
        e_load(ctx, bq_s[:], bqm, b_bq)
        e_load(ctx, bkv_s[:], bkv, b_bkv)
        e_load(ctx, wq_s.rearrange("p (c f) -> p c f", c=8),
               wq.rearrange("(c p) f -> p c f", p=128), b_wq)
        e_load(ctx, wkv_s.rearrange("p (c f) -> p c f", c=8),
               wkv.rearrange("(c p) f -> p c f", p=128), b_wkv)
        ctx.op("dve", lambda E: E.tensor_scalar(bqsc[:], bq_s[:], 0.125, None,
                                                ALU.mult),
               reads=(b_bq,), writes=(b_bqsc,))
        ctx.op("dve", lambda E: E.memset(KTa[64:65, :], 1.0), writes=(b_KTa,))
        bvrow = al("bvrow", [1, DH], BF16).ap()
        b_bvrow = ctx.buf()
        e_load(ctx, bvrow, bkv[64:128, :].rearrange("p one -> one p"), b_bvrow)
        ctx.group("pe", [lambda E: E.matmul(PS[:, 0:DH], ones_row[:], bvrow,
                                            start=True, stop=True)],
                  reads=(b_bvrow, b_ones), writes=(b_bank[0],))
        ctx.op("dve", lambda E: E.tensor_copy(bvB[:], PS[:, 0:DH]),
               reads=(b_bank[0],), writes=(b_bvB,))

        xTr = xT.rearrange("(c p) t -> c p t", p=128)

        for tt in range(8):
            for dc in range(8):
                xb = b_xbc[dc % 2]
                e_load(ctx, xbc[dc % 2][:], xTr[dc][:, tt * 512:(tt + 1) * 512], xb)
                for fc in range(4):
                    ctx.group("pe",
                              [(lambda E, fc=fc, dc=dc: E.matmul(
                                  bank(fc), wq_s[:, dc * HF + fc * 128: dc * HF + fc * 128 + 128],
                                  xbc[dc % 2][:],
                                  start=(dc == 0), stop=(dc == 7)))],
                              reads=(b_wq, xb), writes=(b_bank[fc],))
                ctx.group("pe",
                          [(lambda E, dc=dc: E.matmul(
                              bank(4), wkv_s[:, dc * 128:(dc + 1) * 128],
                              xbc[dc % 2][:],
                              start=(dc == 0), stop=(dc == 7)))],
                          reads=(b_wkv, xb), writes=(b_bank[4],))
            for fc in range(4):
                for half in range(2):
                    hh_ = 2 * fc + half
                    p0_ = half * 64
                    ctx.op("act",
                           lambda E, fc=fc, tt=tt, hh_=hh_, p0_=p0_: E.activation(
                               QT[0:64, hh_ * S + tt * 512: hh_ * S + (tt + 1) * 512],
                               bank(fc)[p0_:p0_ + 64, :], AF.Identity, scale=0.125,
                               bias=bqsc[p0_:p0_ + 64, fc:fc + 1]),
                           reads=(b_bank[fc], b_bqsc), writes=(b_QT,))
            ctx.op("act",
                   lambda E, tt=tt: E.activation(
                       KTa[0:64, tt * 512:(tt + 1) * 512], bank(4)[0:64, :],
                       AF.Identity, bias=bkv_s[0:64, 0:1]),
                   reads=(b_bank[4], b_bkv), writes=(b_KTa,))

        for tt in range(8):
            for dc in range(8):
                xb = b_xbc[dc % 2]
                e_load(ctx, xbc[dc % 2][:], xTr[dc][:, tt * 512:(tt + 1) * 512], xb)
                for sub in range(4):
                    ctx.group("pe",
                              [(lambda E, dc=dc, sub=sub: E.matmul(
                                  PS[:, sub * 512: sub * 512 + DH],
                                  xbc[dc % 2][:, sub * 128:(sub + 1) * 128],
                                  wkv_s[:, dc * 128 + 64:(dc + 1) * 128],
                                  start=(dc == 0), stop=(dc == 7)))],
                              reads=(b_wkv, xb), writes=(b_bank[sub],))
            for sub in range(4):
                ti = tt * 4 + sub
                ctx.op("dve",
                       lambda E, ti=ti, sub=sub: E.tensor_tensor(
                           V[:, ti * DH:(ti + 1) * DH],
                           PS[:, sub * 512: sub * 512 + DH],
                           bvB[:], ALU.add),
                       reads=(b_bank[sub], b_bvB), writes=(b_V,))

        def head_slices(h):
            return h // 2, (h % 2) * 64

        for h in range(NH):
            fc, p0 = head_slices(h)
            for qt in range(32):
                col = h * 32 + qt
                kts = [kt for kt in range(8) if kt * 512 <= qt * 128]
                first = True
                for kt in kts:
                    pb = kt % 2
                    ctx.group("pe",
                              [lambda E, h=h, qt=qt, kt=kt, pb=pb:
                               E.matmul(bank(pb),
                                        QT[0:64, h * S + qt * 128: h * S + qt * 128 + 128],
                                        KTa[0:64, kt * 512:(kt + 1) * 512],
                                        start=True, stop=True)],
                              reads=(b_QT, b_KTa), writes=(b_bank[pb],))
                    Dp_ = kt * 512 - qt * 128
                    if -384 <= Dp_ <= 0:
                        mi = (Dp_ + 384) // 128
                        ctx.op("dve",
                               lambda E, mi=mi, pb=pb: E.tensor_tensor(
                                   bank(pb), bank(pb),
                                   mask_qk[:, mi * 512:(mi + 1) * 512], ALU.add),
                               reads=(b_bank[pb], b_mm["qk"]), writes=(b_bank[pb],))
                    ctx.op("dve",
                           lambda E, pb=pb: E.reduce_max(scr1[:], bank(pb), axis=AX.X),
                           reads=(b_bank[pb],), writes=(b_scr1,))
                    if first:
                        ctx.op("dve",
                               lambda E, col=col: E.tensor_copy(mq_s[:, col:col + 1],
                                                                scr1[:]),
                               reads=(b_scr1,), writes=(b_mq,))
                        first = False
                    else:
                        ctx.op("dve",
                               lambda E, col=col: E.tensor_tensor(
                                   mq_s[:, col:col + 1], mq_s[:, col:col + 1],
                                   scr1[:], ALU.max),
                               reads=(b_scr1, b_mq), writes=(b_mq,))
            for qt in range(32):
                col = h * 32 + qt
                kts = [kt for kt in range(8) if kt * 512 <= qt * 128]
                ctx.op("dve",
                       lambda E, col=col: E.tensor_scalar(
                           nbc_s[:, qt:qt + 1] if False else nbc_s[:, col % 32: col % 32 + 1],
                           mq_s[:, col:col + 1], -1.0, None, ALU.mult),
                       reads=(b_mq,), writes=(b_nbc,))
                first = True
                for kt in kts:
                    pb = kt % 2
                    ctx.group("pe",
                              [lambda E, h=h, qt=qt, kt=kt, pb=pb:
                               E.matmul(bank(pb),
                                        QT[0:64, h * S + qt * 128: h * S + qt * 128 + 128],
                                        KTa[0:64, kt * 512:(kt + 1) * 512],
                                        start=True, stop=True)],
                              reads=(b_QT, b_KTa), writes=(b_bank[pb],))
                    Dp_ = kt * 512 - qt * 128
                    if -384 <= Dp_ <= 0:
                        mi = (Dp_ + 384) // 128
                        ctx.op("dve",
                               lambda E, mi=mi, pb=pb: E.tensor_tensor(
                                   bank(pb), bank(pb),
                                   mask_qk[:, mi * 512:(mi + 1) * 512], ALU.add),
                               reads=(b_bank[pb], b_mm["qk"]), writes=(b_bank[pb],))
                    ctx.op("act",
                           lambda E, pb=pb, col=col: E.activation(
                               expsc[:, 0:512], bank(pb), AF.Exp,
                               bias=nbc_s[:, col % 32: col % 32 + 1],
                               accum_out=scr1[:]),
                           reads=(b_bank[pb], b_nbc), writes=(b_expsc, b_scr1))
                    if first:
                        ctx.op("dve",
                               lambda E, col=col: E.tensor_copy(sq_s[:, col:col + 1],
                                                                scr1[:]),
                               reads=(b_scr1,), writes=(b_sq,))
                        first = False
                    else:
                        ctx.op("dve",
                               lambda E, col=col: E.tensor_tensor(
                                   sq_s[:, col:col + 1], sq_s[:, col:col + 1],
                                   scr1[:], ALU.add),
                               reads=(b_scr1, b_sq), writes=(b_sq,))
                ctx.op("act",
                       lambda E, col=col: E.activation(scr1[:], sq_s[:, col:col + 1],
                                                       AF.Ln),
                       reads=(b_sq,), writes=(b_scr1,))
                ctx.op("dve",
                       lambda E, col=col: E.tensor_tensor(scr1[:], scr1[:],
                                                          mq_s[:, col:col + 1],
                                                          ALU.add),
                       reads=(b_scr1, b_mq), writes=(b_scr1,))
                ctx.op("dve",
                       lambda E, col=col: E.tensor_scalar(
                           nbc_s[:, col % 32: col % 32 + 1], scr1[:], -1.0, None,
                           ALU.mult),
                       reads=(b_scr1,), writes=(b_nbc,))
            ctx.group("pe", [lambda E, h=h: E.transpose(PS[0:32, 0:128].bitcast(F32),
                                                        nbc_s[:, 0:32], id_s[:])],
                      reads=(b_nbc, b_id), writes=(b_bank[0],))
            ctx.op("dve", lambda E: E.tensor_copy(tp_s[:], PS[0:32, 0:128]),
                   reads=(b_bank[0],), writes=(b_tp,))
            e_store(ctx, nb_d[h], tp_s[:], b_tp, dstbufs=(b_nbd,))
            e_load(ctx, nbrow[:], nb_d[h].rearrange("a b -> (a b)").unsqueeze(0),
                   b_rows, srcbufs=(b_nbd,))
            ctx.op("dve", lambda E: E.tensor_copy(nbbf[:], nbrow[:]),
                   reads=(b_rows,), writes=(b_nbbf,))
            ctx.op("dve", lambda E: E.tensor_copy(nbrow2[:], nbbf[:]),
                   reads=(b_nbbf,), writes=(b_rows,))
            ctx.op("dve", lambda E: E.tensor_tensor(nbrow2[:], nbrow[:],
                                                    nbrow2[:], ALU.subtract),
                   reads=(b_rows,), writes=(b_rows,))
            ctx.op("act", lambda E: E.activation(corr[:], nbrow2[:], AF.Exp),
                   reads=(b_rows,), writes=(b_corr,))
            ctx.op("dve",
                   lambda E, h=h: E.tensor_copy(
                       QTa[0:64, :], QT[0:64, h * S:(h + 1) * S]),
                   reads=(b_QT,), writes=(b_QTa,))
            ctx.op("dve", lambda E: E.tensor_copy(QTa[64:65, :], nbbf[:]),
                   reads=(b_nbbf,), writes=(b_QTa,))
            for qt5 in range(8):
                zb = 2 + qt5 % 2
                kts = [kt for kt in range(32) if kt * 128 <= qt5 * 512 + 511]
                nlive = len(kts)
                for j, kt in enumerate(kts):
                    ctx.group("pe",
                              [lambda E, kt=kt, qt5=qt5: E.matmul(
                                  bank(0), KTa[0:65, kt * 128: kt * 128 + 128],
                                  QTa[0:65, qt5 * 512:(qt5 + 1) * 512],
                                  start=True, stop=True)],
                              reads=(b_KTa, b_QTa), writes=(b_bank[0],))
                    Dv = kt * 128 - qt5 * 512
                    if 0 <= Dv <= 384:
                        mi = Dv // 128
                        ctx.op("dve",
                               lambda E, mi=mi: E.tensor_tensor(
                                   bank(0), bank(0),
                                   mask_kq[:, mi * 512:(mi + 1) * 512], ALU.add),
                               reads=(b_bank[0], b_mm["kq"]), writes=(b_bank[0],))
                    ctx.op("act",
                           lambda E: E.activation(alpha[:], bank(0), AF.Exp),
                           reads=(b_bank[0],), writes=(b_alpha,))
                    ctx.group("pe",
                              [lambda E, kt=kt, j=j, nlive=nlive, zb=zb: E.matmul(
                                  bank(zb, 64), V[:, kt * DH:(kt + 1) * DH],
                                  alpha[:],
                                  start=(j == 0), stop=(j == nlive - 1))],
                              reads=(b_V, b_alpha), writes=(b_bank[zb],))
                ctx.group("pe",
                          [lambda E, qt5=qt5: E.matmul(
                              bank(1), ones_row[:],
                              corr[:, qt5 * 512:(qt5 + 1) * 512],
                              start=True, stop=True)],
                          reads=(b_corr, b_ones), writes=(b_bank[1],))
                ctx.op("dve", lambda E: E.tensor_copy(corrB[:], bank(1)),
                       reads=(b_bank[1],), writes=(b_corrB,))
                sb = qt5 % 2
                ctx.op("dve",
                       lambda E, zb=zb, sb=sb: E.tensor_tensor(
                           zseg[sb], bank(zb, 64), corrB[0:64, :], ALU.mult),
                       reads=(b_bank[zb], b_corrB), writes=(b_zseg[sb],))
                e_store(ctx, zT_out[h * 64:(h + 1) * 64, qt5 * 512:(qt5 + 1) * 512],
                        zseg[sb], b_zseg[sb], dstbufs=(b_zout,))

        ctx.barrier_all("sp")
        ctx.run()
    return nc


def _mk_masks_qk():
    out = np.zeros((128, 4 * 512), dtype=np.float32)
    qr = np.arange(128)[:, None]
    kc = np.arange(512)[None, :]
    for i, Dp_ in enumerate((-384, -256, -128, 0)):
        out[:, i * 512:(i + 1) * 512] = np.where(kc + Dp_ > qr, NEG, 0.0)
    return out


def _prep_launchC(inputs, xT_full):
    wqkv = np.asarray(inputs["L1_attn_wqkv"])
    bqkv = np.asarray(inputs["L1_attn_bqkv"])
    mkq = _mk_masks_kq()
    mqk = _mk_masks_qk()
    idm = np.eye(128, dtype=np.float32)
    maps = []
    for c in range(8):
        b, hh = c // 2, c % 2
        wq_ = wqkv[:, hh * 512:(hh + 1) * 512]
        bq_ = bqkv[hh * 512:(hh + 1) * 512]
        wkv_ = wqkv[:, 1024:1152]
        bkv_ = bqkv[1024:1152]
        maps.append({
            "xT": np.ascontiguousarray(xT_full[b]),
            "wq": np.ascontiguousarray(wq_),
            "wkv": np.ascontiguousarray(wkv_),
            "bqm": np.ascontiguousarray(bq_.reshape(4, 128).T),
            "bkv": bkv_.reshape(128, 1).copy(),
            "masksQK": mqk, "masksKQ": mkq, "ident": idm,
        })
    return maps



def build_launchD():
    nc = bass.Bass("TRN2", target_bir_lowering=False)
    xT = nc.dram_tensor("xT", [DM, S], F32, kind="ExternalInput").ap()
    zTf = nc.dram_tensor("zTf", [DM, S], F32, kind="ExternalInput").ap()
    wao = nc.dram_tensor("wao", [DM, DM], F32, kind="ExternalInput").ap()
    bao = nc.dram_tensor("bao", [128, 8], F32, kind="ExternalInput").ap()
    w_u = nc.dram_tensor("w_u", [DM, DM], F32, kind="ExternalInput").ap()
    w_r = nc.dram_tensor("w_r", [DM, DM], F32, kind="ExternalInput").ap()
    convw = nc.dram_tensor("convw", [128, 32], F32, kind="ExternalInput").ap()
    convb = nc.dram_tensor("convb", [128, 8], F32, kind="ExternalInput").ap()
    wxp = nc.dram_tensor("wxp", [DM, 96], F32, kind="ExternalInput").ap()
    x1T_o = nc.dram_tensor("x1T", [DM, S], F32, kind="ExternalOutput").ap()
    uT_o = nc.dram_tensor("uT", [DM, S], BF16, kind="ExternalOutput").ap()
    svT_o = nc.dram_tensor("svT", [DM, S], BF16, kind="ExternalOutput").ap()
    xdbl_o = nc.dram_tensor("xdbl_p", [96, S], F32, kind="ExternalOutput").ap()

    with ExitStack() as st:
        ctx = Ctx(nc, st)
        al = nc.alloc_sbuf_tensor

        zb = al("zb", [128, 4096], BF16).ap()
        xf = al("xf", [128, 4096], F32).ap()
        x1b = al("x1b", [128, 8 * S], BF16).ap()
        wao_s = al("wao_s", [128, 8192], BF16).ap()
        wu_s = al("wu_s", [128, 8192], BF16).ap()
        wr_s = al("wr_s", [128, 8192], BF16).ap()
        wxp_s = al("wxp_s", [128, 8 * 96], BF16).ap()
        bao_s = al("bao_s", [128, 8], F32).ap()
        cw_s = al("cw_s", [128, 32], F32).ap()
        cb_s = al("cb_s", [128, 8], F32).ap()
        u0c = al("u0c", [128, 4100], F32).ap()
        cv = al("cv", [128, 4096], F32).ap()
        uc = al("uc", [128, 4096], BF16).ap()
        ub = [al(f"ub{i}", [128, 512], BF16).ap() for i in range(2)]
        t96 = al("t96", [96, 512], F32).ap()

        PS = nc.alloc_psum_tensor("PS", [128, 4096], F32).ap()

        def bank(i, rows=128):
            return PS[0:rows, i * 512:(i + 1) * 512]

        b_zb = ctx.buf("zb"); b_xf = ctx.buf("xf"); b_x1b = ctx.buf("x1b")
        b_wao = ctx.buf(); b_wu = ctx.buf(); b_wr = ctx.buf(); b_wxp = ctx.buf()
        b_bao = ctx.buf(); b_cw = ctx.buf(); b_cb = ctx.buf()
        b_u0 = ctx.buf("u0"); b_cv = ctx.buf("cv"); b_uc = ctx.buf("uc")
        b_ub = [ctx.buf("ub0"), ctx.buf("ub1")]
        b_t96 = ctx.buf()
        b_bank = [ctx.buf(f"bank{i}") for i in range(8)]
        b_x1o = ctx.buf("x1o"); b_uo = ctx.buf("uo"); b_svo = ctx.buf("svo")
        b_xdo = ctx.buf("xdo")

        e_load(ctx, bao_s[:], bao, b_bao)
        e_load(ctx, cw_s[:], convw, b_cw)
        e_load(ctx, cb_s[:], convb, b_cb)
        e_load(ctx, wao_s.rearrange("p (c f) -> p c f", c=8),
               wao.rearrange("(c p) f -> p c f", p=128), b_wao)
        e_load(ctx, wu_s.rearrange("p (c f) -> p c f", c=8),
               w_u.rearrange("(c p) f -> p c f", p=128), b_wu)
        e_load(ctx, wr_s.rearrange("p (c f) -> p c f", c=8),
               w_r.rearrange("(c p) f -> p c f", p=128), b_wr)
        e_load(ctx, wxp_s.rearrange("p (c f) -> p c f", c=8),
               wxp.rearrange("(c p) f -> p c f", p=128), b_wxp)

        xTr = xT.rearrange("(c p) t -> c p t", p=128)
        zTr = zTf.rearrange("(c p) t -> c p t", p=128)
        x1Tr = x1T_o.rearrange("(c p) t -> c p t", p=128)
        uTr = uT_o.rearrange("(c p) t -> c p t", p=128)
        svTr = svT_o.rearrange("(c p) t -> c p t", p=128)

        for tt in range(8):
            tsl = slice(tt * 512, (tt + 1) * 512)
            e_load(ctx, zb.rearrange("p (c t) -> p c t", c=8),
                   zTr[:, :, tsl].rearrange("c p t -> p c t"), b_zb)
            e_load(ctx, xf.rearrange("p (c t) -> p c t", c=8),
                   xTr[:, :, tsl].rearrange("c p t -> p c t"), b_xf)
            for fc in range(8):
                pb = fc % 2
                ctx.group("pe",
                          [(lambda E, dc=dc, fc=fc, pb=pb: E.matmul(
                              bank(pb), wao_s[:, dc * 1024 + fc * 128: dc * 1024 + fc * 128 + 128],
                              zb[:, dc * 512:(dc + 1) * 512],
                              start=(dc == 0), stop=(dc == 7)))
                           for dc in range(8)],
                          reads=(b_wao, b_zb), writes=(b_bank[pb],))
                ctx.op("dve",
                       lambda E, fc=fc, pb=pb: E.scalar_tensor_tensor(
                           xf[:, fc * 512:(fc + 1) * 512], bank(pb),
                           bao_s[:, fc:fc + 1], xf[:, fc * 512:(fc + 1) * 512],
                           ALU.add, ALU.add),
                       reads=(b_bank[pb], b_bao, b_xf), writes=(b_xf,))
                ctx.op("dve",
                       lambda E, fc=fc, tt=tt: E.tensor_copy(
                           x1b[:, fc * S + tt * 512: fc * S + (tt + 1) * 512],
                           xf[:, fc * 512:(fc + 1) * 512]),
                       reads=(b_xf,), writes=(b_x1b,))
                e_store(ctx, x1Tr[fc, :, tsl], xf[:, fc * 512:(fc + 1) * 512],
                        b_xf, dstbufs=(b_x1o,))

        for g in range(8):
            ctx.op("dve", lambda E: E.memset(u0c[:, 0:3], 0.0), writes=(b_u0,))
            for tt in range(8):
                pb = tt % 2
                ctx.group("pe",
                          [(lambda E, dc=dc, g=g, tt=tt, pb=pb: E.matmul(
                              bank(pb), wu_s[:, dc * 1024 + g * 128: dc * 1024 + g * 128 + 128],
                              x1b[:, dc * S + tt * 512: dc * S + (tt + 1) * 512],
                              start=(dc == 0), stop=(dc == 7)))
                           for dc in range(8)],
                          reads=(b_wu, b_x1b), writes=(b_bank[pb],))
                ctx.op("dve",
                       lambda E, tt=tt, pb=pb: E.tensor_copy(
                           u0c[:, 3 + tt * 512: 3 + (tt + 1) * 512], bank(pb)),
                       reads=(b_bank[pb],), writes=(b_u0,))
            ctx.op("dve",
                   lambda E, g=g: E.tensor_scalar(
                       cv[:], u0c[:, 0:4096], cw_s[:, g * 4: g * 4 + 1], None,
                       ALU.mult),
                   reads=(b_u0, b_cw), writes=(b_cv,))
            for j in range(1, 4):
                ctx.op("dve",
                       lambda E, g=g, j=j: E.scalar_tensor_tensor(
                           cv[:], u0c[:, j: j + 4096], cw_s[:, g * 4 + j: g * 4 + j + 1],
                           cv[:], ALU.mult, ALU.add),
                       reads=(b_u0, b_cw, b_cv), writes=(b_cv,))
            ctx.op("act",
                   lambda E, g=g: E.activation(uc[:], cv[:], AF.Silu,
                                               bias=cb_s[:, g:g + 1]),
                   reads=(b_cv, b_cb), writes=(b_uc,))
            e_store(ctx, uTr[g], uc[:], b_uc, dstbufs=(b_uo,))

        for tt in range(8):
            tsl = slice(tt * 512, (tt + 1) * 512)
            for g in range(8):
                e_load(ctx, ub[g % 2][:], uTr[g][:, tsl], b_ub[g % 2],
                       srcbufs=(b_uo,))
                ctx.group("pe",
                          [(lambda E, g=g: E.matmul(
                              PS[0:96, 0:512], wxp_s[:, g * 96:(g + 1) * 96],
                              ub[g % 2][:],
                              start=(g == 0), stop=(g == 7)))],
                          reads=(b_wxp, b_ub[g % 2]), writes=(b_bank[0],))
            ctx.op("dve", lambda E: E.tensor_copy(t96[:], PS[0:96, 0:512]),
                   reads=(b_bank[0],), writes=(b_t96,))
            e_store(ctx, xdbl_o[:, tsl], t96[:], b_t96, dstbufs=(b_xdo,))

        for g in range(8):
            for tt in range(8):
                pb = tt % 2
                ctx.group("pe",
                          [(lambda E, dc=dc, g=g, tt=tt, pb=pb: E.matmul(
                              bank(pb), wr_s[:, dc * 1024 + g * 128: dc * 1024 + g * 128 + 128],
                              x1b[:, dc * S + tt * 512: dc * S + (tt + 1) * 512],
                              start=(dc == 0), stop=(dc == 7)))
                           for dc in range(8)],
                          reads=(b_wr, b_x1b), writes=(b_bank[pb],))
                ctx.op("act",
                       lambda E, tt=tt, pb=pb: E.activation(
                           uc[:, tt * 512:(tt + 1) * 512], bank(pb), AF.Silu),
                       reads=(b_bank[pb],), writes=(b_uc,))
            e_store(ctx, svTr[g], uc[:], b_uc, dstbufs=(b_svo,))

        ctx.barrier_all("sp")
        ctx.run()
    return nc


def _prep_launchD(inputs, xT_full, zT_full):
    wao = np.asarray(inputs["L1_attn_w_attn_out"])
    bao = np.asarray(inputs["L1_attn_b_attn_out"])
    w_in = np.asarray(inputs["L1_attn_w_in"])
    cw = np.asarray(inputs["L1_attn_conv_w"])
    cb = np.asarray(inputs["L1_attn_conv_b"])
    wx = np.asarray(inputs["L1_attn_w_xproj"])
    maps = []
    for c in range(8):
        b, hh = c // 2, c % 2
        csl = slice(hh * 1024, (hh + 1) * 1024)
        maps.append({
            "xT": np.ascontiguousarray(xT_full[b]),
            "zTf": np.ascontiguousarray(zT_full[b]),
            "wao": np.ascontiguousarray(wao),
            "bao": np.ascontiguousarray(bao.reshape(8, 128).T),
            "w_u": np.ascontiguousarray(w_in[:, csl]),
            "w_r": np.ascontiguousarray(w_in[:, 2048 + hh * 1024: 2048 + (hh + 1) * 1024]),
            "convw": np.ascontiguousarray(cw[csl].reshape(8, 128, 4).transpose(1, 0, 2).reshape(128, 32)),
            "convb": np.ascontiguousarray(cb[csl].reshape(8, 128).T),
            "wxp": np.ascontiguousarray(wx[csl]),
        })
    return maps



DTR = 64
NST = 16


def build_launchE():
    nc = bass.Bass("TRN2", target_bir_lowering=False)
    uT = nc.dram_tensor("uT", [DM, S], BF16, kind="ExternalInput").ap()
    svT = nc.dram_tensor("svT", [DM, S], BF16, kind="ExternalInput").ap()
    xdA = nc.dram_tensor("xdA", [96, S], F32, kind="ExternalInput").ap()
    xdB = nc.dram_tensor("xdB", [96, S], F32, kind="ExternalInput").ap()
    wdt = nc.dram_tensor("wdt", [DTR, DM], F32, kind="ExternalInput").ap()
    bdt = nc.dram_tensor("bdt", [128, 8], F32, kind="ExternalInput").ap()
    Alog = nc.dram_tensor("Alog", [128, 8 * NST], F32, kind="ExternalInput").ap()
    Dp = nc.dram_tensor("Dp", [128, 8], F32, kind="ExternalInput").ap()
    wout = nc.dram_tensor("wout", [DM, DM], F32, kind="ExternalInput").ap()
    pout_o = nc.dram_tensor("poutT", [DM, S], F32, kind="ExternalOutput").ap()
    y_d = nc.dram_tensor("y_d", [DM, S], BF16, kind="Internal").ap()
    xbc_d = nc.dram_tensor("xbc_d", [2 * NST, S], F32, kind="Internal").ap()

    with ExitStack() as st:
        ctx = Ctx(nc, st)
        al = nc.alloc_sbuf_tensor

        xd1b = al("xd1b", [64, 4096], BF16).ap()
        wdt_s = al("wdt_s", [64, 1024], BF16).ap()
        bdt_s = al("bdt_s", [128, 8], F32).ap()
        A_s = al("A_s", [128, 8 * NST], F32).ap()
        Dp_s = al("Dp_s", [128, 8], F32).ap()
        wout_s = al("wout_s", [128, 8192], BF16).ap()
        delta = al("delta", [128, 8 * S], BF16).ap()
        xd2 = delta[0:32, 0:8192].bitcast(F32)
        du = al("du", [128, 4096], BF16).ap()
        ug = al("ug", [128, 4096], BF16).ap()
        dA = al("dA", [128, 4096], BF16).ap()
        dBu = al("dBu", [128, 4096], BF16).ap()
        hs_ = al("hs", [128, 4096], BF16).ap()
        Bb = al("Bb", [128, 4096], BF16).ap()
        Cb = al("Cb", [128, 4096], BF16).ap()
        yg = al("yg", [128, 4096], F32).ap()
        xd1 = yg[0:64, :]
        tmp = al("tmpv", [128, 4096], BF16).ap()
        sp1 = al("sp1", [128, 4096], F32).ap()
        ytile = [sp1[:, 0:256].bitcast(BF16), sp1[:, 256:512].bitcast(BF16)]
        ptile = [sp1[:, 512:1024], sp1[:, 1024:1536]]
        ones_row = al("ones_row", [1, 128], F32).ap()

        PS = nc.alloc_psum_tensor("PS", [128, 4096], F32).ap()

        def bank(i, rows=128):
            return PS[0:rows, i * 512:(i + 1) * 512]

        b_xd1 = ctx.buf(); b_xd2 = ctx.buf(); b_xd1b = ctx.buf()
        b_wdt = ctx.buf(); b_bdt = ctx.buf(); b_A = ctx.buf(); b_Dp = ctx.buf()
        b_wout = ctx.buf()
        b_delta = ctx.buf("delta"); b_du = ctx.buf("du"); b_ug = ctx.buf("ug")
        b_dA = ctx.buf("dA"); b_dBu = ctx.buf("dBu"); b_h = ctx.buf("h")
        b_Bb = ctx.buf(); b_Cb = ctx.buf()
        b_yg = ctx.buf("yg"); b_tmp = ctx.buf("tmp")
        b_yt = [ctx.buf(), ctx.buf()]
        b_pt = [ctx.buf(), ctx.buf()]
        b_sp = ctx.buf()
        b_ones = ctx.buf()
        b_bank = [ctx.buf(f"bank{i}") for i in range(8)]
        b_yd = ctx.buf("y_dram"); b_xbcd = ctx.buf("xbc_dram")
        b_po = ctx.buf("pod")

        ctx.op("dve", lambda E: E.memset(ones_row[:], 1.0), writes=(b_ones,))
        e_load(ctx, bdt_s[:], bdt, b_bdt)
        e_load(ctx, Dp_s[:], Dp, b_Dp)
        e_load(ctx, wdt_s[:], wdt, b_wdt)
        e_load(ctx, wout_s.rearrange("p (c f) -> p c f", c=8),
               wout.rearrange("(c p) f -> p c f", p=128), b_wout)
        e_load(ctx, A_s[:], Alog, b_A)
        ctx.op("act", lambda E: E.activation(A_s[:], A_s[:], AF.Exp),
               reads=(b_A,), writes=(b_A,))
        ctx.op("dve", lambda E: E.tensor_scalar(A_s[:], A_s[:], -1.0, None,
                                                ALU.mult),
               reads=(b_A,), writes=(b_A,))
        e_load(ctx, xd1[:], xdA[0:64, :], b_xd1)
        e_load(ctx, xd2[:], xdA[64:96, :], b_xd2)
        e_load(ctx, sp1[0:64, :], xdB[0:64, :], b_sp)
        ctx.op("dve", lambda E: E.tensor_tensor(xd1[:], xd1[:], sp1[0:64, :],
                                                ALU.add),
               reads=(b_xd1, b_sp), writes=(b_xd1,))
        e_load(ctx, sp1[0:32, :], xdB[64:96, :], b_sp)
        ctx.op("dve", lambda E: E.tensor_tensor(xd2[:], xd2[:], sp1[0:32, :],
                                                ALU.add),
               reads=(b_xd2, b_sp), writes=(b_xd2,))
        ctx.op("dve", lambda E: E.tensor_copy(xd1b[:], xd1[:]),
               reads=(b_xd1,), writes=(b_xd1b,))
        e_store(ctx, xbc_d[:], xd2[:], b_xd2, dstbufs=(b_xbcd,))

        ctx.claim(b_delta, [b_xd2])
        ctx.claim(b_yg, [b_xd1])
        for g in range(8):
            for tt in range(8):
                pb = tt % 2
                ctx.group("pe",
                          [(lambda E, g=g, tt=tt, pb=pb: E.matmul(
                              bank(pb), wdt_s[:, g * 128:(g + 1) * 128],
                              xd1b[:, tt * 512:(tt + 1) * 512],
                              start=True, stop=True))],
                          reads=(b_wdt, b_xd1b), writes=(b_bank[pb],))
                ctx.op("dve",
                       lambda E, g=g, tt=tt, pb=pb: E.tensor_scalar(
                           sp1[:, tt * 512:(tt + 1) * 512], bank(pb),
                           bdt_s[:, g:g + 1], None, ALU.add),
                       reads=(b_bank[pb], b_bdt), writes=(b_sp,))
            ctx.op("act",
                   lambda E: E.activation(tmp[:], sp1[:], AF.Abs),
                   reads=(b_sp,), writes=(b_tmp,))
            ctx.op("act",
                   lambda E: E.activation(tmp[:], tmp[:], AF.Exp, scale=-1.0),
                   reads=(b_tmp,), writes=(b_tmp,))
            c_one = ctx.constcol(1.0)
            ctx.op("act",
                   lambda E: E.activation(tmp[:], tmp[:], AF.Ln,
                                          bias=c_one),
                   reads=(b_tmp, ctx._cbuf), writes=(b_tmp,))
            ctx.op("dve",
                   lambda E: E.tensor_scalar(sp1[:], sp1[:], 0.0, None, ALU.max),
                   reads=(b_sp,), writes=(b_sp,))
            ctx.op("dve",
                   lambda E, g=g: E.tensor_tensor(
                       delta[:, g * S:(g + 1) * S], sp1[:], tmp[:], ALU.add),
                   reads=(b_sp, b_tmp), writes=(b_delta,))

        uTr = uT.rearrange("(c p) t -> c p t", p=128)
        svTr = svT.rearrange("(c p) t -> c p t", p=128)
        ydr = y_d.rearrange("(c p) t -> c p t", p=128)
        for g in range(8):
            e_load(ctx, ug[:], uTr[g], b_ug)
            ctx.op("dve",
                   lambda E, g=g: E.tensor_tensor(du[:], delta[:, g * S:(g + 1) * S],
                                                  ug[:], ALU.mult),
                   reads=(b_delta, b_ug), writes=(b_du,))
            for n in range(NST):
                e_load(ctx, Bb[:], xbc_d[n:n + 1, :].to_broadcast([128, S]),
                       b_Bb, srcbufs=(b_xbcd,))
                e_load(ctx, Cb[:], xbc_d[NST + n:NST + n + 1, :].to_broadcast([128, S]),
                       b_Cb, srcbufs=(b_xbcd,))
                ctx.op("act",
                       lambda E, g=g, n=n: E.activation(
                           dA[:], delta[:, g * S:(g + 1) * S], AF.Exp,
                           scale=A_s[:, g * NST + n: g * NST + n + 1]),
                       reads=(b_delta, b_A), writes=(b_dA,))
                ctx.op("pool",
                       lambda E: E.tensor_tensor(dBu[:], du[:], Bb[:], ALU.mult),
                       reads=(b_du, b_Bb), writes=(b_dBu,))
                ctx.op("dve",
                       lambda E: E.tensor_tensor_scan(hs_[:], dA[:], dBu[:], 0.0,
                                                      ALU.mult, ALU.add),
                       reads=(b_dA, b_dBu), writes=(b_h,))
                ctx.op("dve",
                       lambda E: E.tensor_tensor(tmp[:], hs_[:], Cb[:], ALU.mult),
                       reads=(b_h, b_Cb), writes=(b_tmp,))
                if n == 0:
                    ctx.op("dve",
                           lambda E: E.tensor_copy(yg[:], tmp[:]),
                           reads=(b_tmp,), writes=(b_yg,))
                else:
                    ctx.op("pool",
                           lambda E: E.tensor_tensor(yg[:], yg[:], tmp[:], ALU.add),
                           reads=(b_tmp, b_yg), writes=(b_yg,))
            ctx.op("dve",
                   lambda E, g=g: E.scalar_tensor_tensor(
                       yg[:], ug[:], Dp_s[:, g:g + 1], yg[:],
                       ALU.mult, ALU.add),
                   reads=(b_ug, b_Dp, b_yg), writes=(b_yg,))
            e_load(ctx, ug[:], svTr[g], b_ug)
            ctx.op("dve",
                   lambda E: E.tensor_tensor(hs_[:], yg[:], ug[:], ALU.mult),
                   reads=(b_yg, b_ug), writes=(b_h,))
            e_store(ctx, ydr[g], hs_[:], b_h, dstbufs=(b_yd,))

        ctx.claim(b_yt[0], [b_sp])
        for tt in range(8):
            tsl = slice(tt * 512, (tt + 1) * 512)
            for fc in range(8):
                pb = fc % 2
                builds = []
                for g in range(8):
                    gb = g % 2
                    e_load(ctx, ytile[gb][:], ydr[g][:, tsl], b_yt[gb],
                           srcbufs=(b_yd,))
                    ctx.group("pe",
                              [(lambda E, g=g, fc=fc, gb=gb, pb=pb: E.matmul(
                                  bank(pb), wout_s[:, g * 1024 + fc * 128: g * 1024 + fc * 128 + 128],
                                  ytile[gb][:],
                                  start=(g == 0), stop=(g == 7)))],
                              reads=(b_wout, b_yt[gb]), writes=(b_bank[pb],))
                ctx.op("dve",
                       lambda E, pb=pb, fc=fc: E.tensor_copy(ptile[pb][:], bank(pb)),
                       reads=(b_bank[pb],), writes=(b_pt[pb],))
                e_store(ctx, pout_o.rearrange("(c p) t -> c p t", p=128)[fc, :, tsl],
                        ptile[pb][:], b_pt[pb], dstbufs=(b_po,))

        ctx.barrier_all("sp")
        ctx.run()
    return nc


def _prep_launchE(inputs, uT_all, svT_all, xd_all):
    wdt = np.asarray(inputs["L1_attn_w_dt"])
    bdt = np.asarray(inputs["L1_attn_b_dt"])
    Alog = np.asarray(inputs["L1_attn_A_log"])
    Dpv = np.asarray(inputs["L1_attn_Dp"])
    woutf = np.asarray(inputs["L1_attn_w_out"])
    maps = []
    for c in range(8):
        b, hh = c // 2, c % 2
        csl = slice(hh * 1024, (hh + 1) * 1024)
        Asl = Alog[csl].reshape(8, 128, NST).transpose(1, 0, 2).reshape(128, 8 * NST)
        maps.append({
            "uT": uT_all[c], "svT": svT_all[c],
            "xdA": xd_all[2 * b], "xdB": xd_all[2 * b + 1],
            "wdt": np.ascontiguousarray(wdt[:, csl]),
            "bdt": np.ascontiguousarray(bdt[csl].reshape(8, 128).T),
            "Alog": np.ascontiguousarray(Asl),
            "Dp": np.ascontiguousarray(Dpv[csl].reshape(8, 128).T),
            "wout": np.ascontiguousarray(woutf[csl]),
        })
    return maps



def build_launchF():
    nc = bass.Bass("TRN2", target_bir_lowering=False)
    xT_s = nc.dram_tensor("xT_s", [DM, TB], F32, kind="ExternalInput").ap()
    x1T_s = nc.dram_tensor("x1T_s", [DM, TB], F32, kind="ExternalInput").ap()
    pA_s = nc.dram_tensor("pA_s", [DM, TB], F32, kind="ExternalInput").ap()
    pB_s = nc.dram_tensor("pB_s", [DM, TB], F32, kind="ExternalInput").ap()
    gw = nc.dram_tensor("gw", [DM, 8], F32, kind="ExternalInput").ap()
    gb = nc.dram_tensor("gb", [8, 1], F32, kind="ExternalInput").ap()
    w1e = nc.dram_tensor("w1e", [8, DM, 4096], F32, kind="ExternalInput").ap()
    b1e = nc.dram_tensor("b1e", [128, 8 * 32], F32, kind="ExternalInput").ap()
    w2e = nc.dram_tensor("w2e", [8, 4096, DM], F32, kind="ExternalInput").ap()
    b2e = nc.dram_tensor("b2e", [128, 8 * 8], F32, kind="ExternalInput").ap()
    ident = nc.dram_tensor("ident", [128, 128], F32, kind="ExternalInput").ap()
    outT = nc.dram_tensor("outT", [DM, TB], F32, kind="ExternalOutput").ap()
    x3_d = nc.dram_tensor("x3_d", [DM, TB], F32, kind="Internal").ap()
    w_d = nc.dram_tensor("w_d", [8, TB], F32, kind="Internal").ap()

    with ExitStack() as st:
        ctx = Ctx(nc, st)
        al = nc.alloc_sbuf_tensor

        hbuf = al("hbuf", [128, 32 * 1024], BF16).ap()
        sA = hbuf[:, 0:8192].bitcast(F32)
        sB = hbuf[:, 8192:16384].bitcast(F32)
        x3b = al("x3b", [128, 8 * TB], BF16).ap()
        oacc = al("oacc", [128, 8 * 1024], F32).ap()
        wbuf = [al(f"wbuf{i}", [128, 1024], BF16).ap() for i in range(2)]
        wbuf2 = [al(f"wbuf2{i}", [128, 4096], BF16).ap() for i in range(2)]
        wB = al("wB", [128, TB], F32).ap()
        gw_s = al("gw_s", [128, 8 * 8], F32).ap()
        gb_s = al("gb_s", [8, 1], F32).ap()
        b1_s = al("b1_s", [128, 8 * 32], F32).ap()
        b2_s = al("b2_s", [128, 8 * 8], F32).ap()
        id_s = al("id_s", [128, 128], F32).ap()
        lg = al("lg", [8, 512], F32).ap()
        pb8 = al("pb8", [128, 8], F32).ap()
        wtok = al("wtok", [128, 8], F32).ap()
        wT_s = al("wT_s", [8, TB], F32).ap()
        scr1 = al("scr1", [128, 1], F32).ap()
        scr2 = al("scr2", [128, 1], F32).ap()
        ytile = [al(f"ytile{i}", [128, 512], BF16).ap() for i in range(2)]
        rtile = [al(f"rtile{i}", [128, 512], F32).ap() for i in range(2)]
        tmp5 = al("tmp5", [128, 512], F32).ap()
        ones_col = al("ones_colF", [128, 1], F32).ap()
        ones_row = al("ones_row", [1, 128], BF16).ap()
        scr = LNScr(ctx, nc)

        PS = nc.alloc_psum_tensor("PS", [128, 4096], F32).ap()

        def bank(i, rows=128):
            return PS[0:rows, i * 512:(i + 1) * 512]

        b_h = ctx.buf("hbuf"); b_sA = ctx.buf("sA"); b_sB = ctx.buf("sB")
        b_x3b = ctx.buf("x3b"); b_oacc = ctx.buf("oacc")
        b_w = [ctx.buf("w0"), ctx.buf("w1")]
        b_w2 = [ctx.buf("w20"), ctx.buf("w21")]
        b_wB = ctx.buf("wB")
        b_gw = ctx.buf(); b_gb = ctx.buf(); b_b1 = ctx.buf(); b_b2 = ctx.buf()
        b_id = ctx.buf(); b_lg = ctx.buf(); b_pb8 = ctx.buf(); b_wtok = ctx.buf()
        b_wTs = ctx.buf(); b_scr1 = ctx.buf(); b_scr2 = ctx.buf()
        b_yt = [ctx.buf(), ctx.buf()]
        b_rt = [ctx.buf(), ctx.buf()]
        b_tmp5 = ctx.buf()
        b_ones = ctx.buf()
        b_bank = [ctx.buf(f"bank{i}") for i in range(8)]
        b_x3d = ctx.buf("x3_dram"); b_wd = ctx.buf("w_dram")
        b_out = ctx.buf("outd")

        ctx.op("dve", lambda E: E.memset(ones_col[:], 1.0), writes=(b_ones,))
        ctx.op("dve", lambda E: E.memset(ones_row[:], 1.0), writes=(b_ones,))
        e_load(ctx, gw_s[:, 0:64].rearrange("p (c f) -> p c f", c=8),
               gw.rearrange("(c p) f -> p c f", p=128), b_gw)
        e_load(ctx, gb_s[:], gb, b_gb)
        e_load(ctx, b1_s[:], b1e, b_b1)
        e_load(ctx, b2_s[:], b2e, b_b2)
        e_load(ctx, id_s[:], ident, b_id)

        xTr = xT_s.rearrange("(c p) t -> p c t", p=128)
        x1Tr = x1T_s.rearrange("(c p) t -> p c t", p=128)
        pAr = pA_s.rearrange("(c p) t -> p c t", p=128)
        pBr = pB_s.rearrange("(c p) t -> p c t", p=128)
        x3dr = x3_d.rearrange("(c p) t -> c p t", p=128)
        outTr = outT.rearrange("(c p) t -> c p t", p=128)

        for tt in range(4):
            tsl = slice(tt * 512, (tt + 1) * 512)
            e_load(ctx, sA.rearrange("p (c t) -> p c t", c=8), xTr[:, :, tsl], b_sA)
            for other in (x1Tr, pAr, pBr):
                e_load(ctx, sB.rearrange("p (c t) -> p c t", c=8),
                       other[:, :, tsl], b_sB)
                for fc in range(8):
                    ctx.op("dve",
                           lambda E, fc=fc: E.tensor_tensor(
                               sA[:, fc * 512:(fc + 1) * 512],
                               sA[:, fc * 512:(fc + 1) * 512],
                               sB[:, fc * 512:(fc + 1) * 512], ALU.add),
                           reads=(b_sA, b_sB), writes=(b_sA,))
            sAt = [sA[:, fc * 512:(fc + 1) * 512] for fc in range(8)]

            def applyF1(bcA, bcB, b_bc, tt=tt, tsl=tsl, sAt=sAt):
                for fc in range(8):
                    ctx.op("dve",
                           lambda E, fc=fc, bcA=bcA: E.tensor_tensor(
                               sAt[fc], sAt[fc], bcA, ALU.mult),
                           reads=(b_sA, b_bc), writes=(b_sA,))
                    ctx.op("dve",
                           lambda E, fc=fc, bcB=bcB: E.tensor_tensor(
                               sAt[fc], sAt[fc], bcB, ALU.subtract),
                           reads=(b_sA, b_bc), writes=(b_sA,))
                    ctx.op("dve",
                           lambda E, fc=fc, tt=tt: E.tensor_copy(
                               x3b[:, fc * TB + tt * 512: fc * TB + (tt + 1) * 512],
                               sAt[fc]),
                           reads=(b_sA,), writes=(b_x3b,))
                    e_store(ctx, x3dr[fc, :, tsl], sAt[fc], b_sA,
                            dstbufs=(b_x3d,))

            ln_block(ctx, scr, sAt, [b_sA] * 8, [applyF1], 512, LN_EPS,
                     ones_col[:], ones_row[:], b_ones, bank(0), bank(1),
                     b_bank[0], b_bank[1])

        for tt in range(4):
            tsl = slice(tt * 512, (tt + 1) * 512)
            e_load(ctx, sA.rearrange("p (c t) -> p c t", c=8),
                   x3_d.rearrange("(c p) t -> p c t", p=128)[:, :, tsl], b_sA,
                   srcbufs=(b_x3d,))
            ctx.group("pe",
                      [(lambda E, dc=dc, tt=tt: E.matmul(
                          PS[0:8, 0:512], gw_s[:, dc * 8:(dc + 1) * 8],
                          sA[:, dc * 512:(dc + 1) * 512],
                          start=(dc == 0), stop=(dc == 7)))
                       for dc in range(8)],
                      reads=(b_gw, b_sA), writes=(b_bank[0],))
            ctx.op("act",
                   lambda E: E.activation(lg[:], PS[0:8, 0:512], AF.Identity,
                                          bias=gb_s[:, 0:1]),
                   reads=(b_bank[0], b_gb), writes=(b_lg,))
            for qb in range(4):
                t0 = tt * 512 + qb * 128
                ctx.group("pe",
                          [lambda E, qb=qb: E.transpose(
                              PS[0:128, 512:520], lg[:, qb * 128:(qb + 1) * 128],
                              id_s[0:8, 0:8])],
                          reads=(b_lg, b_id), writes=(b_bank[1],))
                ctx.op("dve",
                       lambda E: E.reduce_max(scr1[:], PS[:, 512:520], axis=AX.X),
                       reads=(b_bank[1],), writes=(b_scr1,))
                ctx.op("dve",
                       lambda E: E.tensor_scalar(scr2[:], scr1[:], -1.0, None,
                                                 ALU.mult),
                       reads=(b_scr1,), writes=(b_scr2,))
                ctx.op("act",
                       lambda E: E.activation(pb8[:], PS[:, 512:520], AF.Exp,
                                              bias=scr2[:], accum_out=scr1[:]),
                       reads=(b_bank[1], b_scr2), writes=(b_pb8, b_scr1))
                ctx.op("dve",
                       lambda E: E.reciprocal(scr2[:], scr1[:]),
                       reads=(b_scr1,), writes=(b_scr2,))
                ctx.op("dve",
                       lambda E: E.tensor_scalar(pb8[:], pb8[:], scr2[:], None,
                                                 ALU.mult),
                       reads=(b_pb8, b_scr2), writes=(b_pb8,))
                ctx.op("dve",
                       lambda E: E.reduce_max(scr1[:], pb8[:], axis=AX.X),
                       reads=(b_pb8,), writes=(b_scr1,))
                ctx.op("dve",
                       lambda E: E.tensor_scalar(wtok[:], pb8[:], scr1[:], None,
                                                 ALU.is_ge),
                       reads=(b_pb8, b_scr1), writes=(b_wtok,))
                ctx.op("dve",
                       lambda E: E.tensor_tensor(wtok[:], wtok[:], pb8[:],
                                                 ALU.mult),
                       reads=(b_wtok, b_pb8), writes=(b_wtok,))
                ctx.group("pe",
                          [lambda E: E.transpose(PS[0:8, 640:768], wtok[:],
                                                 id_s[:])],
                          reads=(b_wtok, b_id), writes=(b_bank[1],))
                ctx.op("dve",
                       lambda E, t0=t0: E.tensor_copy(
                           wT_s[:, t0:t0 + 128], PS[0:8, 640:768]),
                       reads=(b_bank[1],), writes=(b_wTs,))
        e_store(ctx, w_d[:], wT_s[:], b_wTs, dstbufs=(b_wd,))

        ctx.claim(b_h, [b_sA, b_sB])
        w1er = w1e.rearrange("e (c p) f -> e p c f", p=128)
        w2er = w2e.rearrange("e (c p) f -> e p c f", p=128)
        for th in range(2):
            for e in range(8):
                e_load(ctx, wB[:, 0:1024],
                       w_d[e:e + 1, th * 1024:(th + 1) * 1024].to_broadcast(
                           [128, 1024]), b_wB, srcbufs=(b_wd,))
                for ft in range(32):
                    wb = ft % 2
                    e_load(ctx, wbuf[wb].rearrange("p (c f) -> p c f", c=8),
                           w1er[e, :, :, ft * 128:(ft + 1) * 128].rearrange(
                               "p c f -> p c f"), b_w[wb])
                    for sub in range(2):
                        pb = sub
                        ctx.group("pe",
                                  [(lambda E, dc=dc, wb=wb, sub=sub, th=th, pb=pb:
                                    E.matmul(
                                        bank(pb), wbuf[wb][:, dc * 128:(dc + 1) * 128],
                                        x3b[:, dc * TB + th * 1024 + sub * 512:
                                            dc * TB + th * 1024 + (sub + 1) * 512],
                                        start=(dc == 0), stop=(dc == 7)))
                                   for dc in range(8)],
                                  reads=(b_w[wb], b_x3b), writes=(b_bank[pb],))
                        ctx.op("act",
                               lambda E, ft=ft, sub=sub, e=e, pb=pb: E.activation(
                                   hbuf[:, ft * 1024 + sub * 512: ft * 1024 + (sub + 1) * 512],
                                   bank(pb), AF.Gelu,
                                   bias=b1_s[:, e * 32 + ft: e * 32 + ft + 1]),
                               reads=(b_bank[pb], b_b1), writes=(b_h,))
                for fc in range(8):
                    wb = fc % 2
                    e_load(ctx, wbuf2[wb].rearrange("p (c f) -> p c f", c=32),
                           w2er[e, :, :, fc * 128:(fc + 1) * 128], b_w2[wb])
                    for sub in range(2):
                        pb = 2 + sub
                        ctx.group("pe",
                                  [(lambda E, dc=dc, wb=wb, sub=sub, pb=pb: E.matmul(
                                      bank(pb), wbuf2[wb][:, dc * 128:(dc + 1) * 128],
                                      hbuf[:, dc * 1024 + sub * 512: dc * 1024 + (sub + 1) * 512],
                                      start=(dc == 0), stop=(dc == 31)))
                                   for dc in range(32)],
                                  reads=(b_w2[wb], b_h), writes=(b_bank[pb],))
                        yb = sub
                        ctx.op("act",
                               lambda E, fc=fc, e=e, pb=pb, yb=yb: E.activation(
                                   ytile[yb][:], bank(pb), AF.Identity,
                                   bias=b2_s[:, e * 8 + fc: e * 8 + fc + 1]),
                               reads=(b_bank[pb], b_b2), writes=(b_yt[yb],))
                        ctx.op("dve",
                               lambda E, sub=sub, yb=yb: E.tensor_tensor(
                                   tmp5[:], ytile[yb][:],
                                   wB[:, sub * 512:(sub + 1) * 512], ALU.mult),
                               reads=(b_yt[yb], b_wB), writes=(b_tmp5,))
                        if e == 0:
                            ctx.op("dve",
                                   lambda E, fc=fc, sub=sub: E.tensor_copy(
                                       oacc[:, fc * 1024 + sub * 512: fc * 1024 + (sub + 1) * 512],
                                       tmp5[:]),
                                   reads=(b_tmp5,), writes=(b_oacc,))
                        else:
                            ctx.op("dve",
                                   lambda E, fc=fc, sub=sub: E.tensor_tensor(
                                       oacc[:, fc * 1024 + sub * 512: fc * 1024 + (sub + 1) * 512],
                                       oacc[:, fc * 1024 + sub * 512: fc * 1024 + (sub + 1) * 512],
                                       tmp5[:], ALU.add),
                                   reads=(b_tmp5, b_oacc), writes=(b_oacc,))
            for sub in range(2):
                gsl = slice(th * 1024 + sub * 512, th * 1024 + (sub + 1) * 512)
                x4t = []
                for fc in range(8):
                    rb = fc % 2
                    e_load(ctx, rtile[rb][:], x3dr[fc, :, gsl], b_rt[rb],
                           srcbufs=(b_x3d,), q="dact")
                    ctx.op("dve",
                           lambda E, fc=fc, sub=sub, rb=rb: E.tensor_tensor(
                               oacc[:, fc * 1024 + sub * 512: fc * 1024 + (sub + 1) * 512],
                               oacc[:, fc * 1024 + sub * 512: fc * 1024 + (sub + 1) * 512],
                               rtile[rb][:], ALU.add),
                           reads=(b_rt[rb], b_oacc), writes=(b_oacc,))
                    x4t.append(oacc[:, fc * 1024 + sub * 512: fc * 1024 + (sub + 1) * 512])

                def applyF4(bcA, bcB, b_bc, gsl=gsl, x4t=x4t):
                    for fc in range(8):
                        ctx.op("dve",
                               lambda E, fc=fc, bcA=bcA: E.tensor_tensor(
                                   x4t[fc], x4t[fc], bcA, ALU.mult),
                               reads=(b_oacc, b_bc), writes=(b_oacc,))
                        ctx.op("dve",
                               lambda E, fc=fc, bcB=bcB: E.tensor_tensor(
                                   x4t[fc], x4t[fc], bcB, ALU.subtract),
                               reads=(b_oacc, b_bc), writes=(b_oacc,))
                        e_store(ctx, outTr[fc, :, gsl], x4t[fc], b_oacc,
                                dstbufs=(b_out,))

                ln_block(ctx, scr, x4t, [b_oacc] * 8, [applyF4], 512, LN_EPS,
                         ones_col[:], ones_row[:], b_ones, bank(4), bank(5),
                         b_bank[4], b_bank[5])

        ctx.barrier_all("sp")
        ctx.run()
    return nc


def _prep_launchF(inputs, xT_full, x1T_full, pouts):
    gwv = np.asarray(inputs["L1_ffn_gate_w"])
    gbv = np.asarray(inputs["L1_ffn_gate_b"])
    w1 = np.asarray(inputs["L1_ffn_w1"])
    b1 = np.asarray(inputs["L1_ffn_b1"])
    w2 = np.asarray(inputs["L1_ffn_w2"])
    b2 = np.asarray(inputs["L1_ffn_b2"])
    idm = np.eye(128, dtype=np.float32)
    xTf = np.concatenate([xT_full[b] for b in range(4)], axis=1)
    x1Tf = np.concatenate([x1T_full[b] for b in range(4)], axis=1)
    pAf = np.concatenate([pouts[2 * b] for b in range(4)], axis=1)
    pBf = np.concatenate([pouts[2 * b + 1] for b in range(4)], axis=1)
    com = {
        "gw": np.ascontiguousarray(gwv), "gb": gbv.reshape(8, 1).copy(),
        "w1e": np.ascontiguousarray(w1),
        "b1e": np.ascontiguousarray(b1.reshape(8, 32, 128).transpose(2, 0, 1).reshape(128, 256)),
        "w2e": np.ascontiguousarray(w2),
        "b2e": np.ascontiguousarray(b2.reshape(8, 8, 128).transpose(2, 0, 1).reshape(128, 64)),
        "ident": idm,
    }
    maps = []
    for c in range(8):
        tsl = slice(c * TB, (c + 1) * TB)
        m = dict(com)
        m["xT_s"] = np.ascontiguousarray(xTf[:, tsl])
        m["x1T_s"] = np.ascontiguousarray(x1Tf[:, tsl])
        m["pA_s"] = np.ascontiguousarray(pAf[:, tsl])
        m["pB_s"] = np.ascontiguousarray(pBf[:, tsl])
        maps.append(m)
    return maps



def _t(a):
    return np.ascontiguousarray(np.asarray(a).T)


def _mk_masks_kq():
    out = np.zeros((128, 4 * 512), dtype=np.float32)
    kr = np.arange(128)[:, None]
    qc = np.arange(512)[None, :]
    for i, D in enumerate((0, 128, 256, 384)):
        out[:, i * 512:(i + 1) * 512] = np.where(kr - qc > -D, NEG, 0.0)
    return out


def _prep_launchA(inputs):
    x = np.asarray(inputs["L0_attn_x"])
    Wqkv = np.asarray(inputs["L0_attn_W_qkv"])
    bqkv = np.asarray(inputs["L0_attn_b_qkv"])
    Wr = Wqkv.reshape(16, 64, 3, 1024)
    br = bqkv.reshape(16, 64, 3)
    masks = _mk_masks_kq()
    maps = []
    for c in range(8):
        b, hh = c // 2, c % 2
        hsel = slice(hh * 8, hh * 8 + 8)
        wq = Wr[hsel, :, 0, :].reshape(512, 1024)
        wk = Wr[hsel, :, 1, :].reshape(512, 1024)
        wv = Wr[hsel, :, 2, :].reshape(512, 1024)
        bq_ = br[hsel, :, 0].reshape(512)
        bk_ = br[hsel, :, 1].reshape(512)
        bv_ = br[hsel, :, 2].reshape(512)
        maps.append({
            "xT": _t(x[b]),
            "wqT": _t(wq), "wkT": _t(wk), "wvT": _t(wv),
            "bq": np.ascontiguousarray(bq_.reshape(4, 128).T),
            "bk": np.ascontiguousarray(bk_.reshape(4, 128).T),
            "bv": bv_.reshape(1, 512).copy(),
            "masks": masks,
        })
    return maps



MAXPOLL = 200000
TH = 2048


class Arena:

    def __init__(self, ctx, nc, nbytes):
        self.nbytes = nbytes
        self.ap = nc.alloc_sbuf_tensor("ARENA", [128, nbytes], mybir.dt.uint8).ap()
        self.ctx = ctx
        self.off = 0
        self.cur = []
        self.pend = []

    def reset(self):
        if not hasattr(self, "peaks"):
            self.peaks = []
        self.peaks.append(self.off)
        pend = []
        for b in self.cur:
            if b.writer is not None:
                pend.append(b.writer)
            pend += b.readers
        self.pend = list({p: None for p in pend})
        self.cur = []
        self.off = 0

    def take(self, p, n, dtype, name=None):
        esz = mybir.dt.size(dtype)
        off = (self.off + 63) & ~63
        self.off = off + n * esz
        assert self.off <= self.nbytes, f"arena overflow {self.off}"
        ap = self.ap[0:p, off:off + n * esz].bitcast(dtype)
        b = self.ctx.buf(name)
        b.readers = list(self.pend)
        self.cur.append(b)
        return ap, b


def build_fused():
    nc = bass.Bass("TRN2", target_bir_lowering=False,
                   detect_race_conditions=False)
    dt_ = nc.dram_tensor

    xT = dt_("xT", [DM, S], F32, kind="ExternalInput").ap()
    xTh = dt_("xTh", [DM, TH], F32, kind="ExternalInput").ap()
    wqT = dt_("wqT", [DM, HF], F32, kind="ExternalInput").ap()
    wkT = dt_("wkT", [DM, HF], F32, kind="ExternalInput").ap()
    wvT = dt_("wvT", [DM, HF], F32, kind="ExternalInput").ap()
    bq = dt_("bq", [128, 4], F32, kind="ExternalInput").ap()
    bk = dt_("bk", [128, 4], F32, kind="ExternalInput").ap()
    bv = dt_("bv", [1, HF], F32, kind="ExternalInput").ap()
    masksL0 = dt_("masksL0", [128, 2048], F32, kind="ExternalInput").ap()
    masksQK = dt_("masksQK", [128, 2048], F32, kind="ExternalInput").ap()
    masksKQ = dt_("masksKQ", [128, 2048], F32, kind="ExternalInput").ap()
    ident = dt_("ident", [128, 128], F32, kind="ExternalInput").ap()
    woT = dt_("woT", [DM, DM], F32, kind="ExternalInput").ap()
    bo = dt_("bo", [128, 8], F32, kind="ExternalInput").ap()
    w1T = dt_("w1T", [DM, 4096], F32, kind="ExternalInput").ap()
    w2T = dt_("w2T", [DM, 4096], F32, kind="ExternalInput").ap()
    h1T = dt_("h1T", [4096, 4096], F32, kind="ExternalInput").ap()
    h2T = dt_("h2T", [4096, 4096], F32, kind="ExternalInput").ap()
    w3T = dt_("w3T", [4096, DM], F32, kind="ExternalInput").ap()
    s1 = dt_("s1", [128, 32], F32, kind="ExternalInput").ap()
    s2 = dt_("s2", [128, 32], F32, kind="ExternalInput").ap()
    sh1 = dt_("sh1", [128, 32], F32, kind="ExternalInput").ap()
    sh2 = dt_("sh2", [128, 32], F32, kind="ExternalInput").ap()
    s3 = dt_("s3", [128, 8], F32, kind="ExternalInput").ap()
    wq1 = dt_("wq1", [DM, HF], F32, kind="ExternalInput").ap()
    wkv = dt_("wkv", [DM, 128], F32, kind="ExternalInput").ap()
    bqm = dt_("bqm", [128, 4], F32, kind="ExternalInput").ap()
    bkv = dt_("bkv", [128, 1], F32, kind="ExternalInput").ap()
    wao = dt_("wao", [DM, DM], F32, kind="ExternalInput").ap()
    bao = dt_("bao", [128, 8], F32, kind="ExternalInput").ap()
    w_u = dt_("w_u", [DM, DM], F32, kind="ExternalInput").ap()
    w_r = dt_("w_r", [DM, DM], F32, kind="ExternalInput").ap()
    convw = dt_("convw", [128, 32], F32, kind="ExternalInput").ap()
    convb = dt_("convb", [128, 8], F32, kind="ExternalInput").ap()
    wxp = dt_("wxp", [DM, 96], F32, kind="ExternalInput").ap()
    wdt = dt_("wdt", [DTR, DM], F32, kind="ExternalInput").ap()
    bdt = dt_("bdt", [128, 8], F32, kind="ExternalInput").ap()
    Alog = dt_("Alog", [128, 8 * NST], F32, kind="ExternalInput").ap()
    Dpi = dt_("Dp", [128, 8], F32, kind="ExternalInput").ap()
    wout = dt_("wout", [DM, DM], F32, kind="ExternalInput").ap()
    gw = dt_("gw", [DM, 8], F32, kind="ExternalInput").ap()
    gb = dt_("gb", [8, 1], F32, kind="ExternalInput").ap()
    w1e = dt_("w1e", [4, DM, 4096], F32, kind="ExternalInput").ap()
    b1e = dt_("b1e", [128, 4 * 32], F32, kind="ExternalInput").ap()
    w2e = dt_("w2e", [4, 4096, DM], F32, kind="ExternalInput").ap()
    b2e = dt_("b2e", [128, 4 * 8], F32, kind="ExternalInput").ap()
    fv = dt_("fv", [1, 16], I32, kind="ExternalInput").ap()
    fv2 = dt_("fv2", [1, 16], I32, kind="ExternalInput").ap()
    outT = dt_("outT", [DM, TH], F32, kind="ExternalOutput").ap()
    pdiag = dt_("pdiag", [1, 16], I32, kind="ExternalOutput").ap()

    w1b_d = dt_("w1b_d", [DM, 4096], BF16, kind="Internal").ap()
    w2b_d = dt_("w2b_d", [DM, 4096], BF16, kind="Internal").ap()
    h1b_d = dt_("h1b_d", [4096, 4096], BF16, kind="Internal").ap()
    h2b_d = dt_("h2b_d", [4096, 4096], BF16, kind="Internal").ap()
    w3b_d = dt_("w3b_d", [4096, DM], BF16, kind="Internal").ap()
    w1eb_d = dt_("w1eb_d", [4, DM, 4096], BF16, kind="Internal").ap()
    w2eb_d = dt_("w2eb_d", [4, 4096, DM], BF16, kind="Internal").ap()
    xn_d = dt_("xn_d", [DM, S], BF16, kind="Internal").ap()
    x2h_d = dt_("x2h_d", [DM, TH], F32, kind="Internal").ap()
    x1b_d = dt_("x1b_d", [DM, TH], F32, kind="Internal").ap()
    x1bb_d = dt_("x1bb_d", [DM, TH], BF16, kind="Internal").ap()
    x1mh_d = dt_("x1mh_d", [DM, TH], F32, kind="Internal").ap()
    uT_d = dt_("uT_d", [DM, S], BF16, kind="Internal").ap()
    svT_d = dt_("svT_d", [DM, S], BF16, kind="Internal").ap()
    y_d = dt_("y_d", [DM, S], BF16, kind="Internal").ap()
    xbc_d = dt_("xbc_d", [2 * NST, S], BF16, kind="Internal").ap()
    nb_d = dt_("nb_d", [NH, 32, 128], F32, kind="Internal").ap()
    x3h_d = dt_("x3h_d", [DM, TH], F32, kind="Internal").ap()
    dbg_lg = dt_("dbg_lg", [8, TH], F32, kind="Internal").ap()

    sh = dict(
        Z0=dt_("sh_Z0", [DM, S], BF16, kind="Internal", addr_space="Shared").ap(),
        X2=dt_("sh_X2", [DM, S], BF16, kind="Internal", addr_space="Shared").ap(),
        Z1=dt_("sh_Z1", [DM, S], BF16, kind="Internal", addr_space="Shared").ap(),
        X1M=dt_("sh_X1M", [DM, S], BF16, kind="Internal", addr_space="Shared").ap(),
        XD=dt_("sh_XD", [2, 96, S], F32, kind="Internal", addr_space="Shared").ap(),
        PO=dt_("sh_PO", [2, DM, S], BF16, kind="Internal", addr_space="Shared").ap(),
        X3=dt_("sh_X3", [DM, S], BF16, kind="Internal", addr_space="Shared").ap(),
        WG=dt_("sh_WG", [8, S], F32, kind="Internal", addr_space="Shared").ap(),
        MO=dt_("sh_MO", [2, DM, S], BF16, kind="Internal", addr_space="Shared").ap(),
        FL=dt_("sh_FL", [16, 2], I32, kind="Internal", addr_space="Shared").ap(),
    )

    with ExitStack() as st:
        ctx = Ctx(nc, st)
        al = nc.alloc_sbuf_tensor
        psem = st.enter_context(nc.semaphore("psem"))

        ones_col = al("ones_col", [128, 1], F32).ap()
        ones_colB = al("ones_colB", [128, 1], BF16).ap()
        ones_row = al("ones_row", [1, 128], BF16).ap()
        ones_rowF = al("ones_rowF", [1, 128], F32).ap()
        id_s = al("id_s", [128, 128], F32).ap()
        smalls = {}
        b_small = {}
        for nm, src, shp in (("bq", bq, (128, 4)), ("bk", bk, (128, 4)),
                             ("bo", bo, (128, 8)), ("s1", s1, (128, 32)),
                             ("s2", s2, (128, 32)), ("sh1", sh1, (128, 32)),
                             ("sh2", sh2, (128, 32)), ("s3", s3, (128, 8)),
                             ("bqm", bqm, (128, 4)), ("bkv", bkv, (128, 1)),
                             ("bao", bao, (128, 8)), ("convw", convw, (128, 32)),
                             ("convb", convb, (128, 8)), ("bdt", bdt, (128, 8)),
                             ("Alog", Alog, (128, 8 * NST)), ("Dp", Dpi, (128, 8)),
                             ("gb", gb, (8, 1)),
                             ("b1e", b1e, (128, 128)), ("b2e", b2e, (128, 32))):
            smalls[nm] = al(f"sm_{nm}", list(shp), F32).ap()
            b_small[nm] = ctx.buf(f"sm_{nm}")
            e_load(ctx, smalls[nm], src, b_small[nm])
        sfv = al("sfv", [1, 16], I32).ap()
        sfv2 = al("sfv2", [1, 16], I32).ap()
        spoll = al("spoll", [1, 2], I32).ap()
        sdiag = al("sdiag", [1, 16], I32).ap()
        b_sfv = ctx.buf("sfv"); b_sfv2 = ctx.buf("sfv2")
        b_sdiag = ctx.buf("sdiag")
        e_load(ctx, sfv, fv, b_sfv)
        e_load(ctx, sfv2, fv2, b_sfv2)
        b_ones = ctx.buf("ones"); b_id = ctx.buf("id")
        ctx.op("dve", lambda E: E.memset(ones_col[:], 1.0), writes=(b_ones,))
        ctx.op("dve", lambda E: E.memset(ones_colB[:], 1.0), writes=(b_ones,))
        ctx.op("dve", lambda E: E.memset(ones_row[:], 1.0), writes=(b_ones,))
        ctx.op("dve", lambda E: E.memset(ones_rowF[:], 1.0), writes=(b_ones,))
        ctx.op("dve", lambda E: E.memset(spoll[:], 0), writes=(b_sfv,))
        ctx.op("dve", lambda E: E.memset(sdiag[:], 0), writes=(b_sdiag,))
        e_load(ctx, id_s[:], ident, b_id)
        scr = LNScr(ctx, nc)
        bv_row = al("bv_row", [1, HF], BF16).ap()
        b_bvrow = ctx.buf()
        e_load(ctx, bv_row, bv, b_bvrow)
        bvB = al("bvB", [128, HF], F32).ap()
        b_bvB = ctx.buf()
        bvrow1 = al("bvrow1", [1, DH], BF16).ap()
        b_bvrow1 = ctx.buf()
        e_load(ctx, bvrow1, bkv[64:128, :].rearrange("p one -> one p"), b_bvrow1)
        bvB1 = al("bvB1", [128, DH], F32).ap()
        b_bvB1 = ctx.buf()
        bqsc = al("bqsc", [128, 4], F32).ap()
        b_bqsc = ctx.buf()

        PS = nc.alloc_psum_tensor("PS", [128, 4096], F32).ap()

        def bank(i, rows=128):
            return PS[0:rows, i * 512:(i + 1) * 512]

        b_bank = [ctx.buf(f"bank{i}") for i in range(8)]

        AR = Arena(ctx, nc, 176 * 1024)

        b_d = {nm: ctx.buf(nm) for nm in
               ("xn", "x2h", "x1b", "x1bb", "x1mh", "uT", "svT", "y", "xbc",
                "nb", "x3h")}
        b_sh = {nm: ctx.buf("sh" + nm) for nm in sh}
        b_out = ctx.buf("outx")
        b_phase = [ctx.buf(f"ph{i}") for i in range(16)]
        b_diagout = ctx.buf("diagout")

        ctx.hh = None

        ctx.hh_handle = ctx.eng["pool"].alloc_register("r_hh")

        def _pool_regs(E):
            pid = E.partition_id()
            E.reg_mov(ctx.hh_handle, E.compute_val(pid % 2))
        ctx.raw("pool", _pool_regs)

        act = ctx.eng["act"]
        pregs = {k: act.alloc_register(f"pr_{k}") for k in
                 ("want2", "it", "go", "pc", "v0", "v1", "sum", "diff",
                  "neq", "cap")}

        def _act_regs(E):
            E.reg_mov(pregs["pc"], 0)
        ctx.raw("act", _act_regs)

        def send_flag(r, databufs):
            ctx.dma_cond("dpool",
                    lambda E, hh, r=r: E.dma_start(
                        out=sh["FL"][r:r + 1, hh:hh + 1],
                        in_=sfv[:, r:r + 1]),
                    reads=tuple(databufs) + (b_sfv,), writes=(b_phase[r],))

        def poll(r):
            def seq(E, r=r):
                E.load(pregs["want2"], sfv2[0:1, r:r + 1])
                E.reg_mov(pregs["it"], 0)
                E.reg_mov(pregs["go"], 1)
                with E.While(pregs["go"]):
                    E.dma_start(out=spoll, in_=sh["FL"][r:r + 1, :]).then_inc(psem, 16)
                    E.reg_add(pregs["pc"], pregs["pc"], 16)
                    E.wait_ge(psem, pregs["pc"])
                    E.load(pregs["v0"], spoll[0:1, 0:1])
                    E.load(pregs["v1"], spoll[0:1, 1:2])
                    E.reg_add(pregs["sum"], pregs["v0"], pregs["v1"])
                    E.reg_sub(pregs["diff"], pregs["sum"], pregs["want2"])
                    E.reg_add(pregs["it"], pregs["it"], 1)
                    E.reg_alu(pregs["neq"], pregs["diff"], 0, ALU.not_equal)
                    E.reg_alu(pregs["cap"], pregs["it"], MAXPOLL, ALU.is_lt)
                    E.reg_alu(pregs["go"], pregs["neq"], pregs["cap"],
                              ALU.bitwise_and)
                E.reg_save(sdiag[0:1, r:r + 1], pregs["it"])
            ctx._pre("act", (b_sfv2, b_phase[r]), (b_sdiag,))
            ctx.raw("act", seq)
            ctx.op("act", lambda E: E.nop(), writes=(b_phase[r],))

        _voff = (AR.off + 63) & ~63
        xtileA = [AR.take(128, 4096, F32) for _ in range(2)]
        VA = (AR.ap[0:128, _voff:_voff + 32768].bitcast(BF16), ctx.buf("VA"))
        sqA = [AR.take(128, 512, BF16) for _ in range(2)]
        rstdrow = AR.take(1, 512, BF16)
        rstdB = AR.take(128, S, F32)
        xncA = [AR.take(128, S, BF16) for _ in range(2)]
        wqA = AR.take(128, 8 * HF, BF16)
        wkA = AR.take(128, 8 * HF, BF16)
        wvA = AR.take(128, 8 * HF, BF16)
        maskA = AR.take(128, 2048, F32)
        QTA = AR.take(128, 4 * S, BF16)
        KTA = AR.take(128, 4 * S, BF16)
        mA = AR.take(128, NH * 32, F32)
        sA_ = AR.take(128, NH * 32, F32)
        nbA = AR.take(128, NH * 32, F32)
        expA = AR.take(128, 2048, BF16)
        alpP = [AR.take(128, 1024, BF16) for _ in range(2)]
        scr1A = AR.take(128, 1, F32)
        scrHA = AR.take(128, 1, F32)
        scrS1 = AR.take(128, 1, F32)
        scrS2 = AR.take(128, 1, F32)
        scr32A = AR.take(128, 32, F32)
        zsegA = [(rstdB[0][0:64, i * 256:(i + 1) * 256].bitcast(BF16),
                  ctx.buf(f"zsegA{i}")) for i in range(2)]

        e_load(ctx, maskA[0], masksL0, maskA[1], q="dsp")
        e_load(ctx, wqA[0].rearrange("p (c f) -> p c f", c=8),
               wqT.rearrange("(c p) f -> p c f", p=128), wqA[1])
        e_load(ctx, wkA[0].rearrange("p (c f) -> p c f", c=8),
               wkT.rearrange("(c p) f -> p c f", p=128), wkA[1])
        e_load(ctx, wvA[0].rearrange("p (c f) -> p c f", c=8),
               wvT.rearrange("(c p) f -> p c f", p=128), wvA[1])
        b_wcast = ctx.buf("wcast")
        for dst_, src_ in ((w1b_d, w1T), (w2b_d, w2T), (w3b_d, w3T)):
            ctx.dma("dpool", lambda E, dst_=dst_, src_=src_: E.dma_start(
                out=dst_, in_=src_), writes=(b_wcast,))
        for big_dst, big_src in ((h1b_d, h1T), (h2b_d, h2T)):
            for qq in range(4):
                ctx.dma("dpool", lambda E, big_dst=big_dst, big_src=big_src, qq=qq:
                        E.dma_start(out=big_dst[qq * 1024:(qq + 1) * 1024, :],
                                    in_=big_src[qq * 1024:(qq + 1) * 1024, :]),
                        writes=(b_wcast,))
        for e in range(4):
            ctx.dma("dpool", lambda E, e=e: E.dma_start(
                out=w1eb_d[e], in_=w1e[e]), writes=(b_wcast,))
            for qq in range(4):
                ctx.dma("dpool", lambda E, e=e, qq=qq: E.dma_start(
                    out=w2eb_d[e][qq * 1024:(qq + 1) * 1024, :],
                    in_=w2e[e][qq * 1024:(qq + 1) * 1024, :]),
                    writes=(b_wcast,))

        ctx.group("pe", [lambda E: E.matmul(PS[:, 0:HF], ones_row, bv_row,
                                            start=True, stop=True)],
                  reads=(b_bvrow, b_ones), writes=(b_bank[0],))
        ctx.op("dve", lambda E: E.tensor_copy(bvB[:], PS[:, 0:HF]),
               reads=(b_bank[0],), writes=(b_bvB,))

        xTr = xT.rearrange("(c p) t -> c p t", p=128)
        xn_dr = xn_d.rearrange("(c p) t -> c p t", p=128)
        for dc in range(8):
            e_load(ctx, xtileA[dc % 2][0], xTr[dc], xtileA[dc % 2][1],
                   q="dsp")
            for tt in range(8):
                ctx.op("act",
                       lambda E, dc=dc, tt=tt: E.activation(
                           sqA[tt % 2][0],
                           xtileA[dc % 2][0][:, tt * 512:(tt + 1) * 512],
                           AF.Square),
                       reads=(xtileA[dc % 2][1],), writes=(sqA[tt % 2][1],))
                ctx.group("pe",
                          [lambda E, dc=dc, tt=tt: E.matmul(
                              bank(tt, 1), ones_colB, sqA[tt % 2][0],
                              start=(dc == 0), stop=(dc == 7))],
                          reads=(sqA[tt % 2][1], b_ones), writes=(b_bank[tt],))
        c_eps = ctx.constcol(RMS_EPS)
        for tt in range(8):
            ctx.op("act",
                   lambda E, tt=tt: E.activation(
                       rstdrow[0], bank(tt, 1),
                       AF.Ln, bias=c_eps[0:1, :], scale=1.0 / DM),
                   reads=(b_bank[tt], ctx._cbuf), writes=(rstdrow[1],))
            ctx.op("act",
                   lambda E, tt=tt: E.activation(
                       rstdrow[0], rstdrow[0], AF.Exp, scale=-0.5),
                   reads=(rstdrow[1],), writes=(rstdrow[1],))
            ctx.group("pe",
                      [lambda E, tt=tt: E.matmul(
                          bank(tt), ones_row, rstdrow[0],
                          start=True, stop=True)],
                      reads=(rstdrow[1], b_ones), writes=(b_bank[tt],))
            ctx.op("dve",
                   lambda E, tt=tt: E.tensor_copy(
                       rstdB[0][:, tt * 512:(tt + 1) * 512], bank(tt)),
                   reads=(b_bank[tt],), writes=(rstdB[1],))
        for dc in range(8):
            e_load(ctx, xtileA[dc % 2][0], xTr[dc], xtileA[dc % 2][1],
                   q="dsp")
            ctx.op("dve",
                   lambda E, dc=dc: E.tensor_tensor(
                       xncA[dc % 2][0], xtileA[dc % 2][0], rstdB[0], ALU.mult),
                   reads=(xtileA[dc % 2][1], rstdB[1]), writes=(xncA[dc % 2][1],))
            e_store(ctx, xn_dr[dc], xncA[dc % 2][0], xncA[dc % 2][1],
                    dstbufs=(b_d["xn"],))
        ctx.claim(zsegA[0][1], [rstdB[1]])
        zsegA[1][1].readers = list(zsegA[0][1].readers)
        for tt in range(8):
            for dc in range(8):
                xb = xncA[dc % 2][1]
                e_load(ctx, xncA[dc % 2][0][:, 0:512],
                       xn_dr[dc][:, tt * 512:(tt + 1) * 512], xb,
                       srcbufs=(b_d["xn"],), q="dsp")
                for fc in range(4):
                    ctx.group("pe",
                              [(lambda E, fc=fc, dc=dc: E.matmul(
                                  bank(fc),
                                  wqA[0][:, dc * HF + fc * 128: dc * HF + fc * 128 + 128],
                                  xncA[dc % 2][0][:, 0:512],
                                  start=(dc == 0), stop=(dc == 7)))],
                              reads=(wqA[1], xb), writes=(b_bank[fc],))
                    ctx.group("pe",
                              [(lambda E, fc=fc, dc=dc: E.matmul(
                                  bank(4 + fc),
                                  wkA[0][:, dc * HF + fc * 128: dc * HF + fc * 128 + 128],
                                  xncA[dc % 2][0][:, 0:512],
                                  start=(dc == 0), stop=(dc == 7)))],
                              reads=(wkA[1], xb), writes=(b_bank[4 + fc],))
            for fc in range(4):
                ctx.op("act",
                       lambda E, fc=fc, tt=tt: E.activation(
                           QTA[0][:, fc * S + tt * 512: fc * S + (tt + 1) * 512],
                           bank(fc), AF.Identity, bias=smalls["bq"][:, fc:fc + 1]),
                       reads=(b_bank[fc], b_small["bq"]), writes=(QTA[1],))
                ctx.op("act",
                       lambda E, fc=fc, tt=tt: E.activation(
                           KTA[0][:, fc * S + tt * 512: fc * S + (tt + 1) * 512],
                           bank(4 + fc), AF.Identity,
                           bias=smalls["bk"][:, fc:fc + 1]),
                       reads=(b_bank[4 + fc], b_small["bk"]), writes=(KTA[1],))
        ctx.claim(VA[1], [xtileA[0][1], xtileA[1][1]])
        for tt in range(8):
            for dc in range(8):
                xb = xncA[dc % 2][1]
                e_load(ctx, xncA[dc % 2][0][:, 0:512],
                       xn_dr[dc][:, tt * 512:(tt + 1) * 512], xb,
                       srcbufs=(b_d["xn"],), q="dsp")
                for sub in range(4):
                    ctx.group("pe",
                              [(lambda E, dc=dc, sub=sub: E.matmul(
                                  bank(sub),
                                  xncA[dc % 2][0][:, sub * 128:(sub + 1) * 128],
                                  wvA[0][:, dc * HF:(dc + 1) * HF],
                                  start=(dc == 0), stop=(dc == 7)))],
                              reads=(wvA[1], xb), writes=(b_bank[sub],))
            for sub in range(4):
                ti = tt * 4 + sub
                ctx.op("dve",
                       lambda E, ti=ti, sub=sub: E.tensor_tensor(
                           VA[0][:, ti * HF:(ti + 1) * HF], bank(sub), bvB[:],
                           ALU.add),
                       reads=(b_bank[sub], b_bvB), writes=(VA[1],))

        def hsA(h):
            return h // 2, (h % 2) * 64

        def s_matmulA(E, h, kt, qt, pb):
            fc, p0 = hsA(h)
            return E.matmul(
                bank(pb),
                KTA[0][p0:p0 + 64, fc * S + kt * 128: fc * S + kt * 128 + 128],
                QTA[0][p0:p0 + 64, fc * S + qt * 512: fc * S + (qt + 1) * 512],
                start=True, stop=True)

        def mask_addA(E, kt, qt, pb):
            mi = (kt * 128 - qt * 512) // 128
            return E.tensor_tensor(bank(pb), bank(pb),
                                   maskA[0][:, mi * 512:(mi + 1) * 512], ALU.add)

        def emit_scoreA(h, kt, qt, pb):
            ctx.group("pe", [lambda E: s_matmulA(E, h, kt, qt, pb)],
                      reads=(KTA[1], QTA[1]), writes=(b_bank[pb],))
            if 0 <= kt * 128 - qt * 512 <= 384:
                ctx.op("dve", lambda E: mask_addA(E, kt, qt, pb),
                       reads=(b_bank[pb], maskA[1]), writes=(b_bank[pb],))

        stats_par = [0]

        def stats_row(tiles, partials, col, mrow, mbuf, srow, sbuf, nrow, nbuf,
                      emit_tile, emit_mask, mm_reads, mask_buf,
                      scrM, scrS, expbuf, ncol=None):
            if ncol is None:
                ncol = col
            nl = len(tiles)
            chunks = [tiles[0:4]] if nl <= 4 else [tiles[0:4], tiles[4:]]
            info = []
            for ci, tch in enumerate(chunks):
                bb = 4 * stats_par[0]
                stats_par[0] ^= 1
                bnk = tuple(b_bank[bb + i] for i in range(len(tch)))
                ctx.group("pe",
                          [(lambda E, t=t, i=i, bb=bb: emit_tile(E, t, bb + i))
                           for i, t in enumerate(tch)],
                          reads=mm_reads, writes=bnk)
                for i, t in enumerate(tch):
                    if t in partials:
                        ctx.op("dve",
                               lambda E, t=t, i=i, bb=bb: emit_mask(E, t,
                                                                    bb + i),
                               reads=(b_bank[bb + i], mask_buf),
                               writes=(b_bank[bb + i],))
                w = len(tch) * 512
                if len(chunks) == 1:
                    ctx.op("dve",
                           lambda E, w=w, bb=bb, col=col: E.reduce_max(
                               mrow[0][:, col:col + 1],
                               PS[:, bb * 512: bb * 512 + w], axis=AX.X),
                           reads=bnk + (mbuf,), writes=(mbuf,))
                else:
                    scr = scrM[ci]
                    ctx.op("dve",
                           lambda E, w=w, bb=bb, scr=scr: E.reduce_max(
                               scr[0], PS[:, bb * 512: bb * 512 + w],
                               axis=AX.X),
                           reads=bnk, writes=(scr[1],))
                info.append((bb, bnk, w))
            if len(chunks) == 2:
                ctx.op("dve",
                       lambda E, col=col: E.tensor_tensor(
                           mrow[0][:, col:col + 1], scrM[0][0], scrM[1][0],
                           ALU.max),
                       reads=(scrM[0][1], scrM[1][1], mbuf), writes=(mbuf,))
            ctx.op("dve",
                   lambda E, col=col, ncol=ncol: E.tensor_scalar(
                       nrow[0][:, ncol:ncol + 1], mrow[0][:, col:col + 1], -1.0,
                       None, ALU.mult),
                   reads=(mbuf,), writes=(nbuf,))
            if len(chunks) == 1:
                bb, bnk, w = info[0]
                ctx.op("act",
                       lambda E, w=w, bb=bb, col=col, ncol=ncol: E.activation(
                           expbuf[0][:, 0:w], PS[:, bb * 512: bb * 512 + w],
                           AF.Exp, bias=nrow[0][:, ncol:ncol + 1],
                           accum_out=srow[0][:, col:col + 1]),
                       reads=bnk + (nbuf, sbuf), writes=(expbuf[1], sbuf))
            else:
                for ci, (bb, bnk, w) in enumerate(info):
                    scr = scrS[ci]
                    ctx.op("act",
                           lambda E, w=w, bb=bb, col=col, ncol=ncol, scr=scr: E.activation(
                               expbuf[0][:, 0:w], PS[:, bb * 512: bb * 512 + w],
                               AF.Exp, bias=nrow[0][:, ncol:ncol + 1],
                               accum_out=scr[0]),
                           reads=bnk + (nbuf,), writes=(expbuf[1], scr[1]))
                ctx.op("dve",
                       lambda E, col=col: E.tensor_tensor(
                           srow[0][:, col:col + 1], scrS[0][0], scrS[1][0],
                           ALU.add),
                       reads=(scrS[0][1], scrS[1][1], sbuf), writes=(sbuf,))

        for h in range(NH):
            for kt in range(32):
                col = h * 32 + kt
                qts = [qt for qt in range(8) if kt * 128 - qt * 512 <= 384]
                parts = {qt for qt in qts if 0 <= kt * 128 - qt * 512 <= 384}
                stats_row(qts, parts, col, mA, mA[1], sA_, sA_[1], nbA, nbA[1],
                          lambda E, qt, pb, h=h, kt=kt: s_matmulA(E, h, kt, qt,
                                                                  pb),
                          lambda E, qt, pb, kt=kt: mask_addA(E, kt, qt, pb),
                          (KTA[1], QTA[1]), maskA[1],
                          (scr1A, scrHA), (scrS1, scrS2), expA)
            c0 = h * 32
            ctx.op("act",
                   lambda E, c0=c0: E.activation(scr32A[0],
                                                 sA_[0][:, c0:c0 + 32], AF.Ln),
                   reads=(sA_[1],), writes=(scr32A[1],))
            ctx.op("dve",
                   lambda E, c0=c0: E.tensor_tensor(
                       scr32A[0], scr32A[0], mA[0][:, c0:c0 + 32], ALU.add),
                   reads=(scr32A[1], mA[1]), writes=(scr32A[1],))
            ctx.op("dve",
                   lambda E, c0=c0: E.tensor_scalar(
                       nbA[0][:, c0:c0 + 32], scr32A[0], -1.0, None, ALU.mult),
                   reads=(scr32A[1],), writes=(nbA[1],))
            for qp in range(4):
                qa, qb = 2 * qp, 2 * qp + 1
                ncom = 4 * qa + 4
                ntot = ncom + 4

                def z_emit(kt, par, both, h=h, qa=qa, qb=qb, ncom=ncom,
                           ntot=ntot):
                    if both:
                        ctx.group("pe",
                                  [lambda E, h=h, kt=kt, par=par: E.matmul(
                                      bank(2, 64),
                                      VA[0][:, kt * HF + h * 64:
                                            kt * HF + (h + 1) * 64],
                                      alpP[par][0][:, 0:512],
                                      start=(kt == 0), stop=(kt == ncom - 1))],
                                  reads=(VA[1], alpP[par][1]),
                                  writes=(b_bank[2],))
                        ctx.group("pe",
                                  [lambda E, h=h, kt=kt, par=par: E.matmul(
                                      bank(3, 64),
                                      VA[0][:, kt * HF + h * 64:
                                            kt * HF + (h + 1) * 64],
                                      alpP[par][0][:, 512:1024],
                                      start=(kt == 0), stop=False)],
                                  reads=(VA[1], alpP[par][1]),
                                  writes=(b_bank[3],))
                    else:
                        ctx.group("pe",
                                  [lambda E, h=h, kt=kt, par=par: E.matmul(
                                      bank(3, 64),
                                      VA[0][:, kt * HF + h * 64:
                                            kt * HF + (h + 1) * 64],
                                      alpP[par][0][:, 0:512],
                                      start=False, stop=(kt == ntot - 1))],
                                  reads=(VA[1], alpP[par][1]),
                                  writes=(b_bank[3],))

                pend = None
                for kt in range(ntot):
                    col = h * 32 + kt
                    par = kt % 2
                    pb = 4 + 2 * par
                    if kt < ncom:
                        ctx.group("pe",
                                  [lambda E, h=h, kt=kt, qa=qa, pb=pb:
                                   s_matmulA(E, h, kt, qa, pb),
                                   lambda E, h=h, kt=kt, qb=qb, pb=pb:
                                   s_matmulA(E, h, kt, qb, pb + 1)],
                                  reads=(KTA[1], QTA[1]),
                                  writes=(b_bank[pb], b_bank[pb + 1]))
                        if kt >= 4 * qa:
                            ctx.op("dve",
                                   lambda E, kt=kt, qa=qa, pb=pb: mask_addA(
                                       E, kt, qa, pb),
                                   reads=(b_bank[pb], maskA[1]),
                                   writes=(b_bank[pb],))
                        ctx.op("act",
                               lambda E, col=col, pb=pb, par=par: E.activation(
                                   alpP[par][0], PS[:, pb * 512:(pb + 2) * 512],
                                   AF.Exp, bias=nbA[0][:, col:col + 1]),
                               reads=(b_bank[pb], b_bank[pb + 1], nbA[1]),
                               writes=(alpP[par][1],))
                    else:
                        ctx.group("pe",
                                  [lambda E, h=h, kt=kt, qb=qb, pb=pb:
                                   s_matmulA(E, h, kt, qb, pb)],
                                  reads=(KTA[1], QTA[1]), writes=(b_bank[pb],))
                        ctx.op("dve",
                               lambda E, kt=kt, qb=qb, pb=pb: mask_addA(
                                   E, kt, qb, pb),
                               reads=(b_bank[pb], maskA[1]),
                               writes=(b_bank[pb],))
                        ctx.op("act",
                               lambda E, col=col, pb=pb, par=par: E.activation(
                                   alpP[par][0][:, 0:512], bank(pb), AF.Exp,
                                   bias=nbA[0][:, col:col + 1]),
                               reads=(b_bank[pb], nbA[1]),
                               writes=(alpP[par][1],))
                    if pend is not None:
                        pend()
                    pend = (lambda kt=kt, par=par, both=(kt < ncom):
                            z_emit(kt, par, both))
                pend()
                for sb, (qq, zb) in enumerate(((qa, 2), (qb, 3))):
                    ctx.op("dve",
                           lambda E, zb=zb, sb=sb: E.tensor_copy(
                               zsegA[sb][0], bank(zb, 64)),
                           reads=(b_bank[zb],), writes=(zsegA[sb][1],))
                    ctx.dma_cond("dpool",
                                 lambda E, hh, h=h, qq=qq, sb=sb: E.dma_start(
                                     out=sh["Z0"][hh * 512 + h * 64:
                                                  hh * 512 + (h + 1) * 64,
                                                  qq * 512:(qq + 1) * 512],
                                     in_=zsegA[sb][0]),
                                 reads=(zsegA[sb][1],), writes=(b_sh["Z0"],))
        send_flag(0, (b_sh["Z0"],))

        AR.reset()
        poll(0)
        zbuf2P = [AR.take(128, 4096, BF16) for _ in range(2)]
        xbuf2P = [AR.take(128, 4096, F32) for _ in range(2)]
        woA = AR.take(128, 8192, BF16)
        t1A = [AR.take(128, 512, BF16) for _ in range(2)]
        e_load(ctx, woA[0].rearrange("p (c f) -> p c f", c=8),
               woT.rearrange("(c p) f -> p c f", p=128), woA[1])
        Z0r = sh["Z0"].rearrange("(c p) t -> p c t", p=128)
        xThr = xTh.rearrange("(c p) t -> p c t", p=128)
        x1dr = x1b_d.rearrange("(c p) t -> c p t", p=128)
        x1bdr = x1bb_d.rearrange("(c p) t -> c p t", p=128)
        for tt in range(4):
            tsl = slice(tt * 512, (tt + 1) * 512)
            zbuf2 = zbuf2P[tt % 2]
            xbuf2 = xbuf2P[tt % 2]
            ctx.dma_cond("dpool",
                    lambda E, hh, tt=tt, zbuf2=zbuf2: E.dma_start(
                        out=zbuf2[0].rearrange("p (c t) -> p c t", c=8),
                        in_=Z0r[:, :, hh * TH + tt * 512: hh * TH + (tt + 1) * 512]),
                    reads=(b_phase[0], b_sh["Z0"]), writes=(zbuf2[1],))
            e_load(ctx, xbuf2[0].rearrange("p (c t) -> p c t", c=8),
                   xThr[:, :, tsl], xbuf2[1], q="dsp")
            for fc in range(8):
                pb = fc % 2
                ctx.group("pe",
                          [(lambda E, dc=dc, fc=fc, pb=pb, zbuf2=zbuf2:
                            E.matmul(
                              bank(pb), woA[0][:, dc * 1024 + fc * 128: dc * 1024 + fc * 128 + 128],
                              zbuf2[0][:, dc * 512:(dc + 1) * 512],
                              start=(dc == 0), stop=(dc == 7)))
                           for dc in range(8)],
                          reads=(woA[1], zbuf2[1]), writes=(b_bank[pb],))
                ctx.op("dve",
                       lambda E, fc=fc, pb=pb, xbuf2=xbuf2:
                       E.scalar_tensor_tensor(
                           xbuf2[0][:, fc * 512:(fc + 1) * 512], bank(pb),
                           smalls["bo"][:, fc:fc + 1],
                           xbuf2[0][:, fc * 512:(fc + 1) * 512],
                           ALU.add, ALU.add),
                       reads=(b_bank[pb], b_small["bo"], xbuf2[1]),
                       writes=(xbuf2[1],))
            x1tiles = [xbuf2[0][:, fc * 512:(fc + 1) * 512] for fc in range(8)]

            def apply1(bcA, bcB, b_bc, tt=tt, tsl=tsl, x1tiles=x1tiles):
                for fc in range(8):
                    ctx.op("dve",
                           lambda E, fc=fc, bcA=bcA: E.tensor_tensor(
                               x1tiles[fc], x1tiles[fc], bcA, ALU.mult),
                           reads=(xbuf2[1], b_bc), writes=(xbuf2[1],))
                    ctx.op("dve",
                           lambda E, fc=fc, bcB=bcB: E.tensor_tensor(
                               x1tiles[fc], x1tiles[fc], bcB, ALU.subtract),
                           reads=(xbuf2[1], b_bc), writes=(xbuf2[1],))
                    ctx.op("dve",
                           lambda E, fc=fc: E.tensor_copy(t1A[fc % 2][0],
                                                          x1tiles[fc]),
                           reads=(xbuf2[1],), writes=(t1A[fc % 2][1],))
                    e_store(ctx, x1dr[fc, :, tsl], x1tiles[fc], xbuf2[1],
                            dstbufs=(b_d["x1b"],))
                    e_store(ctx, x1bdr[fc, :, tsl], t1A[fc % 2][0],
                            t1A[fc % 2][1], dstbufs=(b_d["x1bb"],), q="dact")

            ln_block(ctx, scr, x1tiles, [xbuf2[1]] * 8, [apply1], 512, LN_EPS,
                     ones_col, ones_row, b_ones, bank(2), bank(3),
                     b_bank[2], b_bank[3])

        AR.reset()
        xinB = AR.take(128, 4096, BF16)
        hAB = AR.take(128, 32 * 512, BF16)
        hBB = AR.take(128, 32 * 512, BF16)
        wbB = [AR.take(128, 4096, BF16) for _ in range(2)]
        rbB = AR.take(128, 512, F32)
        t1B = [AR.take(128, 512, BF16) for _ in range(2)]
        x2bB = AR.take(128, 4096, F32)
        x2hdr = x2h_d.rearrange("(c p) t -> c p t", p=128)
        for q4 in range(4):
            gsl = slice(q4 * 512, (q4 + 1) * 512)
            e_load(ctx, xinB[0].rearrange("p (c t) -> p c t", c=8),
                   x1bdr[:, :, gsl].rearrange("c p t -> p c t"), xinB[1],
                   srcbufs=(b_d["x1bb"],), q="dsp")
            for ft in range(32):
                wb = ft % 2
                e_load(ctx, wbB[wb][0][:, 0:1024].rearrange("p (c f) -> p c f", c=8),
                       w1b_d[:, ft * 128:(ft + 1) * 128].rearrange(
                           "(c p) f -> p c f", p=128), wbB[wb][1],
                       srcbufs=(b_wcast,), q="dsp")
                e_load(ctx, wbB[wb][0][:, 1024:2048].rearrange("p (c f) -> p c f", c=8),
                       w2b_d[:, ft * 128:(ft + 1) * 128].rearrange(
                           "(c p) f -> p c f", p=128), wbB[wb][1],
                       srcbufs=(b_wcast,), q="dact")
                pb = ft % 2
                ctx.group("pe",
                          [(lambda E, dc=dc, wb=wb, pb=pb: E.matmul(
                              bank(pb), wbB[wb][0][:, dc * 128:(dc + 1) * 128],
                              xinB[0][:, dc * 512:(dc + 1) * 512],
                              start=(dc == 0), stop=(dc == 7)))
                           for dc in range(8)],
                          reads=(wbB[wb][1], xinB[1]), writes=(b_bank[pb],))
                ctx.group("pe",
                          [(lambda E, dc=dc, wb=wb, pb=pb: E.matmul(
                              bank(2 + pb),
                              wbB[wb][0][:, 1024 + dc * 128: 1024 + (dc + 1) * 128],
                              xinB[0][:, dc * 512:(dc + 1) * 512],
                              start=(dc == 0), stop=(dc == 7)))
                           for dc in range(8)],
                          reads=(wbB[wb][1], xinB[1]), writes=(b_bank[2 + pb],))
                ctx.op("act",
                       lambda E, ft=ft, pb=pb: E.activation(
                           t1B[pb][0], bank(pb), AF.Silu,
                           scale=smalls["s1"][:, ft:ft + 1]),
                       reads=(b_bank[pb], b_small["s1"]), writes=(t1B[pb][1],))
                ctx.op("dve",
                       lambda E, ft=ft, pb=pb: E.scalar_tensor_tensor(
                           hAB[0][:, ft * 512:(ft + 1) * 512],
                           bank(2 + pb), smalls["s2"][:, ft:ft + 1],
                           t1B[pb][0], ALU.mult, ALU.mult),
                       reads=(b_bank[2 + pb], b_small["s2"], t1B[pb][1]),
                       writes=(hAB[1],))
            for (hTd, ssc, src, dst) in ((h1b_d, "sh1", hAB, hBB),
                                         (h2b_d, "sh2", hBB, hAB)):
                for ft in range(32):
                    wb = ft % 2
                    e_load(ctx, wbB[wb][0].rearrange("p (c f) -> p c f", c=32),
                           hTd[:, ft * 128:(ft + 1) * 128].rearrange(
                               "(c p) f -> p c f", p=128), wbB[wb][1],
                           srcbufs=(b_wcast,), q="dsp")
                    pb = ft % 2
                    ctx.group("pe",
                              [(lambda E, dc=dc, wb=wb, src=src, pb=pb:
                                E.matmul(
                                    bank(pb), wbB[wb][0][:, dc * 128:(dc + 1) * 128],
                                    src[0][:, dc * 512:(dc + 1) * 512],
                                    start=(dc == 0), stop=(dc == 31)))
                               for dc in range(32)],
                              reads=(wbB[wb][1], src[1]), writes=(b_bank[pb],))
                    ctx.op("act",
                           lambda E, ft=ft, ssc=ssc, dst=dst, pb=pb:
                           E.activation(
                               dst[0][:, ft * 512:(ft + 1) * 512],
                               bank(pb), AF.Gelu,
                               scale=smalls[ssc][:, ft:ft + 1]),
                           reads=(b_bank[pb], b_small[ssc]), writes=(dst[1],))
            for fc in range(8):
                wb = fc % 2
                e_load(ctx, wbB[wb][0].rearrange("p (c f) -> p c f", c=32),
                       w3b_d[:, fc * 128:(fc + 1) * 128].rearrange(
                           "(c p) f -> p c f", p=128), wbB[wb][1],
                       srcbufs=(b_wcast,), q="dsp")
                e_load(ctx, rbB[0], x1dr[fc, :, gsl], rbB[1],
                       srcbufs=(b_d["x1b"],), q="dact")
                pb = 4 + fc % 2
                ctx.group("pe",
                          [(lambda E, dc=dc, wb=wb, pb=pb: E.matmul(
                              bank(pb), wbB[wb][0][:, dc * 128:(dc + 1) * 128],
                              hAB[0][:, dc * 512:(dc + 1) * 512],
                              start=(dc == 0), stop=(dc == 31)))
                           for dc in range(32)],
                          reads=(wbB[wb][1], hAB[1]), writes=(b_bank[pb],))
                ctx.op("dve",
                       lambda E, fc=fc, pb=pb: E.scalar_tensor_tensor(
                           x2bB[0][:, fc * 512:(fc + 1) * 512], bank(pb),
                           smalls["s3"][:, fc:fc + 1], rbB[0],
                           ALU.mult, ALU.add),
                       reads=(b_bank[pb], b_small["s3"], rbB[1]),
                       writes=(x2bB[1],))
            x2tiles = [x2bB[0][:, fc * 512:(fc + 1) * 512] for fc in range(8)]

            def apply2(bcA, bcB, b_bc, gsl=gsl, x2tiles=x2tiles):
                for fc in range(8):
                    ctx.op("dve",
                           lambda E, fc=fc, bcA=bcA: E.tensor_tensor(
                               x2tiles[fc], x2tiles[fc], bcA, ALU.mult),
                           reads=(x2bB[1], b_bc), writes=(x2bB[1],))
                    ctx.op("dve",
                           lambda E, fc=fc, bcB=bcB: E.tensor_tensor(
                               x2tiles[fc], x2tiles[fc], bcB, ALU.subtract),
                           reads=(x2bB[1], b_bc), writes=(x2bB[1],))
                    ctx.op("dve",
                           lambda E, fc=fc: E.tensor_copy(t1B[fc % 2][0],
                                                          x2tiles[fc]),
                           reads=(x2bB[1],), writes=(t1B[fc % 2][1],))
                    e_store(ctx, x2hdr[fc, :, gsl], x2tiles[fc], x2bB[1],
                            dstbufs=(b_d["x2h"],))
                    ctx.dma_cond("dpool",
                            lambda E, hh, fc=fc, gsl=gsl, w=fc % 2: E.dma_start(
                                out=sh["X2"].rearrange(
                                    "(c p) t -> c p t", p=128)[fc][
                                    :, hh * TH + gsl.start: hh * TH + gsl.start + 512],
                                in_=t1B[w][0]),
                            reads=(t1B[fc % 2][1],), writes=(b_sh["X2"],))

            ln_block(ctx, scr, x2tiles, [x2bB[1]] * 8, [apply2], 512, LN_EPS,
                     ones_col, ones_row, b_ones, bank(6), bank(7),
                     b_bank[6], b_bank[7])
        send_flag(1, (b_sh["X2"],))

        AR.reset()
        poll(1)
        xbcC = [AR.take(128, 512, BF16) for _ in range(2)]
        wq1C = AR.take(128, 8 * HF, BF16)
        wkvC = AR.take(128, 8 * 128, BF16)
        mqkC = AR.take(128, 2048, F32)
        mkqC = AR.take(128, 2048, F32)
        QT1 = AR.take(64, NH * S, BF16)
        KTa1 = AR.take(128, S, BF16)
        V1 = AR.take(128, 32 * DH, BF16)
        QTa1 = AR.take(128, S, BF16)
        mq1 = AR.take(128, NH * 32, F32)
        sq1 = AR.take(128, NH * 32, F32)
        nbc1 = AR.take(128, 32, F32)
        tp1 = AR.take(32, 128, F32)
        nbrow1 = AR.take(1, S, F32)
        nbbf1 = AR.take(1, S, BF16)
        corr1 = AR.take(1, S, F32)
        corrB1 = AR.take(128, 512, F32)
        alp2C = [AR.take(128, 1024, BF16) for _ in range(2)]
        exp1 = AR.take(128, 2048, BF16)
        scr1C = AR.take(128, 1, F32)
        scrHC = AR.take(128, 1, F32)
        scrSC1 = AR.take(128, 1, F32)
        scrSC2 = AR.take(128, 1, F32)
        scr32C = AR.take(128, 32, F32)
        zsegC = [AR.take(64, 512, BF16) for _ in range(2)]

        e_load(ctx, mqkC[0], masksQK, mqkC[1])
        e_load(ctx, mkqC[0], masksKQ, mkqC[1])
        e_load(ctx, wq1C[0].rearrange("p (c f) -> p c f", c=8),
               wq1.rearrange("(c p) f -> p c f", p=128), wq1C[1])
        e_load(ctx, wkvC[0].rearrange("p (c f) -> p c f", c=8),
               wkv.rearrange("(c p) f -> p c f", p=128), wkvC[1])
        ctx.op("dve", lambda E: E.tensor_scalar(bqsc[:], smalls["bqm"][:], 0.125,
                                                None, ALU.mult),
               reads=(b_small["bqm"],), writes=(b_bqsc,))
        ctx.op("dve", lambda E: E.memset(KTa1[0][64:65, :], 1.0),
               writes=(KTa1[1],))
        ctx.group("pe", [lambda E: E.matmul(PS[:, 0:DH], ones_row, bvrow1,
                                            start=True, stop=True)],
                  reads=(b_bvrow1, b_ones), writes=(b_bank[0],))
        ctx.op("dve", lambda E: E.tensor_copy(bvB1[:], PS[:, 0:DH]),
               reads=(b_bank[0],), writes=(b_bvB1,))

        X2r = sh["X2"].rearrange("(c p) t -> c p t", p=128)
        for tt in range(8):
            for dc in range(8):
                xb = xbcC[dc % 2][1]
                e_load(ctx, xbcC[dc % 2][0],
                       X2r[dc][:, tt * 512:(tt + 1) * 512], xb,
                       srcbufs=(b_phase[1], b_sh["X2"]), q="dsp")
                for fc in range(4):
                    ctx.group("pe",
                              [(lambda E, fc=fc, dc=dc: E.matmul(
                                  bank(fc),
                                  wq1C[0][:, dc * HF + fc * 128: dc * HF + fc * 128 + 128],
                                  xbcC[dc % 2][0],
                                  start=(dc == 0), stop=(dc == 7)))],
                              reads=(wq1C[1], xb), writes=(b_bank[fc],))
                ctx.group("pe",
                          [(lambda E, dc=dc: E.matmul(
                              bank(4), wkvC[0][:, dc * 128:(dc + 1) * 128],
                              xbcC[dc % 2][0],
                              start=(dc == 0), stop=(dc == 7)))],
                          reads=(wkvC[1], xb), writes=(b_bank[4],))
            for fc in range(4):
                for half in range(2):
                    hh_ = 2 * fc + half
                    p0_ = half * 64
                    ctx.op("act",
                           lambda E, fc=fc, tt=tt, hh_=hh_, p0_=p0_: E.activation(
                               QT1[0][0:64, hh_ * S + tt * 512: hh_ * S + (tt + 1) * 512],
                               bank(fc)[p0_:p0_ + 64, :], AF.Identity, scale=0.125,
                               bias=bqsc[p0_:p0_ + 64, fc:fc + 1]),
                           reads=(b_bank[fc], b_bqsc), writes=(QT1[1],))
            ctx.op("act",
                   lambda E, tt=tt: E.activation(
                       KTa1[0][0:64, tt * 512:(tt + 1) * 512], bank(4)[0:64, :],
                       AF.Identity, bias=smalls["bkv"][0:64, 0:1]),
                   reads=(b_bank[4], b_small["bkv"]), writes=(KTa1[1],))
        for tt in range(8):
            for dc in range(8):
                xb = xbcC[dc % 2][1]
                e_load(ctx, xbcC[dc % 2][0],
                       X2r[dc][:, tt * 512:(tt + 1) * 512], xb,
                       srcbufs=(b_sh["X2"],), q="dsp")
                for sub in range(4):
                    ctx.group("pe",
                              [(lambda E, dc=dc, sub=sub: E.matmul(
                                  PS[:, sub * 512: sub * 512 + DH],
                                  xbcC[dc % 2][0][:, sub * 128:(sub + 1) * 128],
                                  wkvC[0][:, dc * 128 + 64:(dc + 1) * 128],
                                  start=(dc == 0), stop=(dc == 7)))],
                              reads=(wkvC[1], xb), writes=(b_bank[sub],))
            for sub in range(4):
                ti = tt * 4 + sub
                ctx.op("dve",
                       lambda E, ti=ti, sub=sub: E.tensor_tensor(
                           V1[0][:, ti * DH:(ti + 1) * DH],
                           PS[:, sub * 512: sub * 512 + DH], bvB1[:], ALU.add),
                       reads=(b_bank[sub], b_bvB1), writes=(V1[1],))

        for h in range(NH):
            for qt in range(32):
                col = h * 32 + qt
                nl = qt // 4 + 1
                mi_last = ((nl - 1) * 512 - qt * 128 + 384) // 128
                stats_row(list(range(nl)), {nl - 1}, col, mq1, mq1[1],
                          sq1, sq1[1], nbc1, nbc1[1],
                          lambda E, kt, pb, h=h, qt=qt: E.matmul(
                              bank(pb),
                              QT1[0][0:64,
                                     h * S + qt * 128: h * S + qt * 128 + 128],
                              KTa1[0][0:64, kt * 512:(kt + 1) * 512],
                              start=True, stop=True),
                          lambda E, kt, pb, mi=mi_last: E.tensor_tensor(
                              bank(pb), bank(pb),
                              mqkC[0][:, mi * 512:(mi + 1) * 512], ALU.add),
                          (QT1[1], KTa1[1]), mqkC[1],
                          (scr1C, scrHC), (scrSC1, scrSC2), exp1, ncol=qt)
            c0 = h * 32
            ctx.op("act",
                   lambda E, c0=c0: E.activation(scr32C[0],
                                                 sq1[0][:, c0:c0 + 32], AF.Ln),
                   reads=(sq1[1],), writes=(scr32C[1],))
            ctx.op("dve",
                   lambda E, c0=c0: E.tensor_tensor(
                       scr32C[0], scr32C[0], mq1[0][:, c0:c0 + 32], ALU.add),
                   reads=(scr32C[1], mq1[1]), writes=(scr32C[1],))
            ctx.op("dve",
                   lambda E: E.tensor_scalar(
                       nbc1[0][:, 0:32], scr32C[0], -1.0, None, ALU.mult),
                   reads=(scr32C[1],), writes=(nbc1[1],))
            ctx.group("pe", [lambda E, h=h: E.transpose(PS[0:32, 0:128],
                                                        nbc1[0][:, 0:32], id_s)],
                      reads=(nbc1[1], b_id), writes=(b_bank[0],))
            ctx.op("dve", lambda E: E.tensor_copy(tp1[0], PS[0:32, 0:128]),
                   reads=(b_bank[0],), writes=(tp1[1],))
            e_store(ctx, nb_d[h], tp1[0], tp1[1], dstbufs=(b_d["nb"],))
            e_load(ctx, nbrow1[0],
                   nb_d[h].rearrange("a b -> (a b)").unsqueeze(0), nbrow1[1],
                   srcbufs=(b_d["nb"],))
            ctx.op("dve", lambda E: E.tensor_copy(nbbf1[0], nbrow1[0]),
                   reads=(nbrow1[1],), writes=(nbbf1[1],))
            ctx.op("dve", lambda E: E.tensor_copy(corr1[0], nbbf1[0]),
                   reads=(nbbf1[1],), writes=(corr1[1],))
            ctx.op("dve", lambda E: E.tensor_tensor(corr1[0], nbrow1[0],
                                                    corr1[0], ALU.subtract),
                   reads=(nbrow1[1], corr1[1]), writes=(corr1[1],))
            ctx.op("act", lambda E: E.activation(corr1[0], corr1[0], AF.Exp),
                   reads=(corr1[1],), writes=(corr1[1],))
            ctx.op("dve",
                   lambda E, h=h: E.tensor_copy(QTa1[0][0:64, :],
                                                QT1[0][0:64, h * S:(h + 1) * S]),
                   reads=(QT1[1],), writes=(QTa1[1],))
            ctx.op("dve", lambda E: E.tensor_copy(QTa1[0][64:65, :], nbbf1[0]),
                   reads=(nbbf1[1],), writes=(QTa1[1],))
            for qt5 in range(8):
                zb = 2 + qt5 % 2
                nlive = 4 * qt5 + 4
                ctx.group("pe",
                          [lambda E, qt5=qt5: E.matmul(
                              bank(1), ones_rowF,
                              corr1[0][:, qt5 * 512:(qt5 + 1) * 512],
                              start=True, stop=True)],
                          reads=(corr1[1], b_ones), writes=(b_bank[1],))
                ctx.op("dve", lambda E: E.tensor_copy(corrB1[0], bank(1)),
                       reads=(b_bank[1],), writes=(corrB1[1],))
                pend = None
                for j in range(nlive // 2):
                    k0 = 2 * j
                    par = j % 2
                    pb = 4 + 2 * par
                    ctx.group("pe",
                              [(lambda E, kt=k0, qt5=qt5, pb=pb: E.matmul(
                                  bank(pb),
                                  KTa1[0][0:65, kt * 128: kt * 128 + 128],
                                  QTa1[0][0:65, qt5 * 512:(qt5 + 1) * 512],
                                  start=True, stop=True)),
                               (lambda E, kt=k0 + 1, qt5=qt5, pb=pb: E.matmul(
                                  bank(pb + 1),
                                  KTa1[0][0:65, kt * 128: kt * 128 + 128],
                                  QTa1[0][0:65, qt5 * 512:(qt5 + 1) * 512],
                                  start=True, stop=True))],
                              reads=(KTa1[1], QTa1[1]),
                              writes=(b_bank[pb], b_bank[pb + 1]))
                    for half in range(2):
                        Dv = (k0 + half) * 128 - qt5 * 512
                        if 0 <= Dv <= 384:
                            ctx.op("dve",
                                   lambda E, mi=Dv // 128, pb=pb + half:
                                   E.tensor_tensor(
                                       bank(pb), bank(pb),
                                       mkqC[0][:, mi * 512:(mi + 1) * 512],
                                       ALU.add),
                                   reads=(b_bank[pb + half], mkqC[1]),
                                   writes=(b_bank[pb + half],))
                    ctx.op("act",
                           lambda E, pb=pb, par=par: E.activation(
                               alp2C[par][0], PS[:, pb * 512:(pb + 2) * 512],
                               AF.Exp),
                           reads=(b_bank[pb], b_bank[pb + 1]),
                           writes=(alp2C[par][1],))
                    if pend is not None:
                        pend()

                    def mk_z(k0=k0, par=par, nlive=nlive, zb=zb):
                        def f():
                            ctx.group("pe",
                                      [(lambda E, kt=k0, par=par, zb=zb:
                                        E.matmul(
                                            bank(zb, 64),
                                            V1[0][:, kt * DH:(kt + 1) * DH],
                                            alp2C[par][0][:, 0:512],
                                            start=(kt == 0), stop=False)),
                                       (lambda E, kt=k0 + 1, par=par, zb=zb,
                                        nlive=nlive: E.matmul(
                                            bank(zb, 64),
                                            V1[0][:, kt * DH:(kt + 1) * DH],
                                            alp2C[par][0][:, 512:1024],
                                            start=False,
                                            stop=(kt == nlive - 1)))],
                                      reads=(V1[1], alp2C[par][1]),
                                      writes=(b_bank[zb],))
                        return f
                    pend = mk_z()
                pend()
                sb = qt5 % 2
                ctx.op("dve",
                       lambda E, zb=zb, sb=sb: E.tensor_tensor(
                           zsegC[sb][0], bank(zb, 64), corrB1[0][0:64, :],
                           ALU.mult),
                       reads=(b_bank[zb], corrB1[1]), writes=(zsegC[sb][1],))
                ctx.dma_cond("dpool",
                        lambda E, hh, h=h, qt5=qt5, sb=sb: E.dma_start(
                            out=sh["Z1"][hh * 512 + h * 64: hh * 512 + (h + 1) * 64,
                                         qt5 * 512:(qt5 + 1) * 512],
                            in_=zsegC[sb][0]),
                        reads=(zsegC[sb][1],), writes=(b_sh["Z1"],))
        send_flag(2, (b_sh["Z1"],))

        AR.reset()
        poll(2)
        zbuf4P = [AR.take(128, 4096, BF16) for _ in range(2)]
        xbuf4P = [AR.take(128, 4096, F32) for _ in range(2)]
        waoA = AR.take(128, 8192, BF16)
        t1D = [AR.take(128, 512, BF16) for _ in range(2)]
        e_load(ctx, waoA[0].rearrange("p (c f) -> p c f", c=8),
               wao.rearrange("(c p) f -> p c f", p=128), waoA[1])
        Z1r = sh["Z1"].rearrange("(c p) t -> p c t", p=128)
        x1mdr = x1mh_d.rearrange("(c p) t -> c p t", p=128)
        for tt in range(4):
            tsl = slice(tt * 512, (tt + 1) * 512)
            zbuf4 = zbuf4P[tt % 2]
            xbuf4 = xbuf4P[tt % 2]
            ctx.dma_cond("dpool",
                    lambda E, hh, tt=tt, zbuf4=zbuf4: E.dma_start(
                        out=zbuf4[0].rearrange("p (c t) -> p c t", c=8),
                        in_=Z1r[:, :, hh * TH + tt * 512: hh * TH + (tt + 1) * 512]),
                    reads=(b_phase[2], b_sh["Z1"]), writes=(zbuf4[1],))
            e_load(ctx, xbuf4[0].rearrange("p (c t) -> p c t", c=8),
                   x2hdr[:, :, tsl].rearrange("c p t -> p c t"), xbuf4[1],
                   srcbufs=(b_d["x2h"],), q="dsp")
            for fc in range(8):
                pb = fc % 2
                ctx.group("pe",
                          [(lambda E, dc=dc, fc=fc, pb=pb, zbuf4=zbuf4:
                            E.matmul(
                              bank(pb), waoA[0][:, dc * 1024 + fc * 128: dc * 1024 + fc * 128 + 128],
                              zbuf4[0][:, dc * 512:(dc + 1) * 512],
                              start=(dc == 0), stop=(dc == 7)))
                           for dc in range(8)],
                          reads=(waoA[1], zbuf4[1]), writes=(b_bank[pb],))
                ctx.op("dve",
                       lambda E, fc=fc, pb=pb, xbuf4=xbuf4:
                       E.scalar_tensor_tensor(
                           xbuf4[0][:, fc * 512:(fc + 1) * 512], bank(pb),
                           smalls["bao"][:, fc:fc + 1],
                           xbuf4[0][:, fc * 512:(fc + 1) * 512],
                           ALU.add, ALU.add),
                       reads=(b_bank[pb], b_small["bao"], xbuf4[1]),
                       writes=(xbuf4[1],))
                ctx.op("dve",
                       lambda E, fc=fc, xbuf4=xbuf4: E.tensor_copy(
                           t1D[fc % 2][0], xbuf4[0][:, fc * 512:(fc + 1) * 512]),
                       reads=(xbuf4[1],), writes=(t1D[fc % 2][1],))
                e_store(ctx, x1mdr[fc, :, tsl],
                        xbuf4[0][:, fc * 512:(fc + 1) * 512], xbuf4[1],
                        dstbufs=(b_d["x1mh"],))
                ctx.dma_cond("dpool",
                        lambda E, hh, fc=fc, tt=tt, w=fc % 2: E.dma_start(
                            out=sh["X1M"].rearrange("(c p) t -> c p t", p=128)[fc][
                                :, hh * TH + tt * 512: hh * TH + (tt + 1) * 512],
                            in_=t1D[w][0]),
                        reads=(t1D[fc % 2][1],), writes=(b_sh["X1M"],))
        send_flag(3, (b_sh["X1M"],))

        AR.reset()
        poll(3)
        wuA = AR.take(128, 8192, BF16)
        wrA = AR.take(128, 8192, BF16)
        wxpA = AR.take(128, 8 * 96, BF16)
        x1mtile = [AR.take(128, 512, BF16) for _ in range(2)]
        u0cG = [AR.take(128, 4100, BF16) for _ in range(8)]
        cvD = AR.take(128, 4096, F32)
        ucD = AR.take(128, 4096, BF16)
        ubD = [AR.take(128, 512, BF16) for _ in range(2)]
        svsegD = [AR.take(128, 512, BF16) for _ in range(2)]
        t96D = AR.take(96, 512, F32)
        e_load(ctx, wuA[0].rearrange("p (c f) -> p c f", c=8),
               w_u.rearrange("(c p) f -> p c f", p=128), wuA[1])
        e_load(ctx, wrA[0].rearrange("p (c f) -> p c f", c=8),
               w_r.rearrange("(c p) f -> p c f", p=128), wrA[1])
        e_load(ctx, wxpA[0].rearrange("p (c f) -> p c f", c=8),
               wxp.rearrange("(c p) f -> p c f", p=128), wxpA[1])
        X1Mr = sh["X1M"].rearrange("(c p) t -> c p t", p=128)
        uTdr = uT_d.rearrange("(c p) t -> c p t", p=128)
        svTdr = svT_d.rearrange("(c p) t -> c p t", p=128)
        for g in range(8):
            ctx.op("dve", lambda E, g=g: E.memset(u0cG[g][0][:, 0:3], 0.0),
                   writes=(u0cG[g][1],))
        for tt in range(8):
            for dc in range(8):
                xb = x1mtile[dc % 2][1]
                e_load(ctx, x1mtile[dc % 2][0],
                       X1Mr[dc][:, tt * 512:(tt + 1) * 512], xb,
                       srcbufs=(b_phase[3], b_sh["X1M"]), q="dsp")
                ctx.group("pe",
                          [(lambda E, dc=dc, g=g: E.matmul(
                              bank(g), wuA[0][:, dc * 1024 + g * 128:
                                              dc * 1024 + g * 128 + 128],
                              x1mtile[dc % 2][0],
                              start=(dc == 0), stop=(dc == 7)))
                           for g in range(8)],
                          reads=(wuA[1], xb),
                          writes=tuple(b_bank[g] for g in range(8)))
            for g in range(8):
                ctx.op("dve",
                       lambda E, g=g, tt=tt: E.tensor_copy(
                           u0cG[g][0][:, 3 + tt * 512: 3 + (tt + 1) * 512],
                           bank(g)),
                       reads=(b_bank[g], u0cG[g][1]), writes=(u0cG[g][1],))
        for g in range(8):
            ctx.op("dve",
                   lambda E, g=g: E.tensor_scalar(
                       cvD[0], u0cG[g][0][:, 0:4096],
                       smalls["convw"][:, g * 4: g * 4 + 1], None, ALU.mult),
                   reads=(u0cG[g][1], b_small["convw"]), writes=(cvD[1],))
            for j in range(1, 4):
                ctx.op("dve",
                       lambda E, g=g, j=j: E.scalar_tensor_tensor(
                           cvD[0], u0cG[g][0][:, j: j + 4096],
                           smalls["convw"][:, g * 4 + j: g * 4 + j + 1],
                           cvD[0], ALU.mult, ALU.add),
                       reads=(u0cG[g][1], b_small["convw"], cvD[1]),
                       writes=(cvD[1],))
            ctx.op("act",
                   lambda E, g=g: E.activation(ucD[0], cvD[0], AF.Silu,
                                               bias=smalls["convb"][:, g:g + 1]),
                   reads=(cvD[1], b_small["convb"]), writes=(ucD[1],))
            e_store(ctx, uTdr[g], ucD[0], ucD[1], dstbufs=(b_d["uT"],))
        for tt in range(8):
            tsl = slice(tt * 512, (tt + 1) * 512)
            for g in range(8):
                e_load(ctx, ubD[g % 2][0], uTdr[g][:, tsl], ubD[g % 2][1],
                       srcbufs=(b_d["uT"],))
                ctx.group("pe",
                          [(lambda E, g=g: E.matmul(
                              PS[0:96, 0:512], wxpA[0][:, g * 96:(g + 1) * 96],
                              ubD[g % 2][0],
                              start=(g == 0), stop=(g == 7)))],
                          reads=(wxpA[1], ubD[g % 2][1]), writes=(b_bank[0],))
            ctx.op("dve", lambda E: E.tensor_copy(t96D[0], PS[0:96, 0:512]),
                   reads=(b_bank[0],), writes=(t96D[1],))
            ctx.dma_cond("dpool",
                    lambda E, hh, tsl=tsl: E.dma_start(
                        out=sh["XD"][hh, :, tsl],
                        in_=t96D[0]),
                    reads=(t96D[1],), writes=(b_sh["XD"],))
        for tt in range(8):
            tsl = slice(tt * 512, (tt + 1) * 512)
            for dc in range(8):
                xb = x1mtile[dc % 2][1]
                e_load(ctx, x1mtile[dc % 2][0],
                       X1Mr[dc][:, tt * 512:(tt + 1) * 512], xb,
                       srcbufs=(b_sh["X1M"],), q="dsp")
                ctx.group("pe",
                          [(lambda E, dc=dc, g=g: E.matmul(
                              bank(g), wrA[0][:, dc * 1024 + g * 128:
                                              dc * 1024 + g * 128 + 128],
                              x1mtile[dc % 2][0],
                              start=(dc == 0), stop=(dc == 7)))
                           for g in range(8)],
                          reads=(wrA[1], xb),
                          writes=tuple(b_bank[g] for g in range(8)))
            for g in range(8):
                ctx.op("act",
                       lambda E, g=g: E.activation(
                           svsegD[g % 2][0], bank(g), AF.Silu),
                       reads=(b_bank[g],), writes=(svsegD[g % 2][1],))
                e_store(ctx, svTdr[g][:, tsl], svsegD[g % 2][0],
                        svsegD[g % 2][1], dstbufs=(b_d["svT"],))
        send_flag(4, (b_sh["XD"],))

        AR.reset()
        poll(4)
        xdt_d = dt_("xdt_d", [DTR, S], BF16, kind="Internal").ap()
        b_d["xdt"] = ctx.buf("xdt")
        xm1 = AR.take(96, 4096, F32)
        xm2 = AR.take(96, 4096, F32)
        xdbE = AR.take(64, 4096, BF16)
        e_load(ctx, xm1[0], sh["XD"][0], xm1[1],
               srcbufs=(b_phase[4], b_sh["XD"]), q="dsp")
        e_load(ctx, xm2[0], sh["XD"][1], xm2[1],
               srcbufs=(b_phase[4], b_sh["XD"]), q="dsp")
        ctx.op("dve", lambda E: E.tensor_tensor(xm1[0], xm1[0], xm2[0], ALU.add),
               reads=(xm1[1], xm2[1]), writes=(xm1[1],))
        ctx.op("dve", lambda E: E.tensor_copy(xdbE[0], xm1[0][0:64, :]),
               reads=(xm1[1],), writes=(xdbE[1],))
        e_store(ctx, xdt_d[:], xdbE[0], xdbE[1], dstbufs=(b_d["xdt"],))
        xbcE = AR.take(32, 4096, BF16)
        ctx.op("dve", lambda E: E.tensor_copy(xbcE[0], xm1[0][64:96, :]),
               reads=(xm1[1],), writes=(xbcE[1],))
        e_store(ctx, xbc_d[:], xbcE[0], xbcE[1], dstbufs=(b_d["xbc"],))

        AR.reset()
        delta_d = dt_("delta_d", [DM, S], BF16, kind="Internal").ap()
        b_d["delta"] = ctx.buf("deltad")
        A_p = al("A_p", [128, 8 * NST], F32).ap()
        b_Ap = ctx.buf("A_p")
        sp1E = AR.take(128, 4096, F32)
        tmpE = AR.take(128, 4096, BF16)
        dchE = AR.take(128, 4096, BF16)
        wdtE = AR.take(64, 1024, BF16)
        xdtE = [AR.take(64, 512, BF16) for _ in range(2)]
        e_load(ctx, wdtE[0], wdt, wdtE[1])
        ctx.op("act", lambda E: E.activation(A_p[:], smalls["Alog"][:], AF.Exp),
               reads=(b_small["Alog"],), writes=(b_Ap,))
        ctx.op("dve", lambda E: E.tensor_scalar(A_p[:], A_p[:], -1.0, None,
                                                ALU.mult),
               reads=(b_Ap,), writes=(b_Ap,))
        c_one = ctx.constcol(1.0)
        delta_dr = delta_d.rearrange("(c p) t -> c p t", p=128)
        for g in range(8):
            for tt in range(8):
                pb = tt % 2
                e_load(ctx, xdtE[tt % 2][0], xdt_d[:, tt * 512:(tt + 1) * 512],
                       xdtE[tt % 2][1], srcbufs=(b_d["xdt"],), q="dsp")
                ctx.group("pe",
                          [(lambda E, g=g, tt=tt, pb=pb: E.matmul(
                              bank(pb), wdtE[0][:, g * 128:(g + 1) * 128],
                              xdtE[tt % 2][0],
                              start=True, stop=True))],
                          reads=(wdtE[1], xdtE[tt % 2][1]), writes=(b_bank[pb],))
                ctx.op("dve",
                       lambda E, g=g, tt=tt, pb=pb: E.tensor_scalar(
                           sp1E[0][:, tt * 512:(tt + 1) * 512], bank(pb),
                           smalls["bdt"][:, g:g + 1], None, ALU.add),
                       reads=(b_bank[pb], b_small["bdt"]), writes=(sp1E[1],))
            ctx.op("act", lambda E: E.activation(tmpE[0], sp1E[0], AF.Abs),
                   reads=(sp1E[1],), writes=(tmpE[1],))
            ctx.op("act", lambda E: E.activation(tmpE[0], tmpE[0], AF.Exp,
                                                 scale=-1.0),
                   reads=(tmpE[1],), writes=(tmpE[1],))
            ctx.op("act", lambda E: E.activation(tmpE[0], tmpE[0], AF.Ln,
                                                 bias=c_one),
                   reads=(tmpE[1], ctx._cbuf), writes=(tmpE[1],))
            ctx.op("dve", lambda E: E.tensor_scalar(sp1E[0], sp1E[0], 0.0, None,
                                                    ALU.max),
                   reads=(sp1E[1],), writes=(sp1E[1],))
            ctx.op("dve",
                   lambda E: E.tensor_tensor(dchE[0], sp1E[0], tmpE[0], ALU.add),
                   reads=(sp1E[1], tmpE[1]), writes=(dchE[1],))
            e_store(ctx, delta_dr[g], dchE[0], dchE[1], dstbufs=(b_d["delta"],))

        AR.reset()
        dgE = [AR.take(128, 4096, BF16) for _ in range(2)]
        ugE = [AR.take(128, 4096, BF16) for _ in range(2)]
        duE = AR.take(128, 4096, BF16)
        dAE = [AR.take(128, 4096, BF16) for _ in range(2)]
        dBuE = [AR.take(128, 4096, BF16) for _ in range(2)]
        BbE = [AR.take(128, 4096, BF16) for _ in range(2)]
        CbE = [AR.take(128, 4096, BF16) for _ in range(2)]
        hsE2 = [AR.take(128, 4096, BF16) for _ in range(2)]
        tmp2E = [AR.take(128, 4096, BF16) for _ in range(2)]
        ygE = AR.take(128, 4096, F32)
        svE = AR.take(128, 4096, BF16)
        youtE = AR.take(128, 4096, BF16)
        ydr = y_d.rearrange("(c p) t -> c p t", p=128)

        def load_bc(n, nb):
            e_load(ctx, BbE[nb][0], xbc_d[n:n + 1, :].to_broadcast([128, S]),
                   BbE[nb][1], srcbufs=(b_d["xbc"],), q="dsp")
            e_load(ctx, CbE[nb][0],
                   xbc_d[NST + n:NST + n + 1, :].to_broadcast([128, S]),
                   CbE[nb][1], srcbufs=(b_d["xbc"],), q="dact")

        def emit_dA(g, n, nb, gb):
            ctx.op("act",
                   lambda E, g=g, n=n, nb=nb, gb=gb: E.activation(
                       dAE[nb][0], dgE[gb][0], AF.Exp,
                       scale=A_p[:, g * NST + n: g * NST + n + 1]),
                   reads=(dgE[gb][1], b_Ap), writes=(dAE[nb][1],))

        def emit_dBu(nb):
            ctx.op("pool",
                   lambda E, nb=nb: E.tensor_tensor(dBuE[nb][0], duE[0],
                                                    BbE[nb][0], ALU.mult),
                   reads=(duE[1], BbE[nb][1]), writes=(dBuE[nb][1],))

        e_load(ctx, dgE[0][0], delta_dr[0], dgE[0][1], srcbufs=(b_d["delta"],))
        e_load(ctx, ugE[0][0], uTdr[0], ugE[0][1], srcbufs=(b_d["uT"],))
        for g in range(8):
            gb = g % 2
            if g + 1 < 8:
                e_load(ctx, dgE[1 - gb][0], delta_dr[g + 1], dgE[1 - gb][1],
                       srcbufs=(b_d["delta"],))
                e_load(ctx, ugE[1 - gb][0], uTdr[g + 1], ugE[1 - gb][1],
                       srcbufs=(b_d["uT"],))
            ctx.op("dve",
                   lambda E, gb=gb: E.tensor_tensor(duE[0], dgE[gb][0],
                                                    ugE[gb][0], ALU.mult),
                   reads=(dgE[gb][1], ugE[gb][1]), writes=(duE[1],))
            load_bc(0, 0)
            emit_dA(g, 0, 0, gb)
            emit_dBu(0)
            for n in range(NST):
                nb = n % 2
                if n + 1 < NST:
                    load_bc(n + 1, 1 - nb)
                    emit_dA(g, n + 1, 1 - nb, gb)
                    emit_dBu(1 - nb)
                hsE = hsE2[n % 2]
                ctx.op("dve",
                       lambda E, nb=nb, hsE=hsE: E.tensor_tensor_scan(
                           hsE[0], dAE[nb][0], dBuE[nb][0], 0.0,
                           ALU.mult, ALU.add),
                       reads=(dAE[nb][1], dBuE[nb][1]), writes=(hsE[1],))
                ctx.op("pool" if n % 2 == 0 else "dve",
                       lambda E, nb=nb, hsE=hsE: E.tensor_tensor(
                           tmp2E[nb][0], hsE[0], CbE[nb][0], ALU.mult),
                       reads=(hsE[1], CbE[nb][1]), writes=(tmp2E[nb][1],))
                if n == 0:
                    ctx.op("dve", lambda E: E.tensor_copy(ygE[0], tmp2E[0][0]),
                           reads=(tmp2E[0][1],), writes=(ygE[1],))
                else:
                    ctx.op("pool" if (n % 4 == 1 or n == 2) else "dve",
                           lambda E, nb=nb: E.tensor_tensor(ygE[0], ygE[0],
                                                            tmp2E[nb][0],
                                                            ALU.add),
                           reads=(tmp2E[nb][1], ygE[1]), writes=(ygE[1],))
            e_load(ctx, svE[0], svTdr[g], svE[1], srcbufs=(b_d["svT"],), q="dsp")
            ctx.op("dve",
                   lambda E, g=g, gb=gb: E.scalar_tensor_tensor(
                       ygE[0], ugE[gb][0], smalls["Dp"][:, g:g + 1], ygE[0],
                       ALU.mult, ALU.add),
                   reads=(ugE[gb][1], b_small["Dp"], ygE[1]), writes=(ygE[1],))
            ctx.op("pool",
                   lambda E: E.tensor_tensor(youtE[0], ygE[0], svE[0], ALU.mult),
                   reads=(ygE[1], svE[1]), writes=(youtE[1],))
            e_store(ctx, ydr[g], youtE[0], youtE[1], dstbufs=(b_d["y"],))

        AR.reset()
        woutE = AR.take(128, 8192, BF16)
        ytE = [AR.take(128, 512, BF16) for _ in range(2)]
        ptE = [AR.take(128, 512, BF16) for _ in range(2)]
        e_load(ctx, woutE[0].rearrange("p (c f) -> p c f", c=8),
               wout.rearrange("(c p) f -> p c f", p=128), woutE[1])
        for tt in range(8):
            tsl = slice(tt * 512, (tt + 1) * 512)
            for fc in range(8):
                pb = fc % 2
                for g in range(8):
                    gb = g % 2
                    e_load(ctx, ytE[gb][0], ydr[g][:, tsl], ytE[gb][1],
                           srcbufs=(b_d["y"],), q="dsp")
                    ctx.group("pe",
                              [(lambda E, g=g, fc=fc, gb=gb, pb=pb: E.matmul(
                                  bank(pb), woutE[0][:, g * 1024 + fc * 128: g * 1024 + fc * 128 + 128],
                                  ytE[gb][0],
                                  start=(g == 0), stop=(g == 7)))],
                              reads=(woutE[1], ytE[gb][1]), writes=(b_bank[pb],))
                ctx.op("dve",
                       lambda E, pb=pb: E.tensor_copy(ptE[pb][0], bank(pb)),
                       reads=(b_bank[pb],), writes=(ptE[pb][1],))
                ctx.dma_cond("dpool",
                        lambda E, hh, fc=fc, tsl=tsl, w=pb: E.dma_start(
                            out=sh["PO"][hh].rearrange(
                                "(c p) t -> c p t", p=128)[fc][:, tsl],
                            in_=ptE[w][0]),
                        reads=(ptE[pb][1],), writes=(b_sh["PO"],))
        send_flag(5, (b_sh["PO"],))

        AR.reset()
        poll(5)
        sAF = AR.take(128, 4096, F32)
        sBF = AR.take(128, 4096, F32)
        t1F = [AR.take(128, 512, BF16) for _ in range(2)]
        lgF = AR.take(8, 2048, F32)
        pb8F = AR.take(128, 8, F32)
        wtokF = AR.take(128, 8, F32)
        wTF = AR.take(8, 2048, F32)
        scr1F = AR.take(128, 1, F32)
        scr2F = AR.take(128, 1, F32)
        gwF = AR.take(128, 64, F32)
        e_load(ctx, gwF[0].rearrange("p (c f) -> p c f", c=8),
               gw.rearrange("(c p) f -> p c f", p=128), gwF[1])
        x3hdr = x3h_d.rearrange("(c p) t -> c p t", p=128)
        POr0 = sh["PO"][0].rearrange("(c p) t -> p c t", p=128)
        POr1 = sh["PO"][1].rearrange("(c p) t -> p c t", p=128)
        for tt in range(4):
            tsl = slice(tt * 512, (tt + 1) * 512)
            e_load(ctx, sAF[0].rearrange("p (c t) -> p c t", c=8),
                   x2hdr[:, :, tsl].rearrange("c p t -> p c t"), sAF[1],
                   srcbufs=(b_d["x2h"],), q="dsp")
            e_load(ctx, sBF[0].rearrange("p (c t) -> p c t", c=8),
                   x1mdr[:, :, tsl].rearrange("c p t -> p c t"), sBF[1],
                   srcbufs=(b_d["x1mh"],), q="dact")
            for fc in range(8):
                ctx.op("dve",
                       lambda E, fc=fc: E.tensor_tensor(
                           sAF[0][:, fc * 512:(fc + 1) * 512],
                           sAF[0][:, fc * 512:(fc + 1) * 512],
                           sBF[0][:, fc * 512:(fc + 1) * 512], ALU.add),
                       reads=(sAF[1], sBF[1]), writes=(sAF[1],))
            for po in (POr0, POr1):
                ctx.dma_cond("dpool",
                        lambda E, hh, tt=tt, po=po: E.dma_start(
                            out=sBF[0].rearrange("p (c t) -> p c t", c=8).bitcast(BF16)[:, :, 0:512],
                            in_=po[:, :, hh * TH + tt * 512: hh * TH + (tt + 1) * 512]),
                        reads=(b_phase[5], b_sh["PO"]), writes=(sBF[1],))
                for fc in range(8):
                    ctx.op("dve",
                           lambda E, fc=fc: E.tensor_tensor(
                               sAF[0][:, fc * 512:(fc + 1) * 512],
                               sAF[0][:, fc * 512:(fc + 1) * 512],
                               sBF[0].rearrange("p (c t) -> p c t", c=8).bitcast(
                                   BF16)[:, fc, 0:512],
                               ALU.add),
                           reads=(sAF[1], sBF[1]), writes=(sAF[1],))
            sAt = [sAF[0][:, fc * 512:(fc + 1) * 512] for fc in range(8)]

            def applyF1(bcA, bcB, b_bc, tt=tt, tsl=tsl, sAt=sAt):
                for fc in range(8):
                    ctx.op("dve",
                           lambda E, fc=fc, bcA=bcA: E.tensor_tensor(
                               sAt[fc], sAt[fc], bcA, ALU.mult),
                           reads=(sAF[1], b_bc), writes=(sAF[1],))
                    ctx.op("dve",
                           lambda E, fc=fc, bcB=bcB: E.tensor_tensor(
                               sAt[fc], sAt[fc], bcB, ALU.subtract),
                           reads=(sAF[1], b_bc), writes=(sAF[1],))
                    ctx.op("dve",
                           lambda E, fc=fc: E.tensor_copy(t1F[fc % 2][0],
                                                          sAt[fc]),
                           reads=(sAF[1],), writes=(t1F[fc % 2][1],))
                    e_store(ctx, x3hdr[fc, :, tsl], sAt[fc], sAF[1],
                            dstbufs=(b_d["x3h"],))
                    ctx.dma_cond("dpool",
                            lambda E, hh, fc=fc, tt=tt, w=fc % 2: E.dma_start(
                                out=sh["X3"].rearrange("(c p) t -> c p t", p=128)[fc][
                                    :, hh * TH + tt * 512: hh * TH + (tt + 1) * 512],
                                in_=t1F[w][0]),
                            reads=(t1F[fc % 2][1],), writes=(b_sh["X3"],))

            ln_block(ctx, scr, sAt, [sAF[1]] * 8, [applyF1], 512, LN_EPS,
                     ones_col, ones_row, b_ones, bank(0), bank(1),
                     b_bank[0], b_bank[1])
            ctx.group("pe",
                      [(lambda E, dc=dc: E.matmul(
                          PS[0:8, 1024:1536], gwF[0][:, dc * 8:(dc + 1) * 8],
                          sAF[0][:, dc * 512:(dc + 1) * 512],
                          start=(dc == 0), stop=(dc == 7)))
                       for dc in range(8)],
                      reads=(gwF[1], sAF[1]), writes=(b_bank[2],))
            ctx.op("act",
                   lambda E, tsl=tsl: E.activation(
                       lgF[0][:, tsl], PS[0:8, 1024:1536], AF.Identity,
                       bias=smalls["gb"][:, 0:1]),
                   reads=(b_bank[2], b_small["gb"]), writes=(lgF[1],))
            for qb in range(4):
                t0 = tt * 512 + qb * 128
                ctx.group("pe",
                          [lambda E, t0=t0: E.transpose(
                              PS[0:128, 1536:1544],
                              lgF[0][:, t0:t0 + 128], id_s[0:8, 0:8])],
                          reads=(lgF[1], b_id), writes=(b_bank[3],))
                ctx.op("dve",
                       lambda E: E.reduce_max(scr1F[0], PS[:, 1536:1544],
                                              axis=AX.X),
                       reads=(b_bank[3],), writes=(scr1F[1],))
                ctx.op("dve",
                       lambda E: E.tensor_scalar(scr2F[0], scr1F[0], -1.0, None,
                                                 ALU.mult),
                       reads=(scr1F[1],), writes=(scr2F[1],))
                ctx.op("act",
                       lambda E: E.activation(pb8F[0], PS[:, 1536:1544], AF.Exp,
                                              bias=scr2F[0], accum_out=scr1F[0]),
                       reads=(b_bank[3], scr2F[1]), writes=(pb8F[1], scr1F[1]))
                ctx.op("dve", lambda E: E.reciprocal(scr2F[0], scr1F[0]),
                       reads=(scr1F[1],), writes=(scr2F[1],))
                ctx.op("dve",
                       lambda E: E.tensor_scalar(pb8F[0], pb8F[0], scr2F[0],
                                                 None, ALU.mult),
                       reads=(pb8F[1], scr2F[1]), writes=(pb8F[1],))
                ctx.op("dve",
                       lambda E: E.reduce_max(scr1F[0], pb8F[0], axis=AX.X),
                       reads=(pb8F[1],), writes=(scr1F[1],))
                ctx.op("dve",
                       lambda E: E.tensor_scalar(wtokF[0], pb8F[0], scr1F[0],
                                                 None, ALU.is_ge),
                       reads=(pb8F[1], scr1F[1]), writes=(wtokF[1],))
                ctx.op("dve",
                       lambda E: E.tensor_tensor(wtokF[0], wtokF[0], pb8F[0],
                                                 ALU.mult),
                       reads=(wtokF[1], pb8F[1]), writes=(wtokF[1],))
                ctx.group("pe",
                          [lambda E: E.transpose(PS[0:8, 1536:1664], wtokF[0],
                                                 id_s)],
                          reads=(wtokF[1], b_id), writes=(b_bank[3],))
                ctx.op("dve",
                       lambda E, t0=t0: E.tensor_copy(wTF[0][:, t0:t0 + 128],
                                                      PS[0:8, 1536:1664]),
                       reads=(b_bank[3],), writes=(wTF[1],))
        ctx.dma_cond("dpool",
                lambda E, hh: E.dma_start(
                    out=sh["WG"][:, hh * TH:(hh + 1) * TH], in_=wTF[0]),
                reads=(wTF[1],), writes=(b_sh["WG"],))
        send_flag(6, (b_sh["X3"], b_sh["WG"]))

        AR.reset()
        poll(6)
        x3tb = AR.take(128, 8192, BF16)
        hbF = AR.take(128, 32 * 1024, BF16)
        oaF = AR.take(128, 8 * 1024, F32)
        wb1F = [AR.take(128, 1024, BF16) for _ in range(2)]
        wb2F = [AR.take(128, 4096, BF16) for _ in range(2)]
        wBF = AR.take(128, 1024, F32)
        ytF = [AR.take(128, 512, BF16) for _ in range(2)]
        tmpF = AR.take(128, 512, F32)
        moF = [AR.take(128, 512, BF16) for _ in range(2)]
        X3r = sh["X3"].rearrange("(c p) t -> p c t", p=128)
        w1er = w1eb_d.rearrange("e (c p) f -> e p c f", p=128)
        w2er = w2eb_d.rearrange("e (c p) f -> e p c f", p=128)
        for tc in range(4):
            csl = slice(tc * 1024, (tc + 1) * 1024)
            e_load(ctx, x3tb[0].rearrange("p (c t) -> p c t", c=8),
                   X3r[:, :, csl], x3tb[1],
                   srcbufs=(b_phase[6], b_sh["X3"]), q="dsp")
            for e in range(4):
                ctx.dma_cond("dpool",
                        lambda E, hh, e=e, csl=csl: E.dma_start(
                            out=wBF[0],
                            in_=sh["WG"][hh * 4 + e: hh * 4 + e + 1, csl].to_broadcast(
                                [128, 1024])),
                        reads=(b_phase[6], b_sh["WG"]), writes=(wBF[1],))
                for ft in range(32):
                    wb = ft % 2
                    e_load(ctx, wb1F[wb][0].rearrange("p (c f) -> p c f", c=8),
                           w1er[e, :, :, ft * 128:(ft + 1) * 128], wb1F[wb][1],
                           srcbufs=(b_wcast,), q="dsp")
                    for sub in range(2):
                        pb = sub
                        ctx.group("pe",
                                  [(lambda E, dc=dc, wb=wb, sub=sub, pb=pb:
                                    E.matmul(
                                        bank(pb),
                                        wb1F[wb][0][:, dc * 128:(dc + 1) * 128],
                                        x3tb[0][:, dc * 1024 + sub * 512: dc * 1024 + (sub + 1) * 512],
                                        start=(dc == 0), stop=(dc == 7)))
                                   for dc in range(8)],
                                  reads=(wb1F[wb][1], x3tb[1]),
                                  writes=(b_bank[pb],))
                        ctx.op("act",
                               lambda E, ft=ft, sub=sub, e=e, pb=pb: E.activation(
                                   hbF[0][:, ft * 1024 + sub * 512: ft * 1024 + (sub + 1) * 512],
                                   bank(pb), AF.Gelu,
                                   bias=smalls["b1e"][:, e * 32 + ft: e * 32 + ft + 1]),
                               reads=(b_bank[pb], b_small["b1e"]), writes=(hbF[1],))
                for fc in range(8):
                    wb = fc % 2
                    e_load(ctx, wb2F[wb][0].rearrange("p (c f) -> p c f", c=32),
                           w2er[e, :, :, fc * 128:(fc + 1) * 128], wb2F[wb][1],
                           srcbufs=(b_wcast,), q="dsp")
                    for sub in range(2):
                        pb = 2 + sub
                        ctx.group("pe",
                                  [(lambda E, dc=dc, wb=wb, sub=sub, pb=pb: E.matmul(
                                      bank(pb),
                                      wb2F[wb][0][:, dc * 128:(dc + 1) * 128],
                                      hbF[0][:, dc * 1024 + sub * 512: dc * 1024 + (sub + 1) * 512],
                                      start=(dc == 0), stop=(dc == 31)))
                                   for dc in range(32)],
                                  reads=(wb2F[wb][1], hbF[1]), writes=(b_bank[pb],))
                        yb = sub
                        ctx.op("act",
                               lambda E, fc=fc, e=e, pb=pb, yb=yb: E.activation(
                                   ytF[yb][0], bank(pb), AF.Identity,
                                   bias=smalls["b2e"][:, e * 8 + fc: e * 8 + fc + 1]),
                               reads=(b_bank[pb], b_small["b2e"]),
                               writes=(ytF[yb][1],))
                        ctx.op("dve",
                               lambda E, sub=sub, yb=yb: E.tensor_tensor(
                                   tmpF[0], ytF[yb][0],
                                   wBF[0][:, sub * 512:(sub + 1) * 512], ALU.mult),
                               reads=(ytF[yb][1], wBF[1]), writes=(tmpF[1],))
                        if e == 0:
                            ctx.op("dve",
                                   lambda E, fc=fc, sub=sub: E.tensor_copy(
                                       oaF[0][:, fc * 1024 + sub * 512: fc * 1024 + (sub + 1) * 512],
                                       tmpF[0]),
                                   reads=(tmpF[1],), writes=(oaF[1],))
                        else:
                            ctx.op("dve",
                                   lambda E, fc=fc, sub=sub: E.tensor_tensor(
                                       oaF[0][:, fc * 1024 + sub * 512: fc * 1024 + (sub + 1) * 512],
                                       oaF[0][:, fc * 1024 + sub * 512: fc * 1024 + (sub + 1) * 512],
                                       tmpF[0], ALU.add),
                                   reads=(tmpF[1], oaF[1]), writes=(oaF[1],))
            for fc in range(8):
                for sub in range(2):
                    mb = sub
                    ctx.op("dve",
                           lambda E, fc=fc, sub=sub, mb=mb: E.tensor_copy(
                               moF[mb][0],
                               oaF[0][:, fc * 1024 + sub * 512: fc * 1024 + (sub + 1) * 512]),
                           reads=(oaF[1],), writes=(moF[mb][1],))
                    ctx.dma_cond("dpool",
                            lambda E, hh, fc=fc, tc=tc, sub=sub, mb=mb: E.dma_start(
                                out=sh["MO"][hh].rearrange(
                                    "(c p) t -> c p t", p=128)[fc][
                                    :, tc * 1024 + sub * 512: tc * 1024 + (sub + 1) * 512],
                                in_=moF[mb][0]),
                            reads=(moF[mb][1],), writes=(b_sh["MO"],))
        send_flag(7, (b_sh["MO"],))

        AR.reset()
        poll(7)
        sAG = AR.take(128, 4096, F32)
        sBG = AR.take(128, 4096, BF16)
        MOr0 = sh["MO"][0].rearrange("(c p) t -> p c t", p=128)
        MOr1 = sh["MO"][1].rearrange("(c p) t -> p c t", p=128)
        outTr = outT.rearrange("(c p) t -> c p t", p=128)
        for tt in range(4):
            tsl = slice(tt * 512, (tt + 1) * 512)
            e_load(ctx, sAG[0].rearrange("p (c t) -> p c t", c=8),
                   x3hdr[:, :, tsl].rearrange("c p t -> p c t"), sAG[1],
                   srcbufs=(b_d["x3h"],), q="dsp")
            for po in (MOr0, MOr1):
                ctx.dma_cond("dpool",
                        lambda E, hh, tt=tt, po=po: E.dma_start(
                            out=sBG[0].rearrange("p (c t) -> p c t", c=8),
                            in_=po[:, :, hh * TH + tt * 512: hh * TH + (tt + 1) * 512]),
                        reads=(b_phase[7], b_sh["MO"]), writes=(sBG[1],))
                for fc in range(8):
                    ctx.op("dve",
                           lambda E, fc=fc: E.tensor_tensor(
                               sAG[0][:, fc * 512:(fc + 1) * 512],
                               sAG[0][:, fc * 512:(fc + 1) * 512],
                               sBG[0][:, fc * 512:(fc + 1) * 512], ALU.add),
                           reads=(sAG[1], sBG[1]), writes=(sAG[1],))
            sGt = [sAG[0][:, fc * 512:(fc + 1) * 512] for fc in range(8)]

            def applyG(bcA, bcB, b_bc, tsl=tsl, sGt=sGt):
                for fc in range(8):
                    ctx.op("dve",
                           lambda E, fc=fc, bcA=bcA: E.tensor_tensor(
                               sGt[fc], sGt[fc], bcA, ALU.mult),
                           reads=(sAG[1], b_bc), writes=(sAG[1],))
                    ctx.op("dve",
                           lambda E, fc=fc, bcB=bcB: E.tensor_tensor(
                               sGt[fc], sGt[fc], bcB, ALU.subtract),
                           reads=(sAG[1], b_bc), writes=(sAG[1],))
                    e_store(ctx, outTr[fc, :, tsl], sGt[fc], sAG[1],
                            dstbufs=(b_out,))

            ln_block(ctx, scr, sGt, [sAG[1]] * 8, [applyG], 512, LN_EPS,
                     ones_col, ones_row, b_ones, bank(0), bank(1),
                     b_bank[0], b_bank[1])

        e_store(ctx, pdiag, sdiag, b_sdiag, dstbufs=(b_diagout,))
        ctx.barrier_all("sp")
        import collections
        qc = collections.Counter(k[0] for k in getattr(ctx, "_slotmap", {}))
        print("FUSED: dma dest bufs per queue:", dict(qc),
              "instrs:", {e: len(v) for e, v in ctx.ops.items()},
              "arena peaks:", getattr(AR, "peaks", []))
        ctx.run()
    return nc

_CACHE = {}


def _get(name, builder):
    if name not in _CACHE:
        _CACHE[name] = builder()
    return _CACHE[name]


def _prep_fused(inputs):
    x = np.asarray(inputs["L0_attn_x"])
    Wqkv = np.asarray(inputs["L0_attn_W_qkv"]).reshape(16, 64, 3, 1024)
    bqkv0 = np.asarray(inputs["L0_attn_b_qkv"]).reshape(16, 64, 3)
    wqkv1 = np.asarray(inputs["L1_attn_wqkv"])
    bqkv1 = np.asarray(inputs["L1_attn_bqkv"])
    w_in = np.asarray(inputs["L1_attn_w_in"])
    cw = np.asarray(inputs["L1_attn_conv_w"])
    cb = np.asarray(inputs["L1_attn_conv_b"])
    wx = np.asarray(inputs["L1_attn_w_xproj"])
    wdtv = np.asarray(inputs["L1_attn_w_dt"])
    bdtv = np.asarray(inputs["L1_attn_b_dt"])
    Alogv = np.asarray(inputs["L1_attn_A_log"])
    Dpv = np.asarray(inputs["L1_attn_Dp"])
    woutv = np.asarray(inputs["L1_attn_w_out"])
    w1 = np.asarray(inputs["L1_ffn_w1"])
    b1 = np.asarray(inputs["L1_ffn_b1"])
    w2 = np.asarray(inputs["L1_ffn_w2"])
    b2 = np.asarray(inputs["L1_ffn_b2"])
    mkq = _mk_masks_kq()
    mqk = _mk_masks_qk()
    idm = np.eye(128, dtype=np.float32)
    rng = np.random.default_rng()
    nonce = int(rng.integers(1, 2**28))
    fvals = np.arange(nonce, nonce + 16, dtype=np.int32).reshape(1, 16)
    com = {
        "woT": _t(np.asarray(inputs["L0_attn_W_o"])),
        "bo": np.ascontiguousarray(np.asarray(inputs["L0_attn_b_o"]).reshape(8, 128).T),
        "w1T": _t(inputs["L0_ffn_w1_w"]), "w2T": _t(inputs["L0_ffn_w2_w"]),
        "h1T": _t(inputs["L0_ffn_h1_w"]), "h2T": _t(inputs["L0_ffn_h2_w"]),
        "w3T": _t(inputs["L0_ffn_w3_w"]),
        "s1": np.ascontiguousarray(np.asarray(inputs["L0_ffn_w1_s"]).reshape(32, 128).T),
        "s2": np.ascontiguousarray(np.asarray(inputs["L0_ffn_w2_s"]).reshape(32, 128).T),
        "sh1": np.ascontiguousarray(np.asarray(inputs["L0_ffn_h1_s"]).reshape(32, 128).T),
        "sh2": np.ascontiguousarray(np.asarray(inputs["L0_ffn_h2_s"]).reshape(32, 128).T),
        "s3": np.ascontiguousarray(np.asarray(inputs["L0_ffn_w3_s"]).reshape(8, 128).T),
        "wkv": np.ascontiguousarray(wqkv1[:, 1024:1152]),
        "bkv": bqkv1[1024:1152].reshape(128, 1).copy(),
        "wao": np.ascontiguousarray(np.asarray(inputs["L1_attn_w_attn_out"])),
        "bao": np.ascontiguousarray(np.asarray(inputs["L1_attn_b_attn_out"]).reshape(8, 128).T),
        "masksL0": mkq, "masksQK": mqk, "masksKQ": mkq, "ident": idm,
        "gw": np.ascontiguousarray(np.asarray(inputs["L1_ffn_gate_w"])),
        "gb": np.asarray(inputs["L1_ffn_gate_b"]).reshape(8, 1).copy(),
        "fv": fvals, "fv2": fvals * 2,
    }
    maps = []
    for c in range(8):
        b, hh = c // 2, c % 2
        hsel = slice(hh * 8, hh * 8 + 8)
        csl = slice(hh * 1024, (hh + 1) * 1024)
        esl = slice(hh * 4, (hh + 1) * 4)
        m = dict(com)
        xTb = _t(x[b])
        m["xT"] = xTb
        m["xTh"] = np.ascontiguousarray(xTb[:, hh * TH:(hh + 1) * TH])
        m["wqT"] = _t(Wqkv[hsel, :, 0, :].reshape(512, 1024))
        m["wkT"] = _t(Wqkv[hsel, :, 1, :].reshape(512, 1024))
        m["wvT"] = _t(Wqkv[hsel, :, 2, :].reshape(512, 1024))
        m["bq"] = np.ascontiguousarray(bqkv0[hsel, :, 0].reshape(4, 128).T)
        m["bk"] = np.ascontiguousarray(bqkv0[hsel, :, 1].reshape(4, 128).T)
        m["bv"] = bqkv0[hsel, :, 2].reshape(1, 512).copy()
        m["wq1"] = np.ascontiguousarray(wqkv1[:, hh * 512:(hh + 1) * 512])
        m["bqm"] = np.ascontiguousarray(bqkv1[hh * 512:(hh + 1) * 512].reshape(4, 128).T)
        m["w_u"] = np.ascontiguousarray(w_in[:, csl])
        m["w_r"] = np.ascontiguousarray(w_in[:, 2048 + hh * 1024: 2048 + (hh + 1) * 1024])
        m["convw"] = np.ascontiguousarray(cw[csl].reshape(8, 128, 4).transpose(1, 0, 2).reshape(128, 32))
        m["convb"] = np.ascontiguousarray(cb[csl].reshape(8, 128).T)
        m["wxp"] = np.ascontiguousarray(wx[csl])
        m["wdt"] = np.ascontiguousarray(wdtv[:, csl])
        m["bdt"] = np.ascontiguousarray(bdtv[csl].reshape(8, 128).T)
        m["Alog"] = np.ascontiguousarray(
            Alogv[csl].reshape(8, 128, NST).transpose(1, 0, 2).reshape(128, 8 * NST))
        m["Dp"] = np.ascontiguousarray(Dpv[csl].reshape(8, 128).T)
        m["wout"] = np.ascontiguousarray(woutv[csl])
        m["w1e"] = np.ascontiguousarray(w1[esl])
        m["b1e"] = np.ascontiguousarray(b1[esl].reshape(4, 32, 128).transpose(2, 0, 1).reshape(128, 128))
        m["w2e"] = np.ascontiguousarray(w2[esl])
        m["b2e"] = np.ascontiguousarray(b2[esl].reshape(4, 8, 128).transpose(2, 0, 1).reshape(128, 32))
        maps.append(m)
    return maps


FUSED = True


def kernel_fused(**inputs):
    ncX = _get("FUSED", build_fused)
    r = run_bass_kernel_spmd(ncX, _prep_fused(inputs),
                             core_ids=list(range(8))).results
    halves = []
    for b in range(4):
        halves.append(np.concatenate([r[2 * b]["outT"], r[2 * b + 1]["outT"]],
                                     axis=1))
    return np.ascontiguousarray(np.stack([h.T for h in halves]))


def kernel(**inputs):
    if FUSED:
        return kernel_fused(**inputs)
    cores = list(range(8))

    ncA = _get("A", build_launchA)
    rA = run_bass_kernel_spmd(ncA, _prep_launchA(inputs), core_ids=cores).results
    zT_full = np.stack([np.concatenate([rA[2 * b]["zT"], rA[2 * b + 1]["zT"]], 0)
                        for b in range(4)])

    ncB = _get("B", build_launchB)
    rB = run_bass_kernel_spmd(ncB, _prep_launchB(inputs, zT_full),
                              core_ids=cores).results
    x2f = np.concatenate([rB[c]["outT"] for c in range(8)], axis=1)
    xT2 = np.stack([np.ascontiguousarray(x2f[:, b * S:(b + 1) * S])
                    for b in range(4)])

    ncC = _get("C", build_launchC)
    rC = run_bass_kernel_spmd(ncC, _prep_launchC(inputs, xT2),
                              core_ids=cores).results
    zT1 = np.stack([np.concatenate([rC[2 * b]["zT"], rC[2 * b + 1]["zT"]], 0)
                    for b in range(4)])

    ncD = _get("D", build_launchD)
    rD = run_bass_kernel_spmd(ncD, _prep_launchD(inputs, xT2, zT1),
                              core_ids=cores).results

    ncE = _get("E", build_launchE)
    uT_all = [rD[c]["uT"] for c in range(8)]
    svT_all = [rD[c]["svT"] for c in range(8)]
    xd_all = [rD[c]["xdbl_p"] for c in range(8)]
    rE = run_bass_kernel_spmd(ncE, _prep_launchE(inputs, uT_all, svT_all, xd_all),
                              core_ids=cores).results

    ncF = _get("F", build_launchF)
    x1T_full = np.stack([rD[2 * b]["x1T"] for b in range(4)])
    pouts = [rE[c]["poutT"] for c in range(8)]
    rF = run_bass_kernel_spmd(ncF, _prep_launchF(inputs, xT2, x1T_full, pouts),
                              core_ids=cores).results
    outf = np.concatenate([rF[c]["outT"] for c in range(8)], axis=1)
    return np.ascontiguousarray(outf.T).reshape(4, S, DM)


# version 10 (final)
# speedup vs baseline: 1.0917x; 1.0917x over previous
import numpy as np
import ml_dtypes
from contextlib import ExitStack

import concourse.bass as bass
import concourse.mybir as mybir
from concourse.bass_utils import run_bass_kernel_spmd
from concourse.expressions_rust import RuntimeValue as _RV

F32 = mybir.dt.float32
BF16 = mybir.dt.bfloat16
I32 = mybir.dt.int32
AF = mybir.ActivationFunctionType
ALU = mybir.AluOpType
AX = mybir.AxisListType

NEG = -1.0e30
LN_EPS = 1e-5
RMS_EPS = 1.1920929e-07

S = 4096
DM = 1024
NH = 8
DH = 64
HF = NH * DH


ENGS = ("pe", "act", "dve", "pool", "sp")
DQS = ("dpool", "dsp", "dact")
_Q2E = {"dpool": "pool", "dsp": "sp", "dact": "act"}


class Buf:
    __slots__ = ("name", "writer", "readers")

    def __init__(self, name):
        self.name = name
        self.writer = None
        self.readers = []


class Ctx:
    def __init__(self, nc, stack: ExitStack):
        self.nc = nc
        self.stack = stack
        self.eng = {"pe": nc.tensor, "act": nc.scalar, "dve": nc.vector,
                    "pool": nc.gpsimd, "sp": nc.sync}
        self.sems = {}
        for s in ENGS:
            self.sems[s] = stack.enter_context(nc.semaphore(f"S_{s}"))
        self.cnt = {s: 0 for s in ENGS}
        self.ops = {e: [] for e in ENGS}
        self.wm = {e: {} for e in ENGS}
        self._nbuf = 0
        self._consts = {}
        self._ctile = None
        self._cbuf = None

    def constcol(self, value):
        if value in self._consts:
            return self._consts[value]
        if self._ctile is None:
            self._ctile = self.nc.alloc_sbuf_tensor("cconsts", [128, 16], F32).ap()
            self._cbuf = self.buf("consts")
        i = len(self._consts)
        assert i < 16
        ap = self._ctile[:, i:i + 1]
        self.op("dve", lambda E, ap=ap, value=value: E.memset(ap, value),
                writes=(self._cbuf,))
        self._consts[value] = ap
        return ap

    def buf(self, name=None):
        self._nbuf += 1
        return Buf(name or f"b{self._nbuf}")

    def claim(self, new, olds):
        rs = []
        for o in olds:
            if o.writer is not None:
                rs.append(o.writer)
            rs += o.readers
            o.writer = None
            o.readers = []
        new.writer = None
        new.readers = rs

    def _wait(self, eng, src, cnt):
        if self.wm[eng].get(src, 0) >= cnt:
            return
        self.wm[eng][src] = cnt
        sem = self.sems[src]
        self.ops[eng].append(lambda E, sem=sem, cnt=cnt: E.wait_ge(sem, cnt))

    def _pre(self, eng, reads, writes):
        for b in reads:
            if b.writer is not None:
                self._wait(eng, *b.writer)
        for b in writes:
            if b.writer is not None:
                self._wait(eng, *b.writer)
            for r in b.readers:
                self._wait(eng, *r)

    def _post(self, me, reads, writes):
        for b in reads:
            b.readers.append(me)
        for b in writes:
            b.writer = me
            b.readers = []

    def op(self, eng, build, reads=(), writes=()):
        self._pre(eng, reads, writes)
        self.cnt[eng] += 1
        me = (eng, self.cnt[eng])
        sem = self.sems[eng]
        self.ops[eng].append(lambda E, b=build, s=sem: b(E).then_inc(s, 1))
        self._post(me, reads, writes)
        return me

    def group(self, eng, builds, reads=(), writes=()):
        self._pre(eng, reads, writes)
        self.cnt[eng] += 1
        me = (eng, self.cnt[eng])
        sem = self.sems[eng]
        for b in builds[:-1]:
            self.ops[eng].append(b)
        self.ops[eng].append(lambda E, b=builds[-1], s=sem: b(E).then_inc(s, 1))
        self._post(me, reads, writes)
        return me

    DMA_SLOTS = {"dpool": 40, "dsp": 30, "dact": 12}

    def _dma_key(self, q, bkey):
        if not hasattr(self, "_slotmap"):
            self._slotmap = {}
        sk = (q, bkey)
        if sk not in self._slotmap:
            nq = sum(1 for k in self._slotmap if k[0] == q)
            self._slotmap[sk] = nq % self.DMA_SLOTS[q]
        key = f"q:{q}:{self._slotmap[sk]}"
        if key not in self.sems:
            self.sems[key] = self.stack.enter_context(
                self.nc.semaphore(key.replace(":", "_")))
            self.cnt[key] = 0
        return key

    def dma(self, q, build, reads=(), writes=()):
        eng = _Q2E[q]
        assert len(writes) >= 1
        key = self._dma_key(q, writes[0].name)
        self._pre(eng, reads, writes)
        self.cnt[key] += 16
        me = (key, self.cnt[key])
        sem = self.sems[key]
        self.ops[eng].append(lambda E, b=build, s=sem: b(E).then_inc(s, 16))
        self._post(me, reads, writes)
        return me

    def dma_cond(self, q, build2, reads=(), writes=()):
        eng = _Q2E[q]
        assert eng == "pool"
        assert len(writes) >= 1
        key = self._dma_key(q, writes[0].name)
        self._pre(eng, reads, writes)
        self.cnt[key] += 16
        me = (key, self.cnt[key])
        sem = self.sems[key]

        def f(E, b2=build2, s=sem):
            with E.If_eq(self.hh_handle, 0):
                b2(E, 0).then_inc(s, 16)
            with E.Else():
                b2(E, 1).then_inc(s, 16)
        self.ops[eng].append(f)
        self._post(me, reads, writes)
        return me

    def raw(self, eng, fn):
        self.ops[eng].append(fn)

    def barrier_all(self, eng="sp"):
        for s in list(self.cnt):
            if s != eng and self.cnt[s] > 0:
                self._wait(eng, s, self.cnt[s])

    def run(self):
        nc = self.nc
        with nc.Block() as blk:
            for ename, dec in (("pe", blk.tensor), ("act", blk.scalar),
                               ("dve", blk.vector), ("pool", blk.gpsimd),
                               ("sp", blk.sync)):
                if self.ops[ename]:
                    def body(E, fns=self.ops[ename]):
                        for f in fns:
                            try:
                                f(E)
                            except Exception:
                                import traceback
                                traceback.print_exc()
                                raise
                    dec(body)


def e_load(ctx, dst_ap, src_ap, dstbuf, srcbufs=(), q="dpool"):
    return ctx.dma(q, lambda E: E.dma_start(out=dst_ap, in_=src_ap),
                   reads=srcbufs, writes=(dstbuf,))


def e_store(ctx, dst_ap, src_ap, srcbuf, dstbufs=(), q="dsp"):
    return ctx.dma(q, lambda E: E.dma_start(out=dst_ap, in_=src_ap),
                   reads=(srcbuf,), writes=dstbufs)


class LNScr:

    def __init__(self, ctx, nc, tag=""):
        al = nc.alloc_sbuf_tensor
        self.sq = [al(f"lnsq{tag}{i}", [128, 512], F32).ap() for i in range(2)]
        self.rows = al(f"lnrows{tag}", [1, 512 * 4], F32).ap()
        self.browA = al(f"lnbrA{tag}", [1, 512], BF16).ap()
        self.browB = al(f"lnbrB{tag}", [1, 512], BF16).ap()
        self.bA = al(f"lnbA{tag}", [128, 512], F32).ap()
        self.bB = al(f"lnbB{tag}", [128, 512], F32).ap()
        self.b_sq = [ctx.buf(), ctx.buf()]
        self.b_rows = ctx.buf()
        self.b_brow = ctx.buf()
        self.b_bc = ctx.buf()


def ln_block(ctx, scr, in_tiles, in_bufs, out_writes, T, eps, ones_col, ones_row,
             b_ones, bankA, bankB, b_bankA, b_bankB, nfeat=DM):
    nchunk = nfeat // 128
    for dc in range(nchunk):
        ctx.op("act",
               lambda E, dc=dc: E.activation(scr.sq[dc % 2][:, 0:T], in_tiles[dc],
                                             AF.Square),
               reads=(in_bufs[dc],), writes=(scr.b_sq[dc % 2],))
        ctx.group("pe",
                  [lambda E, dc=dc: E.matmul(bankA[0:1, 0:T], ones_col,
                                             in_tiles[dc],
                                             start=(dc == 0), stop=(dc == nchunk - 1))],
                  reads=(in_bufs[dc], b_ones), writes=(b_bankA,))
        ctx.group("pe",
                  [lambda E, dc=dc: E.matmul(bankB[0:1, 0:T], ones_col,
                                             scr.sq[dc % 2][:, 0:T],
                                             start=(dc == 0), stop=(dc == nchunk - 1))],
                  reads=(scr.b_sq[dc % 2], b_ones), writes=(b_bankB,))
    inv = 1.0 / nfeat
    ctx.op("dve", lambda E: E.tensor_scalar(scr.rows[:, 0:T], bankA[0:1, 0:T],
                                            inv, None, ALU.mult),
           reads=(b_bankA,), writes=(scr.b_rows,))
    ctx.op("act", lambda E: E.activation(scr.rows[:, 512:512 + T],
                                         scr.rows[:, 0:T], AF.Square),
           reads=(scr.b_rows,), writes=(scr.b_rows,))
    ctx.op("dve", lambda E: E.tensor_scalar(scr.rows[:, 1024:1024 + T],
                                            bankB[0:1, 0:T], inv, None, ALU.mult),
           reads=(b_bankB,), writes=(scr.b_rows,))
    ctx.op("dve", lambda E: E.tensor_tensor(scr.rows[:, 1024:1024 + T],
                                            scr.rows[:, 1024:1024 + T],
                                            scr.rows[:, 512:512 + T], ALU.subtract),
           reads=(scr.b_rows,), writes=(scr.b_rows,))
    c_eps = ctx.constcol(eps)
    ctx.op("act", lambda E: E.activation(scr.rows[:, 1024:1024 + T],
                                         scr.rows[:, 1024:1024 + T], AF.Ln,
                                         bias=c_eps[0:1, :]),
           reads=(scr.b_rows, ctx._cbuf), writes=(scr.b_rows,))
    ctx.op("act", lambda E: E.activation(scr.browA[:, 0:T],
                                         scr.rows[:, 1024:1024 + T], AF.Exp,
                                         scale=-0.5),
           reads=(scr.b_rows,), writes=(scr.b_brow,))
    ctx.op("dve", lambda E: E.tensor_copy(scr.browB[:, 0:T], scr.rows[:, 0:T]),
           reads=(scr.b_rows,), writes=(scr.b_brow,))
    ctx.op("dve", lambda E: E.tensor_tensor(scr.browB[:, 0:T], scr.browB[:, 0:T],
                                            scr.browA[:, 0:T], ALU.mult),
           reads=(scr.b_brow,), writes=(scr.b_brow,))
    ctx.group("pe", [lambda E: E.matmul(bankA[:, 0:T], ones_row, scr.browA[:, 0:T],
                                        start=True, stop=True)],
              reads=(scr.b_brow, b_ones), writes=(b_bankA,))
    ctx.group("pe", [lambda E: E.matmul(bankB[:, 0:T], ones_row, scr.browB[:, 0:T],
                                        start=True, stop=True)],
              reads=(scr.b_brow, b_ones), writes=(b_bankB,))
    ctx.op("dve", lambda E: E.tensor_copy(scr.bA[:, 0:T], bankA[:, 0:T]),
           reads=(b_bankA,), writes=(scr.b_bc,))
    ctx.op("dve", lambda E: E.tensor_copy(scr.bB[:, 0:T], bankB[:, 0:T]),
           reads=(b_bankA, b_bankB), writes=(scr.b_bc,))
    for fn in out_writes:
        fn(scr.bA[:, 0:T], scr.bB[:, 0:T], scr.b_bc)



def build_launchA():
    nc = bass.Bass("TRN2", target_bir_lowering=False)
    xT = nc.dram_tensor("xT", [DM, S], F32, kind="ExternalInput").ap()
    wqT = nc.dram_tensor("wqT", [DM, HF], F32, kind="ExternalInput").ap()
    wkT = nc.dram_tensor("wkT", [DM, HF], F32, kind="ExternalInput").ap()
    wvT = nc.dram_tensor("wvT", [DM, HF], F32, kind="ExternalInput").ap()
    bq = nc.dram_tensor("bq", [128, 4], F32, kind="ExternalInput").ap()
    bk = nc.dram_tensor("bk", [128, 4], F32, kind="ExternalInput").ap()
    bv = nc.dram_tensor("bv", [1, HF], F32, kind="ExternalInput").ap()
    masks = nc.dram_tensor("masks", [128, 4 * 512], F32, kind="ExternalInput").ap()
    zT_out = nc.dram_tensor("zT", [HF, S], F32, kind="ExternalOutput").ap()
    xn_d = nc.dram_tensor("xn_d", [DM, S], BF16, kind="Internal").ap()

    with ExitStack() as st:
        ctx = Ctx(nc, st)
        al = nc.alloc_sbuf_tensor

        arena0 = al("arena0", [128, 8192], F32)
        xtile = [arena0.ap()[:, 0:4096], arena0.ap()[:, 4096:8192]]
        V = arena0.ap().bitcast(BF16)
        sq = [al(f"sq{i}", [128, 512], BF16).ap() for i in range(2)]
        rstdrow = al("rstdrow", [1, S], BF16).ap()
        rstdB = al("rstdB", [128, S], F32).ap()
        xnc = [al(f"xnc{i}", [128, S], BF16).ap() for i in range(2)]
        wq_s = al("wq_s", [128, 8 * HF], BF16).ap()
        wk_s = al("wk_s", [128, 8 * HF], BF16).ap()
        wv_s = al("wv_s", [128, 8 * HF], BF16).ap()
        bq_s = al("bq_s", [128, 4], F32).ap()
        bk_s = al("bk_s", [128, 4], F32).ap()
        bv_row = al("bv_row", [1, HF], BF16).ap()
        bvB = al("bvB", [128, HF], F32).ap()
        ones_col = al("ones_col", [128, 1], BF16).ap()
        ones_row = al("ones_row", [1, 128], BF16).ap()
        mask_s = al("mask_s", [128, 4 * 512], F32).ap()
        QT = al("QT", [128, 4 * S], BF16).ap()
        KT = al("KT", [128, 4 * S], BF16).ap()
        m_s = al("m_s", [128, NH * 32], F32).ap()
        s_s = al("s_s", [128, NH * 32], F32).ap()
        nb_s = al("nb_s", [128, NH * 32], F32).ap()
        expsc = al("expsc", [128, 1024], BF16).ap()
        alpha = al("alpha", [128, 1024], BF16).ap()
        scr1 = al("scr1", [128, 1], F32).ap()
        zseg = [al(f"zseg{i}", [64, 512], F32).ap() for i in range(2)]

        PS = nc.alloc_psum_tensor("PS", [128, 4096], F32).ap()

        def bank(i, rows=128):
            return PS[0:rows, i * 512:(i + 1) * 512]

        def bank2(i):
            return PS[:, i * 512:(i + 2) * 512]

        b_xtile = [ctx.buf("xt0"), ctx.buf("xt1")]
        b_sq = [ctx.buf(), ctx.buf()]
        b_rstdrow = ctx.buf(); b_rstdB = ctx.buf()
        b_xnc = [ctx.buf("xnc0"), ctx.buf("xnc1")]
        b_xnd = ctx.buf("xn_dram")
        b_w = {k: ctx.buf() for k in ("q", "k", "v")}
        b_bq = ctx.buf(); b_bk = ctx.buf(); b_bvrow = ctx.buf(); b_bvB = ctx.buf()
        b_ones = ctx.buf()
        b_mask = ctx.buf()
        b_QT = ctx.buf(); b_KT = ctx.buf(); b_V = ctx.buf("V")
        b_m = ctx.buf(); b_ssum = ctx.buf(); b_nb = ctx.buf()
        b_expsc = ctx.buf(); b_alpha = ctx.buf()
        b_scr1 = ctx.buf()
        b_zseg = [ctx.buf(), ctx.buf()]
        b_zout = ctx.buf("zout")
        b_bank = [ctx.buf(f"bank{i}") for i in range(8)]

        ctx.op("dve", lambda E: E.memset(ones_col[:], 1.0), writes=(b_ones,))
        ctx.op("dve", lambda E: E.memset(ones_row[:], 1.0), writes=(b_ones,))
        e_load(ctx, mask_s[:], masks[:], b_mask)
        e_load(ctx, bq_s[:], bq[:], b_bq)
        e_load(ctx, bk_s[:], bk[:], b_bk)
        e_load(ctx, bv_row[:], bv[:], b_bvrow)
        e_load(ctx, wq_s.rearrange("p (c f) -> p c f", c=8),
               wqT.rearrange("(c p) f -> p c f", p=128), b_w["q"])
        e_load(ctx, wk_s.rearrange("p (c f) -> p c f", c=8),
               wkT.rearrange("(c p) f -> p c f", p=128), b_w["k"])
        e_load(ctx, wv_s.rearrange("p (c f) -> p c f", c=8),
               wvT.rearrange("(c p) f -> p c f", p=128), b_w["v"])

        ctx.group("pe", [lambda E: E.matmul(bank(1), ones_row[:], bv_row[:],
                                            start=True, stop=True)],
                  reads=(b_bvrow, b_ones), writes=(b_bank[1],))
        ctx.op("dve", lambda E: E.tensor_copy(bvB[:], bank(1)),
               reads=(b_bank[1],), writes=(b_bvB,))

        xTr = xT.rearrange("(c p) t -> c p t", p=128)
        for dc in range(8):
            e_load(ctx, xtile[dc % 2], xTr[dc], b_xtile[dc % 2])
            for tt in range(8):
                ctx.op("act",
                       lambda E, dc=dc, tt=tt: E.activation(
                           sq[tt % 2], xtile[dc % 2][:, tt * 512:(tt + 1) * 512],
                           AF.Square),
                       reads=(b_xtile[dc % 2],), writes=(b_sq[tt % 2],))
                ctx.group("pe",
                          [lambda E, dc=dc, tt=tt: E.matmul(
                              bank(tt, 1), ones_col[:], sq[tt % 2],
                              start=(dc == 0), stop=(dc == 7))],
                          reads=(b_sq[tt % 2], b_ones), writes=(b_bank[tt],))
        c_eps = ctx.constcol(RMS_EPS)
        for tt in range(8):
            ctx.op("act",
                   lambda E, tt=tt: E.activation(
                       rstdrow[:, tt * 512:(tt + 1) * 512], bank(tt, 1),
                       AF.Ln, bias=c_eps[0:1, :], scale=1.0 / DM),
                   reads=(b_bank[tt], ctx._cbuf), writes=(b_rstdrow,))
            ctx.op("act",
                   lambda E, tt=tt: E.activation(
                       rstdrow[:, tt * 512:(tt + 1) * 512],
                       rstdrow[:, tt * 512:(tt + 1) * 512],
                       AF.Exp, scale=-0.5),
                   reads=(b_rstdrow,), writes=(b_rstdrow,))
            ctx.group("pe",
                      [lambda E, tt=tt: E.matmul(
                          bank(tt), ones_row[:],
                          rstdrow[:, tt * 512:(tt + 1) * 512],
                          start=True, stop=True)],
                      reads=(b_rstdrow, b_ones), writes=(b_bank[tt],))
            ctx.op("dve",
                   lambda E, tt=tt: E.tensor_copy(
                       rstdB[:, tt * 512:(tt + 1) * 512], bank(tt)),
                   reads=(b_bank[tt],), writes=(b_rstdB,))

        xn_dr = xn_d.rearrange("(c p) t -> c p t", p=128)
        for dc in range(8):
            e_load(ctx, xtile[dc % 2], xTr[dc], b_xtile[dc % 2])
            ctx.op("dve",
                   lambda E, dc=dc: E.tensor_tensor(
                       xnc[dc % 2], xtile[dc % 2], rstdB[:], ALU.mult),
                   reads=(b_xtile[dc % 2], b_rstdB), writes=(b_xnc[dc % 2],))
            e_store(ctx, xn_dr[dc], xnc[dc % 2], b_xnc[dc % 2], dstbufs=(b_xnd,))

        for tt in range(8):
            for dc in range(8):
                xb = b_xnc[dc % 2]
                e_load(ctx, xnc[dc % 2][:, 0:512],
                       xn_dr[dc][:, tt * 512:(tt + 1) * 512], xb,
                       srcbufs=(b_xnd,))
                for fc in range(4):
                    ctx.group("pe",
                              [(lambda E, fc=fc, tt=tt, dc=dc: E.matmul(
                                  bank(fc),
                                  wq_s[:, dc * HF + fc * 128: dc * HF + fc * 128 + 128],
                                  xnc[dc % 2][:, 0:512],
                                  start=(dc == 0), stop=(dc == 7)))],
                              reads=(b_w["q"], xb), writes=(b_bank[fc],))
                    ctx.group("pe",
                              [(lambda E, fc=fc, tt=tt, dc=dc: E.matmul(
                                  bank(4 + fc),
                                  wk_s[:, dc * HF + fc * 128: dc * HF + fc * 128 + 128],
                                  xnc[dc % 2][:, 0:512],
                                  start=(dc == 0), stop=(dc == 7)))],
                              reads=(b_w["k"], xb), writes=(b_bank[4 + fc],))
            for fc in range(4):
                ctx.op("act",
                       lambda E, fc=fc, tt=tt: E.activation(
                           QT[:, fc * S + tt * 512: fc * S + (tt + 1) * 512],
                           bank(fc), AF.Identity, bias=bq_s[:, fc:fc + 1]),
                       reads=(b_bank[fc], b_bq), writes=(b_QT,))
                ctx.op("act",
                       lambda E, fc=fc, tt=tt: E.activation(
                           KT[:, fc * S + tt * 512: fc * S + (tt + 1) * 512],
                           bank(4 + fc), AF.Identity, bias=bk_s[:, fc:fc + 1]),
                       reads=(b_bank[4 + fc], b_bk), writes=(b_KT,))
        ctx.claim(b_V, b_xtile)
        for tt in range(8):
            for dc in range(8):
                xb = b_xnc[dc % 2]
                e_load(ctx, xnc[dc % 2][:, 0:512],
                       xn_dr[dc][:, tt * 512:(tt + 1) * 512], xb,
                       srcbufs=(b_xnd,))
                for sub in range(4):
                    ctx.group("pe",
                              [(lambda E, dc=dc, sub=sub: E.matmul(
                                  bank(sub),
                                  xnc[dc % 2][:, sub * 128:(sub + 1) * 128],
                                  wv_s[:, dc * HF:(dc + 1) * HF],
                                  start=(dc == 0), stop=(dc == 7)))],
                              reads=(b_w["v"], xb), writes=(b_bank[sub],))
            for sub in range(4):
                ti = tt * 4 + sub
                ctx.op("dve",
                       lambda E, ti=ti, sub=sub: E.tensor_tensor(
                           V[:, ti * HF:(ti + 1) * HF], bank(sub), bvB[:], ALU.add),
                       reads=(b_bank[sub], b_bvB), writes=(b_V,))

        def head_slices(h):
            return h // 2, (h % 2) * 64

        def s_matmul(E, h, kt, qt, pb):
            fc, p0 = head_slices(h)
            return E.matmul(
                bank(pb),
                KT[p0:p0 + 64, fc * S + kt * 128: fc * S + kt * 128 + 128],
                QT[p0:p0 + 64, fc * S + qt * 512: fc * S + (qt + 1) * 512],
                start=True, stop=True)

        def mask_add(E, kt, qt, pb):
            mi = (kt * 128 - qt * 512) // 128
            return E.tensor_tensor(bank(pb), bank(pb),
                                   mask_s[:, mi * 512:(mi + 1) * 512], ALU.add)

        def emit_score(h, kt, qt, pb):
            ctx.group("pe", [lambda E: s_matmul(E, h, kt, qt, pb)],
                      reads=(b_KT, b_QT), writes=(b_bank[pb],))
            if 0 <= kt * 128 - qt * 512 <= 384:
                ctx.op("dve", lambda E: mask_add(E, kt, qt, pb),
                       reads=(b_bank[pb], b_mask), writes=(b_bank[pb],))

        for h in range(NH):
            for kt in range(32):
                col = h * 32 + kt
                qts = [qt for qt in range(8) if kt * 128 - qt * 512 <= 384]
                first = True
                for i in range(0, len(qts), 2):
                    pair = qts[i:i + 2]
                    for pb, qt in enumerate(pair):
                        emit_score(h, kt, qt, pb)
                    wid = len(pair) * 512
                    rb = (b_bank[0], b_bank[1]) if len(pair) == 2 else (b_bank[0],)
                    ctx.op("dve",
                           lambda E, wid=wid: E.reduce_max(
                               scr1, PS[:, 0:wid], axis=AX.X),
                           reads=rb, writes=(b_scr1,))
                    if first:
                        ctx.op("dve",
                               lambda E, col=col: E.tensor_copy(
                                   m_s[:, col:col + 1], scr1),
                               reads=(b_scr1,), writes=(b_m,))
                        first = False
                    else:
                        ctx.op("dve",
                               lambda E, col=col: E.tensor_tensor(
                                   m_s[:, col:col + 1], m_s[:, col:col + 1],
                                   scr1, ALU.max),
                               reads=(b_scr1, b_m), writes=(b_m,))
            for kt in range(32):
                col = h * 32 + kt
                qts = [qt for qt in range(8) if kt * 128 - qt * 512 <= 384]
                ctx.op("dve",
                       lambda E, col=col: E.tensor_scalar(
                           nb_s[:, col:col + 1], m_s[:, col:col + 1], -1.0,
                           None, ALU.mult),
                       reads=(b_m,), writes=(b_nb,))
                first = True
                for i in range(0, len(qts), 2):
                    pair = qts[i:i + 2]
                    for pb, qt in enumerate(pair):
                        emit_score(h, kt, qt, pb)
                    wid = len(pair) * 512
                    rb = (b_bank[0], b_bank[1]) if len(pair) == 2 else (b_bank[0],)
                    ctx.op("act",
                           lambda E, wid=wid, col=col: E.activation(
                               expsc[:, 0:wid], PS[:, 0:wid], AF.Exp,
                               bias=nb_s[:, col:col + 1], accum_out=scr1),
                           reads=rb + (b_nb,), writes=(b_expsc, b_scr1))
                    if first:
                        ctx.op("dve",
                               lambda E, col=col: E.tensor_copy(
                                   s_s[:, col:col + 1], scr1),
                               reads=(b_scr1,), writes=(b_ssum,))
                        first = False
                    else:
                        ctx.op("dve",
                               lambda E, col=col: E.tensor_tensor(
                                   s_s[:, col:col + 1], s_s[:, col:col + 1],
                                   scr1, ALU.add),
                               reads=(b_scr1, b_ssum), writes=(b_ssum,))
                ctx.op("act",
                       lambda E, col=col: E.activation(
                           scr1, s_s[:, col:col + 1], AF.Ln),
                       reads=(b_ssum,), writes=(b_scr1,))
                ctx.op("dve",
                       lambda E, col=col: E.tensor_tensor(
                           scr1, scr1, m_s[:, col:col + 1], ALU.add),
                       reads=(b_scr1, b_m), writes=(b_scr1,))
                ctx.op("dve",
                       lambda E, col=col: E.tensor_scalar(
                           nb_s[:, col:col + 1], scr1, -1.0, None, ALU.mult),
                       reads=(b_scr1,), writes=(b_nb,))
            for qt in range(8):
                kts = [kt for kt in range(32) if kt * 128 - qt * 512 <= 384]
                zb = 2 + qt % 2
                nlive = len(kts)
                for j, kt in enumerate(kts):
                    col = h * 32 + kt
                    spb = j % 2
                    emit_score(h, kt, qt, spb)
                    ctx.op("act",
                           lambda E, col=col, spb=spb: E.activation(
                               alpha[:, spb * 512:(spb + 1) * 512],
                               bank(spb), AF.Exp, bias=nb_s[:, col:col + 1]),
                           reads=(b_bank[spb], b_nb), writes=(b_alpha,))
                    ctx.group("pe",
                              [lambda E, h=h, kt=kt, j=j, nlive=nlive, zb=zb, spb=spb:
                               E.matmul(bank(zb, 64),
                                        V[:, kt * HF + h * 64: kt * HF + (h + 1) * 64],
                                        alpha[:, spb * 512:(spb + 1) * 512],
                                        start=(j == 0), stop=(j == nlive - 1))],
                              reads=(b_V, b_alpha), writes=(b_bank[zb],))
                sb = qt % 2
                ctx.op("dve",
                       lambda E, zb=zb, sb=sb: E.tensor_copy(zseg[sb], bank(zb, 64)),
                       reads=(b_bank[zb],), writes=(b_zseg[sb],))
                e_store(ctx, zT_out[h * 64:(h + 1) * 64, qt * 512:(qt + 1) * 512],
                        zseg[sb], b_zseg[sb], dstbufs=(b_zout,))

        ctx.barrier_all("sp")
        ctx.run()
    return nc



TB = 2048


def build_launchB():
    nc = bass.Bass("TRN2", target_bir_lowering=False)
    xT_s = nc.dram_tensor("xT_s", [DM, TB], F32, kind="ExternalInput").ap()
    zT_s = nc.dram_tensor("zT_s", [DM, TB], F32, kind="ExternalInput").ap()
    woT = nc.dram_tensor("woT", [DM, DM], F32, kind="ExternalInput").ap()
    bo = nc.dram_tensor("bo", [128, 8], F32, kind="ExternalInput").ap()
    w1T = nc.dram_tensor("w1T", [DM, 4096], F32, kind="ExternalInput").ap()
    w2T = nc.dram_tensor("w2T", [DM, 4096], F32, kind="ExternalInput").ap()
    h1T = nc.dram_tensor("h1T", [4096, 4096], F32, kind="ExternalInput").ap()
    h2T = nc.dram_tensor("h2T", [4096, 4096], F32, kind="ExternalInput").ap()
    w3T = nc.dram_tensor("w3T", [4096, DM], F32, kind="ExternalInput").ap()
    s1 = nc.dram_tensor("s1", [128, 32], F32, kind="ExternalInput").ap()
    s2 = nc.dram_tensor("s2", [128, 32], F32, kind="ExternalInput").ap()
    sh1 = nc.dram_tensor("sh1", [128, 32], F32, kind="ExternalInput").ap()
    sh2 = nc.dram_tensor("sh2", [128, 32], F32, kind="ExternalInput").ap()
    s3 = nc.dram_tensor("s3", [128, 8], F32, kind="ExternalInput").ap()
    outT = nc.dram_tensor("outT", [DM, TB], F32, kind="ExternalOutput").ap()
    x1_d = nc.dram_tensor("x1_d", [DM, TB], F32, kind="Internal").ap()
    x1b_d = nc.dram_tensor("x1b_d", [DM, TB], BF16, kind="Internal").ap()

    with ExitStack() as st:
        ctx = Ctx(nc, st)
        al = nc.alloc_sbuf_tensor

        arena1 = al("arena1", [128, 8192], F32)
        zbuf = arena1.ap()[:, 0:2048].bitcast(BF16)
        xbuf = arena1.ap()[:, 4096:8192]
        x1buf = xbuf
        xin = arena1.ap()[:, 0:4096].bitcast(BF16)
        x2buf = arena1.ap()[:, 4096:8192]

        hA = al("hA", [128, 32 * 1024], BF16).ap()
        hB = al("hB", [128, 32 * 1024], BF16).ap()
        wbuf = [al(f"wbuf{i}", [128, 4096], BF16).ap() for i in range(2)]
        wobuf = al("wobuf", [128, 8 * 128], BF16).ap()
        rbuf = [al(f"rbuf{i}", [128, 512], F32).ap() for i in range(2)]
        t1 = [al(f"t1_{i}", [128, 512], BF16).ap() for i in range(2)]
        bo_s = al("bo_s", [128, 8], F32).ap()
        s1_s = al("s1_s", [128, 32], F32).ap()
        s2_s = al("s2_s", [128, 32], F32).ap()
        sh1_s = al("sh1_s", [128, 32], F32).ap()
        sh2_s = al("sh2_s", [128, 32], F32).ap()
        s3_s = al("s3_s", [128, 8], F32).ap()
        ones_col = al("ones_col", [128, 1], F32).ap()
        ones_row = al("ones_row", [1, 128], BF16).ap()
        scr = LNScr(ctx, nc)

        PS = nc.alloc_psum_tensor("PS", [128, 4096], F32).ap()

        def bank(i, rows=128):
            return PS[0:rows, i * 512:(i + 1) * 512]

        b_z = ctx.buf("zb"); b_x = ctx.buf("xb"); b_x1 = ctx.buf("x1b")
        b_xin = ctx.buf("xin"); b_x2 = ctx.buf("x2b")
        b_hA = ctx.buf("hA"); b_hB = ctx.buf("hB")
        b_w = [ctx.buf("w0"), ctx.buf("w1")]
        b_wo = ctx.buf("wo")
        b_r = [ctx.buf(), ctx.buf()]
        b_t1 = [ctx.buf(), ctx.buf()]
        b_sc = {k: ctx.buf(k) for k in ("bo", "s1", "s2", "sh1", "sh2", "s3")}
        b_ones = ctx.buf()
        b_bank = [ctx.buf(f"bank{i}") for i in range(8)]
        b_x1d = ctx.buf("x1_dram"); b_x1bd = ctx.buf("x1b_dram")
        b_out = ctx.buf("outd")

        ctx.op("dve", lambda E: E.memset(ones_col[:], 1.0), writes=(b_ones,))
        ctx.op("dve", lambda E: E.memset(ones_row[:], 1.0), writes=(b_ones,))
        e_load(ctx, bo_s[:], bo, b_sc["bo"])
        e_load(ctx, s1_s[:], s1, b_sc["s1"])
        e_load(ctx, s2_s[:], s2, b_sc["s2"])
        e_load(ctx, sh1_s[:], sh1, b_sc["sh1"])
        e_load(ctx, sh2_s[:], sh2, b_sc["sh2"])
        e_load(ctx, s3_s[:], s3, b_sc["s3"])

        zTr = zT_s.rearrange("(c p) t -> c p t", p=128)
        xTr = xT_s.rearrange("(c p) t -> c p t", p=128)
        x1dr = x1_d.rearrange("(c p) t -> c p t", p=128)
        x1bdr = x1b_d.rearrange("(c p) t -> c p t", p=128)
        outTr = outT.rearrange("(c p) t -> c p t", p=128)

        for tt in range(4):
            tsl = slice(tt * 512, (tt + 1) * 512)
            e_load(ctx, zbuf.rearrange("p (c t) -> p c t", c=8),
                   zTr[:, :, tsl].rearrange("c p t -> p c t"), b_z)
            e_load(ctx, xbuf.rearrange("p (c t) -> p c t", c=8),
                   xTr[:, :, tsl].rearrange("c p t -> p c t"), b_x)
            for fc in range(8):
                e_load(ctx, wobuf.rearrange("p (c f) -> p c f", c=8),
                       woT[:, fc * 128:(fc + 1) * 128].rearrange(
                           "(c p) f -> p c f", p=128), b_wo)
                pb = fc % 2
                ctx.group("pe",
                          [(lambda E, dc=dc, pb=pb: E.matmul(
                              bank(pb), wobuf[:, dc * 128:(dc + 1) * 128],
                              zbuf[:, dc * 512:(dc + 1) * 512],
                              start=(dc == 0), stop=(dc == 7)))
                           for dc in range(8)],
                          reads=(b_wo, b_z), writes=(b_bank[pb],))
                ctx.op("dve",
                       lambda E, fc=fc, pb=pb: E.scalar_tensor_tensor(
                           x1buf[:, fc * 512:(fc + 1) * 512], bank(pb),
                           bo_s[:, fc:fc + 1], xbuf[:, fc * 512:(fc + 1) * 512],
                           ALU.add, ALU.add),
                       reads=(b_bank[pb], b_sc["bo"], b_x), writes=(b_x,))
            x1tiles = [x1buf[:, fc * 512:(fc + 1) * 512] for fc in range(8)]

            def apply1(bcA, bcB, b_bc, tt=tt, x1tiles=x1tiles):
                for fc in range(8):
                    ctx.op("dve",
                           lambda E, fc=fc, bcA=bcA: E.tensor_tensor(
                               x1tiles[fc], x1tiles[fc], bcA, ALU.mult),
                           reads=(b_x, b_bc), writes=(b_x,))
                    ctx.op("dve",
                           lambda E, fc=fc, bcB=bcB: E.tensor_tensor(
                               x1tiles[fc], x1tiles[fc], bcB, ALU.subtract),
                           reads=(b_x, b_bc), writes=(b_x,))
                    ctx.op("dve",
                           lambda E, fc=fc: E.tensor_copy(
                               t1[fc % 2][:, 0:512], x1tiles[fc]),
                           reads=(b_x,), writes=(b_t1[fc % 2],))
                    e_store(ctx, x1dr[fc, :, tsl], x1tiles[fc], b_x,
                            dstbufs=(b_x1d,))
                    e_store(ctx, x1bdr[fc, :, tsl], t1[fc % 2][:, 0:512],
                            b_t1[fc % 2], dstbufs=(b_x1bd,), q="dact")

            ln_block(ctx, scr, x1tiles, [b_x] * 8, [apply1], 512, LN_EPS,
                     ones_col[:], ones_row[:], b_ones, bank(2), bank(3),
                     b_bank[2], b_bank[3])

        ctx.claim(b_xin, [b_z])
        ctx.claim(b_x2, [b_x])
        for th in range(2):
            hsl = slice(th * 1024, (th + 1) * 1024)
            e_load(ctx, xin.rearrange("p (c t) -> p c t", c=8),
                   x1bdr[:, :, hsl].rearrange("c p t -> p c t"), b_xin,
                   srcbufs=(b_x1bd,))
            for ft in range(32):
                wb = ft % 2
                e_load(ctx, wbuf[wb][:, 0:1024].rearrange("p (c f) -> p c f", c=8),
                       w1T[:, ft * 128:(ft + 1) * 128].rearrange(
                           "(c p) f -> p c f", p=128), b_w[wb])
                e_load(ctx, wbuf[wb][:, 1024:2048].rearrange("p (c f) -> p c f", c=8),
                       w2T[:, ft * 128:(ft + 1) * 128].rearrange(
                           "(c p) f -> p c f", p=128), b_w[wb])
                for sub in range(2):
                    ssl = slice(sub * 512, (sub + 1) * 512)
                    ctx.group("pe",
                              [(lambda E, dc=dc, wb=wb, sub=sub: E.matmul(
                                  bank(sub), wbuf[wb][:, dc * 128:(dc + 1) * 128],
                                  xin[:, dc * 1024 + sub * 512: dc * 1024 + (sub + 1) * 512],
                                  start=(dc == 0), stop=(dc == 7)))
                               for dc in range(8)],
                              reads=(b_w[wb], b_xin), writes=(b_bank[sub],))
                    ctx.group("pe",
                              [(lambda E, dc=dc, wb=wb, sub=sub: E.matmul(
                                  bank(2 + sub),
                                  wbuf[wb][:, 1024 + dc * 128: 1024 + (dc + 1) * 128],
                                  xin[:, dc * 1024 + sub * 512: dc * 1024 + (sub + 1) * 512],
                                  start=(dc == 0), stop=(dc == 7)))
                               for dc in range(8)],
                              reads=(b_w[wb], b_xin), writes=(b_bank[2 + sub],))
                    ctx.op("act",
                           lambda E, ft=ft, sub=sub: E.activation(
                               t1[sub][:, 0:512], bank(sub), AF.Silu,
                               scale=s1_s[:, ft:ft + 1]),
                           reads=(b_bank[sub], b_sc["s1"]), writes=(b_t1[sub],))
                    ctx.op("dve",
                           lambda E, ft=ft, sub=sub: E.scalar_tensor_tensor(
                               hA[:, ft * 1024 + sub * 512: ft * 1024 + (sub + 1) * 512],
                               bank(2 + sub), s2_s[:, ft:ft + 1], t1[sub][:, 0:512],
                               ALU.mult, ALU.mult),
                           reads=(b_bank[2 + sub], b_sc["s2"], b_t1[sub]),
                           writes=(b_hA,))
            for (hTd, ssc, bsc, src, bsrc, dst, bdst) in (
                    (h1T, sh1_s, "sh1", hA, b_hA, hB, b_hB),
                    (h2T, sh2_s, "sh2", hB, b_hB, hA, b_hA)):
                for ft in range(32):
                    wb = ft % 2
                    e_load(ctx, wbuf[wb].rearrange("p (c f) -> p c f", c=32),
                           hTd[:, ft * 128:(ft + 1) * 128].rearrange(
                               "(c p) f -> p c f", p=128), b_w[wb])
                    for sub in range(2):
                        pb = 2 * (ft % 2) + sub
                        ctx.group("pe",
                                  [(lambda E, dc=dc, wb=wb, sub=sub, src=src, pb=pb:
                                    E.matmul(
                                        bank(pb), wbuf[wb][:, dc * 128:(dc + 1) * 128],
                                        src[:, dc * 1024 + sub * 512: dc * 1024 + (sub + 1) * 512],
                                        start=(dc == 0), stop=(dc == 31)))
                                   for dc in range(32)],
                                  reads=(b_w[wb], bsrc), writes=(b_bank[pb],))
                        ctx.op("act",
                               lambda E, ft=ft, sub=sub, ssc=ssc, dst=dst, pb=pb:
                               E.activation(
                                   dst[:, ft * 1024 + sub * 512: ft * 1024 + (sub + 1) * 512],
                                   bank(pb), AF.Gelu, scale=ssc[:, ft:ft + 1]),
                               reads=(b_bank[pb], b_sc[bsc]), writes=(bdst,))
            for sub in range(2):
                gsl = slice(th * 1024 + sub * 512, th * 1024 + (sub + 1) * 512)
                for fc in range(8):
                    wb = fc % 2
                    e_load(ctx, wbuf[wb].rearrange("p (c f) -> p c f", c=32),
                           w3T[:, fc * 128:(fc + 1) * 128].rearrange(
                               "(c p) f -> p c f", p=128), b_w[wb])
                    rb = fc % 2
                    e_load(ctx, rbuf[rb][:], x1dr[fc, :, gsl], b_r[rb],
                           srcbufs=(b_x1d,), q="dact")
                    pb = 4 + fc % 2
                    ctx.group("pe",
                              [(lambda E, dc=dc, wb=wb, sub=sub, pb=pb: E.matmul(
                                  bank(pb), wbuf[wb][:, dc * 128:(dc + 1) * 128],
                                  hA[:, dc * 1024 + sub * 512: dc * 1024 + (sub + 1) * 512],
                                  start=(dc == 0), stop=(dc == 31)))
                               for dc in range(32)],
                              reads=(b_w[wb], b_hA), writes=(b_bank[pb],))
                    ctx.op("dve",
                           lambda E, fc=fc, rb=rb, pb=pb: E.scalar_tensor_tensor(
                               x2buf[:, fc * 512:(fc + 1) * 512], bank(pb),
                               s3_s[:, fc:fc + 1], rbuf[rb][:],
                               ALU.mult, ALU.add),
                           reads=(b_bank[pb], b_sc["s3"], b_r[rb]), writes=(b_x2,))
                x2tiles = [x2buf[:, fc * 512:(fc + 1) * 512] for fc in range(8)]

                def apply2(bcA, bcB, b_bc, gsl=gsl, x2tiles=x2tiles):
                    for fc in range(8):
                        ctx.op("dve",
                               lambda E, fc=fc, bcA=bcA: E.tensor_tensor(
                                   x2tiles[fc], x2tiles[fc], bcA, ALU.mult),
                               reads=(b_x2, b_bc), writes=(b_x2,))
                        ctx.op("dve",
                               lambda E, fc=fc, bcB=bcB: E.tensor_tensor(
                                   x2tiles[fc], x2tiles[fc], bcB, ALU.subtract),
                               reads=(b_x2, b_bc), writes=(b_x2,))
                        e_store(ctx, outTr[fc, :, gsl], x2tiles[fc], b_x2,
                                dstbufs=(b_out,))

                ln_block(ctx, scr, x2tiles, [b_x2] * 8, [apply2], 512, LN_EPS,
                         ones_col[:], ones_row[:], b_ones, bank(6), bank(7),
                         b_bank[6], b_bank[7])

        ctx.barrier_all("sp")
        ctx.run()
    return nc


def _prep_launchB(inputs, zT_full):
    x = np.asarray(inputs["L0_attn_x"])
    xTf = _t(x.reshape(16384, 1024))
    zTf = np.concatenate([zT_full[b] for b in range(4)], axis=1)
    woT = _t(np.asarray(inputs["L0_attn_W_o"]))
    bo = np.ascontiguousarray(np.asarray(inputs["L0_attn_b_o"]).reshape(8, 128).T)
    com = {
        "woT": woT, "bo": bo,
        "w1T": _t(inputs["L0_ffn_w1_w"]), "w2T": _t(inputs["L0_ffn_w2_w"]),
        "h1T": _t(inputs["L0_ffn_h1_w"]), "h2T": _t(inputs["L0_ffn_h2_w"]),
        "w3T": _t(inputs["L0_ffn_w3_w"]),
        "s1": np.ascontiguousarray(np.asarray(inputs["L0_ffn_w1_s"]).reshape(32, 128).T),
        "s2": np.ascontiguousarray(np.asarray(inputs["L0_ffn_w2_s"]).reshape(32, 128).T),
        "sh1": np.ascontiguousarray(np.asarray(inputs["L0_ffn_h1_s"]).reshape(32, 128).T),
        "sh2": np.ascontiguousarray(np.asarray(inputs["L0_ffn_h2_s"]).reshape(32, 128).T),
        "s3": np.ascontiguousarray(np.asarray(inputs["L0_ffn_w3_s"]).reshape(8, 128).T),
    }
    maps = []
    for c in range(8):
        tsl = slice(c * TB, (c + 1) * TB)
        m = dict(com)
        m["xT_s"] = np.ascontiguousarray(xTf[:, tsl])
        m["zT_s"] = np.ascontiguousarray(zTf[:, tsl])
        maps.append(m)
    return maps



def build_launchC():
    nc = bass.Bass("TRN2", target_bir_lowering=False)
    xT = nc.dram_tensor("xT", [DM, S], F32, kind="ExternalInput").ap()
    wq = nc.dram_tensor("wq", [DM, HF], F32, kind="ExternalInput").ap()
    wkv = nc.dram_tensor("wkv", [DM, 128], F32, kind="ExternalInput").ap()
    bqm = nc.dram_tensor("bqm", [128, 4], F32, kind="ExternalInput").ap()
    bkv = nc.dram_tensor("bkv", [128, 1], F32, kind="ExternalInput").ap()
    masksQK = nc.dram_tensor("masksQK", [128, 4 * 512], F32, kind="ExternalInput").ap()
    masksKQ = nc.dram_tensor("masksKQ", [128, 4 * 512], F32, kind="ExternalInput").ap()
    ident = nc.dram_tensor("ident", [128, 128], F32, kind="ExternalInput").ap()
    zT_out = nc.dram_tensor("zT", [HF, S], F32, kind="ExternalOutput").ap()
    nb_d = nc.dram_tensor("nb_d", [NH, 32, 128], F32, kind="Internal").ap()

    with ExitStack() as st:
        ctx = Ctx(nc, st)
        al = nc.alloc_sbuf_tensor

        xbc = [al(f"xbc{i}", [128, 512], BF16).ap() for i in range(2)]
        wq_s = al("wq_s", [128, 8 * HF], BF16).ap()
        wkv_s = al("wkv_s", [128, 8 * 128], BF16).ap()
        bq_s = al("bq_s", [128, 4], F32).ap()
        bqsc = al("bqsc", [128, 4], F32).ap()
        bkv_s = al("bkv_s", [128, 1], F32).ap()
        QT = al("QT", [64, NH * S], BF16).ap()
        KTa = al("KTa", [128, S], BF16).ap()
        V = al("V", [128, 32 * DH], BF16).ap()
        bvB = al("bvB", [128, DH], F32).ap()
        QTa = al("QTa", [128, S], BF16).ap()
        mq_s = al("mq_s", [128, NH * 32], F32).ap()
        sq_s = al("sq_s", [128, NH * 32], F32).ap()
        nbc_s = al("nbc_s", [128, 32], F32).ap()
        tp_s = al("tp_s", [32, 128], F32).ap()
        nbrow = al("nbrow", [1, S], F32).ap()
        nbrow2 = al("nbrow2", [1, S], F32).ap()
        nbbf = al("nbbf", [1, S], BF16).ap()
        corr = al("corr", [1, S], BF16).ap()
        corrB = al("corrB", [128, 512], F32).ap()
        mask_qk = al("mask_qk", [128, 4 * 512], F32).ap()
        mask_kq = al("mask_kq", [128, 4 * 512], F32).ap()
        id_s = al("id_s", [128, 128], F32).ap()
        alpha = al("alpha", [128, 512], BF16).ap()
        expsc = al("expsc", [128, 1024], BF16).ap()
        scr1 = al("scr1", [128, 1], F32).ap()
        zseg = [al(f"zseg{i}", [64, 512], F32).ap() for i in range(2)]
        ones_row = al("ones_row", [1, 128], BF16).ap()

        PS = nc.alloc_psum_tensor("PS", [128, 4096], F32).ap()

        def bank(i, rows=128):
            return PS[0:rows, i * 512:(i + 1) * 512]

        b_xbc = [ctx.buf("xbc0"), ctx.buf("xbc1")]
        b_wq = ctx.buf(); b_wkv = ctx.buf()
        b_bq = ctx.buf(); b_bqsc = ctx.buf(); b_bkv = ctx.buf()
        b_QT = ctx.buf(); b_KTa = ctx.buf(); b_V = ctx.buf(); b_bvB = ctx.buf()
        b_QTa = ctx.buf()
        b_mq = ctx.buf(); b_sq = ctx.buf(); b_nbc = ctx.buf(); b_tp = ctx.buf()
        b_rows = ctx.buf(); b_nbbf = ctx.buf(); b_corr = ctx.buf(); b_corrB = ctx.buf()
        b_mm = {"qk": ctx.buf(), "kq": ctx.buf()}
        b_id = ctx.buf()
        b_alpha = ctx.buf(); b_expsc = ctx.buf(); b_scr1 = ctx.buf()
        b_zseg = [ctx.buf(), ctx.buf()]
        b_zout = ctx.buf("zout")
        b_nbd = ctx.buf("nb_dram")
        b_ones = ctx.buf()
        b_bank = [ctx.buf(f"bank{i}") for i in range(8)]

        ctx.op("dve", lambda E: E.memset(ones_row[:], 1.0), writes=(b_ones,))
        e_load(ctx, mask_qk[:], masksQK, b_mm["qk"])
        e_load(ctx, mask_kq[:], masksKQ, b_mm["kq"])
        e_load(ctx, id_s[:], ident, b_id)
        e_load(ctx, bq_s[:], bqm, b_bq)
        e_load(ctx, bkv_s[:], bkv, b_bkv)
        e_load(ctx, wq_s.rearrange("p (c f) -> p c f", c=8),
               wq.rearrange("(c p) f -> p c f", p=128), b_wq)
        e_load(ctx, wkv_s.rearrange("p (c f) -> p c f", c=8),
               wkv.rearrange("(c p) f -> p c f", p=128), b_wkv)
        ctx.op("dve", lambda E: E.tensor_scalar(bqsc[:], bq_s[:], 0.125, None,
                                                ALU.mult),
               reads=(b_bq,), writes=(b_bqsc,))
        ctx.op("dve", lambda E: E.memset(KTa[64:65, :], 1.0), writes=(b_KTa,))
        bvrow = al("bvrow", [1, DH], BF16).ap()
        b_bvrow = ctx.buf()
        e_load(ctx, bvrow, bkv[64:128, :].rearrange("p one -> one p"), b_bvrow)
        ctx.group("pe", [lambda E: E.matmul(PS[:, 0:DH], ones_row[:], bvrow,
                                            start=True, stop=True)],
                  reads=(b_bvrow, b_ones), writes=(b_bank[0],))
        ctx.op("dve", lambda E: E.tensor_copy(bvB[:], PS[:, 0:DH]),
               reads=(b_bank[0],), writes=(b_bvB,))

        xTr = xT.rearrange("(c p) t -> c p t", p=128)

        for tt in range(8):
            for dc in range(8):
                xb = b_xbc[dc % 2]
                e_load(ctx, xbc[dc % 2][:], xTr[dc][:, tt * 512:(tt + 1) * 512], xb)
                for fc in range(4):
                    ctx.group("pe",
                              [(lambda E, fc=fc, dc=dc: E.matmul(
                                  bank(fc), wq_s[:, dc * HF + fc * 128: dc * HF + fc * 128 + 128],
                                  xbc[dc % 2][:],
                                  start=(dc == 0), stop=(dc == 7)))],
                              reads=(b_wq, xb), writes=(b_bank[fc],))
                ctx.group("pe",
                          [(lambda E, dc=dc: E.matmul(
                              bank(4), wkv_s[:, dc * 128:(dc + 1) * 128],
                              xbc[dc % 2][:],
                              start=(dc == 0), stop=(dc == 7)))],
                          reads=(b_wkv, xb), writes=(b_bank[4],))
            for fc in range(4):
                for half in range(2):
                    hh_ = 2 * fc + half
                    p0_ = half * 64
                    ctx.op("act",
                           lambda E, fc=fc, tt=tt, hh_=hh_, p0_=p0_: E.activation(
                               QT[0:64, hh_ * S + tt * 512: hh_ * S + (tt + 1) * 512],
                               bank(fc)[p0_:p0_ + 64, :], AF.Identity, scale=0.125,
                               bias=bqsc[p0_:p0_ + 64, fc:fc + 1]),
                           reads=(b_bank[fc], b_bqsc), writes=(b_QT,))
            ctx.op("act",
                   lambda E, tt=tt: E.activation(
                       KTa[0:64, tt * 512:(tt + 1) * 512], bank(4)[0:64, :],
                       AF.Identity, bias=bkv_s[0:64, 0:1]),
                   reads=(b_bank[4], b_bkv), writes=(b_KTa,))

        for tt in range(8):
            for dc in range(8):
                xb = b_xbc[dc % 2]
                e_load(ctx, xbc[dc % 2][:], xTr[dc][:, tt * 512:(tt + 1) * 512], xb)
                for sub in range(4):
                    ctx.group("pe",
                              [(lambda E, dc=dc, sub=sub: E.matmul(
                                  PS[:, sub * 512: sub * 512 + DH],
                                  xbc[dc % 2][:, sub * 128:(sub + 1) * 128],
                                  wkv_s[:, dc * 128 + 64:(dc + 1) * 128],
                                  start=(dc == 0), stop=(dc == 7)))],
                              reads=(b_wkv, xb), writes=(b_bank[sub],))
            for sub in range(4):
                ti = tt * 4 + sub
                ctx.op("dve",
                       lambda E, ti=ti, sub=sub: E.tensor_tensor(
                           V[:, ti * DH:(ti + 1) * DH],
                           PS[:, sub * 512: sub * 512 + DH],
                           bvB[:], ALU.add),
                       reads=(b_bank[sub], b_bvB), writes=(b_V,))

        def head_slices(h):
            return h // 2, (h % 2) * 64

        for h in range(NH):
            fc, p0 = head_slices(h)
            for qt in range(32):
                col = h * 32 + qt
                kts = [kt for kt in range(8) if kt * 512 <= qt * 128]
                first = True
                for kt in kts:
                    pb = kt % 2
                    ctx.group("pe",
                              [lambda E, h=h, qt=qt, kt=kt, pb=pb:
                               E.matmul(bank(pb),
                                        QT[0:64, h * S + qt * 128: h * S + qt * 128 + 128],
                                        KTa[0:64, kt * 512:(kt + 1) * 512],
                                        start=True, stop=True)],
                              reads=(b_QT, b_KTa), writes=(b_bank[pb],))
                    Dp_ = kt * 512 - qt * 128
                    if -384 <= Dp_ <= 0:
                        mi = (Dp_ + 384) // 128
                        ctx.op("dve",
                               lambda E, mi=mi, pb=pb: E.tensor_tensor(
                                   bank(pb), bank(pb),
                                   mask_qk[:, mi * 512:(mi + 1) * 512], ALU.add),
                               reads=(b_bank[pb], b_mm["qk"]), writes=(b_bank[pb],))
                    ctx.op("dve",
                           lambda E, pb=pb: E.reduce_max(scr1[:], bank(pb), axis=AX.X),
                           reads=(b_bank[pb],), writes=(b_scr1,))
                    if first:
                        ctx.op("dve",
                               lambda E, col=col: E.tensor_copy(mq_s[:, col:col + 1],
                                                                scr1[:]),
                               reads=(b_scr1,), writes=(b_mq,))
                        first = False
                    else:
                        ctx.op("dve",
                               lambda E, col=col: E.tensor_tensor(
                                   mq_s[:, col:col + 1], mq_s[:, col:col + 1],
                                   scr1[:], ALU.max),
                               reads=(b_scr1, b_mq), writes=(b_mq,))
            for qt in range(32):
                col = h * 32 + qt
                kts = [kt for kt in range(8) if kt * 512 <= qt * 128]
                ctx.op("dve",
                       lambda E, col=col: E.tensor_scalar(
                           nbc_s[:, qt:qt + 1] if False else nbc_s[:, col % 32: col % 32 + 1],
                           mq_s[:, col:col + 1], -1.0, None, ALU.mult),
                       reads=(b_mq,), writes=(b_nbc,))
                first = True
                for kt in kts:
                    pb = kt % 2
                    ctx.group("pe",
                              [lambda E, h=h, qt=qt, kt=kt, pb=pb:
                               E.matmul(bank(pb),
                                        QT[0:64, h * S + qt * 128: h * S + qt * 128 + 128],
                                        KTa[0:64, kt * 512:(kt + 1) * 512],
                                        start=True, stop=True)],
                              reads=(b_QT, b_KTa), writes=(b_bank[pb],))
                    Dp_ = kt * 512 - qt * 128
                    if -384 <= Dp_ <= 0:
                        mi = (Dp_ + 384) // 128
                        ctx.op("dve",
                               lambda E, mi=mi, pb=pb: E.tensor_tensor(
                                   bank(pb), bank(pb),
                                   mask_qk[:, mi * 512:(mi + 1) * 512], ALU.add),
                               reads=(b_bank[pb], b_mm["qk"]), writes=(b_bank[pb],))
                    ctx.op("act",
                           lambda E, pb=pb, col=col: E.activation(
                               expsc[:, 0:512], bank(pb), AF.Exp,
                               bias=nbc_s[:, col % 32: col % 32 + 1],
                               accum_out=scr1[:]),
                           reads=(b_bank[pb], b_nbc), writes=(b_expsc, b_scr1))
                    if first:
                        ctx.op("dve",
                               lambda E, col=col: E.tensor_copy(sq_s[:, col:col + 1],
                                                                scr1[:]),
                               reads=(b_scr1,), writes=(b_sq,))
                        first = False
                    else:
                        ctx.op("dve",
                               lambda E, col=col: E.tensor_tensor(
                                   sq_s[:, col:col + 1], sq_s[:, col:col + 1],
                                   scr1[:], ALU.add),
                               reads=(b_scr1, b_sq), writes=(b_sq,))
                ctx.op("act",
                       lambda E, col=col: E.activation(scr1[:], sq_s[:, col:col + 1],
                                                       AF.Ln),
                       reads=(b_sq,), writes=(b_scr1,))
                ctx.op("dve",
                       lambda E, col=col: E.tensor_tensor(scr1[:], scr1[:],
                                                          mq_s[:, col:col + 1],
                                                          ALU.add),
                       reads=(b_scr1, b_mq), writes=(b_scr1,))
                ctx.op("dve",
                       lambda E, col=col: E.tensor_scalar(
                           nbc_s[:, col % 32: col % 32 + 1], scr1[:], -1.0, None,
                           ALU.mult),
                       reads=(b_scr1,), writes=(b_nbc,))
            ctx.group("pe", [lambda E, h=h: E.transpose(PS[0:32, 0:128].bitcast(F32),
                                                        nbc_s[:, 0:32], id_s[:])],
                      reads=(b_nbc, b_id), writes=(b_bank[0],))
            ctx.op("dve", lambda E: E.tensor_copy(tp_s[:], PS[0:32, 0:128]),
                   reads=(b_bank[0],), writes=(b_tp,))
            e_store(ctx, nb_d[h], tp_s[:], b_tp, dstbufs=(b_nbd,))
            e_load(ctx, nbrow[:], nb_d[h].rearrange("a b -> (a b)").unsqueeze(0),
                   b_rows, srcbufs=(b_nbd,))
            ctx.op("dve", lambda E: E.tensor_copy(nbbf[:], nbrow[:]),
                   reads=(b_rows,), writes=(b_nbbf,))
            ctx.op("dve", lambda E: E.tensor_copy(nbrow2[:], nbbf[:]),
                   reads=(b_nbbf,), writes=(b_rows,))
            ctx.op("dve", lambda E: E.tensor_tensor(nbrow2[:], nbrow[:],
                                                    nbrow2[:], ALU.subtract),
                   reads=(b_rows,), writes=(b_rows,))
            ctx.op("act", lambda E: E.activation(corr[:], nbrow2[:], AF.Exp),
                   reads=(b_rows,), writes=(b_corr,))
            ctx.op("dve",
                   lambda E, h=h: E.tensor_copy(
                       QTa[0:64, :], QT[0:64, h * S:(h + 1) * S]),
                   reads=(b_QT,), writes=(b_QTa,))
            ctx.op("dve", lambda E: E.tensor_copy(QTa[64:65, :], nbbf[:]),
                   reads=(b_nbbf,), writes=(b_QTa,))
            for qt5 in range(8):
                zb = 2 + qt5 % 2
                kts = [kt for kt in range(32) if kt * 128 <= qt5 * 512 + 511]
                nlive = len(kts)
                for j, kt in enumerate(kts):
                    ctx.group("pe",
                              [lambda E, kt=kt, qt5=qt5: E.matmul(
                                  bank(0), KTa[0:65, kt * 128: kt * 128 + 128],
                                  QTa[0:65, qt5 * 512:(qt5 + 1) * 512],
                                  start=True, stop=True)],
                              reads=(b_KTa, b_QTa), writes=(b_bank[0],))
                    Dv = kt * 128 - qt5 * 512
                    if 0 <= Dv <= 384:
                        mi = Dv // 128
                        ctx.op("dve",
                               lambda E, mi=mi: E.tensor_tensor(
                                   bank(0), bank(0),
                                   mask_kq[:, mi * 512:(mi + 1) * 512], ALU.add),
                               reads=(b_bank[0], b_mm["kq"]), writes=(b_bank[0],))
                    ctx.op("act",
                           lambda E: E.activation(alpha[:], bank(0), AF.Exp),
                           reads=(b_bank[0],), writes=(b_alpha,))
                    ctx.group("pe",
                              [lambda E, kt=kt, j=j, nlive=nlive, zb=zb: E.matmul(
                                  bank(zb, 64), V[:, kt * DH:(kt + 1) * DH],
                                  alpha[:],
                                  start=(j == 0), stop=(j == nlive - 1))],
                              reads=(b_V, b_alpha), writes=(b_bank[zb],))
                ctx.group("pe",
                          [lambda E, qt5=qt5: E.matmul(
                              bank(1), ones_row[:],
                              corr[:, qt5 * 512:(qt5 + 1) * 512],
                              start=True, stop=True)],
                          reads=(b_corr, b_ones), writes=(b_bank[1],))
                ctx.op("dve", lambda E: E.tensor_copy(corrB[:], bank(1)),
                       reads=(b_bank[1],), writes=(b_corrB,))
                sb = qt5 % 2
                ctx.op("dve",
                       lambda E, zb=zb, sb=sb: E.tensor_tensor(
                           zseg[sb], bank(zb, 64), corrB[0:64, :], ALU.mult),
                       reads=(b_bank[zb], b_corrB), writes=(b_zseg[sb],))
                e_store(ctx, zT_out[h * 64:(h + 1) * 64, qt5 * 512:(qt5 + 1) * 512],
                        zseg[sb], b_zseg[sb], dstbufs=(b_zout,))

        ctx.barrier_all("sp")
        ctx.run()
    return nc


def _mk_masks_qk():
    out = np.zeros((128, 4 * 512), dtype=np.float32)
    qr = np.arange(128)[:, None]
    kc = np.arange(512)[None, :]
    for i, Dp_ in enumerate((-384, -256, -128, 0)):
        out[:, i * 512:(i + 1) * 512] = np.where(kc + Dp_ > qr, NEG, 0.0)
    return out


def _prep_launchC(inputs, xT_full):
    wqkv = np.asarray(inputs["L1_attn_wqkv"])
    bqkv = np.asarray(inputs["L1_attn_bqkv"])
    mkq = _mk_masks_kq()
    mqk = _mk_masks_qk()
    idm = np.eye(128, dtype=np.float32)
    maps = []
    for c in range(8):
        b, hh = c // 2, c % 2
        wq_ = wqkv[:, hh * 512:(hh + 1) * 512]
        bq_ = bqkv[hh * 512:(hh + 1) * 512]
        wkv_ = wqkv[:, 1024:1152]
        bkv_ = bqkv[1024:1152]
        maps.append({
            "xT": np.ascontiguousarray(xT_full[b]),
            "wq": np.ascontiguousarray(wq_),
            "wkv": np.ascontiguousarray(wkv_),
            "bqm": np.ascontiguousarray(bq_.reshape(4, 128).T),
            "bkv": bkv_.reshape(128, 1).copy(),
            "masksQK": mqk, "masksKQ": mkq, "ident": idm,
        })
    return maps



def build_launchD():
    nc = bass.Bass("TRN2", target_bir_lowering=False)
    xT = nc.dram_tensor("xT", [DM, S], F32, kind="ExternalInput").ap()
    zTf = nc.dram_tensor("zTf", [DM, S], F32, kind="ExternalInput").ap()
    wao = nc.dram_tensor("wao", [DM, DM], F32, kind="ExternalInput").ap()
    bao = nc.dram_tensor("bao", [128, 8], F32, kind="ExternalInput").ap()
    w_u = nc.dram_tensor("w_u", [DM, DM], F32, kind="ExternalInput").ap()
    w_r = nc.dram_tensor("w_r", [DM, DM], F32, kind="ExternalInput").ap()
    convw = nc.dram_tensor("convw", [128, 32], F32, kind="ExternalInput").ap()
    convb = nc.dram_tensor("convb", [128, 8], F32, kind="ExternalInput").ap()
    wxp = nc.dram_tensor("wxp", [DM, 96], F32, kind="ExternalInput").ap()
    x1T_o = nc.dram_tensor("x1T", [DM, S], F32, kind="ExternalOutput").ap()
    uT_o = nc.dram_tensor("uT", [DM, S], BF16, kind="ExternalOutput").ap()
    svT_o = nc.dram_tensor("svT", [DM, S], BF16, kind="ExternalOutput").ap()
    xdbl_o = nc.dram_tensor("xdbl_p", [96, S], F32, kind="ExternalOutput").ap()

    with ExitStack() as st:
        ctx = Ctx(nc, st)
        al = nc.alloc_sbuf_tensor

        zb = al("zb", [128, 4096], BF16).ap()
        xf = al("xf", [128, 4096], F32).ap()
        x1b = al("x1b", [128, 8 * S], BF16).ap()
        wao_s = al("wao_s", [128, 8192], BF16).ap()
        wu_s = al("wu_s", [128, 8192], BF16).ap()
        wr_s = al("wr_s", [128, 8192], BF16).ap()
        wxp_s = al("wxp_s", [128, 8 * 96], BF16).ap()
        bao_s = al("bao_s", [128, 8], F32).ap()
        cw_s = al("cw_s", [128, 32], F32).ap()
        cb_s = al("cb_s", [128, 8], F32).ap()
        u0c = al("u0c", [128, 4100], F32).ap()
        cv = al("cv", [128, 4096], F32).ap()
        uc = al("uc", [128, 4096], BF16).ap()
        ub = [al(f"ub{i}", [128, 512], BF16).ap() for i in range(2)]
        t96 = al("t96", [96, 512], F32).ap()

        PS = nc.alloc_psum_tensor("PS", [128, 4096], F32).ap()

        def bank(i, rows=128):
            return PS[0:rows, i * 512:(i + 1) * 512]

        b_zb = ctx.buf("zb"); b_xf = ctx.buf("xf"); b_x1b = ctx.buf("x1b")
        b_wao = ctx.buf(); b_wu = ctx.buf(); b_wr = ctx.buf(); b_wxp = ctx.buf()
        b_bao = ctx.buf(); b_cw = ctx.buf(); b_cb = ctx.buf()
        b_u0 = ctx.buf("u0"); b_cv = ctx.buf("cv"); b_uc = ctx.buf("uc")
        b_ub = [ctx.buf("ub0"), ctx.buf("ub1")]
        b_t96 = ctx.buf()
        b_bank = [ctx.buf(f"bank{i}") for i in range(8)]
        b_x1o = ctx.buf("x1o"); b_uo = ctx.buf("uo"); b_svo = ctx.buf("svo")
        b_xdo = ctx.buf("xdo")

        e_load(ctx, bao_s[:], bao, b_bao)
        e_load(ctx, cw_s[:], convw, b_cw)
        e_load(ctx, cb_s[:], convb, b_cb)
        e_load(ctx, wao_s.rearrange("p (c f) -> p c f", c=8),
               wao.rearrange("(c p) f -> p c f", p=128), b_wao)
        e_load(ctx, wu_s.rearrange("p (c f) -> p c f", c=8),
               w_u.rearrange("(c p) f -> p c f", p=128), b_wu)
        e_load(ctx, wr_s.rearrange("p (c f) -> p c f", c=8),
               w_r.rearrange("(c p) f -> p c f", p=128), b_wr)
        e_load(ctx, wxp_s.rearrange("p (c f) -> p c f", c=8),
               wxp.rearrange("(c p) f -> p c f", p=128), b_wxp)

        xTr = xT.rearrange("(c p) t -> c p t", p=128)
        zTr = zTf.rearrange("(c p) t -> c p t", p=128)
        x1Tr = x1T_o.rearrange("(c p) t -> c p t", p=128)
        uTr = uT_o.rearrange("(c p) t -> c p t", p=128)
        svTr = svT_o.rearrange("(c p) t -> c p t", p=128)

        for tt in range(8):
            tsl = slice(tt * 512, (tt + 1) * 512)
            e_load(ctx, zb.rearrange("p (c t) -> p c t", c=8),
                   zTr[:, :, tsl].rearrange("c p t -> p c t"), b_zb)
            e_load(ctx, xf.rearrange("p (c t) -> p c t", c=8),
                   xTr[:, :, tsl].rearrange("c p t -> p c t"), b_xf)
            for fc in range(8):
                pb = fc % 2
                ctx.group("pe",
                          [(lambda E, dc=dc, fc=fc, pb=pb: E.matmul(
                              bank(pb), wao_s[:, dc * 1024 + fc * 128: dc * 1024 + fc * 128 + 128],
                              zb[:, dc * 512:(dc + 1) * 512],
                              start=(dc == 0), stop=(dc == 7)))
                           for dc in range(8)],
                          reads=(b_wao, b_zb), writes=(b_bank[pb],))
                ctx.op("dve",
                       lambda E, fc=fc, pb=pb: E.scalar_tensor_tensor(
                           xf[:, fc * 512:(fc + 1) * 512], bank(pb),
                           bao_s[:, fc:fc + 1], xf[:, fc * 512:(fc + 1) * 512],
                           ALU.add, ALU.add),
                       reads=(b_bank[pb], b_bao, b_xf), writes=(b_xf,))
                ctx.op("dve",
                       lambda E, fc=fc, tt=tt: E.tensor_copy(
                           x1b[:, fc * S + tt * 512: fc * S + (tt + 1) * 512],
                           xf[:, fc * 512:(fc + 1) * 512]),
                       reads=(b_xf,), writes=(b_x1b,))
                e_store(ctx, x1Tr[fc, :, tsl], xf[:, fc * 512:(fc + 1) * 512],
                        b_xf, dstbufs=(b_x1o,))

        for g in range(8):
            ctx.op("dve", lambda E: E.memset(u0c[:, 0:3], 0.0), writes=(b_u0,))
            for tt in range(8):
                pb = tt % 2
                ctx.group("pe",
                          [(lambda E, dc=dc, g=g, tt=tt, pb=pb: E.matmul(
                              bank(pb), wu_s[:, dc * 1024 + g * 128: dc * 1024 + g * 128 + 128],
                              x1b[:, dc * S + tt * 512: dc * S + (tt + 1) * 512],
                              start=(dc == 0), stop=(dc == 7)))
                           for dc in range(8)],
                          reads=(b_wu, b_x1b), writes=(b_bank[pb],))
                ctx.op("dve",
                       lambda E, tt=tt, pb=pb: E.tensor_copy(
                           u0c[:, 3 + tt * 512: 3 + (tt + 1) * 512], bank(pb)),
                       reads=(b_bank[pb],), writes=(b_u0,))
            ctx.op("dve",
                   lambda E, g=g: E.tensor_scalar(
                       cv[:], u0c[:, 0:4096], cw_s[:, g * 4: g * 4 + 1], None,
                       ALU.mult),
                   reads=(b_u0, b_cw), writes=(b_cv,))
            for j in range(1, 4):
                ctx.op("dve",
                       lambda E, g=g, j=j: E.scalar_tensor_tensor(
                           cv[:], u0c[:, j: j + 4096], cw_s[:, g * 4 + j: g * 4 + j + 1],
                           cv[:], ALU.mult, ALU.add),
                       reads=(b_u0, b_cw, b_cv), writes=(b_cv,))
            ctx.op("act",
                   lambda E, g=g: E.activation(uc[:], cv[:], AF.Silu,
                                               bias=cb_s[:, g:g + 1]),
                   reads=(b_cv, b_cb), writes=(b_uc,))
            e_store(ctx, uTr[g], uc[:], b_uc, dstbufs=(b_uo,))

        for tt in range(8):
            tsl = slice(tt * 512, (tt + 1) * 512)
            for g in range(8):
                e_load(ctx, ub[g % 2][:], uTr[g][:, tsl], b_ub[g % 2],
                       srcbufs=(b_uo,))
                ctx.group("pe",
                          [(lambda E, g=g: E.matmul(
                              PS[0:96, 0:512], wxp_s[:, g * 96:(g + 1) * 96],
                              ub[g % 2][:],
                              start=(g == 0), stop=(g == 7)))],
                          reads=(b_wxp, b_ub[g % 2]), writes=(b_bank[0],))
            ctx.op("dve", lambda E: E.tensor_copy(t96[:], PS[0:96, 0:512]),
                   reads=(b_bank[0],), writes=(b_t96,))
            e_store(ctx, xdbl_o[:, tsl], t96[:], b_t96, dstbufs=(b_xdo,))

        for g in range(8):
            for tt in range(8):
                pb = tt % 2
                ctx.group("pe",
                          [(lambda E, dc=dc, g=g, tt=tt, pb=pb: E.matmul(
                              bank(pb), wr_s[:, dc * 1024 + g * 128: dc * 1024 + g * 128 + 128],
                              x1b[:, dc * S + tt * 512: dc * S + (tt + 1) * 512],
                              start=(dc == 0), stop=(dc == 7)))
                           for dc in range(8)],
                          reads=(b_wr, b_x1b), writes=(b_bank[pb],))
                ctx.op("act",
                       lambda E, tt=tt, pb=pb: E.activation(
                           uc[:, tt * 512:(tt + 1) * 512], bank(pb), AF.Silu),
                       reads=(b_bank[pb],), writes=(b_uc,))
            e_store(ctx, svTr[g], uc[:], b_uc, dstbufs=(b_svo,))

        ctx.barrier_all("sp")
        ctx.run()
    return nc


def _prep_launchD(inputs, xT_full, zT_full):
    wao = np.asarray(inputs["L1_attn_w_attn_out"])
    bao = np.asarray(inputs["L1_attn_b_attn_out"])
    w_in = np.asarray(inputs["L1_attn_w_in"])
    cw = np.asarray(inputs["L1_attn_conv_w"])
    cb = np.asarray(inputs["L1_attn_conv_b"])
    wx = np.asarray(inputs["L1_attn_w_xproj"])
    maps = []
    for c in range(8):
        b, hh = c // 2, c % 2
        csl = slice(hh * 1024, (hh + 1) * 1024)
        maps.append({
            "xT": np.ascontiguousarray(xT_full[b]),
            "zTf": np.ascontiguousarray(zT_full[b]),
            "wao": np.ascontiguousarray(wao),
            "bao": np.ascontiguousarray(bao.reshape(8, 128).T),
            "w_u": np.ascontiguousarray(w_in[:, csl]),
            "w_r": np.ascontiguousarray(w_in[:, 2048 + hh * 1024: 2048 + (hh + 1) * 1024]),
            "convw": np.ascontiguousarray(cw[csl].reshape(8, 128, 4).transpose(1, 0, 2).reshape(128, 32)),
            "convb": np.ascontiguousarray(cb[csl].reshape(8, 128).T),
            "wxp": np.ascontiguousarray(wx[csl]),
        })
    return maps



DTR = 64
NST = 16


def build_launchE():
    nc = bass.Bass("TRN2", target_bir_lowering=False)
    uT = nc.dram_tensor("uT", [DM, S], BF16, kind="ExternalInput").ap()
    svT = nc.dram_tensor("svT", [DM, S], BF16, kind="ExternalInput").ap()
    xdA = nc.dram_tensor("xdA", [96, S], F32, kind="ExternalInput").ap()
    xdB = nc.dram_tensor("xdB", [96, S], F32, kind="ExternalInput").ap()
    wdt = nc.dram_tensor("wdt", [DTR, DM], F32, kind="ExternalInput").ap()
    bdt = nc.dram_tensor("bdt", [128, 8], F32, kind="ExternalInput").ap()
    Alog = nc.dram_tensor("Alog", [128, 8 * NST], F32, kind="ExternalInput").ap()
    Dp = nc.dram_tensor("Dp", [128, 8], F32, kind="ExternalInput").ap()
    wout = nc.dram_tensor("wout", [DM, DM], F32, kind="ExternalInput").ap()
    pout_o = nc.dram_tensor("poutT", [DM, S], F32, kind="ExternalOutput").ap()
    y_d = nc.dram_tensor("y_d", [DM, S], BF16, kind="Internal").ap()
    xbc_d = nc.dram_tensor("xbc_d", [2 * NST, S], F32, kind="Internal").ap()

    with ExitStack() as st:
        ctx = Ctx(nc, st)
        al = nc.alloc_sbuf_tensor

        xd1b = al("xd1b", [64, 4096], BF16).ap()
        wdt_s = al("wdt_s", [64, 1024], BF16).ap()
        bdt_s = al("bdt_s", [128, 8], F32).ap()
        A_s = al("A_s", [128, 8 * NST], F32).ap()
        Dp_s = al("Dp_s", [128, 8], F32).ap()
        wout_s = al("wout_s", [128, 8192], BF16).ap()
        delta = al("delta", [128, 8 * S], BF16).ap()
        xd2 = delta[0:32, 0:8192].bitcast(F32)
        du = al("du", [128, 4096], BF16).ap()
        ug = al("ug", [128, 4096], BF16).ap()
        dA = al("dA", [128, 4096], BF16).ap()
        dBu = al("dBu", [128, 4096], BF16).ap()
        hs_ = al("hs", [128, 4096], BF16).ap()
        Bb = al("Bb", [128, 4096], BF16).ap()
        Cb = al("Cb", [128, 4096], BF16).ap()
        yg = al("yg", [128, 4096], F32).ap()
        xd1 = yg[0:64, :]
        tmp = al("tmpv", [128, 4096], BF16).ap()
        sp1 = al("sp1", [128, 4096], F32).ap()
        ytile = [sp1[:, 0:256].bitcast(BF16), sp1[:, 256:512].bitcast(BF16)]
        ptile = [sp1[:, 512:1024], sp1[:, 1024:1536]]
        ones_row = al("ones_row", [1, 128], F32).ap()

        PS = nc.alloc_psum_tensor("PS", [128, 4096], F32).ap()

        def bank(i, rows=128):
            return PS[0:rows, i * 512:(i + 1) * 512]

        b_xd1 = ctx.buf(); b_xd2 = ctx.buf(); b_xd1b = ctx.buf()
        b_wdt = ctx.buf(); b_bdt = ctx.buf(); b_A = ctx.buf(); b_Dp = ctx.buf()
        b_wout = ctx.buf()
        b_delta = ctx.buf("delta"); b_du = ctx.buf("du"); b_ug = ctx.buf("ug")
        b_dA = ctx.buf("dA"); b_dBu = ctx.buf("dBu"); b_h = ctx.buf("h")
        b_Bb = ctx.buf(); b_Cb = ctx.buf()
        b_yg = ctx.buf("yg"); b_tmp = ctx.buf("tmp")
        b_yt = [ctx.buf(), ctx.buf()]
        b_pt = [ctx.buf(), ctx.buf()]
        b_sp = ctx.buf()
        b_ones = ctx.buf()
        b_bank = [ctx.buf(f"bank{i}") for i in range(8)]
        b_yd = ctx.buf("y_dram"); b_xbcd = ctx.buf("xbc_dram")
        b_po = ctx.buf("pod")

        ctx.op("dve", lambda E: E.memset(ones_row[:], 1.0), writes=(b_ones,))
        e_load(ctx, bdt_s[:], bdt, b_bdt)
        e_load(ctx, Dp_s[:], Dp, b_Dp)
        e_load(ctx, wdt_s[:], wdt, b_wdt)
        e_load(ctx, wout_s.rearrange("p (c f) -> p c f", c=8),
               wout.rearrange("(c p) f -> p c f", p=128), b_wout)
        e_load(ctx, A_s[:], Alog, b_A)
        ctx.op("act", lambda E: E.activation(A_s[:], A_s[:], AF.Exp),
               reads=(b_A,), writes=(b_A,))
        ctx.op("dve", lambda E: E.tensor_scalar(A_s[:], A_s[:], -1.0, None,
                                                ALU.mult),
               reads=(b_A,), writes=(b_A,))
        e_load(ctx, xd1[:], xdA[0:64, :], b_xd1)
        e_load(ctx, xd2[:], xdA[64:96, :], b_xd2)
        e_load(ctx, sp1[0:64, :], xdB[0:64, :], b_sp)
        ctx.op("dve", lambda E: E.tensor_tensor(xd1[:], xd1[:], sp1[0:64, :],
                                                ALU.add),
               reads=(b_xd1, b_sp), writes=(b_xd1,))
        e_load(ctx, sp1[0:32, :], xdB[64:96, :], b_sp)
        ctx.op("dve", lambda E: E.tensor_tensor(xd2[:], xd2[:], sp1[0:32, :],
                                                ALU.add),
               reads=(b_xd2, b_sp), writes=(b_xd2,))
        ctx.op("dve", lambda E: E.tensor_copy(xd1b[:], xd1[:]),
               reads=(b_xd1,), writes=(b_xd1b,))
        e_store(ctx, xbc_d[:], xd2[:], b_xd2, dstbufs=(b_xbcd,))

        ctx.claim(b_delta, [b_xd2])
        ctx.claim(b_yg, [b_xd1])
        for g in range(8):
            for tt in range(8):
                pb = tt % 2
                ctx.group("pe",
                          [(lambda E, g=g, tt=tt, pb=pb: E.matmul(
                              bank(pb), wdt_s[:, g * 128:(g + 1) * 128],
                              xd1b[:, tt * 512:(tt + 1) * 512],
                              start=True, stop=True))],
                          reads=(b_wdt, b_xd1b), writes=(b_bank[pb],))
                ctx.op("dve",
                       lambda E, g=g, tt=tt, pb=pb: E.tensor_scalar(
                           sp1[:, tt * 512:(tt + 1) * 512], bank(pb),
                           bdt_s[:, g:g + 1], None, ALU.add),
                       reads=(b_bank[pb], b_bdt), writes=(b_sp,))
            ctx.op("act",
                   lambda E: E.activation(tmp[:], sp1[:], AF.Abs),
                   reads=(b_sp,), writes=(b_tmp,))
            ctx.op("act",
                   lambda E: E.activation(tmp[:], tmp[:], AF.Exp, scale=-1.0),
                   reads=(b_tmp,), writes=(b_tmp,))
            c_one = ctx.constcol(1.0)
            ctx.op("act",
                   lambda E: E.activation(tmp[:], tmp[:], AF.Ln,
                                          bias=c_one),
                   reads=(b_tmp, ctx._cbuf), writes=(b_tmp,))
            ctx.op("dve",
                   lambda E: E.tensor_scalar(sp1[:], sp1[:], 0.0, None, ALU.max),
                   reads=(b_sp,), writes=(b_sp,))
            ctx.op("dve",
                   lambda E, g=g: E.tensor_tensor(
                       delta[:, g * S:(g + 1) * S], sp1[:], tmp[:], ALU.add),
                   reads=(b_sp, b_tmp), writes=(b_delta,))

        uTr = uT.rearrange("(c p) t -> c p t", p=128)
        svTr = svT.rearrange("(c p) t -> c p t", p=128)
        ydr = y_d.rearrange("(c p) t -> c p t", p=128)
        for g in range(8):
            e_load(ctx, ug[:], uTr[g], b_ug)
            ctx.op("dve",
                   lambda E, g=g: E.tensor_tensor(du[:], delta[:, g * S:(g + 1) * S],
                                                  ug[:], ALU.mult),
                   reads=(b_delta, b_ug), writes=(b_du,))
            for n in range(NST):
                e_load(ctx, Bb[:], xbc_d[n:n + 1, :].to_broadcast([128, S]),
                       b_Bb, srcbufs=(b_xbcd,))
                e_load(ctx, Cb[:], xbc_d[NST + n:NST + n + 1, :].to_broadcast([128, S]),
                       b_Cb, srcbufs=(b_xbcd,))
                ctx.op("act",
                       lambda E, g=g, n=n: E.activation(
                           dA[:], delta[:, g * S:(g + 1) * S], AF.Exp,
                           scale=A_s[:, g * NST + n: g * NST + n + 1]),
                       reads=(b_delta, b_A), writes=(b_dA,))
                ctx.op("pool",
                       lambda E: E.tensor_tensor(dBu[:], du[:], Bb[:], ALU.mult),
                       reads=(b_du, b_Bb), writes=(b_dBu,))
                ctx.op("dve",
                       lambda E: E.tensor_tensor_scan(hs_[:], dA[:], dBu[:], 0.0,
                                                      ALU.mult, ALU.add),
                       reads=(b_dA, b_dBu), writes=(b_h,))
                ctx.op("dve",
                       lambda E: E.tensor_tensor(tmp[:], hs_[:], Cb[:], ALU.mult),
                       reads=(b_h, b_Cb), writes=(b_tmp,))
                if n == 0:
                    ctx.op("dve",
                           lambda E: E.tensor_copy(yg[:], tmp[:]),
                           reads=(b_tmp,), writes=(b_yg,))
                else:
                    ctx.op("pool",
                           lambda E: E.tensor_tensor(yg[:], yg[:], tmp[:], ALU.add),
                           reads=(b_tmp, b_yg), writes=(b_yg,))
            ctx.op("dve",
                   lambda E, g=g: E.scalar_tensor_tensor(
                       yg[:], ug[:], Dp_s[:, g:g + 1], yg[:],
                       ALU.mult, ALU.add),
                   reads=(b_ug, b_Dp, b_yg), writes=(b_yg,))
            e_load(ctx, ug[:], svTr[g], b_ug)
            ctx.op("dve",
                   lambda E: E.tensor_tensor(hs_[:], yg[:], ug[:], ALU.mult),
                   reads=(b_yg, b_ug), writes=(b_h,))
            e_store(ctx, ydr[g], hs_[:], b_h, dstbufs=(b_yd,))

        ctx.claim(b_yt[0], [b_sp])
        for tt in range(8):
            tsl = slice(tt * 512, (tt + 1) * 512)
            for fc in range(8):
                pb = fc % 2
                builds = []
                for g in range(8):
                    gb = g % 2
                    e_load(ctx, ytile[gb][:], ydr[g][:, tsl], b_yt[gb],
                           srcbufs=(b_yd,))
                    ctx.group("pe",
                              [(lambda E, g=g, fc=fc, gb=gb, pb=pb: E.matmul(
                                  bank(pb), wout_s[:, g * 1024 + fc * 128: g * 1024 + fc * 128 + 128],
                                  ytile[gb][:],
                                  start=(g == 0), stop=(g == 7)))],
                              reads=(b_wout, b_yt[gb]), writes=(b_bank[pb],))
                ctx.op("dve",
                       lambda E, pb=pb, fc=fc: E.tensor_copy(ptile[pb][:], bank(pb)),
                       reads=(b_bank[pb],), writes=(b_pt[pb],))
                e_store(ctx, pout_o.rearrange("(c p) t -> c p t", p=128)[fc, :, tsl],
                        ptile[pb][:], b_pt[pb], dstbufs=(b_po,))

        ctx.barrier_all("sp")
        ctx.run()
    return nc


def _prep_launchE(inputs, uT_all, svT_all, xd_all):
    wdt = np.asarray(inputs["L1_attn_w_dt"])
    bdt = np.asarray(inputs["L1_attn_b_dt"])
    Alog = np.asarray(inputs["L1_attn_A_log"])
    Dpv = np.asarray(inputs["L1_attn_Dp"])
    woutf = np.asarray(inputs["L1_attn_w_out"])
    maps = []
    for c in range(8):
        b, hh = c // 2, c % 2
        csl = slice(hh * 1024, (hh + 1) * 1024)
        Asl = Alog[csl].reshape(8, 128, NST).transpose(1, 0, 2).reshape(128, 8 * NST)
        maps.append({
            "uT": uT_all[c], "svT": svT_all[c],
            "xdA": xd_all[2 * b], "xdB": xd_all[2 * b + 1],
            "wdt": np.ascontiguousarray(wdt[:, csl]),
            "bdt": np.ascontiguousarray(bdt[csl].reshape(8, 128).T),
            "Alog": np.ascontiguousarray(Asl),
            "Dp": np.ascontiguousarray(Dpv[csl].reshape(8, 128).T),
            "wout": np.ascontiguousarray(woutf[csl]),
        })
    return maps



def build_launchF():
    nc = bass.Bass("TRN2", target_bir_lowering=False)
    xT_s = nc.dram_tensor("xT_s", [DM, TB], F32, kind="ExternalInput").ap()
    x1T_s = nc.dram_tensor("x1T_s", [DM, TB], F32, kind="ExternalInput").ap()
    pA_s = nc.dram_tensor("pA_s", [DM, TB], F32, kind="ExternalInput").ap()
    pB_s = nc.dram_tensor("pB_s", [DM, TB], F32, kind="ExternalInput").ap()
    gw = nc.dram_tensor("gw", [DM, 8], F32, kind="ExternalInput").ap()
    gb = nc.dram_tensor("gb", [8, 1], F32, kind="ExternalInput").ap()
    w1e = nc.dram_tensor("w1e", [8, DM, 4096], F32, kind="ExternalInput").ap()
    b1e = nc.dram_tensor("b1e", [128, 8 * 32], F32, kind="ExternalInput").ap()
    w2e = nc.dram_tensor("w2e", [8, 4096, DM], F32, kind="ExternalInput").ap()
    b2e = nc.dram_tensor("b2e", [128, 8 * 8], F32, kind="ExternalInput").ap()
    ident = nc.dram_tensor("ident", [128, 128], F32, kind="ExternalInput").ap()
    outT = nc.dram_tensor("outT", [DM, TB], F32, kind="ExternalOutput").ap()
    x3_d = nc.dram_tensor("x3_d", [DM, TB], F32, kind="Internal").ap()
    w_d = nc.dram_tensor("w_d", [8, TB], F32, kind="Internal").ap()

    with ExitStack() as st:
        ctx = Ctx(nc, st)
        al = nc.alloc_sbuf_tensor

        hbuf = al("hbuf", [128, 32 * 1024], BF16).ap()
        sA = hbuf[:, 0:8192].bitcast(F32)
        sB = hbuf[:, 8192:16384].bitcast(F32)
        x3b = al("x3b", [128, 8 * TB], BF16).ap()
        oacc = al("oacc", [128, 8 * 1024], F32).ap()
        wbuf = [al(f"wbuf{i}", [128, 1024], BF16).ap() for i in range(2)]
        wbuf2 = [al(f"wbuf2{i}", [128, 4096], BF16).ap() for i in range(2)]
        wB = al("wB", [128, TB], F32).ap()
        gw_s = al("gw_s", [128, 8 * 8], F32).ap()
        gb_s = al("gb_s", [8, 1], F32).ap()
        b1_s = al("b1_s", [128, 8 * 32], F32).ap()
        b2_s = al("b2_s", [128, 8 * 8], F32).ap()
        id_s = al("id_s", [128, 128], F32).ap()
        lg = al("lg", [8, 512], F32).ap()
        pb8 = al("pb8", [128, 8], F32).ap()
        wtok = al("wtok", [128, 8], F32).ap()
        wT_s = al("wT_s", [8, TB], F32).ap()
        scr1 = al("scr1", [128, 1], F32).ap()
        scr2 = al("scr2", [128, 1], F32).ap()
        ytile = [al(f"ytile{i}", [128, 512], BF16).ap() for i in range(2)]
        rtile = [al(f"rtile{i}", [128, 512], F32).ap() for i in range(2)]
        tmp5 = al("tmp5", [128, 512], F32).ap()
        ones_col = al("ones_colF", [128, 1], F32).ap()
        ones_row = al("ones_row", [1, 128], BF16).ap()
        scr = LNScr(ctx, nc)

        PS = nc.alloc_psum_tensor("PS", [128, 4096], F32).ap()

        def bank(i, rows=128):
            return PS[0:rows, i * 512:(i + 1) * 512]

        b_h = ctx.buf("hbuf"); b_sA = ctx.buf("sA"); b_sB = ctx.buf("sB")
        b_x3b = ctx.buf("x3b"); b_oacc = ctx.buf("oacc")
        b_w = [ctx.buf("w0"), ctx.buf("w1")]
        b_w2 = [ctx.buf("w20"), ctx.buf("w21")]
        b_wB = ctx.buf("wB")
        b_gw = ctx.buf(); b_gb = ctx.buf(); b_b1 = ctx.buf(); b_b2 = ctx.buf()
        b_id = ctx.buf(); b_lg = ctx.buf(); b_pb8 = ctx.buf(); b_wtok = ctx.buf()
        b_wTs = ctx.buf(); b_scr1 = ctx.buf(); b_scr2 = ctx.buf()
        b_yt = [ctx.buf(), ctx.buf()]
        b_rt = [ctx.buf(), ctx.buf()]
        b_tmp5 = ctx.buf()
        b_ones = ctx.buf()
        b_bank = [ctx.buf(f"bank{i}") for i in range(8)]
        b_x3d = ctx.buf("x3_dram"); b_wd = ctx.buf("w_dram")
        b_out = ctx.buf("outd")

        ctx.op("dve", lambda E: E.memset(ones_col[:], 1.0), writes=(b_ones,))
        ctx.op("dve", lambda E: E.memset(ones_row[:], 1.0), writes=(b_ones,))
        e_load(ctx, gw_s[:, 0:64].rearrange("p (c f) -> p c f", c=8),
               gw.rearrange("(c p) f -> p c f", p=128), b_gw)
        e_load(ctx, gb_s[:], gb, b_gb)
        e_load(ctx, b1_s[:], b1e, b_b1)
        e_load(ctx, b2_s[:], b2e, b_b2)
        e_load(ctx, id_s[:], ident, b_id)

        xTr = xT_s.rearrange("(c p) t -> p c t", p=128)
        x1Tr = x1T_s.rearrange("(c p) t -> p c t", p=128)
        pAr = pA_s.rearrange("(c p) t -> p c t", p=128)
        pBr = pB_s.rearrange("(c p) t -> p c t", p=128)
        x3dr = x3_d.rearrange("(c p) t -> c p t", p=128)
        outTr = outT.rearrange("(c p) t -> c p t", p=128)

        for tt in range(4):
            tsl = slice(tt * 512, (tt + 1) * 512)
            e_load(ctx, sA.rearrange("p (c t) -> p c t", c=8), xTr[:, :, tsl], b_sA)
            for other in (x1Tr, pAr, pBr):
                e_load(ctx, sB.rearrange("p (c t) -> p c t", c=8),
                       other[:, :, tsl], b_sB)
                for fc in range(8):
                    ctx.op("dve",
                           lambda E, fc=fc: E.tensor_tensor(
                               sA[:, fc * 512:(fc + 1) * 512],
                               sA[:, fc * 512:(fc + 1) * 512],
                               sB[:, fc * 512:(fc + 1) * 512], ALU.add),
                           reads=(b_sA, b_sB), writes=(b_sA,))
            sAt = [sA[:, fc * 512:(fc + 1) * 512] for fc in range(8)]

            def applyF1(bcA, bcB, b_bc, tt=tt, tsl=tsl, sAt=sAt):
                for fc in range(8):
                    ctx.op("dve",
                           lambda E, fc=fc, bcA=bcA: E.tensor_tensor(
                               sAt[fc], sAt[fc], bcA, ALU.mult),
                           reads=(b_sA, b_bc), writes=(b_sA,))
                    ctx.op("dve",
                           lambda E, fc=fc, bcB=bcB: E.tensor_tensor(
                               sAt[fc], sAt[fc], bcB, ALU.subtract),
                           reads=(b_sA, b_bc), writes=(b_sA,))
                    ctx.op("dve",
                           lambda E, fc=fc, tt=tt: E.tensor_copy(
                               x3b[:, fc * TB + tt * 512: fc * TB + (tt + 1) * 512],
                               sAt[fc]),
                           reads=(b_sA,), writes=(b_x3b,))
                    e_store(ctx, x3dr[fc, :, tsl], sAt[fc], b_sA,
                            dstbufs=(b_x3d,))

            ln_block(ctx, scr, sAt, [b_sA] * 8, [applyF1], 512, LN_EPS,
                     ones_col[:], ones_row[:], b_ones, bank(0), bank(1),
                     b_bank[0], b_bank[1])

        for tt in range(4):
            tsl = slice(tt * 512, (tt + 1) * 512)
            e_load(ctx, sA.rearrange("p (c t) -> p c t", c=8),
                   x3_d.rearrange("(c p) t -> p c t", p=128)[:, :, tsl], b_sA,
                   srcbufs=(b_x3d,))
            ctx.group("pe",
                      [(lambda E, dc=dc, tt=tt: E.matmul(
                          PS[0:8, 0:512], gw_s[:, dc * 8:(dc + 1) * 8],
                          sA[:, dc * 512:(dc + 1) * 512],
                          start=(dc == 0), stop=(dc == 7)))
                       for dc in range(8)],
                      reads=(b_gw, b_sA), writes=(b_bank[0],))
            ctx.op("act",
                   lambda E: E.activation(lg[:], PS[0:8, 0:512], AF.Identity,
                                          bias=gb_s[:, 0:1]),
                   reads=(b_bank[0], b_gb), writes=(b_lg,))
            for qb in range(4):
                t0 = tt * 512 + qb * 128
                ctx.group("pe",
                          [lambda E, qb=qb: E.transpose(
                              PS[0:128, 512:520], lg[:, qb * 128:(qb + 1) * 128],
                              id_s[0:8, 0:8])],
                          reads=(b_lg, b_id), writes=(b_bank[1],))
                ctx.op("dve",
                       lambda E: E.reduce_max(scr1[:], PS[:, 512:520], axis=AX.X),
                       reads=(b_bank[1],), writes=(b_scr1,))
                ctx.op("dve",
                       lambda E: E.tensor_scalar(scr2[:], scr1[:], -1.0, None,
                                                 ALU.mult),
                       reads=(b_scr1,), writes=(b_scr2,))
                ctx.op("act",
                       lambda E: E.activation(pb8[:], PS[:, 512:520], AF.Exp,
                                              bias=scr2[:], accum_out=scr1[:]),
                       reads=(b_bank[1], b_scr2), writes=(b_pb8, b_scr1))
                ctx.op("dve",
                       lambda E: E.reciprocal(scr2[:], scr1[:]),
                       reads=(b_scr1,), writes=(b_scr2,))
                ctx.op("dve",
                       lambda E: E.tensor_scalar(pb8[:], pb8[:], scr2[:], None,
                                                 ALU.mult),
                       reads=(b_pb8, b_scr2), writes=(b_pb8,))
                ctx.op("dve",
                       lambda E: E.reduce_max(scr1[:], pb8[:], axis=AX.X),
                       reads=(b_pb8,), writes=(b_scr1,))
                ctx.op("dve",
                       lambda E: E.tensor_scalar(wtok[:], pb8[:], scr1[:], None,
                                                 ALU.is_ge),
                       reads=(b_pb8, b_scr1), writes=(b_wtok,))
                ctx.op("dve",
                       lambda E: E.tensor_tensor(wtok[:], wtok[:], pb8[:],
                                                 ALU.mult),
                       reads=(b_wtok, b_pb8), writes=(b_wtok,))
                ctx.group("pe",
                          [lambda E: E.transpose(PS[0:8, 640:768], wtok[:],
                                                 id_s[:])],
                          reads=(b_wtok, b_id), writes=(b_bank[1],))
                ctx.op("dve",
                       lambda E, t0=t0: E.tensor_copy(
                           wT_s[:, t0:t0 + 128], PS[0:8, 640:768]),
                       reads=(b_bank[1],), writes=(b_wTs,))
        e_store(ctx, w_d[:], wT_s[:], b_wTs, dstbufs=(b_wd,))

        ctx.claim(b_h, [b_sA, b_sB])
        w1er = w1e.rearrange("e (c p) f -> e p c f", p=128)
        w2er = w2e.rearrange("e (c p) f -> e p c f", p=128)
        for th in range(2):
            for e in range(8):
                e_load(ctx, wB[:, 0:1024],
                       w_d[e:e + 1, th * 1024:(th + 1) * 1024].to_broadcast(
                           [128, 1024]), b_wB, srcbufs=(b_wd,))
                for ft in range(32):
                    wb = ft % 2
                    e_load(ctx, wbuf[wb].rearrange("p (c f) -> p c f", c=8),
                           w1er[e, :, :, ft * 128:(ft + 1) * 128].rearrange(
                               "p c f -> p c f"), b_w[wb])
                    for sub in range(2):
                        pb = sub
                        ctx.group("pe",
                                  [(lambda E, dc=dc, wb=wb, sub=sub, th=th, pb=pb:
                                    E.matmul(
                                        bank(pb), wbuf[wb][:, dc * 128:(dc + 1) * 128],
                                        x3b[:, dc * TB + th * 1024 + sub * 512:
                                            dc * TB + th * 1024 + (sub + 1) * 512],
                                        start=(dc == 0), stop=(dc == 7)))
                                   for dc in range(8)],
                                  reads=(b_w[wb], b_x3b), writes=(b_bank[pb],))
                        ctx.op("act",
                               lambda E, ft=ft, sub=sub, e=e, pb=pb: E.activation(
                                   hbuf[:, ft * 1024 + sub * 512: ft * 1024 + (sub + 1) * 512],
                                   bank(pb), AF.Gelu,
                                   bias=b1_s[:, e * 32 + ft: e * 32 + ft + 1]),
                               reads=(b_bank[pb], b_b1), writes=(b_h,))
                for fc in range(8):
                    wb = fc % 2
                    e_load(ctx, wbuf2[wb].rearrange("p (c f) -> p c f", c=32),
                           w2er[e, :, :, fc * 128:(fc + 1) * 128], b_w2[wb])
                    for sub in range(2):
                        pb = 2 + sub
                        ctx.group("pe",
                                  [(lambda E, dc=dc, wb=wb, sub=sub, pb=pb: E.matmul(
                                      bank(pb), wbuf2[wb][:, dc * 128:(dc + 1) * 128],
                                      hbuf[:, dc * 1024 + sub * 512: dc * 1024 + (sub + 1) * 512],
                                      start=(dc == 0), stop=(dc == 31)))
                                   for dc in range(32)],
                                  reads=(b_w2[wb], b_h), writes=(b_bank[pb],))
                        yb = sub
                        ctx.op("act",
                               lambda E, fc=fc, e=e, pb=pb, yb=yb: E.activation(
                                   ytile[yb][:], bank(pb), AF.Identity,
                                   bias=b2_s[:, e * 8 + fc: e * 8 + fc + 1]),
                               reads=(b_bank[pb], b_b2), writes=(b_yt[yb],))
                        ctx.op("dve",
                               lambda E, sub=sub, yb=yb: E.tensor_tensor(
                                   tmp5[:], ytile[yb][:],
                                   wB[:, sub * 512:(sub + 1) * 512], ALU.mult),
                               reads=(b_yt[yb], b_wB), writes=(b_tmp5,))
                        if e == 0:
                            ctx.op("dve",
                                   lambda E, fc=fc, sub=sub: E.tensor_copy(
                                       oacc[:, fc * 1024 + sub * 512: fc * 1024 + (sub + 1) * 512],
                                       tmp5[:]),
                                   reads=(b_tmp5,), writes=(b_oacc,))
                        else:
                            ctx.op("dve",
                                   lambda E, fc=fc, sub=sub: E.tensor_tensor(
                                       oacc[:, fc * 1024 + sub * 512: fc * 1024 + (sub + 1) * 512],
                                       oacc[:, fc * 1024 + sub * 512: fc * 1024 + (sub + 1) * 512],
                                       tmp5[:], ALU.add),
                                   reads=(b_tmp5, b_oacc), writes=(b_oacc,))
            for sub in range(2):
                gsl = slice(th * 1024 + sub * 512, th * 1024 + (sub + 1) * 512)
                x4t = []
                for fc in range(8):
                    rb = fc % 2
                    e_load(ctx, rtile[rb][:], x3dr[fc, :, gsl], b_rt[rb],
                           srcbufs=(b_x3d,), q="dact")
                    ctx.op("dve",
                           lambda E, fc=fc, sub=sub, rb=rb: E.tensor_tensor(
                               oacc[:, fc * 1024 + sub * 512: fc * 1024 + (sub + 1) * 512],
                               oacc[:, fc * 1024 + sub * 512: fc * 1024 + (sub + 1) * 512],
                               rtile[rb][:], ALU.add),
                           reads=(b_rt[rb], b_oacc), writes=(b_oacc,))
                    x4t.append(oacc[:, fc * 1024 + sub * 512: fc * 1024 + (sub + 1) * 512])

                def applyF4(bcA, bcB, b_bc, gsl=gsl, x4t=x4t):
                    for fc in range(8):
                        ctx.op("dve",
                               lambda E, fc=fc, bcA=bcA: E.tensor_tensor(
                                   x4t[fc], x4t[fc], bcA, ALU.mult),
                               reads=(b_oacc, b_bc), writes=(b_oacc,))
                        ctx.op("dve",
                               lambda E, fc=fc, bcB=bcB: E.tensor_tensor(
                                   x4t[fc], x4t[fc], bcB, ALU.subtract),
                               reads=(b_oacc, b_bc), writes=(b_oacc,))
                        e_store(ctx, outTr[fc, :, gsl], x4t[fc], b_oacc,
                                dstbufs=(b_out,))

                ln_block(ctx, scr, x4t, [b_oacc] * 8, [applyF4], 512, LN_EPS,
                         ones_col[:], ones_row[:], b_ones, bank(4), bank(5),
                         b_bank[4], b_bank[5])

        ctx.barrier_all("sp")
        ctx.run()
    return nc


def _prep_launchF(inputs, xT_full, x1T_full, pouts):
    gwv = np.asarray(inputs["L1_ffn_gate_w"])
    gbv = np.asarray(inputs["L1_ffn_gate_b"])
    w1 = np.asarray(inputs["L1_ffn_w1"])
    b1 = np.asarray(inputs["L1_ffn_b1"])
    w2 = np.asarray(inputs["L1_ffn_w2"])
    b2 = np.asarray(inputs["L1_ffn_b2"])
    idm = np.eye(128, dtype=np.float32)
    xTf = np.concatenate([xT_full[b] for b in range(4)], axis=1)
    x1Tf = np.concatenate([x1T_full[b] for b in range(4)], axis=1)
    pAf = np.concatenate([pouts[2 * b] for b in range(4)], axis=1)
    pBf = np.concatenate([pouts[2 * b + 1] for b in range(4)], axis=1)
    com = {
        "gw": np.ascontiguousarray(gwv), "gb": gbv.reshape(8, 1).copy(),
        "w1e": np.ascontiguousarray(w1),
        "b1e": np.ascontiguousarray(b1.reshape(8, 32, 128).transpose(2, 0, 1).reshape(128, 256)),
        "w2e": np.ascontiguousarray(w2),
        "b2e": np.ascontiguousarray(b2.reshape(8, 8, 128).transpose(2, 0, 1).reshape(128, 64)),
        "ident": idm,
    }
    maps = []
    for c in range(8):
        tsl = slice(c * TB, (c + 1) * TB)
        m = dict(com)
        m["xT_s"] = np.ascontiguousarray(xTf[:, tsl])
        m["x1T_s"] = np.ascontiguousarray(x1Tf[:, tsl])
        m["pA_s"] = np.ascontiguousarray(pAf[:, tsl])
        m["pB_s"] = np.ascontiguousarray(pBf[:, tsl])
        maps.append(m)
    return maps



def _t(a):
    return np.ascontiguousarray(np.asarray(a).T)


def _mk_masks_kq():
    out = np.zeros((128, 4 * 512), dtype=np.float32)
    kr = np.arange(128)[:, None]
    qc = np.arange(512)[None, :]
    for i, D in enumerate((0, 128, 256, 384)):
        out[:, i * 512:(i + 1) * 512] = np.where(kr - qc > -D, NEG, 0.0)
    return out


def _prep_launchA(inputs):
    x = np.asarray(inputs["L0_attn_x"])
    Wqkv = np.asarray(inputs["L0_attn_W_qkv"])
    bqkv = np.asarray(inputs["L0_attn_b_qkv"])
    Wr = Wqkv.reshape(16, 64, 3, 1024)
    br = bqkv.reshape(16, 64, 3)
    masks = _mk_masks_kq()
    maps = []
    for c in range(8):
        b, hh = c // 2, c % 2
        hsel = slice(hh * 8, hh * 8 + 8)
        wq = Wr[hsel, :, 0, :].reshape(512, 1024)
        wk = Wr[hsel, :, 1, :].reshape(512, 1024)
        wv = Wr[hsel, :, 2, :].reshape(512, 1024)
        bq_ = br[hsel, :, 0].reshape(512)
        bk_ = br[hsel, :, 1].reshape(512)
        bv_ = br[hsel, :, 2].reshape(512)
        maps.append({
            "xT": _t(x[b]),
            "wqT": _t(wq), "wkT": _t(wk), "wvT": _t(wv),
            "bq": np.ascontiguousarray(bq_.reshape(4, 128).T),
            "bk": np.ascontiguousarray(bk_.reshape(4, 128).T),
            "bv": bv_.reshape(1, 512).copy(),
            "masks": masks,
        })
    return maps



MAXPOLL = 200000
TH = 2048


class Arena:

    def __init__(self, ctx, nc, nbytes):
        self.nbytes = nbytes
        self.ap = nc.alloc_sbuf_tensor("ARENA", [128, nbytes], mybir.dt.uint8).ap()
        self.ctx = ctx
        self.off = 0
        self.cur = []
        self.pend = []

    def reset(self):
        if not hasattr(self, "peaks"):
            self.peaks = []
        self.peaks.append(self.off)
        pend = []
        for b in self.cur:
            if b.writer is not None:
                pend.append(b.writer)
            pend += b.readers
        self.pend = list({p: None for p in pend})
        self.cur = []
        self.off = 0

    def take(self, p, n, dtype, name=None):
        esz = mybir.dt.size(dtype)
        off = (self.off + 63) & ~63
        self.off = off + n * esz
        assert self.off <= self.nbytes, f"arena overflow {self.off}"
        ap = self.ap[0:p, off:off + n * esz].bitcast(dtype)
        b = self.ctx.buf(name)
        b.readers = list(self.pend)
        self.cur.append(b)
        return ap, b


def build_fused():
    nc = bass.Bass("TRN2", target_bir_lowering=False,
                   detect_race_conditions=False)
    dt_ = nc.dram_tensor

    xT = dt_("xT", [DM, S], F32, kind="ExternalInput").ap()
    xTh = dt_("xTh", [DM, TH], F32, kind="ExternalInput").ap()
    wqT = dt_("wqT", [DM, HF], F32, kind="ExternalInput").ap()
    wkT = dt_("wkT", [DM, HF], F32, kind="ExternalInput").ap()
    wvT = dt_("wvT", [DM, HF], F32, kind="ExternalInput").ap()
    bq = dt_("bq", [128, 4], F32, kind="ExternalInput").ap()
    bk = dt_("bk", [128, 4], F32, kind="ExternalInput").ap()
    bv = dt_("bv", [1, HF], F32, kind="ExternalInput").ap()
    masksL0 = dt_("masksL0", [128, 2048], F32, kind="ExternalInput").ap()
    masksQK = dt_("masksQK", [128, 2048], F32, kind="ExternalInput").ap()
    masksKQ = dt_("masksKQ", [128, 2048], F32, kind="ExternalInput").ap()
    ident = dt_("ident", [128, 128], F32, kind="ExternalInput").ap()
    woT = dt_("woT", [DM, DM], F32, kind="ExternalInput").ap()
    bo = dt_("bo", [128, 8], F32, kind="ExternalInput").ap()
    w1T = dt_("w1T", [DM, 4096], F32, kind="ExternalInput").ap()
    w2T = dt_("w2T", [DM, 4096], F32, kind="ExternalInput").ap()
    h1T = dt_("h1T", [4096, 4096], F32, kind="ExternalInput").ap()
    h2T = dt_("h2T", [4096, 4096], F32, kind="ExternalInput").ap()
    w3T = dt_("w3T", [4096, DM], F32, kind="ExternalInput").ap()
    s1 = dt_("s1", [128, 32], F32, kind="ExternalInput").ap()
    s2 = dt_("s2", [128, 32], F32, kind="ExternalInput").ap()
    sh1 = dt_("sh1", [128, 32], F32, kind="ExternalInput").ap()
    sh2 = dt_("sh2", [128, 32], F32, kind="ExternalInput").ap()
    s3 = dt_("s3", [128, 8], F32, kind="ExternalInput").ap()
    wq1 = dt_("wq1", [DM, HF], F32, kind="ExternalInput").ap()
    wkv = dt_("wkv", [DM, 128], F32, kind="ExternalInput").ap()
    bqm = dt_("bqm", [128, 4], F32, kind="ExternalInput").ap()
    bkv = dt_("bkv", [128, 1], F32, kind="ExternalInput").ap()
    wao = dt_("wao", [DM, DM], F32, kind="ExternalInput").ap()
    bao = dt_("bao", [128, 8], F32, kind="ExternalInput").ap()
    w_u = dt_("w_u", [DM, DM], F32, kind="ExternalInput").ap()
    w_r = dt_("w_r", [DM, DM], F32, kind="ExternalInput").ap()
    convw = dt_("convw", [128, 32], F32, kind="ExternalInput").ap()
    convb = dt_("convb", [128, 8], F32, kind="ExternalInput").ap()
    wxp = dt_("wxp", [DM, 96], F32, kind="ExternalInput").ap()
    wdt = dt_("wdt", [DTR, DM], F32, kind="ExternalInput").ap()
    bdt = dt_("bdt", [128, 8], F32, kind="ExternalInput").ap()
    Alog = dt_("Alog", [128, 8 * NST], F32, kind="ExternalInput").ap()
    Dpi = dt_("Dp", [128, 8], F32, kind="ExternalInput").ap()
    wout = dt_("wout", [DM, DM], F32, kind="ExternalInput").ap()
    gw = dt_("gw", [DM, 8], F32, kind="ExternalInput").ap()
    gb = dt_("gb", [8, 1], F32, kind="ExternalInput").ap()
    w1e = dt_("w1e", [4, DM, 4096], F32, kind="ExternalInput").ap()
    b1e = dt_("b1e", [128, 4 * 32], F32, kind="ExternalInput").ap()
    w2e = dt_("w2e", [4, 4096, DM], F32, kind="ExternalInput").ap()
    b2e = dt_("b2e", [128, 4 * 8], F32, kind="ExternalInput").ap()
    fv = dt_("fv", [1, 16], I32, kind="ExternalInput").ap()
    fv2 = dt_("fv2", [1, 16], I32, kind="ExternalInput").ap()
    outT = dt_("outT", [DM, TH], F32, kind="ExternalOutput").ap()
    pdiag = dt_("pdiag", [1, 16], I32, kind="ExternalOutput").ap()

    w1b_d = dt_("w1b_d", [DM, 4096], BF16, kind="Internal").ap()
    w2b_d = dt_("w2b_d", [DM, 4096], BF16, kind="Internal").ap()
    h1b_d = dt_("h1b_d", [4096, 4096], BF16, kind="Internal").ap()
    h2b_d = dt_("h2b_d", [4096, 4096], BF16, kind="Internal").ap()
    w3b_d = dt_("w3b_d", [4096, DM], BF16, kind="Internal").ap()
    w1eb_d = dt_("w1eb_d", [4, DM, 4096], BF16, kind="Internal").ap()
    w2eb_d = dt_("w2eb_d", [4, 4096, DM], BF16, kind="Internal").ap()
    xn_d = dt_("xn_d", [DM, S], BF16, kind="Internal").ap()
    x2h_d = dt_("x2h_d", [DM, TH], F32, kind="Internal").ap()
    x1b_d = dt_("x1b_d", [DM, TH], F32, kind="Internal").ap()
    x1bb_d = dt_("x1bb_d", [DM, TH], BF16, kind="Internal").ap()
    x1mh_d = dt_("x1mh_d", [DM, TH], F32, kind="Internal").ap()
    uT_d = dt_("uT_d", [DM, S], BF16, kind="Internal").ap()
    svT_d = dt_("svT_d", [DM, S], BF16, kind="Internal").ap()
    y_d = dt_("y_d", [DM, S], BF16, kind="Internal").ap()
    xbc_d = dt_("xbc_d", [2 * NST, S], BF16, kind="Internal").ap()
    nb_d = dt_("nb_d", [NH, 32, 128], F32, kind="Internal").ap()
    x3h_d = dt_("x3h_d", [DM, TH], F32, kind="Internal").ap()
    dbg_lg = dt_("dbg_lg", [8, TH], F32, kind="Internal").ap()

    sh = dict(
        Z0=dt_("sh_Z0", [DM, S], BF16, kind="Internal", addr_space="Shared").ap(),
        X2=dt_("sh_X2", [DM, S], BF16, kind="Internal", addr_space="Shared").ap(),
        Z1=dt_("sh_Z1", [DM, S], BF16, kind="Internal", addr_space="Shared").ap(),
        X1M=dt_("sh_X1M", [DM, S], BF16, kind="Internal", addr_space="Shared").ap(),
        XD=dt_("sh_XD", [2, 96, S], F32, kind="Internal", addr_space="Shared").ap(),
        PO=dt_("sh_PO", [2, DM, S], BF16, kind="Internal", addr_space="Shared").ap(),
        X3=dt_("sh_X3", [DM, S], BF16, kind="Internal", addr_space="Shared").ap(),
        WG=dt_("sh_WG", [8, S], F32, kind="Internal", addr_space="Shared").ap(),
        MO=dt_("sh_MO", [2, DM, S], BF16, kind="Internal", addr_space="Shared").ap(),
        FL=dt_("sh_FL", [16, 2], I32, kind="Internal", addr_space="Shared").ap(),
    )

    with ExitStack() as st:
        ctx = Ctx(nc, st)
        al = nc.alloc_sbuf_tensor
        psem = st.enter_context(nc.semaphore("psem"))

        ones_col = al("ones_col", [128, 1], F32).ap()
        ones_colB = al("ones_colB", [128, 1], BF16).ap()
        ones_row = al("ones_row", [1, 128], BF16).ap()
        ones_rowF = al("ones_rowF", [1, 128], F32).ap()
        id_s = al("id_s", [128, 128], F32).ap()
        smalls = {}
        b_small = {}
        for nm, src, shp in (("bq", bq, (128, 4)), ("bk", bk, (128, 4)),
                             ("bo", bo, (128, 8)), ("s1", s1, (128, 32)),
                             ("s2", s2, (128, 32)), ("sh1", sh1, (128, 32)),
                             ("sh2", sh2, (128, 32)), ("s3", s3, (128, 8)),
                             ("bqm", bqm, (128, 4)), ("bkv", bkv, (128, 1)),
                             ("bao", bao, (128, 8)), ("convw", convw, (128, 32)),
                             ("convb", convb, (128, 8)), ("bdt", bdt, (128, 8)),
                             ("Alog", Alog, (128, 8 * NST)), ("Dp", Dpi, (128, 8)),
                             ("gb", gb, (8, 1)),
                             ("b1e", b1e, (128, 128)), ("b2e", b2e, (128, 32))):
            smalls[nm] = al(f"sm_{nm}", list(shp), F32).ap()
            b_small[nm] = ctx.buf(f"sm_{nm}")
            e_load(ctx, smalls[nm], src, b_small[nm])
        sfv = al("sfv", [1, 16], I32).ap()
        sfv2 = al("sfv2", [1, 16], I32).ap()
        spoll = al("spoll", [1, 2], I32).ap()
        sdiag = al("sdiag", [1, 16], I32).ap()
        b_sfv = ctx.buf("sfv"); b_sfv2 = ctx.buf("sfv2")
        b_sdiag = ctx.buf("sdiag")
        e_load(ctx, sfv, fv, b_sfv)
        e_load(ctx, sfv2, fv2, b_sfv2)
        b_ones = ctx.buf("ones"); b_id = ctx.buf("id")
        ctx.op("dve", lambda E: E.memset(ones_col[:], 1.0), writes=(b_ones,))
        ctx.op("dve", lambda E: E.memset(ones_colB[:], 1.0), writes=(b_ones,))
        ctx.op("dve", lambda E: E.memset(ones_row[:], 1.0), writes=(b_ones,))
        ctx.op("dve", lambda E: E.memset(ones_rowF[:], 1.0), writes=(b_ones,))
        ctx.op("dve", lambda E: E.memset(spoll[:], 0), writes=(b_sfv,))
        ctx.op("dve", lambda E: E.memset(sdiag[:], 0), writes=(b_sdiag,))
        e_load(ctx, id_s[:], ident, b_id)
        scr = LNScr(ctx, nc)
        bv_row = al("bv_row", [1, HF], BF16).ap()
        b_bvrow = ctx.buf()
        e_load(ctx, bv_row, bv, b_bvrow)
        bvB = al("bvB", [128, HF], F32).ap()
        b_bvB = ctx.buf()
        bvrow1 = al("bvrow1", [1, DH], BF16).ap()
        b_bvrow1 = ctx.buf()
        e_load(ctx, bvrow1, bkv[64:128, :].rearrange("p one -> one p"), b_bvrow1)
        bvB1 = al("bvB1", [128, DH], F32).ap()
        b_bvB1 = ctx.buf()
        bqsc = al("bqsc", [128, 4], F32).ap()
        b_bqsc = ctx.buf()

        PS = nc.alloc_psum_tensor("PS", [128, 4096], F32).ap()

        def bank(i, rows=128):
            return PS[0:rows, i * 512:(i + 1) * 512]

        b_bank = [ctx.buf(f"bank{i}") for i in range(8)]

        AR = Arena(ctx, nc, 176 * 1024)

        b_d = {nm: ctx.buf(nm) for nm in
               ("xn", "x2h", "x1b", "x1bb", "x1mh", "uT", "svT", "y", "xbc",
                "nb", "x3h")}
        b_sh = {nm: ctx.buf("sh" + nm) for nm in sh}
        b_out = ctx.buf("outx")
        b_phase = [ctx.buf(f"ph{i}") for i in range(16)]
        b_diagout = ctx.buf("diagout")

        ctx.hh = None

        ctx.hh_handle = ctx.eng["pool"].alloc_register("r_hh")

        def _pool_regs(E):
            pid = E.partition_id()
            E.reg_mov(ctx.hh_handle, E.compute_val(pid % 2))
        ctx.raw("pool", _pool_regs)

        act = ctx.eng["act"]
        pregs = {k: act.alloc_register(f"pr_{k}") for k in
                 ("want2", "it", "go", "pc", "v0", "v1", "sum", "diff",
                  "neq", "cap")}

        def _act_regs(E):
            E.reg_mov(pregs["pc"], 0)
        ctx.raw("act", _act_regs)

        def send_flag(r, databufs):
            ctx.dma_cond("dpool",
                    lambda E, hh, r=r: E.dma_start(
                        out=sh["FL"][r:r + 1, hh:hh + 1],
                        in_=sfv[:, r:r + 1]),
                    reads=tuple(databufs) + (b_sfv,), writes=(b_phase[r],))

        def poll(r):
            def seq(E, r=r):
                E.load(pregs["want2"], sfv2[0:1, r:r + 1])
                E.reg_mov(pregs["it"], 0)
                E.reg_mov(pregs["go"], 1)
                with E.While(pregs["go"]):
                    E.dma_start(out=spoll, in_=sh["FL"][r:r + 1, :]).then_inc(psem, 16)
                    E.reg_add(pregs["pc"], pregs["pc"], 16)
                    E.wait_ge(psem, pregs["pc"])
                    E.load(pregs["v0"], spoll[0:1, 0:1])
                    E.load(pregs["v1"], spoll[0:1, 1:2])
                    E.reg_add(pregs["sum"], pregs["v0"], pregs["v1"])
                    E.reg_sub(pregs["diff"], pregs["sum"], pregs["want2"])
                    E.reg_add(pregs["it"], pregs["it"], 1)
                    E.reg_alu(pregs["neq"], pregs["diff"], 0, ALU.not_equal)
                    E.reg_alu(pregs["cap"], pregs["it"], MAXPOLL, ALU.is_lt)
                    E.reg_alu(pregs["go"], pregs["neq"], pregs["cap"],
                              ALU.bitwise_and)
                E.reg_save(sdiag[0:1, r:r + 1], pregs["it"])
            ctx._pre("act", (b_sfv2, b_phase[r]), (b_sdiag,))
            ctx.raw("act", seq)
            ctx.op("act", lambda E: E.nop(), writes=(b_phase[r],))

        _voff = (AR.off + 63) & ~63
        xtileA = [AR.take(128, 4096, F32) for _ in range(2)]
        VA = (AR.ap[0:128, _voff:_voff + 32768].bitcast(BF16), ctx.buf("VA"))
        sqA = [AR.take(128, 512, BF16) for _ in range(2)]
        rstdrow = AR.take(1, 512, BF16)
        rstdB = AR.take(128, S, F32)
        xncA = [AR.take(128, S, BF16) for _ in range(2)]
        wqA = AR.take(128, 8 * HF, BF16)
        wkA = AR.take(128, 8 * HF, BF16)
        wvA = AR.take(128, 8 * HF, BF16)
        maskA = AR.take(128, 2048, F32)
        QTA = AR.take(128, 4 * S, BF16)
        KTA = AR.take(128, 4 * S, BF16)
        mA = AR.take(128, NH * 32, F32)
        sA_ = AR.take(128, NH * 32, F32)
        nbA = AR.take(128, NH * 32, F32)
        expA = AR.take(128, 2048, BF16)
        alpP = [AR.take(128, 1024, BF16) for _ in range(2)]
        scr1A = AR.take(128, 1, F32)
        scrHA = AR.take(128, 1, F32)
        scrS1 = AR.take(128, 1, F32)
        scrS2 = AR.take(128, 1, F32)
        scr32A = AR.take(128, 32, F32)
        zsegA = [(rstdB[0][0:64, i * 256:(i + 1) * 256].bitcast(BF16),
                  ctx.buf(f"zsegA{i}")) for i in range(2)]

        e_load(ctx, maskA[0], masksL0, maskA[1], q="dsp")
        e_load(ctx, wqA[0].rearrange("p (c f) -> p c f", c=8),
               wqT.rearrange("(c p) f -> p c f", p=128), wqA[1])
        e_load(ctx, wkA[0].rearrange("p (c f) -> p c f", c=8),
               wkT.rearrange("(c p) f -> p c f", p=128), wkA[1])
        e_load(ctx, wvA[0].rearrange("p (c f) -> p c f", c=8),
               wvT.rearrange("(c p) f -> p c f", p=128), wvA[1])
        b_wcast = ctx.buf("wcast")
        for dst_, src_ in ((w1b_d, w1T), (w2b_d, w2T), (w3b_d, w3T)):
            ctx.dma("dpool", lambda E, dst_=dst_, src_=src_: E.dma_start(
                out=dst_, in_=src_), writes=(b_wcast,))
        for big_dst, big_src in ((h1b_d, h1T), (h2b_d, h2T)):
            for qq in range(4):
                ctx.dma("dpool", lambda E, big_dst=big_dst, big_src=big_src, qq=qq:
                        E.dma_start(out=big_dst[qq * 1024:(qq + 1) * 1024, :],
                                    in_=big_src[qq * 1024:(qq + 1) * 1024, :]),
                        writes=(b_wcast,))
        for e in range(4):
            ctx.dma("dpool", lambda E, e=e: E.dma_start(
                out=w1eb_d[e], in_=w1e[e]), writes=(b_wcast,))
            for qq in range(4):
                ctx.dma("dpool", lambda E, e=e, qq=qq: E.dma_start(
                    out=w2eb_d[e][qq * 1024:(qq + 1) * 1024, :],
                    in_=w2e[e][qq * 1024:(qq + 1) * 1024, :]),
                    writes=(b_wcast,))

        ctx.group("pe", [lambda E: E.matmul(PS[:, 0:HF], ones_row, bv_row,
                                            start=True, stop=True)],
                  reads=(b_bvrow, b_ones), writes=(b_bank[0],))
        ctx.op("dve", lambda E: E.tensor_copy(bvB[:], PS[:, 0:HF]),
               reads=(b_bank[0],), writes=(b_bvB,))

        xTr = xT.rearrange("(c p) t -> c p t", p=128)
        xn_dr = xn_d.rearrange("(c p) t -> c p t", p=128)
        for dc in range(8):
            e_load(ctx, xtileA[dc % 2][0], xTr[dc], xtileA[dc % 2][1],
                   q="dsp")
            for tt in range(8):
                ctx.op("act",
                       lambda E, dc=dc, tt=tt: E.activation(
                           sqA[tt % 2][0],
                           xtileA[dc % 2][0][:, tt * 512:(tt + 1) * 512],
                           AF.Square),
                       reads=(xtileA[dc % 2][1],), writes=(sqA[tt % 2][1],))
                ctx.group("pe",
                          [lambda E, dc=dc, tt=tt: E.matmul(
                              bank(tt, 1), ones_colB, sqA[tt % 2][0],
                              start=(dc == 0), stop=(dc == 7))],
                          reads=(sqA[tt % 2][1], b_ones), writes=(b_bank[tt],))
        c_eps = ctx.constcol(RMS_EPS)
        for tt in range(8):
            ctx.op("act",
                   lambda E, tt=tt: E.activation(
                       rstdrow[0], bank(tt, 1),
                       AF.Ln, bias=c_eps[0:1, :], scale=1.0 / DM),
                   reads=(b_bank[tt], ctx._cbuf), writes=(rstdrow[1],))
            ctx.op("act",
                   lambda E, tt=tt: E.activation(
                       rstdrow[0], rstdrow[0], AF.Exp, scale=-0.5),
                   reads=(rstdrow[1],), writes=(rstdrow[1],))
            ctx.group("pe",
                      [lambda E, tt=tt: E.matmul(
                          bank(tt), ones_row, rstdrow[0],
                          start=True, stop=True)],
                      reads=(rstdrow[1], b_ones), writes=(b_bank[tt],))
            ctx.op("dve",
                   lambda E, tt=tt: E.tensor_copy(
                       rstdB[0][:, tt * 512:(tt + 1) * 512], bank(tt)),
                   reads=(b_bank[tt],), writes=(rstdB[1],))
        for dc in range(8):
            e_load(ctx, xtileA[dc % 2][0], xTr[dc], xtileA[dc % 2][1],
                   q="dsp")
            ctx.op("dve",
                   lambda E, dc=dc: E.tensor_tensor(
                       xncA[dc % 2][0], xtileA[dc % 2][0], rstdB[0], ALU.mult),
                   reads=(xtileA[dc % 2][1], rstdB[1]), writes=(xncA[dc % 2][1],))
            e_store(ctx, xn_dr[dc], xncA[dc % 2][0], xncA[dc % 2][1],
                    dstbufs=(b_d["xn"],))
        ctx.claim(zsegA[0][1], [rstdB[1]])
        zsegA[1][1].readers = list(zsegA[0][1].readers)
        for tt in range(8):
            for dc in range(8):
                xb = xncA[dc % 2][1]
                e_load(ctx, xncA[dc % 2][0][:, 0:512],
                       xn_dr[dc][:, tt * 512:(tt + 1) * 512], xb,
                       srcbufs=(b_d["xn"],), q="dsp")
                for fc in range(4):
                    ctx.group("pe",
                              [(lambda E, fc=fc, dc=dc: E.matmul(
                                  bank(fc),
                                  wqA[0][:, dc * HF + fc * 128: dc * HF + fc * 128 + 128],
                                  xncA[dc % 2][0][:, 0:512],
                                  start=(dc == 0), stop=(dc == 7)))],
                              reads=(wqA[1], xb), writes=(b_bank[fc],))
                    ctx.group("pe",
                              [(lambda E, fc=fc, dc=dc: E.matmul(
                                  bank(4 + fc),
                                  wkA[0][:, dc * HF + fc * 128: dc * HF + fc * 128 + 128],
                                  xncA[dc % 2][0][:, 0:512],
                                  start=(dc == 0), stop=(dc == 7)))],
                              reads=(wkA[1], xb), writes=(b_bank[4 + fc],))
            for fc in range(4):
                ctx.op("act",
                       lambda E, fc=fc, tt=tt: E.activation(
                           QTA[0][:, fc * S + tt * 512: fc * S + (tt + 1) * 512],
                           bank(fc), AF.Identity, bias=smalls["bq"][:, fc:fc + 1]),
                       reads=(b_bank[fc], b_small["bq"]), writes=(QTA[1],))
                ctx.op("act",
                       lambda E, fc=fc, tt=tt: E.activation(
                           KTA[0][:, fc * S + tt * 512: fc * S + (tt + 1) * 512],
                           bank(4 + fc), AF.Identity,
                           bias=smalls["bk"][:, fc:fc + 1]),
                       reads=(b_bank[4 + fc], b_small["bk"]), writes=(KTA[1],))
        ctx.claim(VA[1], [xtileA[0][1], xtileA[1][1]])
        for tt in range(8):
            for dc in range(8):
                xb = xncA[dc % 2][1]
                e_load(ctx, xncA[dc % 2][0][:, 0:512],
                       xn_dr[dc][:, tt * 512:(tt + 1) * 512], xb,
                       srcbufs=(b_d["xn"],), q="dsp")
                for sub in range(4):
                    ctx.group("pe",
                              [(lambda E, dc=dc, sub=sub: E.matmul(
                                  bank(sub),
                                  xncA[dc % 2][0][:, sub * 128:(sub + 1) * 128],
                                  wvA[0][:, dc * HF:(dc + 1) * HF],
                                  start=(dc == 0), stop=(dc == 7)))],
                              reads=(wvA[1], xb), writes=(b_bank[sub],))
            for sub in range(4):
                ti = tt * 4 + sub
                ctx.op("dve",
                       lambda E, ti=ti, sub=sub: E.tensor_tensor(
                           VA[0][:, ti * HF:(ti + 1) * HF], bank(sub), bvB[:],
                           ALU.add),
                       reads=(b_bank[sub], b_bvB), writes=(VA[1],))

        def hsA(h):
            return h // 2, (h % 2) * 64

        def s_matmulA(E, h, kt, qt, pb):
            fc, p0 = hsA(h)
            return E.matmul(
                bank(pb),
                KTA[0][p0:p0 + 64, fc * S + kt * 128: fc * S + kt * 128 + 128],
                QTA[0][p0:p0 + 64, fc * S + qt * 512: fc * S + (qt + 1) * 512],
                start=True, stop=True)

        def mask_addA(E, kt, qt, pb):
            mi = (kt * 128 - qt * 512) // 128
            return E.tensor_tensor(bank(pb), bank(pb),
                                   maskA[0][:, mi * 512:(mi + 1) * 512], ALU.add)

        def emit_scoreA(h, kt, qt, pb):
            ctx.group("pe", [lambda E: s_matmulA(E, h, kt, qt, pb)],
                      reads=(KTA[1], QTA[1]), writes=(b_bank[pb],))
            if 0 <= kt * 128 - qt * 512 <= 384:
                ctx.op("dve", lambda E: mask_addA(E, kt, qt, pb),
                       reads=(b_bank[pb], maskA[1]), writes=(b_bank[pb],))

        stats_par = [0]

        def stats_row(tiles, partials, col, mrow, mbuf, srow, sbuf, nrow, nbuf,
                      emit_tile, emit_mask, mm_reads, mask_buf,
                      scrM, scrS, expbuf, ncol=None):
            if ncol is None:
                ncol = col
            nl = len(tiles)
            chunks = [tiles[0:4]] if nl <= 4 else [tiles[0:4], tiles[4:]]
            info = []
            for ci, tch in enumerate(chunks):
                bb = 4 * stats_par[0]
                stats_par[0] ^= 1
                bnk = tuple(b_bank[bb + i] for i in range(len(tch)))
                ctx.group("pe",
                          [(lambda E, t=t, i=i, bb=bb: emit_tile(E, t, bb + i))
                           for i, t in enumerate(tch)],
                          reads=mm_reads, writes=bnk)
                for i, t in enumerate(tch):
                    if t in partials:
                        ctx.op("dve",
                               lambda E, t=t, i=i, bb=bb: emit_mask(E, t,
                                                                    bb + i),
                               reads=(b_bank[bb + i], mask_buf),
                               writes=(b_bank[bb + i],))
                w = len(tch) * 512
                if len(chunks) == 1:
                    ctx.op("dve",
                           lambda E, w=w, bb=bb, col=col: E.reduce_max(
                               mrow[0][:, col:col + 1],
                               PS[:, bb * 512: bb * 512 + w], axis=AX.X),
                           reads=bnk + (mbuf,), writes=(mbuf,))
                else:
                    scr = scrM[ci]
                    ctx.op("dve",
                           lambda E, w=w, bb=bb, scr=scr: E.reduce_max(
                               scr[0], PS[:, bb * 512: bb * 512 + w],
                               axis=AX.X),
                           reads=bnk, writes=(scr[1],))
                info.append((bb, bnk, w))
            if len(chunks) == 2:
                ctx.op("dve",
                       lambda E, col=col: E.tensor_tensor(
                           mrow[0][:, col:col + 1], scrM[0][0], scrM[1][0],
                           ALU.max),
                       reads=(scrM[0][1], scrM[1][1], mbuf), writes=(mbuf,))
            ctx.op("dve",
                   lambda E, col=col, ncol=ncol: E.tensor_scalar(
                       nrow[0][:, ncol:ncol + 1], mrow[0][:, col:col + 1], -1.0,
                       None, ALU.mult),
                   reads=(mbuf,), writes=(nbuf,))
            if len(chunks) == 1:
                bb, bnk, w = info[0]
                ctx.op("act",
                       lambda E, w=w, bb=bb, col=col, ncol=ncol: E.activation(
                           expbuf[0][:, 0:w], PS[:, bb * 512: bb * 512 + w],
                           AF.Exp, bias=nrow[0][:, ncol:ncol + 1],
                           accum_out=srow[0][:, col:col + 1]),
                       reads=bnk + (nbuf, sbuf), writes=(expbuf[1], sbuf))
            else:
                for ci, (bb, bnk, w) in enumerate(info):
                    scr = scrS[ci]
                    ctx.op("act",
                           lambda E, w=w, bb=bb, col=col, ncol=ncol, scr=scr: E.activation(
                               expbuf[0][:, 0:w], PS[:, bb * 512: bb * 512 + w],
                               AF.Exp, bias=nrow[0][:, ncol:ncol + 1],
                               accum_out=scr[0]),
                           reads=bnk + (nbuf,), writes=(expbuf[1], scr[1]))
                ctx.op("dve",
                       lambda E, col=col: E.tensor_tensor(
                           srow[0][:, col:col + 1], scrS[0][0], scrS[1][0],
                           ALU.add),
                       reads=(scrS[0][1], scrS[1][1], sbuf), writes=(sbuf,))

        for h in range(NH):
            for kt in range(32):
                col = h * 32 + kt
                qts = [qt for qt in range(8) if kt * 128 - qt * 512 <= 384]
                parts = {qt for qt in qts if 0 <= kt * 128 - qt * 512 <= 384}
                stats_row(qts, parts, col, mA, mA[1], sA_, sA_[1], nbA, nbA[1],
                          lambda E, qt, pb, h=h, kt=kt: s_matmulA(E, h, kt, qt,
                                                                  pb),
                          lambda E, qt, pb, kt=kt: mask_addA(E, kt, qt, pb),
                          (KTA[1], QTA[1]), maskA[1],
                          (scr1A, scrHA), (scrS1, scrS2), expA)
            c0 = h * 32
            ctx.op("act",
                   lambda E, c0=c0: E.activation(scr32A[0],
                                                 sA_[0][:, c0:c0 + 32], AF.Ln),
                   reads=(sA_[1],), writes=(scr32A[1],))
            ctx.op("dve",
                   lambda E, c0=c0: E.tensor_tensor(
                       scr32A[0], scr32A[0], mA[0][:, c0:c0 + 32], ALU.add),
                   reads=(scr32A[1], mA[1]), writes=(scr32A[1],))
            ctx.op("dve",
                   lambda E, c0=c0: E.tensor_scalar(
                       nbA[0][:, c0:c0 + 32], scr32A[0], -1.0, None, ALU.mult),
                   reads=(scr32A[1],), writes=(nbA[1],))
            for qp in range(4):
                qa, qb = 2 * qp, 2 * qp + 1
                ncom = 4 * qa + 4
                ntot = ncom + 4

                def z_emit(kt, par, both, h=h, qa=qa, qb=qb, ncom=ncom,
                           ntot=ntot):
                    if both:
                        ctx.group("pe",
                                  [lambda E, h=h, kt=kt, par=par: E.matmul(
                                      bank(2, 64),
                                      VA[0][:, kt * HF + h * 64:
                                            kt * HF + (h + 1) * 64],
                                      alpP[par][0][:, 0:512],
                                      start=(kt == 0), stop=(kt == ncom - 1))],
                                  reads=(VA[1], alpP[par][1]),
                                  writes=(b_bank[2],))
                        ctx.group("pe",
                                  [lambda E, h=h, kt=kt, par=par: E.matmul(
                                      bank(3, 64),
                                      VA[0][:, kt * HF + h * 64:
                                            kt * HF + (h + 1) * 64],
                                      alpP[par][0][:, 512:1024],
                                      start=(kt == 0), stop=False)],
                                  reads=(VA[1], alpP[par][1]),
                                  writes=(b_bank[3],))
                    else:
                        ctx.group("pe",
                                  [lambda E, h=h, kt=kt, par=par: E.matmul(
                                      bank(3, 64),
                                      VA[0][:, kt * HF + h * 64:
                                            kt * HF + (h + 1) * 64],
                                      alpP[par][0][:, 0:512],
                                      start=False, stop=(kt == ntot - 1))],
                                  reads=(VA[1], alpP[par][1]),
                                  writes=(b_bank[3],))

                pend = None
                for kt in range(ntot):
                    col = h * 32 + kt
                    par = kt % 2
                    pb = 4 + 2 * par
                    if kt < ncom:
                        ctx.group("pe",
                                  [lambda E, h=h, kt=kt, qa=qa, pb=pb:
                                   s_matmulA(E, h, kt, qa, pb),
                                   lambda E, h=h, kt=kt, qb=qb, pb=pb:
                                   s_matmulA(E, h, kt, qb, pb + 1)],
                                  reads=(KTA[1], QTA[1]),
                                  writes=(b_bank[pb], b_bank[pb + 1]))
                        if kt >= 4 * qa:
                            ctx.op("dve",
                                   lambda E, kt=kt, qa=qa, pb=pb: mask_addA(
                                       E, kt, qa, pb),
                                   reads=(b_bank[pb], maskA[1]),
                                   writes=(b_bank[pb],))
                        ctx.op("act",
                               lambda E, col=col, pb=pb, par=par: E.activation(
                                   alpP[par][0], PS[:, pb * 512:(pb + 2) * 512],
                                   AF.Exp, bias=nbA[0][:, col:col + 1]),
                               reads=(b_bank[pb], b_bank[pb + 1], nbA[1]),
                               writes=(alpP[par][1],))
                    else:
                        ctx.group("pe",
                                  [lambda E, h=h, kt=kt, qb=qb, pb=pb:
                                   s_matmulA(E, h, kt, qb, pb)],
                                  reads=(KTA[1], QTA[1]), writes=(b_bank[pb],))
                        ctx.op("dve",
                               lambda E, kt=kt, qb=qb, pb=pb: mask_addA(
                                   E, kt, qb, pb),
                               reads=(b_bank[pb], maskA[1]),
                               writes=(b_bank[pb],))
                        ctx.op("act",
                               lambda E, col=col, pb=pb, par=par: E.activation(
                                   alpP[par][0][:, 0:512], bank(pb), AF.Exp,
                                   bias=nbA[0][:, col:col + 1]),
                               reads=(b_bank[pb], nbA[1]),
                               writes=(alpP[par][1],))
                    if pend is not None:
                        pend()
                    pend = (lambda kt=kt, par=par, both=(kt < ncom):
                            z_emit(kt, par, both))
                pend()
                for sb, (qq, zb) in enumerate(((qa, 2), (qb, 3))):
                    ctx.op("dve",
                           lambda E, zb=zb, sb=sb: E.tensor_copy(
                               zsegA[sb][0], bank(zb, 64)),
                           reads=(b_bank[zb],), writes=(zsegA[sb][1],))
                    ctx.dma_cond("dpool",
                                 lambda E, hh, h=h, qq=qq, sb=sb: E.dma_start(
                                     out=sh["Z0"][hh * 512 + h * 64:
                                                  hh * 512 + (h + 1) * 64,
                                                  qq * 512:(qq + 1) * 512],
                                     in_=zsegA[sb][0]),
                                 reads=(zsegA[sb][1],), writes=(b_sh["Z0"],))
        send_flag(0, (b_sh["Z0"],))

        AR.reset()
        poll(0)
        zbuf2P = [AR.take(128, 4096, BF16) for _ in range(2)]
        xbuf2P = [AR.take(128, 4096, F32) for _ in range(2)]
        woA = AR.take(128, 8192, BF16)
        t1A = [AR.take(128, 512, BF16) for _ in range(2)]
        e_load(ctx, woA[0].rearrange("p (c f) -> p c f", c=8),
               woT.rearrange("(c p) f -> p c f", p=128), woA[1])
        Z0r = sh["Z0"].rearrange("(c p) t -> p c t", p=128)
        xThr = xTh.rearrange("(c p) t -> p c t", p=128)
        x1dr = x1b_d.rearrange("(c p) t -> c p t", p=128)
        x1bdr = x1bb_d.rearrange("(c p) t -> c p t", p=128)
        for tt in range(4):
            tsl = slice(tt * 512, (tt + 1) * 512)
            zbuf2 = zbuf2P[tt % 2]
            xbuf2 = xbuf2P[tt % 2]
            ctx.dma_cond("dpool",
                    lambda E, hh, tt=tt, zbuf2=zbuf2: E.dma_start(
                        out=zbuf2[0].rearrange("p (c t) -> p c t", c=8),
                        in_=Z0r[:, :, hh * TH + tt * 512: hh * TH + (tt + 1) * 512]),
                    reads=(b_phase[0], b_sh["Z0"]), writes=(zbuf2[1],))
            e_load(ctx, xbuf2[0].rearrange("p (c t) -> p c t", c=8),
                   xThr[:, :, tsl], xbuf2[1], q="dsp")
            for fc in range(8):
                pb = fc % 2
                ctx.group("pe",
                          [(lambda E, dc=dc, fc=fc, pb=pb, zbuf2=zbuf2:
                            E.matmul(
                              bank(pb), woA[0][:, dc * 1024 + fc * 128: dc * 1024 + fc * 128 + 128],
                              zbuf2[0][:, dc * 512:(dc + 1) * 512],
                              start=(dc == 0), stop=(dc == 7)))
                           for dc in range(8)],
                          reads=(woA[1], zbuf2[1]), writes=(b_bank[pb],))
                ctx.op("dve",
                       lambda E, fc=fc, pb=pb, xbuf2=xbuf2:
                       E.scalar_tensor_tensor(
                           xbuf2[0][:, fc * 512:(fc + 1) * 512], bank(pb),
                           smalls["bo"][:, fc:fc + 1],
                           xbuf2[0][:, fc * 512:(fc + 1) * 512],
                           ALU.add, ALU.add),
                       reads=(b_bank[pb], b_small["bo"], xbuf2[1]),
                       writes=(xbuf2[1],))
            x1tiles = [xbuf2[0][:, fc * 512:(fc + 1) * 512] for fc in range(8)]

            def apply1(bcA, bcB, b_bc, tt=tt, tsl=tsl, x1tiles=x1tiles):
                for fc in range(8):
                    ctx.op("dve",
                           lambda E, fc=fc, bcA=bcA: E.tensor_tensor(
                               x1tiles[fc], x1tiles[fc], bcA, ALU.mult),
                           reads=(xbuf2[1], b_bc), writes=(xbuf2[1],))
                    ctx.op("dve",
                           lambda E, fc=fc, bcB=bcB: E.tensor_tensor(
                               x1tiles[fc], x1tiles[fc], bcB, ALU.subtract),
                           reads=(xbuf2[1], b_bc), writes=(xbuf2[1],))
                    ctx.op("dve",
                           lambda E, fc=fc: E.tensor_copy(t1A[fc % 2][0],
                                                          x1tiles[fc]),
                           reads=(xbuf2[1],), writes=(t1A[fc % 2][1],))
                    e_store(ctx, x1dr[fc, :, tsl], x1tiles[fc], xbuf2[1],
                            dstbufs=(b_d["x1b"],))
                    e_store(ctx, x1bdr[fc, :, tsl], t1A[fc % 2][0],
                            t1A[fc % 2][1], dstbufs=(b_d["x1bb"],), q="dact")

            ln_block(ctx, scr, x1tiles, [xbuf2[1]] * 8, [apply1], 512, LN_EPS,
                     ones_col, ones_row, b_ones, bank(2), bank(3),
                     b_bank[2], b_bank[3])

        AR.reset()
        xinB = AR.take(128, 4096, BF16)
        hAB = AR.take(128, 32 * 512, BF16)
        hBB = AR.take(128, 32 * 512, BF16)
        wbB = [AR.take(128, 4096, BF16) for _ in range(2)]
        rbB = AR.take(128, 512, F32)
        t1B = [AR.take(128, 512, BF16) for _ in range(2)]
        x2bB = AR.take(128, 4096, F32)
        x2hdr = x2h_d.rearrange("(c p) t -> c p t", p=128)
        for q4 in range(4):
            gsl = slice(q4 * 512, (q4 + 1) * 512)
            e_load(ctx, xinB[0].rearrange("p (c t) -> p c t", c=8),
                   x1bdr[:, :, gsl].rearrange("c p t -> p c t"), xinB[1],
                   srcbufs=(b_d["x1bb"],), q="dsp")
            for ft in range(32):
                wb = ft % 2
                e_load(ctx, wbB[wb][0][:, 0:1024].rearrange("p (c f) -> p c f", c=8),
                       w1b_d[:, ft * 128:(ft + 1) * 128].rearrange(
                           "(c p) f -> p c f", p=128), wbB[wb][1],
                       srcbufs=(b_wcast,), q="dsp")
                e_load(ctx, wbB[wb][0][:, 1024:2048].rearrange("p (c f) -> p c f", c=8),
                       w2b_d[:, ft * 128:(ft + 1) * 128].rearrange(
                           "(c p) f -> p c f", p=128), wbB[wb][1],
                       srcbufs=(b_wcast,), q="dact")
                pb = ft % 2
                ctx.group("pe",
                          [(lambda E, dc=dc, wb=wb, pb=pb: E.matmul(
                              bank(pb), wbB[wb][0][:, dc * 128:(dc + 1) * 128],
                              xinB[0][:, dc * 512:(dc + 1) * 512],
                              start=(dc == 0), stop=(dc == 7)))
                           for dc in range(8)],
                          reads=(wbB[wb][1], xinB[1]), writes=(b_bank[pb],))
                ctx.group("pe",
                          [(lambda E, dc=dc, wb=wb, pb=pb: E.matmul(
                              bank(2 + pb),
                              wbB[wb][0][:, 1024 + dc * 128: 1024 + (dc + 1) * 128],
                              xinB[0][:, dc * 512:(dc + 1) * 512],
                              start=(dc == 0), stop=(dc == 7)))
                           for dc in range(8)],
                          reads=(wbB[wb][1], xinB[1]), writes=(b_bank[2 + pb],))
                ctx.op("act",
                       lambda E, ft=ft, pb=pb: E.activation(
                           t1B[pb][0], bank(pb), AF.Silu,
                           scale=smalls["s1"][:, ft:ft + 1]),
                       reads=(b_bank[pb], b_small["s1"]), writes=(t1B[pb][1],))
                ctx.op("dve",
                       lambda E, ft=ft, pb=pb: E.scalar_tensor_tensor(
                           hAB[0][:, ft * 512:(ft + 1) * 512],
                           bank(2 + pb), smalls["s2"][:, ft:ft + 1],
                           t1B[pb][0], ALU.mult, ALU.mult),
                       reads=(b_bank[2 + pb], b_small["s2"], t1B[pb][1]),
                       writes=(hAB[1],))
            for (hTd, ssc, src, dst) in ((h1b_d, "sh1", hAB, hBB),
                                         (h2b_d, "sh2", hBB, hAB)):
                for ft in range(32):
                    wb = ft % 2
                    e_load(ctx, wbB[wb][0].rearrange("p (c f) -> p c f", c=32),
                           hTd[:, ft * 128:(ft + 1) * 128].rearrange(
                               "(c p) f -> p c f", p=128), wbB[wb][1],
                           srcbufs=(b_wcast,), q="dsp")
                    pb = ft % 2
                    ctx.group("pe",
                              [(lambda E, dc=dc, wb=wb, src=src, pb=pb:
                                E.matmul(
                                    bank(pb), wbB[wb][0][:, dc * 128:(dc + 1) * 128],
                                    src[0][:, dc * 512:(dc + 1) * 512],
                                    start=(dc == 0), stop=(dc == 31)))
                               for dc in range(32)],
                              reads=(wbB[wb][1], src[1]), writes=(b_bank[pb],))
                    ctx.op("act",
                           lambda E, ft=ft, ssc=ssc, dst=dst, pb=pb:
                           E.activation(
                               dst[0][:, ft * 512:(ft + 1) * 512],
                               bank(pb), AF.Gelu,
                               scale=smalls[ssc][:, ft:ft + 1]),
                           reads=(b_bank[pb], b_small[ssc]), writes=(dst[1],))
            for fc in range(8):
                wb = fc % 2
                e_load(ctx, wbB[wb][0].rearrange("p (c f) -> p c f", c=32),
                       w3b_d[:, fc * 128:(fc + 1) * 128].rearrange(
                           "(c p) f -> p c f", p=128), wbB[wb][1],
                       srcbufs=(b_wcast,), q="dsp")
                e_load(ctx, rbB[0], x1dr[fc, :, gsl], rbB[1],
                       srcbufs=(b_d["x1b"],), q="dact")
                pb = 4 + fc % 2
                ctx.group("pe",
                          [(lambda E, dc=dc, wb=wb, pb=pb: E.matmul(
                              bank(pb), wbB[wb][0][:, dc * 128:(dc + 1) * 128],
                              hAB[0][:, dc * 512:(dc + 1) * 512],
                              start=(dc == 0), stop=(dc == 31)))
                           for dc in range(32)],
                          reads=(wbB[wb][1], hAB[1]), writes=(b_bank[pb],))
                ctx.op("dve",
                       lambda E, fc=fc, pb=pb: E.scalar_tensor_tensor(
                           x2bB[0][:, fc * 512:(fc + 1) * 512], bank(pb),
                           smalls["s3"][:, fc:fc + 1], rbB[0],
                           ALU.mult, ALU.add),
                       reads=(b_bank[pb], b_small["s3"], rbB[1]),
                       writes=(x2bB[1],))
            x2tiles = [x2bB[0][:, fc * 512:(fc + 1) * 512] for fc in range(8)]

            def apply2(bcA, bcB, b_bc, gsl=gsl, x2tiles=x2tiles):
                for fc in range(8):
                    ctx.op("dve",
                           lambda E, fc=fc, bcA=bcA: E.tensor_tensor(
                               x2tiles[fc], x2tiles[fc], bcA, ALU.mult),
                           reads=(x2bB[1], b_bc), writes=(x2bB[1],))
                    ctx.op("dve",
                           lambda E, fc=fc, bcB=bcB: E.tensor_tensor(
                               x2tiles[fc], x2tiles[fc], bcB, ALU.subtract),
                           reads=(x2bB[1], b_bc), writes=(x2bB[1],))
                    ctx.op("dve",
                           lambda E, fc=fc: E.tensor_copy(t1B[fc % 2][0],
                                                          x2tiles[fc]),
                           reads=(x2bB[1],), writes=(t1B[fc % 2][1],))
                    e_store(ctx, x2hdr[fc, :, gsl], x2tiles[fc], x2bB[1],
                            dstbufs=(b_d["x2h"],))
                    ctx.dma_cond("dpool",
                            lambda E, hh, fc=fc, gsl=gsl, w=fc % 2: E.dma_start(
                                out=sh["X2"].rearrange(
                                    "(c p) t -> c p t", p=128)[fc][
                                    :, hh * TH + gsl.start: hh * TH + gsl.start + 512],
                                in_=t1B[w][0]),
                            reads=(t1B[fc % 2][1],), writes=(b_sh["X2"],))

            ln_block(ctx, scr, x2tiles, [x2bB[1]] * 8, [apply2], 512, LN_EPS,
                     ones_col, ones_row, b_ones, bank(6), bank(7),
                     b_bank[6], b_bank[7])
        send_flag(1, (b_sh["X2"],))

        AR.reset()
        poll(1)
        xbcC = [AR.take(128, 512, BF16) for _ in range(2)]
        wq1C = AR.take(128, 8 * HF, BF16)
        wkvC = AR.take(128, 8 * 128, BF16)
        mqkC = AR.take(128, 2048, F32)
        mkqC = AR.take(128, 2048, F32)
        QT1 = AR.take(64, NH * S, BF16)
        KTa1 = AR.take(128, S, BF16)
        V1 = AR.take(128, 32 * DH, BF16)
        QTa1 = AR.take(128, S, BF16)
        mq1 = AR.take(128, NH * 32, F32)
        sq1 = AR.take(128, NH * 32, F32)
        nbc1 = AR.take(128, 32, F32)
        tp1 = AR.take(32, 128, F32)
        nbrow1 = AR.take(1, S, F32)
        nbbf1 = AR.take(1, S, BF16)
        corr1 = AR.take(1, S, F32)
        corrB1 = AR.take(128, 512, F32)
        alp2C = [AR.take(128, 1024, BF16) for _ in range(2)]
        exp1 = AR.take(128, 2048, BF16)
        scr1C = AR.take(128, 1, F32)
        scrHC = AR.take(128, 1, F32)
        scrSC1 = AR.take(128, 1, F32)
        scrSC2 = AR.take(128, 1, F32)
        scr32C = AR.take(128, 32, F32)
        zsegC = [AR.take(64, 512, BF16) for _ in range(2)]

        e_load(ctx, mqkC[0], masksQK, mqkC[1])
        e_load(ctx, mkqC[0], masksKQ, mkqC[1])
        e_load(ctx, wq1C[0].rearrange("p (c f) -> p c f", c=8),
               wq1.rearrange("(c p) f -> p c f", p=128), wq1C[1])
        e_load(ctx, wkvC[0].rearrange("p (c f) -> p c f", c=8),
               wkv.rearrange("(c p) f -> p c f", p=128), wkvC[1])
        ctx.op("dve", lambda E: E.tensor_scalar(bqsc[:], smalls["bqm"][:], 0.125,
                                                None, ALU.mult),
               reads=(b_small["bqm"],), writes=(b_bqsc,))
        ctx.op("dve", lambda E: E.memset(KTa1[0][64:65, :], 1.0),
               writes=(KTa1[1],))
        ctx.group("pe", [lambda E: E.matmul(PS[:, 0:DH], ones_row, bvrow1,
                                            start=True, stop=True)],
                  reads=(b_bvrow1, b_ones), writes=(b_bank[0],))
        ctx.op("dve", lambda E: E.tensor_copy(bvB1[:], PS[:, 0:DH]),
               reads=(b_bank[0],), writes=(b_bvB1,))

        X2r = sh["X2"].rearrange("(c p) t -> c p t", p=128)
        for tt in range(8):
            for dc in range(8):
                xb = xbcC[dc % 2][1]
                e_load(ctx, xbcC[dc % 2][0],
                       X2r[dc][:, tt * 512:(tt + 1) * 512], xb,
                       srcbufs=(b_phase[1], b_sh["X2"]), q="dsp")
                for fc in range(4):
                    ctx.group("pe",
                              [(lambda E, fc=fc, dc=dc: E.matmul(
                                  bank(fc),
                                  wq1C[0][:, dc * HF + fc * 128: dc * HF + fc * 128 + 128],
                                  xbcC[dc % 2][0],
                                  start=(dc == 0), stop=(dc == 7)))],
                              reads=(wq1C[1], xb), writes=(b_bank[fc],))
                ctx.group("pe",
                          [(lambda E, dc=dc: E.matmul(
                              bank(4), wkvC[0][:, dc * 128:(dc + 1) * 128],
                              xbcC[dc % 2][0],
                              start=(dc == 0), stop=(dc == 7)))],
                          reads=(wkvC[1], xb), writes=(b_bank[4],))
            for fc in range(4):
                for half in range(2):
                    hh_ = 2 * fc + half
                    p0_ = half * 64
                    ctx.op("act",
                           lambda E, fc=fc, tt=tt, hh_=hh_, p0_=p0_: E.activation(
                               QT1[0][0:64, hh_ * S + tt * 512: hh_ * S + (tt + 1) * 512],
                               bank(fc)[p0_:p0_ + 64, :], AF.Identity, scale=0.125,
                               bias=bqsc[p0_:p0_ + 64, fc:fc + 1]),
                           reads=(b_bank[fc], b_bqsc), writes=(QT1[1],))
            ctx.op("act",
                   lambda E, tt=tt: E.activation(
                       KTa1[0][0:64, tt * 512:(tt + 1) * 512], bank(4)[0:64, :],
                       AF.Identity, bias=smalls["bkv"][0:64, 0:1]),
                   reads=(b_bank[4], b_small["bkv"]), writes=(KTa1[1],))
        for tt in range(8):
            for dc in range(8):
                xb = xbcC[dc % 2][1]
                e_load(ctx, xbcC[dc % 2][0],
                       X2r[dc][:, tt * 512:(tt + 1) * 512], xb,
                       srcbufs=(b_sh["X2"],), q="dsp")
                for sub in range(4):
                    ctx.group("pe",
                              [(lambda E, dc=dc, sub=sub: E.matmul(
                                  PS[:, sub * 512: sub * 512 + DH],
                                  xbcC[dc % 2][0][:, sub * 128:(sub + 1) * 128],
                                  wkvC[0][:, dc * 128 + 64:(dc + 1) * 128],
                                  start=(dc == 0), stop=(dc == 7)))],
                              reads=(wkvC[1], xb), writes=(b_bank[sub],))
            for sub in range(4):
                ti = tt * 4 + sub
                ctx.op("dve",
                       lambda E, ti=ti, sub=sub: E.tensor_tensor(
                           V1[0][:, ti * DH:(ti + 1) * DH],
                           PS[:, sub * 512: sub * 512 + DH], bvB1[:], ALU.add),
                       reads=(b_bank[sub], b_bvB1), writes=(V1[1],))

        for h in range(NH):
            for qt in range(32):
                col = h * 32 + qt
                nl = qt // 4 + 1
                mi_last = ((nl - 1) * 512 - qt * 128 + 384) // 128
                stats_row(list(range(nl)), {nl - 1}, col, mq1, mq1[1],
                          sq1, sq1[1], nbc1, nbc1[1],
                          lambda E, kt, pb, h=h, qt=qt: E.matmul(
                              bank(pb),
                              QT1[0][0:64,
                                     h * S + qt * 128: h * S + qt * 128 + 128],
                              KTa1[0][0:64, kt * 512:(kt + 1) * 512],
                              start=True, stop=True),
                          lambda E, kt, pb, mi=mi_last: E.tensor_tensor(
                              bank(pb), bank(pb),
                              mqkC[0][:, mi * 512:(mi + 1) * 512], ALU.add),
                          (QT1[1], KTa1[1]), mqkC[1],
                          (scr1C, scrHC), (scrSC1, scrSC2), exp1, ncol=qt)
            c0 = h * 32
            ctx.op("act",
                   lambda E, c0=c0: E.activation(scr32C[0],
                                                 sq1[0][:, c0:c0 + 32], AF.Ln),
                   reads=(sq1[1],), writes=(scr32C[1],))
            ctx.op("dve",
                   lambda E, c0=c0: E.tensor_tensor(
                       scr32C[0], scr32C[0], mq1[0][:, c0:c0 + 32], ALU.add),
                   reads=(scr32C[1], mq1[1]), writes=(scr32C[1],))
            ctx.op("dve",
                   lambda E: E.tensor_scalar(
                       nbc1[0][:, 0:32], scr32C[0], -1.0, None, ALU.mult),
                   reads=(scr32C[1],), writes=(nbc1[1],))
            ctx.group("pe", [lambda E, h=h: E.transpose(PS[0:32, 0:128],
                                                        nbc1[0][:, 0:32], id_s)],
                      reads=(nbc1[1], b_id), writes=(b_bank[0],))
            ctx.op("dve", lambda E: E.tensor_copy(tp1[0], PS[0:32, 0:128]),
                   reads=(b_bank[0],), writes=(tp1[1],))
            e_store(ctx, nb_d[h], tp1[0], tp1[1], dstbufs=(b_d["nb"],))
            e_load(ctx, nbrow1[0],
                   nb_d[h].rearrange("a b -> (a b)").unsqueeze(0), nbrow1[1],
                   srcbufs=(b_d["nb"],))
            ctx.op("dve", lambda E: E.tensor_copy(nbbf1[0], nbrow1[0]),
                   reads=(nbrow1[1],), writes=(nbbf1[1],))
            ctx.op("dve", lambda E: E.tensor_copy(corr1[0], nbbf1[0]),
                   reads=(nbbf1[1],), writes=(corr1[1],))
            ctx.op("dve", lambda E: E.tensor_tensor(corr1[0], nbrow1[0],
                                                    corr1[0], ALU.subtract),
                   reads=(nbrow1[1], corr1[1]), writes=(corr1[1],))
            ctx.op("act", lambda E: E.activation(corr1[0], corr1[0], AF.Exp),
                   reads=(corr1[1],), writes=(corr1[1],))
            ctx.op("dve",
                   lambda E, h=h: E.tensor_copy(QTa1[0][0:64, :],
                                                QT1[0][0:64, h * S:(h + 1) * S]),
                   reads=(QT1[1],), writes=(QTa1[1],))
            ctx.op("dve", lambda E: E.tensor_copy(QTa1[0][64:65, :], nbbf1[0]),
                   reads=(nbbf1[1],), writes=(QTa1[1],))
            for qt5 in range(8):
                zb = 2 + qt5 % 2
                nlive = 4 * qt5 + 4
                ctx.group("pe",
                          [lambda E, qt5=qt5: E.matmul(
                              bank(1), ones_rowF,
                              corr1[0][:, qt5 * 512:(qt5 + 1) * 512],
                              start=True, stop=True)],
                          reads=(corr1[1], b_ones), writes=(b_bank[1],))
                ctx.op("dve", lambda E: E.tensor_copy(corrB1[0], bank(1)),
                       reads=(b_bank[1],), writes=(corrB1[1],))
                pend = None
                for j in range(nlive // 2):
                    k0 = 2 * j
                    par = j % 2
                    pb = 4 + 2 * par
                    ctx.group("pe",
                              [(lambda E, kt=k0, qt5=qt5, pb=pb: E.matmul(
                                  bank(pb),
                                  KTa1[0][0:65, kt * 128: kt * 128 + 128],
                                  QTa1[0][0:65, qt5 * 512:(qt5 + 1) * 512],
                                  start=True, stop=True)),
                               (lambda E, kt=k0 + 1, qt5=qt5, pb=pb: E.matmul(
                                  bank(pb + 1),
                                  KTa1[0][0:65, kt * 128: kt * 128 + 128],
                                  QTa1[0][0:65, qt5 * 512:(qt5 + 1) * 512],
                                  start=True, stop=True))],
                              reads=(KTa1[1], QTa1[1]),
                              writes=(b_bank[pb], b_bank[pb + 1]))
                    for half in range(2):
                        Dv = (k0 + half) * 128 - qt5 * 512
                        if 0 <= Dv <= 384:
                            ctx.op("dve",
                                   lambda E, mi=Dv // 128, pb=pb + half:
                                   E.tensor_tensor(
                                       bank(pb), bank(pb),
                                       mkqC[0][:, mi * 512:(mi + 1) * 512],
                                       ALU.add),
                                   reads=(b_bank[pb + half], mkqC[1]),
                                   writes=(b_bank[pb + half],))
                    ctx.op("act",
                           lambda E, pb=pb, par=par: E.activation(
                               alp2C[par][0], PS[:, pb * 512:(pb + 2) * 512],
                               AF.Exp),
                           reads=(b_bank[pb], b_bank[pb + 1]),
                           writes=(alp2C[par][1],))
                    if pend is not None:
                        pend()

                    def mk_z(k0=k0, par=par, nlive=nlive, zb=zb):
                        def f():
                            ctx.group("pe",
                                      [(lambda E, kt=k0, par=par, zb=zb:
                                        E.matmul(
                                            bank(zb, 64),
                                            V1[0][:, kt * DH:(kt + 1) * DH],
                                            alp2C[par][0][:, 0:512],
                                            start=(kt == 0), stop=False)),
                                       (lambda E, kt=k0 + 1, par=par, zb=zb,
                                        nlive=nlive: E.matmul(
                                            bank(zb, 64),
                                            V1[0][:, kt * DH:(kt + 1) * DH],
                                            alp2C[par][0][:, 512:1024],
                                            start=False,
                                            stop=(kt == nlive - 1)))],
                                      reads=(V1[1], alp2C[par][1]),
                                      writes=(b_bank[zb],))
                        return f
                    pend = mk_z()
                pend()
                sb = qt5 % 2
                ctx.op("dve",
                       lambda E, zb=zb, sb=sb: E.tensor_tensor(
                           zsegC[sb][0], bank(zb, 64), corrB1[0][0:64, :],
                           ALU.mult),
                       reads=(b_bank[zb], corrB1[1]), writes=(zsegC[sb][1],))
                ctx.dma_cond("dpool",
                        lambda E, hh, h=h, qt5=qt5, sb=sb: E.dma_start(
                            out=sh["Z1"][hh * 512 + h * 64: hh * 512 + (h + 1) * 64,
                                         qt5 * 512:(qt5 + 1) * 512],
                            in_=zsegC[sb][0]),
                        reads=(zsegC[sb][1],), writes=(b_sh["Z1"],))
        send_flag(2, (b_sh["Z1"],))

        AR.reset()
        poll(2)
        zbuf4P = [AR.take(128, 4096, BF16) for _ in range(2)]
        xbuf4P = [AR.take(128, 4096, F32) for _ in range(2)]
        waoA = AR.take(128, 8192, BF16)
        t1D = [AR.take(128, 512, BF16) for _ in range(2)]
        e_load(ctx, waoA[0].rearrange("p (c f) -> p c f", c=8),
               wao.rearrange("(c p) f -> p c f", p=128), waoA[1])
        Z1r = sh["Z1"].rearrange("(c p) t -> p c t", p=128)
        x1mdr = x1mh_d.rearrange("(c p) t -> c p t", p=128)
        for tt in range(4):
            tsl = slice(tt * 512, (tt + 1) * 512)
            zbuf4 = zbuf4P[tt % 2]
            xbuf4 = xbuf4P[tt % 2]
            ctx.dma_cond("dpool",
                    lambda E, hh, tt=tt, zbuf4=zbuf4: E.dma_start(
                        out=zbuf4[0].rearrange("p (c t) -> p c t", c=8),
                        in_=Z1r[:, :, hh * TH + tt * 512: hh * TH + (tt + 1) * 512]),
                    reads=(b_phase[2], b_sh["Z1"]), writes=(zbuf4[1],))
            e_load(ctx, xbuf4[0].rearrange("p (c t) -> p c t", c=8),
                   x2hdr[:, :, tsl].rearrange("c p t -> p c t"), xbuf4[1],
                   srcbufs=(b_d["x2h"],), q="dsp")
            for fc in range(8):
                pb = fc % 2
                ctx.group("pe",
                          [(lambda E, dc=dc, fc=fc, pb=pb, zbuf4=zbuf4:
                            E.matmul(
                              bank(pb), waoA[0][:, dc * 1024 + fc * 128: dc * 1024 + fc * 128 + 128],
                              zbuf4[0][:, dc * 512:(dc + 1) * 512],
                              start=(dc == 0), stop=(dc == 7)))
                           for dc in range(8)],
                          reads=(waoA[1], zbuf4[1]), writes=(b_bank[pb],))
                ctx.op("dve",
                       lambda E, fc=fc, pb=pb, xbuf4=xbuf4:
                       E.scalar_tensor_tensor(
                           xbuf4[0][:, fc * 512:(fc + 1) * 512], bank(pb),
                           smalls["bao"][:, fc:fc + 1],
                           xbuf4[0][:, fc * 512:(fc + 1) * 512],
                           ALU.add, ALU.add),
                       reads=(b_bank[pb], b_small["bao"], xbuf4[1]),
                       writes=(xbuf4[1],))
                ctx.op("dve",
                       lambda E, fc=fc, xbuf4=xbuf4: E.tensor_copy(
                           t1D[fc % 2][0], xbuf4[0][:, fc * 512:(fc + 1) * 512]),
                       reads=(xbuf4[1],), writes=(t1D[fc % 2][1],))
                e_store(ctx, x1mdr[fc, :, tsl],
                        xbuf4[0][:, fc * 512:(fc + 1) * 512], xbuf4[1],
                        dstbufs=(b_d["x1mh"],))
                ctx.dma_cond("dpool",
                        lambda E, hh, fc=fc, tt=tt, w=fc % 2: E.dma_start(
                            out=sh["X1M"].rearrange("(c p) t -> c p t", p=128)[fc][
                                :, hh * TH + tt * 512: hh * TH + (tt + 1) * 512],
                            in_=t1D[w][0]),
                        reads=(t1D[fc % 2][1],), writes=(b_sh["X1M"],))
        send_flag(3, (b_sh["X1M"],))

        AR.reset()
        poll(3)
        wuA = AR.take(128, 8192, BF16)
        wrA = AR.take(128, 8192, BF16)
        wxpA = AR.take(128, 8 * 96, BF16)
        x1mtile = [AR.take(128, 512, BF16) for _ in range(2)]
        u0cG = [AR.take(128, 4100, BF16) for _ in range(8)]
        cvD = AR.take(128, 4096, F32)
        ucD = AR.take(128, 4096, BF16)
        ubD = [AR.take(128, 512, BF16) for _ in range(2)]
        svsegD = [AR.take(128, 512, BF16) for _ in range(2)]
        t96D = AR.take(96, 512, F32)
        e_load(ctx, wuA[0].rearrange("p (c f) -> p c f", c=8),
               w_u.rearrange("(c p) f -> p c f", p=128), wuA[1])
        e_load(ctx, wrA[0].rearrange("p (c f) -> p c f", c=8),
               w_r.rearrange("(c p) f -> p c f", p=128), wrA[1])
        e_load(ctx, wxpA[0].rearrange("p (c f) -> p c f", c=8),
               wxp.rearrange("(c p) f -> p c f", p=128), wxpA[1])
        X1Mr = sh["X1M"].rearrange("(c p) t -> c p t", p=128)
        uTdr = uT_d.rearrange("(c p) t -> c p t", p=128)
        svTdr = svT_d.rearrange("(c p) t -> c p t", p=128)
        for g in range(8):
            ctx.op("dve", lambda E, g=g: E.memset(u0cG[g][0][:, 0:3], 0.0),
                   writes=(u0cG[g][1],))
        for tt in range(8):
            for dc in range(8):
                xb = x1mtile[dc % 2][1]
                e_load(ctx, x1mtile[dc % 2][0],
                       X1Mr[dc][:, tt * 512:(tt + 1) * 512], xb,
                       srcbufs=(b_phase[3], b_sh["X1M"]), q="dsp")
                ctx.group("pe",
                          [(lambda E, dc=dc, g=g: E.matmul(
                              bank(g), wuA[0][:, dc * 1024 + g * 128:
                                              dc * 1024 + g * 128 + 128],
                              x1mtile[dc % 2][0],
                              start=(dc == 0), stop=(dc == 7)))
                           for g in range(8)],
                          reads=(wuA[1], xb),
                          writes=tuple(b_bank[g] for g in range(8)))
            for g in range(8):
                ctx.op("dve",
                       lambda E, g=g, tt=tt: E.tensor_copy(
                           u0cG[g][0][:, 3 + tt * 512: 3 + (tt + 1) * 512],
                           bank(g)),
                       reads=(b_bank[g], u0cG[g][1]), writes=(u0cG[g][1],))
        for g in range(8):
            ctx.op("dve",
                   lambda E, g=g: E.tensor_scalar(
                       cvD[0], u0cG[g][0][:, 0:4096],
                       smalls["convw"][:, g * 4: g * 4 + 1], None, ALU.mult),
                   reads=(u0cG[g][1], b_small["convw"]), writes=(cvD[1],))
            for j in range(1, 4):
                ctx.op("dve",
                       lambda E, g=g, j=j: E.scalar_tensor_tensor(
                           cvD[0], u0cG[g][0][:, j: j + 4096],
                           smalls["convw"][:, g * 4 + j: g * 4 + j + 1],
                           cvD[0], ALU.mult, ALU.add),
                       reads=(u0cG[g][1], b_small["convw"], cvD[1]),
                       writes=(cvD[1],))
            ctx.op("act",
                   lambda E, g=g: E.activation(ucD[0], cvD[0], AF.Silu,
                                               bias=smalls["convb"][:, g:g + 1]),
                   reads=(cvD[1], b_small["convb"]), writes=(ucD[1],))
            e_store(ctx, uTdr[g], ucD[0], ucD[1], dstbufs=(b_d["uT"],))
        for tt in range(8):
            tsl = slice(tt * 512, (tt + 1) * 512)
            for g in range(8):
                e_load(ctx, ubD[g % 2][0], uTdr[g][:, tsl], ubD[g % 2][1],
                       srcbufs=(b_d["uT"],))
                ctx.group("pe",
                          [(lambda E, g=g: E.matmul(
                              PS[0:96, 0:512], wxpA[0][:, g * 96:(g + 1) * 96],
                              ubD[g % 2][0],
                              start=(g == 0), stop=(g == 7)))],
                          reads=(wxpA[1], ubD[g % 2][1]), writes=(b_bank[0],))
            ctx.op("dve", lambda E: E.tensor_copy(t96D[0], PS[0:96, 0:512]),
                   reads=(b_bank[0],), writes=(t96D[1],))
            ctx.dma_cond("dpool",
                    lambda E, hh, tsl=tsl: E.dma_start(
                        out=sh["XD"][hh, :, tsl],
                        in_=t96D[0]),
                    reads=(t96D[1],), writes=(b_sh["XD"],))
        for tt in range(8):
            tsl = slice(tt * 512, (tt + 1) * 512)
            for dc in range(8):
                xb = x1mtile[dc % 2][1]
                e_load(ctx, x1mtile[dc % 2][0],
                       X1Mr[dc][:, tt * 512:(tt + 1) * 512], xb,
                       srcbufs=(b_sh["X1M"],), q="dsp")
                ctx.group("pe",
                          [(lambda E, dc=dc, g=g: E.matmul(
                              bank(g), wrA[0][:, dc * 1024 + g * 128:
                                              dc * 1024 + g * 128 + 128],
                              x1mtile[dc % 2][0],
                              start=(dc == 0), stop=(dc == 7)))
                           for g in range(8)],
                          reads=(wrA[1], xb),
                          writes=tuple(b_bank[g] for g in range(8)))
            for g in range(8):
                ctx.op("act",
                       lambda E, g=g: E.activation(
                           svsegD[g % 2][0], bank(g), AF.Silu),
                       reads=(b_bank[g],), writes=(svsegD[g % 2][1],))
                e_store(ctx, svTdr[g][:, tsl], svsegD[g % 2][0],
                        svsegD[g % 2][1], dstbufs=(b_d["svT"],))
        send_flag(4, (b_sh["XD"],))

        AR.reset()
        poll(4)
        xdt_d = dt_("xdt_d", [DTR, S], BF16, kind="Internal").ap()
        b_d["xdt"] = ctx.buf("xdt")
        xm1 = AR.take(96, 4096, F32)
        xm2 = AR.take(96, 4096, F32)
        xdbE = AR.take(64, 4096, BF16)
        e_load(ctx, xm1[0], sh["XD"][0], xm1[1],
               srcbufs=(b_phase[4], b_sh["XD"]), q="dsp")
        e_load(ctx, xm2[0], sh["XD"][1], xm2[1],
               srcbufs=(b_phase[4], b_sh["XD"]), q="dsp")
        ctx.op("dve", lambda E: E.tensor_tensor(xm1[0], xm1[0], xm2[0], ALU.add),
               reads=(xm1[1], xm2[1]), writes=(xm1[1],))
        ctx.op("dve", lambda E: E.tensor_copy(xdbE[0], xm1[0][0:64, :]),
               reads=(xm1[1],), writes=(xdbE[1],))
        e_store(ctx, xdt_d[:], xdbE[0], xdbE[1], dstbufs=(b_d["xdt"],))
        xbcE = AR.take(32, 4096, BF16)
        ctx.op("dve", lambda E: E.tensor_copy(xbcE[0], xm1[0][64:96, :]),
               reads=(xm1[1],), writes=(xbcE[1],))
        e_store(ctx, xbc_d[:], xbcE[0], xbcE[1], dstbufs=(b_d["xbc"],))

        AR.reset()
        delta_d = dt_("delta_d", [DM, S], BF16, kind="Internal").ap()
        b_d["delta"] = ctx.buf("deltad")
        A_p = al("A_p", [128, 8 * NST], F32).ap()
        b_Ap = ctx.buf("A_p")
        sp1E = AR.take(128, 4096, F32)
        tmpE = AR.take(128, 4096, BF16)
        dchE = AR.take(128, 4096, BF16)
        wdtE = AR.take(64, 1024, BF16)
        xdtE = [AR.take(64, 512, BF16) for _ in range(2)]
        e_load(ctx, wdtE[0], wdt, wdtE[1])
        ctx.op("act", lambda E: E.activation(A_p[:], smalls["Alog"][:], AF.Exp),
               reads=(b_small["Alog"],), writes=(b_Ap,))
        ctx.op("dve", lambda E: E.tensor_scalar(A_p[:], A_p[:], -1.0, None,
                                                ALU.mult),
               reads=(b_Ap,), writes=(b_Ap,))
        c_one = ctx.constcol(1.0)
        delta_dr = delta_d.rearrange("(c p) t -> c p t", p=128)
        for g in range(8):
            for tt in range(8):
                pb = tt % 2
                e_load(ctx, xdtE[tt % 2][0], xdt_d[:, tt * 512:(tt + 1) * 512],
                       xdtE[tt % 2][1], srcbufs=(b_d["xdt"],), q="dsp")
                ctx.group("pe",
                          [(lambda E, g=g, tt=tt, pb=pb: E.matmul(
                              bank(pb), wdtE[0][:, g * 128:(g + 1) * 128],
                              xdtE[tt % 2][0],
                              start=True, stop=True))],
                          reads=(wdtE[1], xdtE[tt % 2][1]), writes=(b_bank[pb],))
                ctx.op("dve",
                       lambda E, g=g, tt=tt, pb=pb: E.tensor_scalar(
                           sp1E[0][:, tt * 512:(tt + 1) * 512], bank(pb),
                           smalls["bdt"][:, g:g + 1], None, ALU.add),
                       reads=(b_bank[pb], b_small["bdt"]), writes=(sp1E[1],))
            ctx.op("act", lambda E: E.activation(tmpE[0], sp1E[0], AF.Abs),
                   reads=(sp1E[1],), writes=(tmpE[1],))
            ctx.op("act", lambda E: E.activation(tmpE[0], tmpE[0], AF.Exp,
                                                 scale=-1.0),
                   reads=(tmpE[1],), writes=(tmpE[1],))
            ctx.op("act", lambda E: E.activation(tmpE[0], tmpE[0], AF.Ln,
                                                 bias=c_one),
                   reads=(tmpE[1], ctx._cbuf), writes=(tmpE[1],))
            ctx.op("dve", lambda E: E.tensor_scalar(sp1E[0], sp1E[0], 0.0, None,
                                                    ALU.max),
                   reads=(sp1E[1],), writes=(sp1E[1],))
            ctx.op("dve",
                   lambda E: E.tensor_tensor(dchE[0], sp1E[0], tmpE[0], ALU.add),
                   reads=(sp1E[1], tmpE[1]), writes=(dchE[1],))
            e_store(ctx, delta_dr[g], dchE[0], dchE[1], dstbufs=(b_d["delta"],))

        AR.reset()
        dgE = [AR.take(128, 4096, BF16) for _ in range(2)]
        ugE = [AR.take(128, 4096, BF16) for _ in range(2)]
        duE = AR.take(128, 4096, BF16)
        dAE = [AR.take(128, 4096, BF16) for _ in range(2)]
        dBuE = [AR.take(128, 4096, BF16) for _ in range(2)]
        BbE = [AR.take(128, 4096, BF16) for _ in range(2)]
        CbE = [AR.take(128, 4096, BF16) for _ in range(2)]
        hsE2 = [AR.take(128, 4096, BF16) for _ in range(2)]
        tmp2E = [AR.take(128, 4096, BF16) for _ in range(2)]
        ygE = AR.take(128, 4096, F32)
        svE = AR.take(128, 4096, BF16)
        youtE = AR.take(128, 4096, BF16)
        ydr = y_d.rearrange("(c p) t -> c p t", p=128)

        def load_bc(n, nb):
            e_load(ctx, BbE[nb][0], xbc_d[n:n + 1, :].to_broadcast([128, S]),
                   BbE[nb][1], srcbufs=(b_d["xbc"],), q="dsp")
            e_load(ctx, CbE[nb][0],
                   xbc_d[NST + n:NST + n + 1, :].to_broadcast([128, S]),
                   CbE[nb][1], srcbufs=(b_d["xbc"],), q="dact")

        def emit_dA(g, n, nb, gb):
            ctx.op("act",
                   lambda E, g=g, n=n, nb=nb, gb=gb: E.activation(
                       dAE[nb][0], dgE[gb][0], AF.Exp,
                       scale=A_p[:, g * NST + n: g * NST + n + 1]),
                   reads=(dgE[gb][1], b_Ap), writes=(dAE[nb][1],))

        def emit_dBu(nb):
            ctx.op("pool",
                   lambda E, nb=nb: E.tensor_tensor(dBuE[nb][0], duE[0],
                                                    BbE[nb][0], ALU.mult),
                   reads=(duE[1], BbE[nb][1]), writes=(dBuE[nb][1],))

        e_load(ctx, dgE[0][0], delta_dr[0], dgE[0][1], srcbufs=(b_d["delta"],))
        e_load(ctx, ugE[0][0], uTdr[0], ugE[0][1], srcbufs=(b_d["uT"],))
        for g in range(8):
            gb = g % 2
            if g + 1 < 8:
                e_load(ctx, dgE[1 - gb][0], delta_dr[g + 1], dgE[1 - gb][1],
                       srcbufs=(b_d["delta"],))
                e_load(ctx, ugE[1 - gb][0], uTdr[g + 1], ugE[1 - gb][1],
                       srcbufs=(b_d["uT"],))
            ctx.op("dve",
                   lambda E, gb=gb: E.tensor_tensor(duE[0], dgE[gb][0],
                                                    ugE[gb][0], ALU.mult),
                   reads=(dgE[gb][1], ugE[gb][1]), writes=(duE[1],))
            load_bc(0, 0)
            emit_dA(g, 0, 0, gb)
            emit_dBu(0)
            for n in range(NST):
                nb = n % 2
                if n + 1 < NST:
                    load_bc(n + 1, 1 - nb)
                    emit_dA(g, n + 1, 1 - nb, gb)
                    emit_dBu(1 - nb)
                hsE = hsE2[n % 2]
                ctx.op("dve",
                       lambda E, nb=nb, hsE=hsE: E.tensor_tensor_scan(
                           hsE[0], dAE[nb][0], dBuE[nb][0], 0.0,
                           ALU.mult, ALU.add),
                       reads=(dAE[nb][1], dBuE[nb][1]), writes=(hsE[1],))
                ctx.op("pool" if n % 2 == 0 else "dve",
                       lambda E, nb=nb, hsE=hsE: E.tensor_tensor(
                           tmp2E[nb][0], hsE[0], CbE[nb][0], ALU.mult),
                       reads=(hsE[1], CbE[nb][1]), writes=(tmp2E[nb][1],))
                if n == 0:
                    ctx.op("dve", lambda E: E.tensor_copy(ygE[0], tmp2E[0][0]),
                           reads=(tmp2E[0][1],), writes=(ygE[1],))
                else:
                    ctx.op("pool" if (n % 4 == 1 or n == 2) else "dve",
                           lambda E, nb=nb: E.tensor_tensor(ygE[0], ygE[0],
                                                            tmp2E[nb][0],
                                                            ALU.add),
                           reads=(tmp2E[nb][1], ygE[1]), writes=(ygE[1],))
            e_load(ctx, svE[0], svTdr[g], svE[1], srcbufs=(b_d["svT"],), q="dsp")
            ctx.op("dve",
                   lambda E, g=g, gb=gb: E.scalar_tensor_tensor(
                       ygE[0], ugE[gb][0], smalls["Dp"][:, g:g + 1], ygE[0],
                       ALU.mult, ALU.add),
                   reads=(ugE[gb][1], b_small["Dp"], ygE[1]), writes=(ygE[1],))
            ctx.op("pool",
                   lambda E: E.tensor_tensor(youtE[0], ygE[0], svE[0], ALU.mult),
                   reads=(ygE[1], svE[1]), writes=(youtE[1],))
            e_store(ctx, ydr[g], youtE[0], youtE[1], dstbufs=(b_d["y"],))

        AR.reset()
        woutE = AR.take(128, 8192, BF16)
        ytE = [AR.take(128, 512, BF16) for _ in range(2)]
        ptE = [AR.take(128, 512, BF16) for _ in range(2)]
        e_load(ctx, woutE[0].rearrange("p (c f) -> p c f", c=8),
               wout.rearrange("(c p) f -> p c f", p=128), woutE[1])
        for tt in range(8):
            tsl = slice(tt * 512, (tt + 1) * 512)
            for g in range(8):
                gb = g % 2
                e_load(ctx, ytE[gb][0], ydr[g][:, tsl], ytE[gb][1],
                       srcbufs=(b_d["y"],), q="dsp")
                ctx.group("pe",
                          [(lambda E, g=g, fc=fc, gb=gb: E.matmul(
                              bank(fc), woutE[0][:, g * 1024 + fc * 128: g * 1024 + fc * 128 + 128],
                              ytE[gb][0],
                              start=(g == 0), stop=(g == 7)))
                           for fc in range(8)],
                          reads=(woutE[1], ytE[gb][1]),
                          writes=tuple(b_bank[fc] for fc in range(8)))
            for fc in range(8):
                pb = fc % 2
                ctx.op("dve",
                       lambda E, fc=fc, pb=pb: E.tensor_copy(ptE[pb][0],
                                                             bank(fc)),
                       reads=(b_bank[fc],), writes=(ptE[pb][1],))
                ctx.dma_cond("dpool",
                        lambda E, hh, fc=fc, tsl=tsl, w=pb: E.dma_start(
                            out=sh["PO"][hh].rearrange(
                                "(c p) t -> c p t", p=128)[fc][:, tsl],
                            in_=ptE[w][0]),
                        reads=(ptE[pb][1],), writes=(b_sh["PO"],))
        send_flag(5, (b_sh["PO"],))

        AR.reset()
        poll(5)
        sAF = AR.take(128, 4096, F32)
        sBF = AR.take(128, 4096, F32)
        t1F = [AR.take(128, 512, BF16) for _ in range(2)]
        lgF = AR.take(8, 2048, F32)
        pb8F = AR.take(128, 8, F32)
        wtokF = AR.take(128, 8, F32)
        wTF = AR.take(8, 2048, F32)
        scr1F = AR.take(128, 1, F32)
        scr2F = AR.take(128, 1, F32)
        gwF = AR.take(128, 64, F32)
        e_load(ctx, gwF[0].rearrange("p (c f) -> p c f", c=8),
               gw.rearrange("(c p) f -> p c f", p=128), gwF[1])
        x3hdr = x3h_d.rearrange("(c p) t -> c p t", p=128)
        POr0 = sh["PO"][0].rearrange("(c p) t -> p c t", p=128)
        POr1 = sh["PO"][1].rearrange("(c p) t -> p c t", p=128)
        for tt in range(4):
            tsl = slice(tt * 512, (tt + 1) * 512)
            e_load(ctx, sAF[0].rearrange("p (c t) -> p c t", c=8),
                   x2hdr[:, :, tsl].rearrange("c p t -> p c t"), sAF[1],
                   srcbufs=(b_d["x2h"],), q="dsp")
            e_load(ctx, sBF[0].rearrange("p (c t) -> p c t", c=8),
                   x1mdr[:, :, tsl].rearrange("c p t -> p c t"), sBF[1],
                   srcbufs=(b_d["x1mh"],), q="dact")
            for fc in range(8):
                ctx.op("dve",
                       lambda E, fc=fc: E.tensor_tensor(
                           sAF[0][:, fc * 512:(fc + 1) * 512],
                           sAF[0][:, fc * 512:(fc + 1) * 512],
                           sBF[0][:, fc * 512:(fc + 1) * 512], ALU.add),
                       reads=(sAF[1], sBF[1]), writes=(sAF[1],))
            for po in (POr0, POr1):
                ctx.dma_cond("dpool",
                        lambda E, hh, tt=tt, po=po: E.dma_start(
                            out=sBF[0].rearrange("p (c t) -> p c t", c=8).bitcast(BF16)[:, :, 0:512],
                            in_=po[:, :, hh * TH + tt * 512: hh * TH + (tt + 1) * 512]),
                        reads=(b_phase[5], b_sh["PO"]), writes=(sBF[1],))
                for fc in range(8):
                    ctx.op("dve",
                           lambda E, fc=fc: E.tensor_tensor(
                               sAF[0][:, fc * 512:(fc + 1) * 512],
                               sAF[0][:, fc * 512:(fc + 1) * 512],
                               sBF[0].rearrange("p (c t) -> p c t", c=8).bitcast(
                                   BF16)[:, fc, 0:512],
                               ALU.add),
                           reads=(sAF[1], sBF[1]), writes=(sAF[1],))
            sAt = [sAF[0][:, fc * 512:(fc + 1) * 512] for fc in range(8)]

            def applyF1(bcA, bcB, b_bc, tt=tt, tsl=tsl, sAt=sAt):
                for fc in range(8):
                    ctx.op("dve",
                           lambda E, fc=fc, bcA=bcA: E.tensor_tensor(
                               sAt[fc], sAt[fc], bcA, ALU.mult),
                           reads=(sAF[1], b_bc), writes=(sAF[1],))
                    ctx.op("dve",
                           lambda E, fc=fc, bcB=bcB: E.tensor_tensor(
                               sAt[fc], sAt[fc], bcB, ALU.subtract),
                           reads=(sAF[1], b_bc), writes=(sAF[1],))
                    ctx.op("dve",
                           lambda E, fc=fc: E.tensor_copy(t1F[fc % 2][0],
                                                          sAt[fc]),
                           reads=(sAF[1],), writes=(t1F[fc % 2][1],))
                    e_store(ctx, x3hdr[fc, :, tsl], sAt[fc], sAF[1],
                            dstbufs=(b_d["x3h"],))
                    ctx.dma_cond("dpool",
                            lambda E, hh, fc=fc, tt=tt, w=fc % 2: E.dma_start(
                                out=sh["X3"].rearrange("(c p) t -> c p t", p=128)[fc][
                                    :, hh * TH + tt * 512: hh * TH + (tt + 1) * 512],
                                in_=t1F[w][0]),
                            reads=(t1F[fc % 2][1],), writes=(b_sh["X3"],))

            ln_block(ctx, scr, sAt, [sAF[1]] * 8, [applyF1], 512, LN_EPS,
                     ones_col, ones_row, b_ones, bank(0), bank(1),
                     b_bank[0], b_bank[1])
            ctx.group("pe",
                      [(lambda E, dc=dc: E.matmul(
                          PS[0:8, 1024:1536], gwF[0][:, dc * 8:(dc + 1) * 8],
                          sAF[0][:, dc * 512:(dc + 1) * 512],
                          start=(dc == 0), stop=(dc == 7)))
                       for dc in range(8)],
                      reads=(gwF[1], sAF[1]), writes=(b_bank[2],))
            ctx.op("act",
                   lambda E, tsl=tsl: E.activation(
                       lgF[0][:, tsl], PS[0:8, 1024:1536], AF.Identity,
                       bias=smalls["gb"][:, 0:1]),
                   reads=(b_bank[2], b_small["gb"]), writes=(lgF[1],))
            for qb in range(4):
                t0 = tt * 512 + qb * 128
                ctx.group("pe",
                          [lambda E, t0=t0: E.transpose(
                              PS[0:128, 1536:1544],
                              lgF[0][:, t0:t0 + 128], id_s[0:8, 0:8])],
                          reads=(lgF[1], b_id), writes=(b_bank[3],))
                ctx.op("dve",
                       lambda E: E.reduce_max(scr1F[0], PS[:, 1536:1544],
                                              axis=AX.X),
                       reads=(b_bank[3],), writes=(scr1F[1],))
                ctx.op("dve",
                       lambda E: E.tensor_scalar(scr2F[0], scr1F[0], -1.0, None,
                                                 ALU.mult),
                       reads=(scr1F[1],), writes=(scr2F[1],))
                ctx.op("act",
                       lambda E: E.activation(pb8F[0], PS[:, 1536:1544], AF.Exp,
                                              bias=scr2F[0], accum_out=scr1F[0]),
                       reads=(b_bank[3], scr2F[1]), writes=(pb8F[1], scr1F[1]))
                ctx.op("dve", lambda E: E.reciprocal(scr2F[0], scr1F[0]),
                       reads=(scr1F[1],), writes=(scr2F[1],))
                ctx.op("dve",
                       lambda E: E.tensor_scalar(pb8F[0], pb8F[0], scr2F[0],
                                                 None, ALU.mult),
                       reads=(pb8F[1], scr2F[1]), writes=(pb8F[1],))
                ctx.op("dve",
                       lambda E: E.reduce_max(scr1F[0], pb8F[0], axis=AX.X),
                       reads=(pb8F[1],), writes=(scr1F[1],))
                ctx.op("dve",
                       lambda E: E.tensor_scalar(wtokF[0], pb8F[0], scr1F[0],
                                                 None, ALU.is_ge),
                       reads=(pb8F[1], scr1F[1]), writes=(wtokF[1],))
                ctx.op("dve",
                       lambda E: E.tensor_tensor(wtokF[0], wtokF[0], pb8F[0],
                                                 ALU.mult),
                       reads=(wtokF[1], pb8F[1]), writes=(wtokF[1],))
                ctx.group("pe",
                          [lambda E: E.transpose(PS[0:8, 1536:1664], wtokF[0],
                                                 id_s)],
                          reads=(wtokF[1], b_id), writes=(b_bank[3],))
                ctx.op("dve",
                       lambda E, t0=t0: E.tensor_copy(wTF[0][:, t0:t0 + 128],
                                                      PS[0:8, 1536:1664]),
                       reads=(b_bank[3],), writes=(wTF[1],))
        ctx.dma_cond("dpool",
                lambda E, hh: E.dma_start(
                    out=sh["WG"][:, hh * TH:(hh + 1) * TH], in_=wTF[0]),
                reads=(wTF[1],), writes=(b_sh["WG"],))
        send_flag(6, (b_sh["X3"], b_sh["WG"]))

        AR.reset()
        poll(6)
        x3tb = AR.take(128, 8192, BF16)
        hbF = AR.take(128, 32 * 1024, BF16)
        oaF = AR.take(128, 8 * 1024, F32)
        wb1F = [AR.take(128, 1024, BF16) for _ in range(2)]
        wb2F = [AR.take(128, 4096, BF16) for _ in range(2)]
        wBF = AR.take(128, 1024, F32)
        ytF = [AR.take(128, 512, BF16) for _ in range(2)]
        tmpF = AR.take(128, 512, F32)
        moF = [AR.take(128, 512, BF16) for _ in range(2)]
        X3r = sh["X3"].rearrange("(c p) t -> p c t", p=128)
        w1er = w1eb_d.rearrange("e (c p) f -> e p c f", p=128)
        w2er = w2eb_d.rearrange("e (c p) f -> e p c f", p=128)
        for tc in range(4):
            csl = slice(tc * 1024, (tc + 1) * 1024)
            e_load(ctx, x3tb[0].rearrange("p (c t) -> p c t", c=8),
                   X3r[:, :, csl], x3tb[1],
                   srcbufs=(b_phase[6], b_sh["X3"]), q="dsp")
            for e in range(4):
                ctx.dma_cond("dpool",
                        lambda E, hh, e=e, csl=csl: E.dma_start(
                            out=wBF[0],
                            in_=sh["WG"][hh * 4 + e: hh * 4 + e + 1, csl].to_broadcast(
                                [128, 1024])),
                        reads=(b_phase[6], b_sh["WG"]), writes=(wBF[1],))
                for ft in range(32):
                    wb = ft % 2
                    e_load(ctx, wb1F[wb][0].rearrange("p (c f) -> p c f", c=8),
                           w1er[e, :, :, ft * 128:(ft + 1) * 128], wb1F[wb][1],
                           srcbufs=(b_wcast,), q="dsp")
                    for sub in range(2):
                        pb = sub
                        ctx.group("pe",
                                  [(lambda E, dc=dc, wb=wb, sub=sub, pb=pb:
                                    E.matmul(
                                        bank(pb),
                                        wb1F[wb][0][:, dc * 128:(dc + 1) * 128],
                                        x3tb[0][:, dc * 1024 + sub * 512: dc * 1024 + (sub + 1) * 512],
                                        start=(dc == 0), stop=(dc == 7)))
                                   for dc in range(8)],
                                  reads=(wb1F[wb][1], x3tb[1]),
                                  writes=(b_bank[pb],))
                        ctx.op("act",
                               lambda E, ft=ft, sub=sub, e=e, pb=pb: E.activation(
                                   hbF[0][:, ft * 1024 + sub * 512: ft * 1024 + (sub + 1) * 512],
                                   bank(pb), AF.Gelu,
                                   bias=smalls["b1e"][:, e * 32 + ft: e * 32 + ft + 1]),
                               reads=(b_bank[pb], b_small["b1e"]), writes=(hbF[1],))
                for fc in range(8):
                    wb = fc % 2
                    e_load(ctx, wb2F[wb][0].rearrange("p (c f) -> p c f", c=32),
                           w2er[e, :, :, fc * 128:(fc + 1) * 128], wb2F[wb][1],
                           srcbufs=(b_wcast,), q="dsp")
                    for sub in range(2):
                        pb = 2 + sub
                        ctx.group("pe",
                                  [(lambda E, dc=dc, wb=wb, sub=sub, pb=pb: E.matmul(
                                      bank(pb),
                                      wb2F[wb][0][:, dc * 128:(dc + 1) * 128],
                                      hbF[0][:, dc * 1024 + sub * 512: dc * 1024 + (sub + 1) * 512],
                                      start=(dc == 0), stop=(dc == 31)))
                                   for dc in range(32)],
                                  reads=(wb2F[wb][1], hbF[1]), writes=(b_bank[pb],))
                        yb = sub
                        ctx.op("act",
                               lambda E, fc=fc, e=e, pb=pb, yb=yb: E.activation(
                                   ytF[yb][0], bank(pb), AF.Identity,
                                   bias=smalls["b2e"][:, e * 8 + fc: e * 8 + fc + 1]),
                               reads=(b_bank[pb], b_small["b2e"]),
                               writes=(ytF[yb][1],))
                        ctx.op("dve",
                               lambda E, sub=sub, yb=yb: E.tensor_tensor(
                                   tmpF[0], ytF[yb][0],
                                   wBF[0][:, sub * 512:(sub + 1) * 512], ALU.mult),
                               reads=(ytF[yb][1], wBF[1]), writes=(tmpF[1],))
                        if e == 0:
                            ctx.op("dve",
                                   lambda E, fc=fc, sub=sub: E.tensor_copy(
                                       oaF[0][:, fc * 1024 + sub * 512: fc * 1024 + (sub + 1) * 512],
                                       tmpF[0]),
                                   reads=(tmpF[1],), writes=(oaF[1],))
                        else:
                            ctx.op("dve",
                                   lambda E, fc=fc, sub=sub: E.tensor_tensor(
                                       oaF[0][:, fc * 1024 + sub * 512: fc * 1024 + (sub + 1) * 512],
                                       oaF[0][:, fc * 1024 + sub * 512: fc * 1024 + (sub + 1) * 512],
                                       tmpF[0], ALU.add),
                                   reads=(tmpF[1], oaF[1]), writes=(oaF[1],))
            for fc in range(8):
                for sub in range(2):
                    mb = sub
                    ctx.op("dve",
                           lambda E, fc=fc, sub=sub, mb=mb: E.tensor_copy(
                               moF[mb][0],
                               oaF[0][:, fc * 1024 + sub * 512: fc * 1024 + (sub + 1) * 512]),
                           reads=(oaF[1],), writes=(moF[mb][1],))
                    ctx.dma_cond("dpool",
                            lambda E, hh, fc=fc, tc=tc, sub=sub, mb=mb: E.dma_start(
                                out=sh["MO"][hh].rearrange(
                                    "(c p) t -> c p t", p=128)[fc][
                                    :, tc * 1024 + sub * 512: tc * 1024 + (sub + 1) * 512],
                                in_=moF[mb][0]),
                            reads=(moF[mb][1],), writes=(b_sh["MO"],))
        send_flag(7, (b_sh["MO"],))

        AR.reset()
        poll(7)
        sAG = AR.take(128, 4096, F32)
        sBG = AR.take(128, 4096, BF16)
        MOr0 = sh["MO"][0].rearrange("(c p) t -> p c t", p=128)
        MOr1 = sh["MO"][1].rearrange("(c p) t -> p c t", p=128)
        outTr = outT.rearrange("(c p) t -> c p t", p=128)
        for tt in range(4):
            tsl = slice(tt * 512, (tt + 1) * 512)
            e_load(ctx, sAG[0].rearrange("p (c t) -> p c t", c=8),
                   x3hdr[:, :, tsl].rearrange("c p t -> p c t"), sAG[1],
                   srcbufs=(b_d["x3h"],), q="dsp")
            for po in (MOr0, MOr1):
                ctx.dma_cond("dpool",
                        lambda E, hh, tt=tt, po=po: E.dma_start(
                            out=sBG[0].rearrange("p (c t) -> p c t", c=8),
                            in_=po[:, :, hh * TH + tt * 512: hh * TH + (tt + 1) * 512]),
                        reads=(b_phase[7], b_sh["MO"]), writes=(sBG[1],))
                for fc in range(8):
                    ctx.op("dve",
                           lambda E, fc=fc: E.tensor_tensor(
                               sAG[0][:, fc * 512:(fc + 1) * 512],
                               sAG[0][:, fc * 512:(fc + 1) * 512],
                               sBG[0][:, fc * 512:(fc + 1) * 512], ALU.add),
                           reads=(sAG[1], sBG[1]), writes=(sAG[1],))
            sGt = [sAG[0][:, fc * 512:(fc + 1) * 512] for fc in range(8)]

            def applyG(bcA, bcB, b_bc, tsl=tsl, sGt=sGt):
                for fc in range(8):
                    ctx.op("dve",
                           lambda E, fc=fc, bcA=bcA: E.tensor_tensor(
                               sGt[fc], sGt[fc], bcA, ALU.mult),
                           reads=(sAG[1], b_bc), writes=(sAG[1],))
                    ctx.op("dve",
                           lambda E, fc=fc, bcB=bcB: E.tensor_tensor(
                               sGt[fc], sGt[fc], bcB, ALU.subtract),
                           reads=(sAG[1], b_bc), writes=(sAG[1],))
                    e_store(ctx, outTr[fc, :, tsl], sGt[fc], sAG[1],
                            dstbufs=(b_out,))

            ln_block(ctx, scr, sGt, [sAG[1]] * 8, [applyG], 512, LN_EPS,
                     ones_col, ones_row, b_ones, bank(0), bank(1),
                     b_bank[0], b_bank[1])

        e_store(ctx, pdiag, sdiag, b_sdiag, dstbufs=(b_diagout,))
        ctx.barrier_all("sp")
        import collections
        qc = collections.Counter(k[0] for k in getattr(ctx, "_slotmap", {}))
        print("FUSED: dma dest bufs per queue:", dict(qc),
              "instrs:", {e: len(v) for e, v in ctx.ops.items()},
              "arena peaks:", getattr(AR, "peaks", []))
        ctx.run()
    return nc

_CACHE = {}


def _get(name, builder):
    if name not in _CACHE:
        _CACHE[name] = builder()
    return _CACHE[name]


def _prep_fused(inputs):
    x = np.asarray(inputs["L0_attn_x"])
    Wqkv = np.asarray(inputs["L0_attn_W_qkv"]).reshape(16, 64, 3, 1024)
    bqkv0 = np.asarray(inputs["L0_attn_b_qkv"]).reshape(16, 64, 3)
    wqkv1 = np.asarray(inputs["L1_attn_wqkv"])
    bqkv1 = np.asarray(inputs["L1_attn_bqkv"])
    w_in = np.asarray(inputs["L1_attn_w_in"])
    cw = np.asarray(inputs["L1_attn_conv_w"])
    cb = np.asarray(inputs["L1_attn_conv_b"])
    wx = np.asarray(inputs["L1_attn_w_xproj"])
    wdtv = np.asarray(inputs["L1_attn_w_dt"])
    bdtv = np.asarray(inputs["L1_attn_b_dt"])
    Alogv = np.asarray(inputs["L1_attn_A_log"])
    Dpv = np.asarray(inputs["L1_attn_Dp"])
    woutv = np.asarray(inputs["L1_attn_w_out"])
    w1 = np.asarray(inputs["L1_ffn_w1"])
    b1 = np.asarray(inputs["L1_ffn_b1"])
    w2 = np.asarray(inputs["L1_ffn_w2"])
    b2 = np.asarray(inputs["L1_ffn_b2"])
    mkq = _mk_masks_kq()
    mqk = _mk_masks_qk()
    idm = np.eye(128, dtype=np.float32)
    rng = np.random.default_rng()
    nonce = int(rng.integers(1, 2**28))
    fvals = np.arange(nonce, nonce + 16, dtype=np.int32).reshape(1, 16)
    com = {
        "woT": _t(np.asarray(inputs["L0_attn_W_o"])),
        "bo": np.ascontiguousarray(np.asarray(inputs["L0_attn_b_o"]).reshape(8, 128).T),
        "w1T": _t(inputs["L0_ffn_w1_w"]), "w2T": _t(inputs["L0_ffn_w2_w"]),
        "h1T": _t(inputs["L0_ffn_h1_w"]), "h2T": _t(inputs["L0_ffn_h2_w"]),
        "w3T": _t(inputs["L0_ffn_w3_w"]),
        "s1": np.ascontiguousarray(np.asarray(inputs["L0_ffn_w1_s"]).reshape(32, 128).T),
        "s2": np.ascontiguousarray(np.asarray(inputs["L0_ffn_w2_s"]).reshape(32, 128).T),
        "sh1": np.ascontiguousarray(np.asarray(inputs["L0_ffn_h1_s"]).reshape(32, 128).T),
        "sh2": np.ascontiguousarray(np.asarray(inputs["L0_ffn_h2_s"]).reshape(32, 128).T),
        "s3": np.ascontiguousarray(np.asarray(inputs["L0_ffn_w3_s"]).reshape(8, 128).T),
        "wkv": np.ascontiguousarray(wqkv1[:, 1024:1152]),
        "bkv": bqkv1[1024:1152].reshape(128, 1).copy(),
        "wao": np.ascontiguousarray(np.asarray(inputs["L1_attn_w_attn_out"])),
        "bao": np.ascontiguousarray(np.asarray(inputs["L1_attn_b_attn_out"]).reshape(8, 128).T),
        "masksL0": mkq, "masksQK": mqk, "masksKQ": mkq, "ident": idm,
        "gw": np.ascontiguousarray(np.asarray(inputs["L1_ffn_gate_w"])),
        "gb": np.asarray(inputs["L1_ffn_gate_b"]).reshape(8, 1).copy(),
        "fv": fvals, "fv2": fvals * 2,
    }
    maps = []
    for c in range(8):
        b, hh = c // 2, c % 2
        hsel = slice(hh * 8, hh * 8 + 8)
        csl = slice(hh * 1024, (hh + 1) * 1024)
        esl = slice(hh * 4, (hh + 1) * 4)
        m = dict(com)
        xTb = _t(x[b])
        m["xT"] = xTb
        m["xTh"] = np.ascontiguousarray(xTb[:, hh * TH:(hh + 1) * TH])
        m["wqT"] = _t(Wqkv[hsel, :, 0, :].reshape(512, 1024))
        m["wkT"] = _t(Wqkv[hsel, :, 1, :].reshape(512, 1024))
        m["wvT"] = _t(Wqkv[hsel, :, 2, :].reshape(512, 1024))
        m["bq"] = np.ascontiguousarray(bqkv0[hsel, :, 0].reshape(4, 128).T)
        m["bk"] = np.ascontiguousarray(bqkv0[hsel, :, 1].reshape(4, 128).T)
        m["bv"] = bqkv0[hsel, :, 2].reshape(1, 512).copy()
        m["wq1"] = np.ascontiguousarray(wqkv1[:, hh * 512:(hh + 1) * 512])
        m["bqm"] = np.ascontiguousarray(bqkv1[hh * 512:(hh + 1) * 512].reshape(4, 128).T)
        m["w_u"] = np.ascontiguousarray(w_in[:, csl])
        m["w_r"] = np.ascontiguousarray(w_in[:, 2048 + hh * 1024: 2048 + (hh + 1) * 1024])
        m["convw"] = np.ascontiguousarray(cw[csl].reshape(8, 128, 4).transpose(1, 0, 2).reshape(128, 32))
        m["convb"] = np.ascontiguousarray(cb[csl].reshape(8, 128).T)
        m["wxp"] = np.ascontiguousarray(wx[csl])
        m["wdt"] = np.ascontiguousarray(wdtv[:, csl])
        m["bdt"] = np.ascontiguousarray(bdtv[csl].reshape(8, 128).T)
        m["Alog"] = np.ascontiguousarray(
            Alogv[csl].reshape(8, 128, NST).transpose(1, 0, 2).reshape(128, 8 * NST))
        m["Dp"] = np.ascontiguousarray(Dpv[csl].reshape(8, 128).T)
        m["wout"] = np.ascontiguousarray(woutv[csl])
        m["w1e"] = np.ascontiguousarray(w1[esl])
        m["b1e"] = np.ascontiguousarray(b1[esl].reshape(4, 32, 128).transpose(2, 0, 1).reshape(128, 128))
        m["w2e"] = np.ascontiguousarray(w2[esl])
        m["b2e"] = np.ascontiguousarray(b2[esl].reshape(4, 8, 128).transpose(2, 0, 1).reshape(128, 32))
        maps.append(m)
    return maps


FUSED = True


def kernel_fused(**inputs):
    ncX = _get("FUSED", build_fused)
    r = run_bass_kernel_spmd(ncX, _prep_fused(inputs),
                             core_ids=list(range(8))).results
    halves = []
    for b in range(4):
        halves.append(np.concatenate([r[2 * b]["outT"], r[2 * b + 1]["outT"]],
                                     axis=1))
    return np.ascontiguousarray(np.stack([h.T for h in halves]))


def kernel(**inputs):
    if FUSED:
        return kernel_fused(**inputs)
    cores = list(range(8))

    ncA = _get("A", build_launchA)
    rA = run_bass_kernel_spmd(ncA, _prep_launchA(inputs), core_ids=cores).results
    zT_full = np.stack([np.concatenate([rA[2 * b]["zT"], rA[2 * b + 1]["zT"]], 0)
                        for b in range(4)])

    ncB = _get("B", build_launchB)
    rB = run_bass_kernel_spmd(ncB, _prep_launchB(inputs, zT_full),
                              core_ids=cores).results
    x2f = np.concatenate([rB[c]["outT"] for c in range(8)], axis=1)
    xT2 = np.stack([np.ascontiguousarray(x2f[:, b * S:(b + 1) * S])
                    for b in range(4)])

    ncC = _get("C", build_launchC)
    rC = run_bass_kernel_spmd(ncC, _prep_launchC(inputs, xT2),
                              core_ids=cores).results
    zT1 = np.stack([np.concatenate([rC[2 * b]["zT"], rC[2 * b + 1]["zT"]], 0)
                    for b in range(4)])

    ncD = _get("D", build_launchD)
    rD = run_bass_kernel_spmd(ncD, _prep_launchD(inputs, xT2, zT1),
                              core_ids=cores).results

    ncE = _get("E", build_launchE)
    uT_all = [rD[c]["uT"] for c in range(8)]
    svT_all = [rD[c]["svT"] for c in range(8)]
    xd_all = [rD[c]["xdbl_p"] for c in range(8)]
    rE = run_bass_kernel_spmd(ncE, _prep_launchE(inputs, uT_all, svT_all, xd_all),
                              core_ids=cores).results

    ncF = _get("F", build_launchF)
    x1T_full = np.stack([rD[2 * b]["x1T"] for b in range(4)])
    pouts = [rE[c]["poutT"] for c in range(8)]
    rF = run_bass_kernel_spmd(ncF, _prep_launchF(inputs, xT2, x1T_full, pouts),
                              core_ids=cores).results
    outf = np.concatenate([rF[c]["outT"] for c in range(8)], axis=1)
    return np.ascontiguousarray(outf.T).reshape(4, S, DM)
